# version 03
# baseline (speedup 1.0000x reference)
import contextlib
import numpy as np
import ml_dtypes
import concourse.bass as bass
import concourse.mybir as mybir
from concourse.bass_utils import run_bass_kernel_spmd
from concourse import dve_ops as _dvo
from concourse.dve_op import DveOp
from concourse.dve_spec import Spec, Src0, Src1, C0, C1, lower as _dve_lower

F32 = mybir.dt.float32
BF16 = mybir.dt.bfloat16
F16 = mybir.dt.float16
AF = mybir.ActivationFunctionType
ALU = mybir.AluOpType
AX = mybir.AxisListType

MAGIC = 12582912.0
INV_SQRT_HD = 0.08838834764831845
EPS = 1e-5

S = 4096
T = 2048
D = 2048
H = 8192
NH = 16


def _as_f(x, like):
    dt = np.float64 if np.asarray(like).dtype == np.float64 else np.float32
    return np.asarray(x, dtype=dt)


def _ref_round_ts(in0, in1, s0, s1, imm2):
    return np.round(_as_f(in0, in0) * _as_f(s0, in0))


def _ref_round_tt(in0, in1, s0, s1, imm2):
    return np.round(_as_f(in0, in0) * _as_f(in1, in0))


def _register_round_ops():
    ops = {}
    for name, spec, ref in (
        ("ROUND_MUL_TS_ANT", Spec(body=((Src0 * C0) + C1) - C1), _ref_round_ts),
        ("ROUND_MUL_TT_ANT", Spec(body=((Src0 * Src1) + C0) - C0), _ref_round_tt),
    ):
        existing = next((o for o in _dvo.OPS if o.name == name), None)
        if existing is not None:
            ops[name] = existing
            continue
        compiled = _dve_lower(spec, ver="v3", perf_en=False, name=name)
        op = DveOp(name, spec, subdim=False,
                   uops_sha={"v3": compiled.sha("v3")}, reference=ref)
        _dvo.OPS.append(op)
        _dvo._SUB_OPCODE_FOR_NAME[name] = _dvo._CUSTOM_DVE_ROW_BASE + len(_dvo.OPS) - 1
        _dvo.CUSTOM_DVE_SPECS[name] = spec
        _dvo.CUSTOM_DVE_REFS[name] = ref
        ops[name] = op
    return ops["ROUND_MUL_TS_ANT"], ops["ROUND_MUL_TT_ANT"]


ROUND_TS, ROUND_TT = _register_round_ops()

ENGINES = ("sync", "tensor", "scalar", "vector", "gpsimd")
UNTIL = 99
DEBUG_SCRATCH = False


class Sem:
    def __init__(self, h):
        self.h = h
        self.n = 0


class Phase:
    def __init__(self, nc):
        self.nc = nc
        self.q = {e: [] for e in ENGINES}

    def on(self, eng, fn):
        self.q[eng].append(fn)

    def wait(self, eng, sem, val):
        if val and val > 0:
            self.on(eng, lambda e, _s=sem.h, _v=val: e.wait_ge(_s, _v))

    def run(self, eng, method, *args, inc=None, **kw):
        v = None
        if inc is not None:
            inc.n += 1
            v = inc.n
            def _f(e, _m=method, _a=args, _k=kw, _s=inc.h):
                getattr(e, _m)(*_a, **_k).then_inc(_s, 1)
        else:
            def _f(e, _m=method, _a=args, _k=kw):
                getattr(e, _m)(*_a, **_k)
        self.on(eng, _f)
        return v

    def runs(self, eng, sem, method, *args, **kw):
        v = self.run(eng, method, *args, inc=sem, **kw)
        self.wait(eng, sem, v)
        return v

    def custom(self, op, inc=None, **kw):
        v = None
        if inc is not None:
            inc.n += 1
            v = inc.n
            def _f(e, _o=op, _k=kw, _s=inc.h):
                e._custom_dve(_o, **_k).then_inc(_s, 1)
        else:
            def _f(e, _o=op, _k=kw):
                e._custom_dve(_o, **_k)
        self.on("vector", _f)
        return v

    def dma(self, sem, dst, src, transpose=False):
        sem.n += 16
        v = sem.n
        if transpose:
            self.on("sync", lambda e, _d=dst, _s=src, _h=sem.h:
                    e.dma_start(_d, _s, transpose=True).then_inc(_h, 16))
        else:
            self.on("sync", lambda e, _d=dst, _s=src, _h=sem.h:
                    e.dma_start(_d, _s).then_inc(_h, 16))
        return v

    def emit(self):
        with self.nc.Block() as block:
            @block.sync
            def _(e):
                for fn in self.q["sync"]:
                    fn(e)

            @block.tensor
            def _(e):
                for fn in self.q["tensor"]:
                    fn(e)

            @block.scalar
            def _(e):
                for fn in self.q["scalar"]:
                    fn(e)

            @block.vector
            def _(e):
                for fn in self.q["vector"]:
                    fn(e)

            @block.gpsimd
            def _(e):
                for fn in self.q["gpsimd"]:
                    fn(e)


def _build():
    nc = bass.Bass("TRN2", target_bir_lowering=False)

    xin = nc.dram_tensor("xin", [S, D], F32, kind="ExternalInput")
    wqkv = nc.dram_tensor("wqkv", [D, 3 * D], F32, kind="ExternalInput")
    wout = nc.dram_tensor("wout", [D, D], F32, kind="ExternalInput")
    w1t = nc.dram_tensor("w1t", [D, H], F32, kind="ExternalInput")
    w2t = nc.dram_tensor("w2t", [H, D], F32, kind="ExternalInput")
    out = nc.dram_tensor("out", [T, D], F32, kind="ExternalOutput")

    sk = "ExternalOutput" if DEBUG_SCRATCH else "Internal"
    ln1s = nc.dram_tensor("ln1s", [16, S, 128], BF16, kind=sk)
    Ks = nc.dram_tensor("Ks", [D, S], BF16, kind=sk)
    Vs = nc.dram_tensor("Vs", [S, D], BF16, kind=sk)
    Qs = nc.dram_tensor("Qs", [D, T], BF16, kind=sk)
    x2s = nc.dram_tensor("x2s", [T, D], F32, kind=sk)
    z1s = nc.dram_tensor("z1s", [16, T, 128], BF16, kind=sk)
    gs = nc.dram_tensor("gs", [64, T, 128], F16, kind=sk)
    s2s = nc.dram_tensor("s2s", [1, T], F32, kind=sk)

    octx = contextlib.ExitStack()

    def mksem(name):
        return Sem(octx.enter_context(nc.semaphore(name)))

    dx = [mksem("dx0"), mksem("dx1")]
    dst = [mksem("dst0"), mksem("dst1")]
    dw = [mksem("dw0"), mksem("dw1")]
    dkst = [mksem("dkst0"), mksem("dkst1")]
    dvst = [mksem(f"dvst{i}") for i in range(4)]
    dvw = mksem("dvw")
    dfm = [mksem("dfm0"), mksem("dfm1")]
    dkh = mksem("dkh")
    dwm = mksem("dwm")
    dgst = mksem("dgst")
    dz1 = mksem("dz1")
    ds2 = [mksem("ds2a"), mksem("ds2b")]
    store_lanes = dst + dkst + dvst + [dgst, dz1] + ds2
    all_lanes = dx + dst + dw + dkst + dvst + [dvw] + dfm + [dkh, dwm, dgst, dz1] + ds2
    s_pe = Sem(octx.enter_context(nc.semaphore("s_pe")))
    s_act = Sem(octx.enter_context(nc.semaphore("s_act")))
    s_dve = Sem(octx.enter_context(nc.semaphore("s_dve")))
    s_pool = Sem(octx.enter_context(nc.semaphore("s_pool")))

    def sb(st, name, shape, dtype):
        return st.enter_context(nc.sbuf_tensor(name, shape, dtype))

    def phase_entry(ph):
        for eng in ENGINES:
            for l in all_lanes:
                ph.wait(eng, l, l.n)
            for sem in (s_pe, s_act, s_dve, s_pool):
                ph.wait(eng, sem, sem.n)

    ones_col_bf = sb(octx, "ones_col_bf", [128, 1], BF16)
    ones_col_f = sb(octx, "ones_col_f", [128, 1], F32)
    ones_row_f = sb(octx, "ones_row_f", [1, 128], F32)
    eps_col = sb(octx, "eps_col", [128, 1], F32)
    mw1_col = sb(octx, "mw1_col", [128, 1], F32)
    sw1_col = sb(octx, "sw1_col", [128, 1], F32)
    mw2_col = sb(octx, "mw2_col", [128, 1], F32)
    sw2_col = sb(octx, "sw2_col", [128, 1], F32)
    s1_cols = sb(octx, "s1_cols", [128, 16], F32)
    al1_cols = sb(octx, "al1_cols", [128, 16], F32)
    a2_cols = sb(octx, "a2_cols", [128, 16], F32)
    al2_cols = sb(octx, "al2_cols", [128, 16], F32)
    w1acc = sb(octx, "w1acc", [128, 1], F32)
    w2acc = sb(octx, "w2acc", [128, 1], F32)
    tiny = sb(octx, "tiny", [128, 32], F32)

    ps_a = octx.enter_context(nc.psum_tensor("ps_a", [128, 2048], F32))
    ps_lt0 = octx.enter_context(nc.psum_tensor("ps_lt0", [128, 1024], F32))
    ps_lt1 = octx.enter_context(nc.psum_tensor("ps_lt1", [128, 1024], F32))
    ps_lt = [ps_lt0, ps_lt1]
    ps_rot = [ps_lt0[:, 0:512], ps_lt0[:, 512:1024],
              ps_lt1[:, 0:512], ps_lt1[:, 512:1024]]

    ph = Phase(nc)
    for tile, val in ((ones_col_bf, 1.0), (ones_col_f, 1.0), (ones_row_f, 1.0),
                      (eps_col, EPS), (w1acc, 0.0), (w2acc, 0.0), (a2_cols, 0.0)):
        ph.run("gpsimd", "memset", tile[:], val, inc=s_pool)
    init_done = s_pool.n
    ph.emit()

    def ln_stats(ph, xtile, ncols, sq_tile, cb=0):
        inv = 1.0 / ncols
        mu_t, m2_t, var_t, sd_t = (tiny[:, cb:cb + 1], tiny[:, cb + 1:cb + 2],
                                   tiny[:, cb + 2:cb + 3], tiny[:, cb + 3:cb + 4])
        rstd_t, nmu_t = tiny[:, cb + 4:cb + 5], tiny[:, cb + 5:cb + 6]
        ph.run("vector", "reduce_sum", mu_t, xtile, AX.X)
        v_sq = ph.run("scalar", "activation", sq_tile, xtile, AF.Square, inc=s_act)
        ph.wait("vector", s_act, v_sq)
        v_ss = ph.run("vector", "reduce_sum", var_t, sq_tile, AX.X, inc=s_dve)
        ph.wait("vector", s_dve, v_ss)
        ph.runs("vector", s_dve, "tensor_scalar", mu_t, mu_t, inv, None, ALU.mult)
        ph.runs("vector", s_dve, "tensor_tensor", m2_t, mu_t, mu_t, ALU.mult)
        ph.runs("vector", s_dve, "scalar_tensor_tensor", var_t, var_t, inv, m2_t,
                ALU.mult, ALU.subtract)
        v_var = ph.runs("vector", s_dve, "tensor_scalar", nmu_t, mu_t, -1.0, None,
                        ALU.mult)
        ph.wait("scalar", s_dve, v_var)
        v_sd = ph.run("scalar", "activation", sd_t, var_t, AF.Sqrt,
                      bias=eps_col[:], inc=s_act)
        ph.wait("vector", s_act, v_sd)
        v_r = ph.run("vector", "reciprocal", rstd_t, sd_t, inc=s_dve)
        ph.wait("vector", s_dve, v_r)
        return nmu_t, rstd_t

    if UNTIL >= 1:
        st = contextlib.ExitStack()
        xt = [sb(st, f"p0_x{i}", [128, D], F32) for i in range(2)]
        sqt = [sb(st, f"p0_sq{i}", [128, D], F32) for i in range(2)]
        lnt = [sb(st, f"p0_ln{i}", [128, D], BF16) for i in range(2)]
        ph = Phase(nc)
        phase_entry(ph)
        for e in ("scalar", "vector", "gpsimd", "tensor", "sync"):
            ph.wait(e, s_pool, init_done)
        ln_done = [0, 0]
        ln_stored = [0, 0]
        for i in range(S // 128):
            j = i % 2
            ph.wait("sync", s_dve, ln_done[j])
            v_x = ph.dma(dx[j], xt[j][:], xin[i * 128:(i + 1) * 128, :])
            ph.wait("vector", dx[j], v_x)
            ph.wait("scalar", dx[j], v_x)
            nmu, rstd = ln_stats(ph, xt[j][:], D, sqt[j][:], cb=8 + 6 * j)
            ph.wait("vector", dst[j], ln_stored[j])
            v_ln = ph.run("vector", "tensor_scalar", lnt[j][:], xt[j][:],
                          nmu, rstd, ALU.add, ALU.mult, inc=s_dve)
            ln_done[j] = v_ln
            ph.wait("sync", s_dve, v_ln)
            dstap = bass.AP(ln1s, i * 128 * 128, [[128, 128], [S * 128, 16], [1, 128]])
            ln_stored[j] = ph.dma(dst[j], dstap, lnt[j][:])
        ph.emit()
        st.close()

    if UNTIL >= 2:
        st_fm = contextlib.ExitStack()
        ln1_fm = [sb(st_fm, f"ln1fm{dc}", [128, S], BF16) for dc in range(16)]
        st = contextlib.ExitStack()
        ph = Phase(nc)
        phase_entry(ph)
        for dc in range(16):
            src = bass.AP(ln1s, dc * S * 128, [[128, S], [1, 128]])
            ph.dma(dfm[0], ln1_fm[dc][:], src, transpose=True)
        v_lnfm = dfm[0].n
        ph.wait("tensor", dfm[0], v_lnfm)
        ph.wait("gpsimd", dfm[0], v_lnfm)

        ps_busy = [0, 0, 0, 0]

        def mm_group(ph, ps_ap, slot, pieces):
            if slot is not None and ps_busy[slot]:
                ph.wait("tensor", s_act, ps_busy[slot])
            n = len(pieces)
            v = None
            for idx, (l, r) in enumerate(pieces):
                if idx == n - 1:
                    v = ph.run("tensor", "matmul", ps_ap, l, r,
                               start=(idx == 0), stop=True, inc=s_pe)
                else:
                    ph.run("tensor", "matmul", ps_ap, l, r,
                           start=(idx == 0), stop=False)
            return v

        wkf = [sb(st, f"p2_wkf{i}", [128, 2048], F32) for i in range(2)]
        wkb = [sb(st, f"p2_wkb{i}", [128, 2048], BF16) for i in range(2)]
        kst = [sb(st, f"p2_kst{i}", [128, S], BF16) for i in range(2)]
        wk_cast = [0, 0]
        wk_used = [0, 0]
        kst_stored = [0, 0]

        for idx in range(32):
            kf = idx % 16
            is_k = idx < 16
            colbase = (D + kf * 128) if is_k else (kf * 128)
            ntok = S if is_k else T
            scratch = Ks if is_k else Qs
            j = idx % 2
            ph.wait("sync", s_pool, wk_cast[j])
            src = bass.AP(wqkv, colbase, [[3 * D, 128], [128 * 3 * D, 16], [1, 128]])
            v_w = ph.dma(dw[j], wkf[j][:], src)
            ph.wait("gpsimd", dw[j], v_w)
            ph.wait("gpsimd", s_pe, wk_used[j])
            v_cast = ph.run("gpsimd", "tensor_copy", wkb[j][:], wkf[j][:],
                            inc=s_pool)
            wk_cast[j] = v_cast
            ph.wait("tensor", s_pool, v_cast)
            ph.wait("scalar", dkst[j], kst_stored[j])
            for t in range(ntok // 512):
                slot = t % 4
                pieces = [(wkb[j][:, dc * 128:(dc + 1) * 128],
                           ln1_fm[dc][:, t * 512:(t + 1) * 512]) for dc in range(16)]
                v_mm = mm_group(ph, ps_rot[slot], slot, pieces)
                ph.wait("scalar", s_pe, v_mm)
                v_ev = ph.run("scalar", "copy", kst[j][:, t * 512:(t + 1) * 512],
                              ps_rot[slot], inc=s_act)
                ps_busy[slot] = v_ev
            wk_used[j] = v_mm
            ph.wait("sync", s_act, v_ev)
            kst_stored[j] = ph.dma(dkst[j], scratch[kf * 128:(kf + 1) * 128, :ntok],
                                   kst[j][:, :ntok])

        ph.emit()
        st.close()
        st = contextlib.ExitStack()
        ph = Phase(nc)
        phase_entry(ph)
        wvf = sb(st, "p2_wvf", [128, 8192], F32)
        wvb = sb(st, "p2_wvb", [128, 8192], BF16)
        vst = [sb(st, f"p2_vst{i}", [128, 512], BF16) for i in range(4)]
        vst_stored = [0, 0, 0, 0]
        wv_cast = 0
        wv_used = 0
        for vt in range(4):
            ph.wait("sync", s_pool, wv_cast)
            src = bass.AP(wqkv, 2 * D + vt * 512,
                          [[3 * D, 128], [128 * 3 * D, 16], [1, 512]])
            v_w = ph.dma(dvw, wvf[:], src)
            ph.wait("gpsimd", dvw, v_w)
            ph.wait("gpsimd", s_pe, wv_used)
            wv_cast = ph.run("gpsimd", "tensor_copy", wvb[:], wvf[:], inc=s_pool)
            ph.wait("tensor", s_pool, wv_cast)
            for tokc in range(32):
                slot = tokc % 4
                pieces = [(ln1_fm[dc][:, tokc * 128:(tokc + 1) * 128],
                           wvb[:, dc * 512:(dc + 1) * 512]) for dc in range(16)]
                v_mm = mm_group(ph, ps_rot[slot], slot, pieces)
                sj = tokc % 4
                ph.wait("scalar", dvst[sj], vst_stored[sj])
                ph.wait("scalar", s_pe, v_mm)
                v_ev = ph.run("scalar", "copy", vst[sj][:], ps_rot[slot], inc=s_act)
                ps_busy[slot] = v_ev
                ph.wait("sync", s_act, v_ev)
                vst_stored[sj] = ph.dma(
                    dvst[sj], Vs[tokc * 128:(tokc + 1) * 128, vt * 512:(vt + 1) * 512],
                    vst[sj][:])
            wv_used = v_mm
        ph.emit()
        st.close()
        st_fm.close()

    comb_st = contextlib.ExitStack()
    if UNTIL >= 3:
        combined = [sb(comb_st, f"comb{hc}", [128, T], BF16) for hc in range(16)]
        st = contextlib.ExitStack()
        kh = sb(st, "p3_kh", [128, S], BF16)
        vh = sb(st, "p3_vh", [128, S], BF16)
        qh = sb(st, "p3_qh", [128, T], BF16)
        pbuf = sb(st, "p3_p", [128, 32 * 512], BF16)
        sqq = sb(st, "p3_sqq", [128, S], BF16)
        lacc = sb(st, "p3_lacc", [128, 512], F32)
        lacc2 = sb(st, "p3_lacc2", [128, 512], F32)
        rb_sb = sb(st, "p3_rb", [128, 512], F32)
        negc = sb(st, "p3_negc", [128, 1], F32)
        rinv = sb(st, "p3_rinv", [1, 512], F32)
        w1m = sb(st, "p3_w1m", [128, 2048], F32)
        w1ab = sb(st, "p3_w1ab", [128, 2048], BF16)
        ph = Phase(nc)
        phase_entry(ph)

        w1m_used = 0

        def w1_mean_piece(i):
            nonlocal w1m_used
            rc, cp = i // 4, i % 4
            ph.wait("sync", s_act, w1m_used)
            v_w = ph.dma(dwm, w1m[:],
                         w1t[rc * 128:(rc + 1) * 128, cp * 2048:(cp + 1) * 2048])
            ph.wait("scalar", dwm, v_w)
            ph.wait("scalar", s_act, w1m_used)
            ph.wait("scalar", s_dve, s_dve.n)
            va = ph.run("scalar", "activation", w1ab[:], w1m[:], AF.Abs,
                        accum_out=tiny[:, 6:7], inc=s_act)
            w1m_used = va
            ph.wait("vector", s_act, va)
            ph.runs("vector", s_dve, "tensor_tensor", w1acc[:], w1acc[:],
                    tiny[:, 6:7], ALU.add)

        lacc_used = 0
        rinv_used = 0
        for h in range(NH):
            ph.wait("sync", s_pe, s_pe.n)
            ph.wait("sync", s_dve, s_dve.n)
            ph.wait("sync", s_act, s_act.n)
            ph.wait("tensor", s_dve, s_dve.n)
            ph.wait("tensor", s_act, s_act.n)
            ph.dma(dkh, kh[:], Ks[h * 128:(h + 1) * 128, :])
            ph.dma(dkh, vh[:],
                   bass.AP(Vs, h * 128, [[D, 128], [D * 128, 32], [1, 128]]))
            v_kvq = ph.dma(dkh, qh[:], Qs[h * 128:(h + 1) * 128, :])
            ph.wait("scalar", dkh, v_kvq)
            va = ph.run("scalar", "activation", sqq[:, :T], qh[:], AF.Square,
                        inc=s_act)
            ph.wait("tensor", s_act, va)
            vqn = None
            for t in range(4):
                vqn = ph.run("tensor", "matmul", ps_a[0:1, t * 512:(t + 1) * 512],
                             ones_col_bf[:], sqq[:, t * 512:(t + 1) * 512],
                             start=True, stop=True, inc=s_pe if t == 3 else None) or vqn
            ph.wait("vector", s_pe, s_pe.n)
            v1 = ph.run("vector", "tensor_reduce", tiny[:1, 0:1], ps_a[0:1, 0:T],
                        AX.X, ALU.max, inc=s_dve)
            ph.wait("scalar", dkh, v_kvq)
            ph.wait("scalar", s_pe, vqn)
            va = ph.run("scalar", "activation", sqq[:], kh[:], AF.Square,
                        inc=s_act)
            ph.wait("tensor", s_act, va)
            ph.wait("tensor", s_dve, v1)
            kmx = tiny[:1, 1:2]
            for rnd in range(2):
                for t in range(4):
                    tt = rnd * 4 + t
                    ph.run("tensor", "matmul", ps_a[0:1, t * 512:(t + 1) * 512],
                           ones_col_bf[:], sqq[:, tt * 512:(tt + 1) * 512],
                           start=True, stop=True,
                           inc=s_pe if t == 3 else None)
                ph.wait("vector", s_pe, s_pe.n)
                ph.wait("vector", s_dve, s_dve.n)
                vr = ph.run("vector", "tensor_reduce",
                            kmx if rnd == 0 else tiny[:1, 2:3],
                            ps_a[0:1, 0:T], AX.X, ALU.max, inc=s_dve)
                if rnd == 1:
                    ph.wait("vector", s_dve, vr)
                    ph.runs("vector", s_dve, "tensor_tensor", kmx, kmx,
                            tiny[:1, 2:3], ALU.max)
                else:
                    ph.wait("tensor", s_dve, vr)
            ph.runs("vector", s_dve, "tensor_tensor", tiny[:1, 2:3], tiny[:1, 0:1],
                    kmx, ALU.mult)
            v3 = ph.runs("vector", s_dve, "tensor_scalar", tiny[:1, 2:3],
                         tiny[:1, 2:3], 1.0 / 128.0, None, ALU.mult)
            ph.wait("scalar", s_dve, v3)
            v4 = ph.run("scalar", "activation", tiny[:1, 3:4], tiny[:1, 2:3],
                        AF.Sqrt, inc=s_act)
            ph.wait("vector", s_act, v4)
            v5 = ph.run("vector", "tensor_scalar", tiny[:1, 4:5], tiny[:1, 3:4],
                        -1.0, None, ALU.mult, inc=s_dve)
            ph.wait("tensor", s_dve, v5)
            v6 = ph.run("tensor", "matmul", ps_a[:, 1024:1025], ones_row_f[:],
                        tiny[:1, 4:5], start=True, stop=True, inc=s_pe)
            ph.wait("scalar", s_pe, v6)
            vc = ph.run("scalar", "copy", negc[:], ps_a[:, 1024:1025], inc=s_act)

            ph.wait("tensor", dkh, v_kvq)
            evac_val = 0
            prev_tail = [0, 0]
            for g in range(4):
                exp_vals = []
                for pr in range(16):
                    plt = ps_lt[pr % 2]
                    if pr >= 2:
                        ph.wait("tensor", s_act, exp_vals[pr - 2])
                    else:
                        ph.wait("tensor", s_act, prev_tail[pr])
                    for sub in range(2):
                        kc = pr * 2 + sub
                        v_mm = ph.run("tensor", "matmul",
                                      plt[:, sub * 512:(sub + 1) * 512],
                                      kh[:, kc * 128:(kc + 1) * 128],
                                      qh[:, g * 512:(g + 1) * 512],
                                      start=True, stop=True,
                                      inc=s_pe if sub == 1 else None)
                    ph.wait("scalar", s_pe, v_mm)
                    ph.wait("scalar", s_act, vc)
                    ev = ph.run("scalar", "activation",
                                pbuf[:, pr * 1024:(pr + 1) * 1024], plt[:],
                                AF.Exp, bias=negc[:], scale=INV_SQRT_HD,
                                inc=s_act)
                    exp_vals.append(ev)
                ph.wait("vector", s_act, exp_vals[0])
                ph.wait("vector", s_pe, lacc_used)
                ph.runs("vector", s_dve, "tensor_copy", lacc[:], pbuf[:, 0:512])
                ph.runs("vector", s_dve, "tensor_tensor", lacc[:], lacc[:],
                        pbuf[:, 512:1024], ALU.add)
                ph.wait("gpsimd", s_act, exp_vals[11])
                ph.wait("gpsimd", s_pe, lacc_used)
                ph.runs("gpsimd", s_pool, "tensor_copy", lacc2[:],
                        pbuf[:, 11 * 1024:11 * 1024 + 512])
                for pr in range(1, 11):
                    ph.wait("vector", s_act, exp_vals[pr])
                    for sub in range(2):
                        off = pr * 1024 + sub * 512
                        ph.runs("vector", s_dve, "tensor_tensor", lacc[:],
                                lacc[:], pbuf[:, off:off + 512], ALU.add)
                first = True
                for pr in range(11, 16):
                    ph.wait("gpsimd", s_act, exp_vals[pr])
                    for sub in range(2):
                        if first:
                            first = False
                            continue
                        off = pr * 1024 + sub * 512
                        ph.runs("gpsimd", s_pool, "tensor_tensor", lacc2[:],
                                lacc2[:], pbuf[:, off:off + 512], ALU.add)
                ph.wait("vector", s_pool, s_pool.n)
                v_l = ph.run("vector", "tensor_tensor", lacc[:], lacc[:],
                             lacc2[:], ALU.add, inc=s_dve)
                ph.wait("tensor", s_act, exp_vals[15])
                ph.wait("tensor", s_dve, evac_val)
                for kc in range(32):
                    v_pv = ph.run("tensor", "matmul", ps_a[:, 1536:2048],
                                  vh[:, kc * 128:(kc + 1) * 128],
                                  pbuf[:, kc * 512:(kc + 1) * 512],
                                  start=(kc == 0), stop=(kc == 31),
                                  inc=s_pe if kc == 31 else None)
                ph.wait("tensor", s_dve, v_l)
                v_lm = ph.run("tensor", "matmul", ps_a[0:1, 0:512],
                              ones_col_f[:], lacc[:], start=True, stop=True,
                              inc=s_pe)
                lacc_used = v_lm
                ph.wait("vector", s_pe, v_lm)
                ph.wait("vector", s_pe, rinv_used)
                v_r = ph.run("vector", "reciprocal", rinv[:], ps_a[0:1, 0:512],
                             inc=s_dve)
                ph.wait("vector", s_dve, v_r)
                ph.wait("tensor", s_dve, v_r)
                v_bm = ph.run("tensor", "matmul", ps_a[:, 512:1024],
                              ones_row_f[:], rinv[:], start=True, stop=True,
                              inc=s_pe)
                rinv_used = v_bm
                ph.wait("scalar", s_pe, v_bm)
                v_rb = ph.run("scalar", "copy", rb_sb[:], ps_a[:, 512:1024],
                              inc=s_act)
                ph.wait("vector", s_act, v_rb)
                ph.wait("vector", s_pe, v_pv)
                evac_val = ph.run("vector", "tensor_tensor",
                                  combined[h][:, g * 512:(g + 1) * 512],
                                  ps_a[:, 1536:2048], rb_sb[:], ALU.mult,
                                  inc=s_dve)
                prev_tail = [exp_vals[14], exp_vals[15]]
            for pp in range(4):
                w1_mean_piece(h * 4 + pp)

        ph.wait("tensor", s_dve, s_dve.n)
        v = ph.run("tensor", "matmul", ps_a[0:1, 0:1], ones_col_f[:], w1acc[:],
                   start=True, stop=True, inc=s_pe)
        ph.wait("vector", s_pe, v)
        v = ph.runs("vector", s_dve, "tensor_scalar", tiny[:1, 7:8],
                    ps_a[0:1, 0:1], 1.0 / 16777216.0, EPS, ALU.mult, ALU.max)
        ph.wait("tensor", s_dve, v)
        v = ph.run("tensor", "matmul", ps_a[:, 0:1], ones_row_f[:],
                   tiny[:1, 7:8], start=True, stop=True, inc=s_pe)
        ph.wait("scalar", s_pe, v)
        v = ph.run("scalar", "copy", mw1_col[:], ps_a[:, 0:1], inc=s_act)
        ph.wait("vector", s_act, v)
        v = ph.run("vector", "reciprocal", sw1_col[:], mw1_col[:], inc=s_dve)
        ph.wait("vector", s_dve, v)
        ph.emit()
        st.close()

    if UNTIL >= 4:
        st = contextlib.ExitStack()
        wof = sb(st, "p4_wof", [128, 4096], F32)
        wob = [sb(st, f"p4_wob{hc}", [128, 2048], BF16) for hc in range(16)]
        xtt = [sb(st, f"p4_x{i}", [128, D], F32) for i in range(2)]
        x2t = [sb(st, f"p4_x2{i}", [128, D], F32) for i in range(2)]
        ln2t = sb(st, "p4_ln2", [128, D], F32)
        sq2t = sb(st, "p4_sq2", [128, D], F32)
        z1t = sb(st, "p4_z1", [128, D], BF16)
        ph = Phase(nc)
        phase_entry(ph)
        wof_used = 0
        for jp in range(8):
            ph.wait("sync", s_pool, wof_used)
            src = bass.AP(wout, jp * 256, [[D, 128], [128 * D, 16], [1, 256]])
            v_w = ph.dma(dvw, wof[:], src)
            ph.wait("gpsimd", dvw, v_w)
            for hc in range(16):
                inc = s_pool if hc == 15 else None
                v_c = ph.run("gpsimd", "tensor_copy",
                             wob[hc][:, jp * 256:(jp + 1) * 256],
                             wof[:, hc * 256:(hc + 1) * 256], inc=inc)
            wof_used = v_c
        ph.wait("tensor", s_pool, s_pool.n)
        x2_stored = [0, 0]
        z1_stored = 0
        for tc in range(16):
            j = tc % 2
            ph.wait("sync", s_dve, s_dve.n)
            v_x = ph.dma(dx[j], xtt[j][:], xin[tc * 128:(tc + 1) * 128, :])
            for of_t in range(4):
                for hc in range(16):
                    inc = s_pe if hc == 15 else None
                    v_mm = ph.run("tensor", "matmul",
                                  ps_a[:, of_t * 512:(of_t + 1) * 512],
                                  combined[hc][:, tc * 128:(tc + 1) * 128],
                                  wob[hc][:, of_t * 512:(of_t + 1) * 512],
                                  start=(hc == 0), stop=(hc == 15), inc=inc)
            ph.wait("vector", dst[j], x2_stored[j])
            ph.wait("vector", s_pe, v_mm)
            ph.wait("vector", dx[j], v_x)
            v_x2 = ph.run("vector", "tensor_tensor", x2t[j][:], ps_a[:],
                          xtt[j][:], ALU.add, inc=s_dve)
            ph.wait("tensor", s_dve, v_x2)
            ph.wait("sync", s_dve, v_x2)
            x2_stored[j] = ph.dma(dst[j], x2s[tc * 128:(tc + 1) * 128, :],
                                  x2t[j][:])
            ph.wait("scalar", s_dve, v_x2)
            nmu, rstd = ln_stats(ph, x2t[j][:], D, sq2t[:], cb=8 + 6 * j)
            ph.runs("vector", s_dve, "tensor_scalar", ln2t[:], x2t[j][:],
                    nmu, rstd, ALU.add, ALU.mult)
            v_a = ph.run("vector", "tensor_reduce", tiny[:, 6:7], ln2t[:],
                         AX.X, ALU.max, True, inc=s_dve)
            ph.wait("vector", s_dve, v_a)
            ph.runs("vector", s_dve, "tensor_scalar", tiny[:, 6:7], tiny[:, 6:7],
                    EPS, None, ALU.max)
            v_r = ph.run("vector", "reciprocal", tiny[:, 7:8], tiny[:, 6:7],
                         inc=s_dve)
            ph.wait("vector", s_dve, v_r)
            ph.runs("vector", s_dve, "tensor_scalar", s1_cols[:, tc:tc + 1],
                    tiny[:, 7:8], 127.0, None, ALU.mult)
            ph.runs("vector", s_dve, "scalar_tensor_tensor", al1_cols[:, tc:tc + 1],
                    tiny[:, 6:7], 1.0 / 127.0, mw1_col[:], ALU.mult, ALU.mult)
            ph.wait("vector", dz1, z1_stored)
            v_z = ph.custom(ROUND_TS, out=z1t[:], in0=ln2t[:],
                            s0=s1_cols[:, tc:tc + 1], s1=MAGIC, inc=s_dve)
            ph.wait("sync", s_dve, v_z)
            dstap = bass.AP(z1s, tc * 128 * 128, [[128, 128], [T * 128, 16], [1, 128]])
            z1_stored = ph.dma(dz1, dstap, z1t[:])
        ph.emit()
        st.close()
    comb_st.close()

    if UNTIL >= 5:
        st = contextlib.ExitStack()
        z1_fm = [sb(st, f"z1fm{dc}", [128, T], BF16) for dc in range(16)]
        w1f = [sb(st, f"p6_w1f{i}", [128, 4096], F32) for i in range(2)]
        w1q = [sb(st, f"p6_w1q{i}", [128, 8192], BF16) for i in range(2)]
        gst = sb(st, "p6_gst", [128, 8192], F16)
        gtmp = [sb(st, f"p6_g{i}", [128, 512], F32) for i in range(2)]
        w2m = sb(st, "p6_w2m", [128, 2048], F32)
        w2ab = sb(st, "p6_w2ab", [128, 2048], BF16)
        ph = Phase(nc)
        phase_entry(ph)
        for dc in range(16):
            src = bass.AP(z1s, dc * T * 128, [[128, T], [1, 128]])
            ph.dma(dfm[0], z1_fm[dc][:], src, transpose=True)
        ph.wait("tensor", dfm[0], dfm[0].n)

        w2m_used = 0

        def w2_mean_piece(i):
            nonlocal w2m_used
            ph.wait("sync", s_act, w2m_used)
            v_w = ph.dma(dwm, w2m[:], w2t[i * 128:(i + 1) * 128, :])
            ph.wait("scalar", dwm, v_w)
            ph.wait("scalar", s_act, w2m_used)
            ph.wait("scalar", s_dve, s_dve.n)
            va = ph.run("scalar", "activation", w2ab[:], w2m[:], AF.Abs,
                        accum_out=tiny[:, 6:7], inc=s_act)
            w2m_used = va
            ph.wait("vector", s_act, va)
            ph.runs("vector", s_dve, "tensor_tensor", w2acc[:], w2acc[:],
                    tiny[:, 6:7], ALU.add)

        w1q_ready = [0, 0]
        w1q_used = [0, 0]
        gst_stored = 0
        gdone = {}
        gcopy = {}
        gi = 0
        for hs in range(16):
            j = hs % 2
            ph.wait("vector", s_pe, w1q_used[j])
            for half in range(2):
                src = bass.AP(w1t, hs * 512 + half * 256,
                              [[H, 128], [128 * H, 16], [1, 256]])
                ph.wait("sync", s_dve, w1q_ready[j])
                v_w = ph.dma(dw[j], w1f[j][:], src)
                ph.wait("vector", dw[j], v_w)
                v_rnd = ph.custom(ROUND_TS, out=w1f[j][:], in0=w1f[j][:],
                                  s0=sw1_col[:], s1=MAGIC, inc=s_dve)
                ph.wait("vector", s_dve, v_rnd)
                w1q_dst = bass.AP(w1q[j], half * 256,
                                  [[8192, 128], [512, 16], [1, 256]])
                v_q = ph.run("vector", "tensor_scalar", w1q_dst, w1f[j][:],
                             -1.0, 1.0, ALU.max, ALU.min, inc=s_dve)
                w1q_ready[j] = v_q
            ph.wait("tensor", s_dve, w1q_ready[j])
            ph.wait("gpsimd", dgst, gst_stored)
            for tc in range(16):
                slot = gi % 4
                plt = ps_lt[slot // 2]
                ps_ap = plt[:, (slot % 2) * 512:(slot % 2 + 1) * 512]
                if gdone.get(gi - 4):
                    ph.wait("tensor", s_act, gdone[gi - 4])
                for dc in range(16):
                    inc = s_pe if dc == 15 else None
                    v_mm = ph.run("tensor", "matmul", ps_ap,
                                  z1_fm[dc][:, tc * 128:(tc + 1) * 128],
                                  w1q[j][:, dc * 512:(dc + 1) * 512],
                                  start=(dc == 0), stop=(dc == 15), inc=inc)
                gj = tc % 2
                ph.wait("scalar", s_pe, v_mm)
                if gcopy.get(gi - 2):
                    ph.wait("scalar", s_pool, gcopy[gi - 2])
                ph.wait("scalar", s_dve, s_dve.n)
                v_g = ph.run("scalar", "activation", gtmp[gj][:], ps_ap,
                             AF.Gelu_apprx_tanh, scale=al1_cols[:, tc:tc + 1],
                             inc=s_act)
                gdone[gi] = v_g
                ph.wait("vector", s_act, v_g)
                v_red = ph.run("vector", "tensor_reduce", tiny[:, 0:1],
                               gtmp[gj][:], AX.X, ALU.max, True, inc=s_dve)
                ph.wait("vector", s_dve, v_red)
                ph.runs("vector", s_dve, "tensor_tensor", a2_cols[:, tc:tc + 1],
                        a2_cols[:, tc:tc + 1], tiny[:, 0:1], ALU.max)
                ph.wait("gpsimd", s_act, v_g)
                gcopy[gi] = ph.run("gpsimd", "tensor_copy",
                                   gst[:, tc * 512:(tc + 1) * 512], gtmp[gj][:],
                                   inc=s_pool)
                gi += 1
            w1q_used[j] = v_mm
            ph.wait("sync", s_pool, gcopy[gi - 1])
            for hc4 in range(4):
                dstap = bass.AP(gs, (hs * 4 + hc4) * T * 128,
                                [[128, 128], [128 * 128, 16], [1, 128]])
                srcap = bass.AP(gst, hc4 * 128, [[8192, 128], [512, 16], [1, 128]])
                gst_stored = ph.dma(dgst, dstap, srcap)
            w2_mean_piece(hs)

        for i in range(16, 64):
            w2_mean_piece(i)
        ph.wait("tensor", s_dve, s_dve.n)
        ph.wait("tensor", s_act, s_act.n)
        v = ph.run("tensor", "matmul", ps_a[0:1, 0:1], ones_col_f[:], w2acc[:],
                   start=True, stop=True, inc=s_pe)
        ph.wait("vector", s_pe, v)
        v = ph.runs("vector", s_dve, "tensor_scalar", tiny[:1, 7:8],
                    ps_a[0:1, 0:1], 1.0 / 16777216.0, EPS, ALU.mult, ALU.max)
        ph.wait("tensor", s_dve, v)
        v = ph.run("tensor", "matmul", ps_a[:, 0:1], ones_row_f[:],
                   tiny[:1, 7:8], start=True, stop=True, inc=s_pe)
        ph.wait("scalar", s_pe, v)
        v = ph.run("scalar", "copy", mw2_col[:], ps_a[:, 0:1], inc=s_act)
        ph.wait("vector", s_act, v)
        v = ph.run("vector", "reciprocal", sw2_col[:], mw2_col[:], inc=s_dve)
        ph.wait("vector", s_dve, v)
        s2_stored = [0, 0]
        for tc in range(16):
            cb = (tc % 2) * 3
            ph.wait("vector", ds2[tc % 2], s2_stored[tc % 2])
            ph.runs("vector", s_dve, "tensor_scalar", tiny[:, cb:cb + 1],
                    a2_cols[:, tc:tc + 1], EPS, None, ALU.max)
            v_r = ph.run("vector", "reciprocal", tiny[:, cb + 1:cb + 2],
                         tiny[:, cb:cb + 1], inc=s_dve)
            ph.wait("vector", s_dve, v_r)
            ph.runs("vector", s_dve, "tensor_scalar", tiny[:, cb + 2:cb + 3],
                    tiny[:, cb + 1:cb + 2], 127.0, None, ALU.mult)
            vv = ph.run("vector", "scalar_tensor_tensor", al2_cols[:, tc:tc + 1],
                        tiny[:, cb:cb + 1], 1.0 / 127.0, mw2_col[:], ALU.mult,
                        ALU.mult, inc=s_dve)
            ph.wait("sync", s_dve, vv)
            s2_stored[tc % 2] = ph.dma(ds2[tc % 2],
                                       s2s[0:1, tc * 128:(tc + 1) * 128],
                                       tiny[:, cb + 2:cb + 3])
        ph.emit()
        st.close()

    if UNTIL >= 6:
        st = contextlib.ExitStack()
        s2row = sb(st, "p7_s2row", [1, T], F32)
        s2b = sb(st, "p7_s2b", [128, T], F32)
        acc = [sb(st, f"p7_acc{i}", [128, D], F32) for i in range(8)]
        gfm = [sb(st, f"p7_gfm{i}", [128, 1024], F16) for i in range(2)]
        z2fm = [sb(st, f"p7_z2{i}", [128, 1024], BF16) for i in range(8)]
        w2f = [sb(st, f"p7_w2f{i}", [128, 2048], F32) for i in range(2)]
        w2q = [sb(st, f"p7_w2q{i}", [128, 2048], BF16) for i in range(8)]
        x2tt = [sb(st, f"p8_x2{i}", [128, D], F32) for i in range(2)]
        ot = [sb(st, f"p8_o{i}", [128, D], F32) for i in range(2)]
        ph = Phase(nc)
        phase_entry(ph)
        v_s2 = ph.dma(dfm[1], s2row[:], s2s[:])
        ph.wait("tensor", dfm[1], v_s2)
        for t in range(4):
            v = ph.run("tensor", "matmul", ps_a[:, t * 512:(t + 1) * 512],
                       ones_row_f[:], s2row[:, t * 512:(t + 1) * 512],
                       start=True, stop=True, inc=s_pe if t == 3 else None)
        ph.wait("scalar", s_pe, v)
        v_sb = ph.run("scalar", "copy", s2b[:], ps_a[:], inc=s_act)
        ph.wait("vector", s_act, v_sb)
        ph.wait("tensor", s_act, v_sb)

        out_stored = [0, 0]
        x2_used = [0, 0]
        prev_mm = 0
        for half in range(2):
            tokslice = slice(half * 1024, (half + 1) * 1024)
            for hsl in range(8):
                for k8 in range(8):
                    hc = hsl * 8 + k8
                    j = hc % 2
                    ph.wait("sync", s_dve, s_dve.n)
                    ph.wait("vector", s_pe, prev_mm)
                    src = bass.AP(gs, hc * T * 128 + half * 1024 * 128,
                                  [[128, 1024], [1, 128]])
                    v_g = ph.dma(dfm[j], gfm[j][:], src, transpose=True)
                    ph.wait("vector", dfm[j], v_g)
                    ph.custom(ROUND_TT, out=z2fm[k8][:], in0=gfm[j][:],
                              in1=s2b[:, tokslice], s0=MAGIC)
                    v_w = ph.dma(dw[j], w2f[j][:], w2t[hc * 128:(hc + 1) * 128, :])
                    ph.wait("vector", dw[j], v_w)
                    v_rnd = ph.custom(ROUND_TS, out=w2f[j][:], in0=w2f[j][:],
                                      s0=sw2_col[:], s1=MAGIC, inc=s_dve)
                    ph.wait("vector", s_dve, v_rnd)
                    v_q = ph.run("vector", "tensor_scalar", w2q[k8][:],
                                 w2f[j][:], -1.0, 1.0, ALU.max, ALU.min,
                                 inc=s_dve)
                ph.wait("tensor", s_dve, v_q)
                for tc8 in range(8):
                    for k8 in range(8):
                        for nt in range(4):
                            inc = s_pe if (k8 == 7 and nt == 3) else None
                            v_mm = ph.run(
                                "tensor", "matmul",
                                ps_a[:, nt * 512:(nt + 1) * 512],
                                z2fm[k8][:, tc8 * 128:(tc8 + 1) * 128],
                                w2q[k8][:, nt * 512:(nt + 1) * 512],
                                start=(k8 == 0), stop=(k8 == 7), inc=inc)
                    ph.wait("vector", s_pe, v_mm)
                    if hsl == 0:
                        v_e = ph.run("vector", "tensor_copy", acc[tc8][:],
                                     ps_a[:], inc=s_dve)
                    else:
                        v_e = ph.run("vector", "tensor_tensor", acc[tc8][:],
                                     acc[tc8][:], ps_a[:], ALU.add, inc=s_dve)
                    ph.wait("tensor", s_dve, v_e)
                prev_mm = v_mm
            ph.wait("vector", s_dve, s_dve.n)
            for tc8 in range(8):
                tc = half * 8 + tc8
                j = tc8 % 2
                ph.wait("sync", s_dve, x2_used[j])
                v_x2 = ph.dma(dx[j], x2tt[j][:], x2s[tc * 128:(tc + 1) * 128, :])
                ph.wait("vector", dx[j], v_x2)
                ph.wait("vector", dst[j], out_stored[j])
                v_o = ph.run("vector", "scalar_tensor_tensor", ot[j][:],
                             acc[tc8][:], al2_cols[:, tc:tc + 1], x2tt[j][:],
                             ALU.mult, ALU.add, inc=s_dve)
                x2_used[j] = v_o
                ph.wait("sync", s_dve, v_o)
                out_stored[j] = ph.dma(dst[j], out[tc * 128:(tc + 1) * 128, :],
                                       ot[j][:])
        for l in store_lanes:
            ph.wait("sync", l, l.n)
        ph.emit()
        st.close()

    mybir.codegen_inst_isa_subclasses(nc)
    octx.close()
    return nc


def kernel(L0_attn_x, L0_attn_w_qkv, L0_attn_w_out, L0_ffn_w1, L0_ffn_w2):
    x = np.ascontiguousarray(np.asarray(L0_attn_x, dtype=np.float32))
    wqkv = np.ascontiguousarray(np.asarray(L0_attn_w_qkv, dtype=np.float32))
    wout = np.ascontiguousarray(np.asarray(L0_attn_w_out, dtype=np.float32))
    w1t = np.ascontiguousarray(np.asarray(L0_ffn_w1, dtype=np.float32).T)
    w2t = np.ascontiguousarray(np.asarray(L0_ffn_w2, dtype=np.float32).T)

    nc = _build()
    in_maps = []
    for c in range(8):
        b, half = c // 2, c % 2
        own = x[b, half * T:(half + 1) * T]
        other = x[b, (1 - half) * T:(2 - half) * T]
        xc = np.ascontiguousarray(np.concatenate([own, other], axis=0))
        in_maps.append({"xin": xc, "wqkv": wqkv, "wout": wout,
                        "w1t": w1t, "w2t": w2t})
    res = run_bass_kernel_spmd(nc, in_maps, core_ids=list(range(8)))
    o = np.empty((4, 4096, 2048), dtype=np.float32)
    for c in range(8):
        b, half = c // 2, c % 2
        o[b, half * T:(half + 1) * T] = res.results[c]["out"]
    return o


# version 04
# speedup vs baseline: 1.1542x; 1.1542x over previous
import contextlib
import numpy as np
import ml_dtypes
import concourse.bass as bass
import concourse.mybir as mybir
from concourse.bass_utils import run_bass_kernel_spmd
from concourse import dve_ops as _dvo
from concourse.dve_op import DveOp
from concourse.dve_spec import Spec, Src0, Src1, C0, C1, lower as _dve_lower

F32 = mybir.dt.float32
BF16 = mybir.dt.bfloat16
F16 = mybir.dt.float16
AF = mybir.ActivationFunctionType
ALU = mybir.AluOpType
AX = mybir.AxisListType

MAGIC = 12582912.0
INV_SQRT_HD = 0.08838834764831845
EPS = 1e-5

S = 4096
T = 2048
D = 2048
H = 8192
NH = 16


def _as_f(x, like):
    dt = np.float64 if np.asarray(like).dtype == np.float64 else np.float32
    return np.asarray(x, dtype=dt)


def _ref_round_ts(in0, in1, s0, s1, imm2):
    return np.round(_as_f(in0, in0) * _as_f(s0, in0))


def _ref_round_tt(in0, in1, s0, s1, imm2):
    return np.round(_as_f(in0, in0) * _as_f(in1, in0))


def _ref_round_clip(in0, in1, s0, s1, imm2):
    return np.minimum(np.maximum(np.round(_as_f(in0, in0) * _as_f(s0, in0)),
                                 -1.0), 1.0)


def _register_round_ops():
    from concourse.dve_spec import Zero, One, maxx, minn
    ops = {}
    defs = (
        ("ROUND_MUL_TS_ANT", Spec(body=((Src0 * C0) + C1) - C1), _ref_round_ts),
        ("ROUND_MUL_TT_ANT", Spec(body=((Src0 * Src1) + C0) - C0), _ref_round_tt),
        ("ROUND_CLIP_TS_ANT",
         Spec(body=minn(maxx(((Src0 * C0) + C1) - C1, Zero - One), One)),
         _ref_round_clip),
    )
    replaceable = ["DVEHASH32L_DRAIN", "DVEHASH32L_BODY", "DVEHASH32L_SEED"]
    for name, spec, ref in defs:
        existing = next((o for o in _dvo.OPS if o.name == name), None)
        if existing is not None:
            ops[name] = existing
            continue
        compiled = _dve_lower(spec, ver="v3", perf_en=False, name=name)
        op = DveOp(name, spec, subdim=False,
                   uops_sha={"v3": compiled.sha("v3")}, reference=ref)
        if len(_dvo.OPS) < 31:
            _dvo.OPS.append(op)
            row = len(_dvo.OPS)
        else:
            victim = next(n for n in replaceable
                          if any(o.name == n for o in _dvo.OPS))
            idx = next(i for i, o in enumerate(_dvo.OPS) if o.name == victim)
            replaceable.remove(victim)
            _dvo.OPS[idx] = op
            row = idx + 1
            del _dvo._SUB_OPCODE_FOR_NAME[victim]
            _dvo.CUSTOM_DVE_SPECS.pop(victim, None)
            _dvo.CUSTOM_DVE_REFS.pop(victim, None)
        _dvo._SUB_OPCODE_FOR_NAME[name] = row
        _dvo.CUSTOM_DVE_SPECS[name] = spec
        _dvo.CUSTOM_DVE_REFS[name] = ref
        ops[name] = op
    return (ops["ROUND_MUL_TS_ANT"], ops["ROUND_MUL_TT_ANT"],
            ops["ROUND_CLIP_TS_ANT"])


ROUND_TS, ROUND_TT, ROUND_CLIP = _register_round_ops()

ENGINES = ("sync", "tensor", "scalar", "vector", "gpsimd")
UNTIL = 99
DEBUG_SCRATCH = False


class Sem:
    def __init__(self, h):
        self.h = h
        self.n = 0


class Phase:
    def __init__(self, nc):
        self.nc = nc
        self.q = {e: [] for e in ENGINES}

    def on(self, eng, fn):
        self.q[eng].append(fn)

    def wait(self, eng, sem, val):
        if val and val > 0:
            self.on(eng, lambda e, _s=sem.h, _v=val: e.wait_ge(_s, _v))

    def run(self, eng, method, *args, inc=None, **kw):
        v = None
        if inc is not None:
            inc.n += 1
            v = inc.n
            def _f(e, _m=method, _a=args, _k=kw, _s=inc.h):
                getattr(e, _m)(*_a, **_k).then_inc(_s, 1)
        else:
            def _f(e, _m=method, _a=args, _k=kw):
                getattr(e, _m)(*_a, **_k)
        self.on(eng, _f)
        return v

    def runs(self, eng, sem, method, *args, **kw):
        v = self.run(eng, method, *args, inc=sem, **kw)
        self.wait(eng, sem, v)
        return v

    def custom(self, op, inc=None, **kw):
        v = None
        if inc is not None:
            inc.n += 1
            v = inc.n
            def _f(e, _o=op, _k=kw, _s=inc.h):
                e._custom_dve(_o, **_k).then_inc(_s, 1)
        else:
            def _f(e, _o=op, _k=kw):
                e._custom_dve(_o, **_k)
        self.on("vector", _f)
        return v

    def dma(self, sem, dst, src, transpose=False):
        sem.n += 16
        v = sem.n
        if transpose:
            self.on("sync", lambda e, _d=dst, _s=src, _h=sem.h:
                    e.dma_start(_d, _s, transpose=True).then_inc(_h, 16))
        else:
            self.on("sync", lambda e, _d=dst, _s=src, _h=sem.h:
                    e.dma_start(_d, _s).then_inc(_h, 16))
        return v

    def emit(self):
        with self.nc.Block() as block:
            @block.sync
            def _(e):
                for fn in self.q["sync"]:
                    fn(e)

            @block.tensor
            def _(e):
                for fn in self.q["tensor"]:
                    fn(e)

            @block.scalar
            def _(e):
                for fn in self.q["scalar"]:
                    fn(e)

            @block.vector
            def _(e):
                for fn in self.q["vector"]:
                    fn(e)

            @block.gpsimd
            def _(e):
                for fn in self.q["gpsimd"]:
                    fn(e)


def _build():
    nc = bass.Bass("TRN2", target_bir_lowering=False)

    xin = nc.dram_tensor("xin", [S, D], F32, kind="ExternalInput")
    wqkv = nc.dram_tensor("wqkv", [D, 3 * D], F32, kind="ExternalInput")
    wout = nc.dram_tensor("wout", [D, D], F32, kind="ExternalInput")
    w1t = nc.dram_tensor("w1t", [D, H], F32, kind="ExternalInput")
    w2t = nc.dram_tensor("w2t", [H, D], F32, kind="ExternalInput")
    out = nc.dram_tensor("out", [T, D], F32, kind="ExternalOutput")

    sk = "ExternalOutput" if DEBUG_SCRATCH else "Internal"
    ln1s = nc.dram_tensor("ln1s", [16, S, 128], BF16, kind=sk)
    Ks = nc.dram_tensor("Ks", [D, S], BF16, kind=sk)
    Vs = nc.dram_tensor("Vs", [S, D], BF16, kind=sk)
    Qs = nc.dram_tensor("Qs", [D, T], BF16, kind=sk)
    x2s = nc.dram_tensor("x2s", [T, D], F32, kind=sk)
    z1s = nc.dram_tensor("z1s", [16, T, 128], BF16, kind=sk)
    gs = nc.dram_tensor("gs", [64, T, 128], F16, kind=sk)
    s2s = nc.dram_tensor("s2s", [1, T], F32, kind=sk)

    octx = contextlib.ExitStack()

    def mksem(name):
        return Sem(octx.enter_context(nc.semaphore(name)))

    dx = [mksem("dx0"), mksem("dx1")]
    dst = [mksem("dst0"), mksem("dst1")]
    dw = [mksem("dw0"), mksem("dw1")]
    dkst = [mksem("dkst0"), mksem("dkst1")]
    dvst = [mksem(f"dvst{i}") for i in range(4)]
    dvw = mksem("dvw")
    dfm = [mksem("dfm0"), mksem("dfm1")]
    dkh = mksem("dkh")
    dwm = mksem("dwm")
    dgst = mksem("dgst")
    dz1 = mksem("dz1")
    ds2 = [mksem("ds2a"), mksem("ds2b")]
    store_lanes = dst + dkst + dvst + [dgst, dz1] + ds2
    all_lanes = dx + dst + dw + dkst + dvst + [dvw] + dfm + [dkh, dwm, dgst, dz1] + ds2
    s_pe = Sem(octx.enter_context(nc.semaphore("s_pe")))
    s_act = Sem(octx.enter_context(nc.semaphore("s_act")))
    s_dve = Sem(octx.enter_context(nc.semaphore("s_dve")))
    s_pool = Sem(octx.enter_context(nc.semaphore("s_pool")))

    def sb(st, name, shape, dtype):
        return st.enter_context(nc.sbuf_tensor(name, shape, dtype))

    def phase_entry(ph):
        for eng in ENGINES:
            for l in all_lanes:
                ph.wait(eng, l, l.n)
            for sem in (s_pe, s_act, s_dve, s_pool):
                ph.wait(eng, sem, sem.n)

    ones_col_bf = sb(octx, "ones_col_bf", [128, 1], BF16)
    ones_col_f = sb(octx, "ones_col_f", [128, 1], F32)
    ones_row_f = sb(octx, "ones_row_f", [1, 128], F32)
    eps_col = sb(octx, "eps_col", [128, 1], F32)
    mw1_col = sb(octx, "mw1_col", [128, 1], F32)
    sw1_col = sb(octx, "sw1_col", [128, 1], F32)
    mw2_col = sb(octx, "mw2_col", [128, 1], F32)
    sw2_col = sb(octx, "sw2_col", [128, 1], F32)
    s1_cols = sb(octx, "s1_cols", [128, 16], F32)
    al1_cols = sb(octx, "al1_cols", [128, 16], F32)
    a2_cols = sb(octx, "a2_cols", [128, 16], F32)
    al2_cols = sb(octx, "al2_cols", [128, 16], F32)
    w1acc = sb(octx, "w1acc", [128, 1], F32)
    w2acc = sb(octx, "w2acc", [128, 1], F32)
    tiny = sb(octx, "tiny", [128, 32], F32)

    ps_a = octx.enter_context(nc.psum_tensor("ps_a", [128, 2048], F32))
    ps_lt0 = octx.enter_context(nc.psum_tensor("ps_lt0", [128, 1024], F32))
    ps_lt1 = octx.enter_context(nc.psum_tensor("ps_lt1", [128, 1024], F32))
    ps_lt = [ps_lt0, ps_lt1]
    ps_rot = [ps_lt0[:, 0:512], ps_lt0[:, 512:1024],
              ps_lt1[:, 0:512], ps_lt1[:, 512:1024]]

    ph = Phase(nc)
    for tile, val in ((ones_col_bf, 1.0), (ones_col_f, 1.0), (ones_row_f, 1.0),
                      (eps_col, EPS), (w1acc, 0.0), (w2acc, 0.0), (a2_cols, 0.0)):
        ph.run("gpsimd", "memset", tile[:], val, inc=s_pool)
    init_done = s_pool.n
    ph.emit()

    def ln_stats(ph, xtile, ncols, sq_tile, cb=0):
        inv = 1.0 / ncols
        mu_t, m2_t, var_t, sd_t = (tiny[:, cb:cb + 1], tiny[:, cb + 1:cb + 2],
                                   tiny[:, cb + 2:cb + 3], tiny[:, cb + 3:cb + 4])
        rstd_t, nmu_t = tiny[:, cb + 4:cb + 5], tiny[:, cb + 5:cb + 6]
        ph.run("vector", "reduce_sum", mu_t, xtile, AX.X)
        v_sq = ph.run("scalar", "activation", sq_tile, xtile, AF.Square, inc=s_act)
        ph.wait("vector", s_act, v_sq)
        v_ss = ph.run("vector", "reduce_sum", var_t, sq_tile, AX.X, inc=s_dve)
        ph.wait("vector", s_dve, v_ss)
        ph.runs("vector", s_dve, "tensor_scalar", mu_t, mu_t, inv, None, ALU.mult)
        ph.runs("vector", s_dve, "tensor_tensor", m2_t, mu_t, mu_t, ALU.mult)
        ph.runs("vector", s_dve, "scalar_tensor_tensor", var_t, var_t, inv, m2_t,
                ALU.mult, ALU.subtract)
        v_var = ph.runs("vector", s_dve, "tensor_scalar", nmu_t, mu_t, -1.0, None,
                        ALU.mult)
        ph.wait("scalar", s_dve, v_var)
        v_sd = ph.run("scalar", "activation", sd_t, var_t, AF.Sqrt,
                      bias=eps_col[:], inc=s_act)
        ph.wait("vector", s_act, v_sd)
        v_r = ph.run("vector", "reciprocal", rstd_t, sd_t, inc=s_dve)
        ph.wait("vector", s_dve, v_r)
        return nmu_t, rstd_t

    if UNTIL >= 1:
        st = contextlib.ExitStack()
        xt = [sb(st, f"p0_x{i}", [128, D], F32) for i in range(2)]
        sqt = [sb(st, f"p0_sq{i}", [128, D], F32) for i in range(2)]
        lnt = [sb(st, f"p0_ln{i}", [128, D], BF16) for i in range(2)]
        ph = Phase(nc)
        phase_entry(ph)
        for e in ("scalar", "vector", "gpsimd", "tensor", "sync"):
            ph.wait(e, s_pool, init_done)
        ln_done = [0, 0]
        ln_stored = [0, 0]
        for i in range(S // 128):
            j = i % 2
            ph.wait("sync", s_dve, ln_done[j])
            v_x = ph.dma(dx[j], xt[j][:], xin[i * 128:(i + 1) * 128, :])
            ph.wait("vector", dx[j], v_x)
            ph.wait("scalar", dx[j], v_x)
            nmu, rstd = ln_stats(ph, xt[j][:], D, sqt[j][:], cb=8 + 6 * j)
            ph.wait("vector", dst[j], ln_stored[j])
            v_ln = ph.run("vector", "tensor_scalar", lnt[j][:], xt[j][:],
                          nmu, rstd, ALU.add, ALU.mult, inc=s_dve)
            ln_done[j] = v_ln
            ph.wait("sync", s_dve, v_ln)
            dstap = bass.AP(ln1s, i * 128 * 128, [[128, 128], [S * 128, 16], [1, 128]])
            ln_stored[j] = ph.dma(dst[j], dstap, lnt[j][:])
        ph.emit()
        st.close()

    if UNTIL >= 2:
        st_fm = contextlib.ExitStack()
        ln1_fm = [sb(st_fm, f"ln1fm{dc}", [128, S], BF16) for dc in range(16)]
        st = contextlib.ExitStack()
        ph = Phase(nc)
        phase_entry(ph)
        for dc in range(16):
            src = bass.AP(ln1s, dc * S * 128, [[128, S], [1, 128]])
            ph.dma(dfm[0], ln1_fm[dc][:], src, transpose=True)
        v_lnfm = dfm[0].n
        ph.wait("tensor", dfm[0], v_lnfm)
        ph.wait("gpsimd", dfm[0], v_lnfm)

        ps_busy = [0, 0, 0, 0]

        def mm_group(ph, ps_ap, slot, pieces):
            if slot is not None and ps_busy[slot]:
                ph.wait("tensor", s_act, ps_busy[slot])
            n = len(pieces)
            v = None
            for idx, (l, r) in enumerate(pieces):
                if idx == n - 1:
                    v = ph.run("tensor", "matmul", ps_ap, l, r,
                               start=(idx == 0), stop=True, inc=s_pe)
                else:
                    ph.run("tensor", "matmul", ps_ap, l, r,
                           start=(idx == 0), stop=False)
            return v

        wkf = [sb(st, f"p2_wkf{i}", [128, 2048], F32) for i in range(2)]
        wkb = [sb(st, f"p2_wkb{i}", [128, 2048], BF16) for i in range(2)]
        kst = [sb(st, f"p2_kst{i}", [128, S], BF16) for i in range(2)]
        wk_cast = [0, 0]
        wk_used = [0, 0]
        kst_stored = [0, 0]

        for idx in range(32):
            kf = idx % 16
            is_k = idx < 16
            colbase = (D + kf * 128) if is_k else (kf * 128)
            ntok = S if is_k else T
            scratch = Ks if is_k else Qs
            j = idx % 2
            ph.wait("sync", s_pool, wk_cast[j])
            src = bass.AP(wqkv, colbase, [[3 * D, 128], [128 * 3 * D, 16], [1, 128]])
            v_w = ph.dma(dw[j], wkf[j][:], src)
            ph.wait("gpsimd", dw[j], v_w)
            ph.wait("gpsimd", s_pe, wk_used[j])
            v_cast = ph.run("gpsimd", "tensor_copy", wkb[j][:], wkf[j][:],
                            inc=s_pool)
            wk_cast[j] = v_cast
            ph.wait("tensor", s_pool, v_cast)
            ph.wait("scalar", dkst[j], kst_stored[j])
            for t in range(ntok // 512):
                slot = t % 4
                pieces = [(wkb[j][:, dc * 128:(dc + 1) * 128],
                           ln1_fm[dc][:, t * 512:(t + 1) * 512]) for dc in range(16)]
                v_mm = mm_group(ph, ps_rot[slot], slot, pieces)
                ph.wait("scalar", s_pe, v_mm)
                v_ev = ph.run("scalar", "copy", kst[j][:, t * 512:(t + 1) * 512],
                              ps_rot[slot], inc=s_act)
                ps_busy[slot] = v_ev
            wk_used[j] = v_mm
            ph.wait("sync", s_act, v_ev)
            kst_stored[j] = ph.dma(dkst[j], scratch[kf * 128:(kf + 1) * 128, :ntok],
                                   kst[j][:, :ntok])

        ph.emit()
        st.close()
        st = contextlib.ExitStack()
        ph = Phase(nc)
        phase_entry(ph)
        wvf = sb(st, "p2_wvf", [128, 8192], F32)
        wvb = sb(st, "p2_wvb", [128, 8192], BF16)
        vst = [sb(st, f"p2_vst{i}", [128, 512], BF16) for i in range(4)]
        vst_stored = [0, 0, 0, 0]
        wv_cast = 0
        wv_used = 0
        for vt in range(4):
            ph.wait("sync", s_pool, wv_cast)
            src = bass.AP(wqkv, 2 * D + vt * 512,
                          [[3 * D, 128], [128 * 3 * D, 16], [1, 512]])
            v_w = ph.dma(dvw, wvf[:], src)
            ph.wait("gpsimd", dvw, v_w)
            ph.wait("gpsimd", s_pe, wv_used)
            wv_cast = ph.run("gpsimd", "tensor_copy", wvb[:], wvf[:], inc=s_pool)
            ph.wait("tensor", s_pool, wv_cast)
            for tokc in range(32):
                slot = tokc % 4
                pieces = [(ln1_fm[dc][:, tokc * 128:(tokc + 1) * 128],
                           wvb[:, dc * 512:(dc + 1) * 512]) for dc in range(16)]
                v_mm = mm_group(ph, ps_rot[slot], slot, pieces)
                sj = tokc % 4
                ph.wait("scalar", dvst[sj], vst_stored[sj])
                ph.wait("scalar", s_pe, v_mm)
                v_ev = ph.run("scalar", "copy", vst[sj][:], ps_rot[slot], inc=s_act)
                ps_busy[slot] = v_ev
                ph.wait("sync", s_act, v_ev)
                vst_stored[sj] = ph.dma(
                    dvst[sj], Vs[tokc * 128:(tokc + 1) * 128, vt * 512:(vt + 1) * 512],
                    vst[sj][:])
            wv_used = v_mm
        ph.emit()
        st.close()
        st_fm.close()

    comb_st = contextlib.ExitStack()
    if UNTIL >= 3:
        combined = [sb(comb_st, f"comb{hc}", [128, T], BF16) for hc in range(16)]
        st = contextlib.ExitStack()
        kh = sb(st, "p3_kh", [128, S], BF16)
        vh = sb(st, "p3_vh", [128, S], BF16)
        qh = sb(st, "p3_qh", [128, T], BF16)
        pbuf = sb(st, "p3_p", [128, 32 * 512], BF16)
        sqq = sb(st, "p3_sqq", [128, S], BF16)
        lacc = sb(st, "p3_lacc", [128, 512], F32)
        lacc2 = sb(st, "p3_lacc2", [128, 512], F32)
        rb_sb = sb(st, "p3_rb", [128, 512], F32)
        negc = sb(st, "p3_negc", [128, 1], F32)
        rinv = sb(st, "p3_rinv", [1, 512], F32)
        w1m = sb(st, "p3_w1m", [128, 2048], F32)
        w1ab = sb(st, "p3_w1ab", [128, 2048], BF16)
        ph = Phase(nc)
        phase_entry(ph)

        w1m_used = 0

        def w1_mean_piece(i):
            nonlocal w1m_used
            rc, cp = i // 4, i % 4
            ph.wait("sync", s_act, w1m_used)
            v_w = ph.dma(dwm, w1m[:],
                         w1t[rc * 128:(rc + 1) * 128, cp * 2048:(cp + 1) * 2048])
            ph.wait("scalar", dwm, v_w)
            ph.wait("scalar", s_act, w1m_used)
            ph.wait("scalar", s_dve, s_dve.n)
            va = ph.run("scalar", "activation", w1ab[:], w1m[:], AF.Abs,
                        accum_out=tiny[:, 6:7], inc=s_act)
            w1m_used = va
            ph.wait("vector", s_act, va)
            ph.runs("vector", s_dve, "tensor_tensor", w1acc[:], w1acc[:],
                    tiny[:, 6:7], ALU.add)

        lacc_used = 0
        rinv_used = 0
        for h in range(NH):
            ph.wait("sync", s_pe, s_pe.n)
            ph.wait("sync", s_dve, s_dve.n)
            ph.wait("sync", s_act, s_act.n)
            ph.wait("tensor", s_dve, s_dve.n)
            ph.wait("tensor", s_act, s_act.n)
            ph.dma(dkh, kh[:], Ks[h * 128:(h + 1) * 128, :])
            ph.dma(dkh, vh[:],
                   bass.AP(Vs, h * 128, [[D, 128], [D * 128, 32], [1, 128]]))
            v_kvq = ph.dma(dkh, qh[:], Qs[h * 128:(h + 1) * 128, :])
            ph.wait("scalar", dkh, v_kvq)
            va = ph.run("scalar", "activation", sqq[:, :T], qh[:], AF.Square,
                        inc=s_act)
            ph.wait("tensor", s_act, va)
            vqn = None
            for t in range(4):
                vqn = ph.run("tensor", "matmul", ps_a[0:1, t * 512:(t + 1) * 512],
                             ones_col_bf[:], sqq[:, t * 512:(t + 1) * 512],
                             start=True, stop=True, inc=s_pe if t == 3 else None) or vqn
            ph.wait("vector", s_pe, s_pe.n)
            v1 = ph.run("vector", "tensor_reduce", tiny[:1, 0:1], ps_a[0:1, 0:T],
                        AX.X, ALU.max, inc=s_dve)
            ph.wait("scalar", dkh, v_kvq)
            ph.wait("scalar", s_pe, vqn)
            va = ph.run("scalar", "activation", sqq[:], kh[:], AF.Square,
                        inc=s_act)
            ph.wait("tensor", s_act, va)
            ph.wait("tensor", s_dve, v1)
            kmx = tiny[:1, 1:2]
            for rnd in range(2):
                for t in range(4):
                    tt = rnd * 4 + t
                    ph.run("tensor", "matmul", ps_a[0:1, t * 512:(t + 1) * 512],
                           ones_col_bf[:], sqq[:, tt * 512:(tt + 1) * 512],
                           start=True, stop=True,
                           inc=s_pe if t == 3 else None)
                ph.wait("vector", s_pe, s_pe.n)
                ph.wait("vector", s_dve, s_dve.n)
                vr = ph.run("vector", "tensor_reduce",
                            kmx if rnd == 0 else tiny[:1, 2:3],
                            ps_a[0:1, 0:T], AX.X, ALU.max, inc=s_dve)
                if rnd == 1:
                    ph.wait("vector", s_dve, vr)
                    ph.runs("vector", s_dve, "tensor_tensor", kmx, kmx,
                            tiny[:1, 2:3], ALU.max)
                else:
                    ph.wait("tensor", s_dve, vr)
            ph.runs("vector", s_dve, "tensor_tensor", tiny[:1, 2:3], tiny[:1, 0:1],
                    kmx, ALU.mult)
            v3 = ph.runs("vector", s_dve, "tensor_scalar", tiny[:1, 2:3],
                         tiny[:1, 2:3], 1.0 / 128.0, None, ALU.mult)
            ph.wait("scalar", s_dve, v3)
            v4 = ph.run("scalar", "activation", tiny[:1, 3:4], tiny[:1, 2:3],
                        AF.Sqrt, inc=s_act)
            ph.wait("vector", s_act, v4)
            v5 = ph.run("vector", "tensor_scalar", tiny[:1, 4:5], tiny[:1, 3:4],
                        -1.0, None, ALU.mult, inc=s_dve)
            ph.wait("tensor", s_dve, v5)
            v6 = ph.run("tensor", "matmul", ps_a[:, 1024:1025], ones_row_f[:],
                        tiny[:1, 4:5], start=True, stop=True, inc=s_pe)
            ph.wait("scalar", s_pe, v6)
            vc = ph.run("scalar", "copy", negc[:], ps_a[:, 1024:1025], inc=s_act)

            ph.wait("tensor", dkh, v_kvq)
            evac_val = 0
            prev_tail = [0, 0]
            for g in range(4):
                exp_vals = []
                for pr in range(16):
                    plt = ps_lt[pr % 2]
                    if pr >= 2:
                        ph.wait("tensor", s_act, exp_vals[pr - 2])
                    else:
                        ph.wait("tensor", s_act, prev_tail[pr])
                    for sub in range(2):
                        kc = pr * 2 + sub
                        v_mm = ph.run("tensor", "matmul",
                                      plt[:, sub * 512:(sub + 1) * 512],
                                      kh[:, kc * 128:(kc + 1) * 128],
                                      qh[:, g * 512:(g + 1) * 512],
                                      start=True, stop=True,
                                      inc=s_pe if sub == 1 else None)
                    ph.wait("scalar", s_pe, v_mm)
                    ph.wait("scalar", s_act, vc)
                    ev = ph.run("scalar", "activation",
                                pbuf[:, pr * 1024:(pr + 1) * 1024], plt[:],
                                AF.Exp, bias=negc[:], scale=INV_SQRT_HD,
                                inc=s_act)
                    exp_vals.append(ev)
                ph.wait("vector", s_act, exp_vals[0])
                ph.wait("vector", s_pe, lacc_used)
                ph.runs("vector", s_dve, "tensor_copy", lacc[:], pbuf[:, 0:512])
                ph.runs("vector", s_dve, "tensor_tensor", lacc[:], lacc[:],
                        pbuf[:, 512:1024], ALU.add)
                ph.wait("gpsimd", s_act, exp_vals[11])
                ph.wait("gpsimd", s_pe, lacc_used)
                ph.runs("gpsimd", s_pool, "tensor_copy", lacc2[:],
                        pbuf[:, 11 * 1024:11 * 1024 + 512])
                for pr in range(1, 11):
                    ph.wait("vector", s_act, exp_vals[pr])
                    for sub in range(2):
                        off = pr * 1024 + sub * 512
                        ph.runs("vector", s_dve, "tensor_tensor", lacc[:],
                                lacc[:], pbuf[:, off:off + 512], ALU.add)
                first = True
                for pr in range(11, 16):
                    ph.wait("gpsimd", s_act, exp_vals[pr])
                    for sub in range(2):
                        if first:
                            first = False
                            continue
                        off = pr * 1024 + sub * 512
                        ph.runs("gpsimd", s_pool, "tensor_tensor", lacc2[:],
                                lacc2[:], pbuf[:, off:off + 512], ALU.add)
                ph.wait("vector", s_pool, s_pool.n)
                v_l = ph.run("vector", "tensor_tensor", lacc[:], lacc[:],
                             lacc2[:], ALU.add, inc=s_dve)
                ph.wait("tensor", s_act, exp_vals[15])
                ph.wait("tensor", s_dve, evac_val)
                for kc in range(32):
                    v_pv = ph.run("tensor", "matmul", ps_a[:, 1536:2048],
                                  vh[:, kc * 128:(kc + 1) * 128],
                                  pbuf[:, kc * 512:(kc + 1) * 512],
                                  start=(kc == 0), stop=(kc == 31),
                                  inc=s_pe if kc == 31 else None)
                ph.wait("tensor", s_dve, v_l)
                v_lm = ph.run("tensor", "matmul", ps_a[0:1, 0:512],
                              ones_col_f[:], lacc[:], start=True, stop=True,
                              inc=s_pe)
                lacc_used = v_lm
                ph.wait("vector", s_pe, v_lm)
                ph.wait("vector", s_pe, rinv_used)
                v_r = ph.run("vector", "reciprocal", rinv[:], ps_a[0:1, 0:512],
                             inc=s_dve)
                ph.wait("vector", s_dve, v_r)
                ph.wait("tensor", s_dve, v_r)
                v_bm = ph.run("tensor", "matmul", ps_a[:, 512:1024],
                              ones_row_f[:], rinv[:], start=True, stop=True,
                              inc=s_pe)
                rinv_used = v_bm
                ph.wait("scalar", s_pe, v_bm)
                v_rb = ph.run("scalar", "copy", rb_sb[:], ps_a[:, 512:1024],
                              inc=s_act)
                ph.wait("vector", s_act, v_rb)
                ph.wait("vector", s_pe, v_pv)
                evac_val = ph.run("vector", "tensor_tensor",
                                  combined[h][:, g * 512:(g + 1) * 512],
                                  ps_a[:, 1536:2048], rb_sb[:], ALU.mult,
                                  inc=s_dve)
                prev_tail = [exp_vals[14], exp_vals[15]]
            for pp in range(4):
                w1_mean_piece(h * 4 + pp)

        ph.wait("tensor", s_dve, s_dve.n)
        v = ph.run("tensor", "matmul", ps_a[0:1, 0:1], ones_col_f[:], w1acc[:],
                   start=True, stop=True, inc=s_pe)
        ph.wait("vector", s_pe, v)
        v = ph.runs("vector", s_dve, "tensor_scalar", tiny[:1, 7:8],
                    ps_a[0:1, 0:1], 1.0 / 16777216.0, EPS, ALU.mult, ALU.max)
        ph.wait("tensor", s_dve, v)
        v = ph.run("tensor", "matmul", ps_a[:, 0:1], ones_row_f[:],
                   tiny[:1, 7:8], start=True, stop=True, inc=s_pe)
        ph.wait("scalar", s_pe, v)
        v = ph.run("scalar", "copy", mw1_col[:], ps_a[:, 0:1], inc=s_act)
        ph.wait("vector", s_act, v)
        v = ph.run("vector", "reciprocal", sw1_col[:], mw1_col[:], inc=s_dve)
        ph.wait("vector", s_dve, v)
        ph.emit()
        st.close()

    if UNTIL >= 4:
        st = contextlib.ExitStack()
        wof = sb(st, "p4_wof", [128, 4096], F32)
        wob = [sb(st, f"p4_wob{hc}", [128, 2048], BF16) for hc in range(16)]
        xtt = [sb(st, f"p4_x{i}", [128, D], F32) for i in range(2)]
        x2t = [sb(st, f"p4_x2{i}", [128, D], F32) for i in range(2)]
        ln2t = sb(st, "p4_ln2", [128, D], F32)
        sq2t = sb(st, "p4_sq2", [128, D], F32)
        z1t = sb(st, "p4_z1", [128, D], BF16)
        ph = Phase(nc)
        phase_entry(ph)
        wof_used = 0
        for jp in range(8):
            ph.wait("sync", s_pool, wof_used)
            src = bass.AP(wout, jp * 256, [[D, 128], [128 * D, 16], [1, 256]])
            v_w = ph.dma(dvw, wof[:], src)
            ph.wait("gpsimd", dvw, v_w)
            for hc in range(16):
                inc = s_pool if hc == 15 else None
                v_c = ph.run("gpsimd", "tensor_copy",
                             wob[hc][:, jp * 256:(jp + 1) * 256],
                             wof[:, hc * 256:(hc + 1) * 256], inc=inc)
            wof_used = v_c
        ph.wait("tensor", s_pool, s_pool.n)
        x2_stored = [0, 0]
        z1_stored = 0
        for tc in range(16):
            j = tc % 2
            ph.wait("sync", s_dve, s_dve.n)
            v_x = ph.dma(dx[j], xtt[j][:], xin[tc * 128:(tc + 1) * 128, :])
            for of_t in range(4):
                for hc in range(16):
                    inc = s_pe if hc == 15 else None
                    v_mm = ph.run("tensor", "matmul",
                                  ps_a[:, of_t * 512:(of_t + 1) * 512],
                                  combined[hc][:, tc * 128:(tc + 1) * 128],
                                  wob[hc][:, of_t * 512:(of_t + 1) * 512],
                                  start=(hc == 0), stop=(hc == 15), inc=inc)
            ph.wait("vector", dst[j], x2_stored[j])
            ph.wait("vector", s_pe, v_mm)
            ph.wait("vector", dx[j], v_x)
            v_x2 = ph.run("vector", "tensor_tensor", x2t[j][:], ps_a[:],
                          xtt[j][:], ALU.add, inc=s_dve)
            ph.wait("tensor", s_dve, v_x2)
            ph.wait("sync", s_dve, v_x2)
            x2_stored[j] = ph.dma(dst[j], x2s[tc * 128:(tc + 1) * 128, :],
                                  x2t[j][:])
            ph.wait("scalar", s_dve, v_x2)
            nmu, rstd = ln_stats(ph, x2t[j][:], D, sq2t[:], cb=8 + 6 * j)
            ph.runs("vector", s_dve, "tensor_scalar", ln2t[:], x2t[j][:],
                    nmu, rstd, ALU.add, ALU.mult)
            v_a = ph.run("vector", "tensor_reduce", tiny[:, 6:7], ln2t[:],
                         AX.X, ALU.max, True, inc=s_dve)
            ph.wait("vector", s_dve, v_a)
            ph.runs("vector", s_dve, "tensor_scalar", tiny[:, 6:7], tiny[:, 6:7],
                    EPS, None, ALU.max)
            v_r = ph.run("vector", "reciprocal", tiny[:, 7:8], tiny[:, 6:7],
                         inc=s_dve)
            ph.wait("vector", s_dve, v_r)
            ph.runs("vector", s_dve, "tensor_scalar", s1_cols[:, tc:tc + 1],
                    tiny[:, 7:8], 127.0, None, ALU.mult)
            ph.runs("vector", s_dve, "scalar_tensor_tensor", al1_cols[:, tc:tc + 1],
                    tiny[:, 6:7], 1.0 / 127.0, mw1_col[:], ALU.mult, ALU.mult)
            ph.wait("vector", dz1, z1_stored)
            v_z = ph.custom(ROUND_TS, out=z1t[:], in0=ln2t[:],
                            s0=s1_cols[:, tc:tc + 1], s1=MAGIC, inc=s_dve)
            ph.wait("sync", s_dve, v_z)
            dstap = bass.AP(z1s, tc * 128 * 128, [[128, 128], [T * 128, 16], [1, 128]])
            z1_stored = ph.dma(dz1, dstap, z1t[:])
        ph.emit()
        st.close()
    comb_st.close()

    if UNTIL >= 5:
        st = contextlib.ExitStack()
        z1_fm = [sb(st, f"z1fm{dc}", [128, T], BF16) for dc in range(16)]
        w1f = [sb(st, f"p6_w1f{i}", [128, 4096], F32) for i in range(2)]
        w1q = [sb(st, f"p6_w1q{i}", [128, 8192], BF16) for i in range(2)]
        gst = sb(st, "p6_gst", [128, 8192], F16)
        gtmp = [sb(st, f"p6_g{i}", [128, 512], F32) for i in range(2)]
        w2m = sb(st, "p6_w2m", [128, 2048], F32)
        w2ab = sb(st, "p6_w2ab", [128, 2048], BF16)
        ph = Phase(nc)
        phase_entry(ph)
        for dc in range(16):
            src = bass.AP(z1s, dc * T * 128, [[128, T], [1, 128]])
            ph.dma(dfm[0], z1_fm[dc][:], src, transpose=True)
        ph.wait("tensor", dfm[0], dfm[0].n)

        w2m_used = 0

        def w2_mean_piece(i):
            nonlocal w2m_used
            ph.wait("sync", s_act, w2m_used)
            v_w = ph.dma(dwm, w2m[:], w2t[i * 128:(i + 1) * 128, :])
            ph.wait("scalar", dwm, v_w)
            ph.wait("scalar", s_act, w2m_used)
            ph.wait("scalar", s_dve, s_dve.n)
            va = ph.run("scalar", "activation", w2ab[:], w2m[:], AF.Abs,
                        accum_out=tiny[:, 6:7], inc=s_act)
            w2m_used = va
            ph.wait("vector", s_act, va)
            ph.runs("vector", s_dve, "tensor_tensor", w2acc[:], w2acc[:],
                    tiny[:, 6:7], ALU.add)

        w1q_ready = [0, 0]
        w1q_used = [0, 0]
        gst_stored = 0
        gdone = {}
        gcopy = {}
        gi = 0
        for hs in range(16):
            j = hs % 2
            ph.wait("vector", s_pe, w1q_used[j])
            for half in range(2):
                src = bass.AP(w1t, hs * 512 + half * 256,
                              [[H, 128], [128 * H, 16], [1, 256]])
                ph.wait("sync", s_dve, w1q_ready[j])
                v_w = ph.dma(dw[j], w1f[j][:], src)
                ph.wait("vector", dw[j], v_w)
                v_rnd = ph.custom(ROUND_TS, out=w1f[j][:], in0=w1f[j][:],
                                  s0=sw1_col[:], s1=MAGIC, inc=s_dve)
                ph.wait("vector", s_dve, v_rnd)
                w1q_dst = bass.AP(w1q[j], half * 256,
                                  [[8192, 128], [512, 16], [1, 256]])
                v_q = ph.run("vector", "tensor_scalar", w1q_dst, w1f[j][:],
                             -1.0, 1.0, ALU.max, ALU.min, inc=s_dve)
                w1q_ready[j] = v_q
            ph.wait("tensor", s_dve, w1q_ready[j])
            ph.wait("gpsimd", dgst, gst_stored)
            for tc in range(16):
                slot = gi % 4
                plt = ps_lt[slot // 2]
                ps_ap = plt[:, (slot % 2) * 512:(slot % 2 + 1) * 512]
                if gdone.get(gi - 4):
                    ph.wait("tensor", s_act, gdone[gi - 4])
                for dc in range(16):
                    inc = s_pe if dc == 15 else None
                    v_mm = ph.run("tensor", "matmul", ps_ap,
                                  z1_fm[dc][:, tc * 128:(tc + 1) * 128],
                                  w1q[j][:, dc * 512:(dc + 1) * 512],
                                  start=(dc == 0), stop=(dc == 15), inc=inc)
                gj = tc % 2
                ph.wait("scalar", s_pe, v_mm)
                if gcopy.get(gi - 2):
                    ph.wait("scalar", s_pool, gcopy[gi - 2])
                ph.wait("scalar", s_dve, s_dve.n)
                v_g = ph.run("scalar", "activation", gtmp[gj][:], ps_ap,
                             AF.Gelu_apprx_tanh, scale=al1_cols[:, tc:tc + 1],
                             inc=s_act)
                gdone[gi] = v_g
                ph.wait("vector", s_act, v_g)
                v_red = ph.run("vector", "tensor_reduce", tiny[:, 0:1],
                               gtmp[gj][:], AX.X, ALU.max, True, inc=s_dve)
                ph.wait("vector", s_dve, v_red)
                ph.runs("vector", s_dve, "tensor_tensor", a2_cols[:, tc:tc + 1],
                        a2_cols[:, tc:tc + 1], tiny[:, 0:1], ALU.max)
                ph.wait("gpsimd", s_act, v_g)
                gcopy[gi] = ph.run("gpsimd", "tensor_copy",
                                   gst[:, tc * 512:(tc + 1) * 512], gtmp[gj][:],
                                   inc=s_pool)
                gi += 1
            w1q_used[j] = v_mm
            ph.wait("sync", s_pool, gcopy[gi - 1])
            for hc4 in range(4):
                dstap = bass.AP(gs, (hs * 4 + hc4) * T * 128,
                                [[128, 128], [128 * 128, 16], [1, 128]])
                srcap = bass.AP(gst, hc4 * 128, [[8192, 128], [512, 16], [1, 128]])
                gst_stored = ph.dma(dgst, dstap, srcap)
            w2_mean_piece(hs)

        for i in range(16, 64):
            w2_mean_piece(i)
        ph.wait("tensor", s_dve, s_dve.n)
        ph.wait("tensor", s_act, s_act.n)
        v = ph.run("tensor", "matmul", ps_a[0:1, 0:1], ones_col_f[:], w2acc[:],
                   start=True, stop=True, inc=s_pe)
        ph.wait("vector", s_pe, v)
        v = ph.runs("vector", s_dve, "tensor_scalar", tiny[:1, 7:8],
                    ps_a[0:1, 0:1], 1.0 / 16777216.0, EPS, ALU.mult, ALU.max)
        ph.wait("tensor", s_dve, v)
        v = ph.run("tensor", "matmul", ps_a[:, 0:1], ones_row_f[:],
                   tiny[:1, 7:8], start=True, stop=True, inc=s_pe)
        ph.wait("scalar", s_pe, v)
        v = ph.run("scalar", "copy", mw2_col[:], ps_a[:, 0:1], inc=s_act)
        ph.wait("vector", s_act, v)
        v = ph.run("vector", "reciprocal", sw2_col[:], mw2_col[:], inc=s_dve)
        ph.wait("vector", s_dve, v)
        s2_stored = [0, 0]
        for tc in range(16):
            cb = (tc % 2) * 3
            ph.wait("vector", ds2[tc % 2], s2_stored[tc % 2])
            ph.runs("vector", s_dve, "tensor_scalar", tiny[:, cb:cb + 1],
                    a2_cols[:, tc:tc + 1], EPS, None, ALU.max)
            v_r = ph.run("vector", "reciprocal", tiny[:, cb + 1:cb + 2],
                         tiny[:, cb:cb + 1], inc=s_dve)
            ph.wait("vector", s_dve, v_r)
            ph.runs("vector", s_dve, "tensor_scalar", tiny[:, cb + 2:cb + 3],
                    tiny[:, cb + 1:cb + 2], 127.0, None, ALU.mult)
            vv = ph.run("vector", "scalar_tensor_tensor", al2_cols[:, tc:tc + 1],
                        tiny[:, cb:cb + 1], 1.0 / 127.0, mw2_col[:], ALU.mult,
                        ALU.mult, inc=s_dve)
            ph.wait("sync", s_dve, vv)
            s2_stored[tc % 2] = ph.dma(ds2[tc % 2],
                                       s2s[0:1, tc * 128:(tc + 1) * 128],
                                       tiny[:, cb + 2:cb + 3])
        ph.emit()
        st.close()

    if UNTIL >= 6:
        st = contextlib.ExitStack()
        s2row = sb(st, "p7_s2row", [1, T], F32)
        s2b = sb(st, "p7_s2b", [128, T], F32)
        acc = [sb(st, f"p7_acc{i}", [128, D], F32) for i in range(8)]
        gfm = [sb(st, f"p7_gfm{i}", [128, 1024], F16) for i in range(2)]
        z2fm = [sb(st, f"p7_z2{i}", [128, 1024], BF16) for i in range(16)]
        w2f = [sb(st, f"p7_w2f{i}", [128, 2048], F32) for i in range(2)]
        w2q = [sb(st, f"p7_w2q{i}", [128, 2048], BF16) for i in range(16)]
        x2tt = [sb(st, "p8_x2", [128, D], F32)] * 2
        ot = x2tt
        ph = Phase(nc)
        phase_entry(ph)
        v_s2 = ph.dma(dwm, s2row[:], s2s[:])
        ph.wait("tensor", dwm, v_s2)
        for t in range(4):
            v = ph.run("tensor", "matmul", ps_a[:, t * 512:(t + 1) * 512],
                       ones_row_f[:], s2row[:, t * 512:(t + 1) * 512],
                       start=True, stop=True, inc=s_pe if t == 3 else None)
        ph.wait("scalar", s_pe, v)
        v_sb = ph.run("scalar", "copy", s2b[:], ps_a[:], inc=s_act)
        ph.wait("vector", s_act, v_sb)
        ph.wait("tensor", s_act, v_sb)

        stages = [(hf, hsl) for hf in range(2) for hsl in range(8)]
        set_ready = [0, 0]
        set_mm_done = [0, 0]
        gfm_used = [0, 0]
        w2f_used = [0, 0]
        pacc_lt = [ps_lt0[:, 0:512], ps_lt0[:, 512:1024],
                   ps_lt1[:, 0:512], ps_lt1[:, 512:1024]]

        def quant_chunk(stage_i, k8):
            hf, hsl = stages[stage_i]
            p = stage_i % 2
            hc = hsl * 8 + k8
            j = (stage_i * 8 + k8) % 2
            ph.wait("sync", s_dve, max(gfm_used[j], w2f_used[j]))
            ph.wait("sync", s_pe, set_mm_done[p])
            src = bass.AP(gs, hc * T * 128 + hf * 1024 * 128,
                          [[128, 1024], [1, 128]])
            v_g = ph.dma(dfm[j], gfm[j][:], src, transpose=True)
            v_w = ph.dma(dw[j], w2f[j][:], w2t[hc * 128:(hc + 1) * 128, :])
            ph.wait("vector", dfm[j], v_g)
            ph.wait("vector", s_pe, set_mm_done[p])
            tok = slice(hf * 1024, (hf + 1) * 1024)
            gfm_used[j] = ph.custom(ROUND_TT, out=z2fm[p * 8 + k8][:],
                                    in0=gfm[j][:], in1=s2b[:, tok], s0=MAGIC,
                                    inc=s_dve)
            ph.wait("vector", dw[j], v_w)
            w2f_used[j] = ph.custom(ROUND_CLIP, out=w2q[p * 8 + k8][:],
                                    in0=w2f[j][:], s0=sw2_col[:], s1=MAGIC,
                                    inc=s_dve)
            set_ready[p] = w2f_used[j]

        out_stored = [0, 0]
        x2_used = [0, 0]
        evac_vals = {}
        for k8 in range(8):
            quant_chunk(0, k8)
        for si, (hf, hsl) in enumerate(stages):
            p = si % 2
            ph.wait("tensor", s_dve, set_ready[p])
            for tc8 in range(8):
                par = tc8 % 2
                if evac_vals.get(tc8 - 2) is not None:
                    ph.wait("tensor", s_dve, evac_vals[tc8 - 2])
                for k8 in range(8):
                    for nt in range(4):
                        pac = ps_a[:, nt * 512:(nt + 1) * 512] if par == 0                             else pacc_lt[nt]
                        inc = s_pe if (k8 == 7 and nt == 3) else None
                        v_mm = ph.run(
                            "tensor", "matmul", pac,
                            z2fm[p * 8 + k8][:, tc8 * 128:(tc8 + 1) * 128],
                            w2q[p * 8 + k8][:, nt * 512:(nt + 1) * 512],
                            start=(k8 == 0), stop=(k8 == 7), inc=inc)
                ph.wait("vector", s_pe, v_mm)
                first = (hsl == 0)
                if par == 0:
                    if first:
                        v_e = ph.run("vector", "tensor_copy", acc[tc8][:],
                                     ps_a[:], inc=s_dve)
                    else:
                        v_e = ph.run("vector", "tensor_tensor", acc[tc8][:],
                                     acc[tc8][:], ps_a[:], ALU.add, inc=s_dve)
                else:
                    for hh in range(2):
                        m, a_sl = (ps_lt[hh][:],
                                   acc[tc8][:, hh * 1024:(hh + 1) * 1024])
                        if first:
                            v_e = ph.run("vector", "tensor_copy", a_sl, m,
                                         inc=s_dve)
                        else:
                            v_e = ph.run("vector", "tensor_tensor", a_sl,
                                         a_sl, m, ALU.add, inc=s_dve)
                evac_vals[tc8] = v_e
                if si + 1 < len(stages):
                    quant_chunk(si + 1, tc8)
            set_mm_done[p] = v_mm
            if hsl == 7:
                ph.wait("vector", s_dve, s_dve.n)
                for tc8 in range(8):
                    tc = hf * 8 + tc8
                    ph.wait("sync", s_dve, x2_used[0])
                    ph.wait("sync", dst[0], out_stored[0])
                    v_x2 = ph.dma(dx[0], x2tt[0][:],
                                  x2s[tc * 128:(tc + 1) * 128, :])
                    ph.wait("vector", dx[0], v_x2)
                    v_o = ph.run("vector", "scalar_tensor_tensor", ot[0][:],
                                 acc[tc8][:], al2_cols[:, tc:tc + 1],
                                 x2tt[0][:], ALU.mult, ALU.add, inc=s_dve)
                    x2_used[0] = v_o
                    ph.wait("sync", s_dve, v_o)
                    out_stored[0] = ph.dma(dst[0],
                                           out[tc * 128:(tc + 1) * 128, :],
                                           ot[0][:])
                evac_vals = {}
        for l in store_lanes:
            ph.wait("sync", l, l.n)
        ph.emit()
        st.close()

    mybir.codegen_inst_isa_subclasses(nc)
    octx.close()
    return nc


def kernel(L0_attn_x, L0_attn_w_qkv, L0_attn_w_out, L0_ffn_w1, L0_ffn_w2):
    x = np.ascontiguousarray(np.asarray(L0_attn_x, dtype=np.float32))
    wqkv = np.ascontiguousarray(np.asarray(L0_attn_w_qkv, dtype=np.float32))
    wout = np.ascontiguousarray(np.asarray(L0_attn_w_out, dtype=np.float32))
    w1t = np.ascontiguousarray(np.asarray(L0_ffn_w1, dtype=np.float32).T)
    w2t = np.ascontiguousarray(np.asarray(L0_ffn_w2, dtype=np.float32).T)

    nc = _build()
    in_maps = []
    for c in range(8):
        b, half = c // 2, c % 2
        own = x[b, half * T:(half + 1) * T]
        other = x[b, (1 - half) * T:(2 - half) * T]
        xc = np.ascontiguousarray(np.concatenate([own, other], axis=0))
        in_maps.append({"xin": xc, "wqkv": wqkv, "wout": wout,
                        "w1t": w1t, "w2t": w2t})
    res = run_bass_kernel_spmd(nc, in_maps, core_ids=list(range(8)))
    o = np.empty((4, 4096, 2048), dtype=np.float32)
    for c in range(8):
        b, half = c // 2, c % 2
        o[b, half * T:(half + 1) * T] = res.results[c]["out"]
    return o


# version 05
# speedup vs baseline: 1.1806x; 1.0229x over previous
import contextlib
import numpy as np
import ml_dtypes
import concourse.bass as bass
import concourse.mybir as mybir
from concourse.bass_utils import run_bass_kernel_spmd
from concourse import dve_ops as _dvo
from concourse.dve_op import DveOp
from concourse.dve_spec import Spec, Src0, Src1, C0, C1, lower as _dve_lower

F32 = mybir.dt.float32
BF16 = mybir.dt.bfloat16
F16 = mybir.dt.float16
AF = mybir.ActivationFunctionType
ALU = mybir.AluOpType
AX = mybir.AxisListType

MAGIC = 12582912.0
INV_SQRT_HD = 0.08838834764831845
EPS = 1e-5

S = 4096
T = 2048
D = 2048
H = 8192
NH = 16


def _as_f(x, like):
    dt = np.float64 if np.asarray(like).dtype == np.float64 else np.float32
    return np.asarray(x, dtype=dt)


def _ref_round_ts(in0, in1, s0, s1, imm2):
    return np.round(_as_f(in0, in0) * _as_f(s0, in0))


def _ref_round_tt(in0, in1, s0, s1, imm2):
    return np.round(_as_f(in0, in0) * _as_f(in1, in0))


def _ref_round_clip(in0, in1, s0, s1, imm2):
    return np.minimum(np.maximum(np.round(_as_f(in0, in0) * _as_f(s0, in0)),
                                 -1.0), 1.0)


def _register_round_ops():
    from concourse.dve_spec import Zero, One, maxx, minn
    ops = {}
    defs = (
        ("ROUND_MUL_TS_ANT", Spec(body=((Src0 * C0) + C1) - C1), _ref_round_ts),
        ("ROUND_MUL_TT_ANT", Spec(body=((Src0 * Src1) + C0) - C0), _ref_round_tt),
        ("ROUND_CLIP_TS_ANT",
         Spec(body=minn(maxx(((Src0 * C0) + C1) - C1, Zero - One), One)),
         _ref_round_clip),
    )
    replaceable = ["DVEHASH32L_DRAIN", "DVEHASH32L_BODY", "DVEHASH32L_SEED"]
    for name, spec, ref in defs:
        existing = next((o for o in _dvo.OPS if o.name == name), None)
        if existing is not None:
            ops[name] = existing
            continue
        compiled = _dve_lower(spec, ver="v3", perf_en=False, name=name)
        op = DveOp(name, spec, subdim=False,
                   uops_sha={"v3": compiled.sha("v3")}, reference=ref)
        if len(_dvo.OPS) < 31:
            _dvo.OPS.append(op)
            row = len(_dvo.OPS)
        else:
            victim = next(n for n in replaceable
                          if any(o.name == n for o in _dvo.OPS))
            idx = next(i for i, o in enumerate(_dvo.OPS) if o.name == victim)
            replaceable.remove(victim)
            _dvo.OPS[idx] = op
            row = idx + 1
            del _dvo._SUB_OPCODE_FOR_NAME[victim]
            _dvo.CUSTOM_DVE_SPECS.pop(victim, None)
            _dvo.CUSTOM_DVE_REFS.pop(victim, None)
        _dvo._SUB_OPCODE_FOR_NAME[name] = row
        _dvo.CUSTOM_DVE_SPECS[name] = spec
        _dvo.CUSTOM_DVE_REFS[name] = ref
        ops[name] = op
    return (ops["ROUND_MUL_TS_ANT"], ops["ROUND_MUL_TT_ANT"],
            ops["ROUND_CLIP_TS_ANT"])


ROUND_TS, ROUND_TT, ROUND_CLIP = _register_round_ops()

ENGINES = ("sync", "tensor", "scalar", "vector", "gpsimd")
UNTIL = 99
DEBUG_SCRATCH = False


class Sem:
    def __init__(self, h):
        self.h = h
        self.n = 0


class Phase:
    def __init__(self, nc):
        self.nc = nc
        self.q = {e: [] for e in ENGINES}

    def on(self, eng, fn):
        self.q[eng].append(fn)

    def wait(self, eng, sem, val):
        if val and val > 0:
            self.on(eng, lambda e, _s=sem.h, _v=val: e.wait_ge(_s, _v))

    def run(self, eng, method, *args, inc=None, **kw):
        v = None
        if inc is not None:
            inc.n += 1
            v = inc.n
            def _f(e, _m=method, _a=args, _k=kw, _s=inc.h):
                getattr(e, _m)(*_a, **_k).then_inc(_s, 1)
        else:
            def _f(e, _m=method, _a=args, _k=kw):
                getattr(e, _m)(*_a, **_k)
        self.on(eng, _f)
        return v

    def runs(self, eng, sem, method, *args, **kw):
        v = self.run(eng, method, *args, inc=sem, **kw)
        self.wait(eng, sem, v)
        return v

    def custom(self, op, inc=None, **kw):
        v = None
        if inc is not None:
            inc.n += 1
            v = inc.n
            def _f(e, _o=op, _k=kw, _s=inc.h):
                e._custom_dve(_o, **_k).then_inc(_s, 1)
        else:
            def _f(e, _o=op, _k=kw):
                e._custom_dve(_o, **_k)
        self.on("vector", _f)
        return v

    def dma(self, sem, dst, src, transpose=False):
        sem.n += 16
        v = sem.n
        if transpose:
            self.on("sync", lambda e, _d=dst, _s=src, _h=sem.h:
                    e.dma_start(_d, _s, transpose=True).then_inc(_h, 16))
        else:
            self.on("sync", lambda e, _d=dst, _s=src, _h=sem.h:
                    e.dma_start(_d, _s).then_inc(_h, 16))
        return v

    def emit(self):
        with self.nc.Block() as block:
            @block.sync
            def _(e):
                for fn in self.q["sync"]:
                    fn(e)

            @block.tensor
            def _(e):
                for fn in self.q["tensor"]:
                    fn(e)

            @block.scalar
            def _(e):
                for fn in self.q["scalar"]:
                    fn(e)

            @block.vector
            def _(e):
                for fn in self.q["vector"]:
                    fn(e)

            @block.gpsimd
            def _(e):
                for fn in self.q["gpsimd"]:
                    fn(e)


def _build():
    nc = bass.Bass("TRN2", target_bir_lowering=False)

    xin = nc.dram_tensor("xin", [S, D], F32, kind="ExternalInput")
    wqkv = nc.dram_tensor("wqkv", [D, 3 * D], F32, kind="ExternalInput")
    wout = nc.dram_tensor("wout", [D, D], F32, kind="ExternalInput")
    w1t = nc.dram_tensor("w1t", [D, H], F32, kind="ExternalInput")
    w2t = nc.dram_tensor("w2t", [H, D], F32, kind="ExternalInput")
    out = nc.dram_tensor("out", [T, D], F32, kind="ExternalOutput")

    sk = "ExternalOutput" if DEBUG_SCRATCH else "Internal"
    ln1s = nc.dram_tensor("ln1s", [16, S, 128], BF16, kind=sk)
    Ks = nc.dram_tensor("Ks", [D, S], BF16, kind=sk)
    Vs = nc.dram_tensor("Vs", [S, D], BF16, kind=sk)
    Qs = nc.dram_tensor("Qs", [D, T], BF16, kind=sk)
    x2s = nc.dram_tensor("x2s", [T, D], F32, kind=sk)
    z1s = nc.dram_tensor("z1s", [16, T, 128], BF16, kind=sk)
    gs = nc.dram_tensor("gs", [64, T, 128], F16, kind=sk)
    s2s = nc.dram_tensor("s2s", [1, T], F32, kind=sk)

    octx = contextlib.ExitStack()

    def mksem(name):
        return Sem(octx.enter_context(nc.semaphore(name)))

    dx = [mksem("dx0"), mksem("dx1")]
    dst = [mksem("dst0"), mksem("dst1")]
    dw = [mksem("dw0"), mksem("dw1")]
    dkst = [mksem("dkst0"), mksem("dkst1")]
    dvst = [mksem(f"dvst{i}") for i in range(4)]
    dvw = mksem("dvw")
    dfm = [mksem("dfm0"), mksem("dfm1")]
    dkh = mksem("dkh")
    dwm = mksem("dwm")
    dgst = mksem("dgst")
    dz1 = mksem("dz1")
    ds2 = [mksem("ds2a"), mksem("ds2b")]
    store_lanes = dst + dkst + dvst + [dgst, dz1] + ds2
    all_lanes = dx + dst + dw + dkst + dvst + [dvw] + dfm + [dkh, dwm, dgst, dz1] + ds2
    s_pe = Sem(octx.enter_context(nc.semaphore("s_pe")))
    s_act = Sem(octx.enter_context(nc.semaphore("s_act")))
    s_dve = Sem(octx.enter_context(nc.semaphore("s_dve")))
    s_pool = Sem(octx.enter_context(nc.semaphore("s_pool")))

    def sb(st, name, shape, dtype):
        return st.enter_context(nc.sbuf_tensor(name, shape, dtype))

    def phase_entry(ph):
        for eng in ENGINES:
            for l in all_lanes:
                ph.wait(eng, l, l.n)
            for sem in (s_pe, s_act, s_dve, s_pool):
                ph.wait(eng, sem, sem.n)

    ones_col_bf = sb(octx, "ones_col_bf", [128, 1], BF16)
    ones_col_f = sb(octx, "ones_col_f", [128, 1], F32)
    ones_row_f = sb(octx, "ones_row_f", [1, 128], F32)
    eps_col = sb(octx, "eps_col", [128, 1], F32)
    mw1_col = sb(octx, "mw1_col", [128, 1], F32)
    sw1_col = sb(octx, "sw1_col", [128, 1], F32)
    mw2_col = sb(octx, "mw2_col", [128, 1], F32)
    sw2_col = sb(octx, "sw2_col", [128, 1], F32)
    s1_cols = sb(octx, "s1_cols", [128, 16], F32)
    al1_cols = sb(octx, "al1_cols", [128, 16], F32)
    a2_cols = sb(octx, "a2_cols", [128, 16], F32)
    al2_cols = sb(octx, "al2_cols", [128, 16], F32)
    w1acc = sb(octx, "w1acc", [128, 1], F32)
    w2acc = sb(octx, "w2acc", [128, 1], F32)
    tiny = sb(octx, "tiny", [128, 32], F32)

    ps_a = octx.enter_context(nc.psum_tensor("ps_a", [128, 2048], F32))
    ps_lt0 = octx.enter_context(nc.psum_tensor("ps_lt0", [128, 1024], F32))
    ps_lt1 = octx.enter_context(nc.psum_tensor("ps_lt1", [128, 1024], F32))
    ps_lt = [ps_lt0, ps_lt1]
    ps_rot = [ps_lt0[:, 0:512], ps_lt0[:, 512:1024],
              ps_lt1[:, 0:512], ps_lt1[:, 512:1024]]

    ph = Phase(nc)
    for tile, val in ((ones_col_bf, 1.0), (ones_col_f, 1.0), (ones_row_f, 1.0),
                      (eps_col, EPS), (w1acc, 0.0), (w2acc, 0.0), (a2_cols, 0.0)):
        ph.run("gpsimd", "memset", tile[:], val, inc=s_pool)
    init_done = s_pool.n
    ph.emit()

    def ln_stats(ph, xtile, ncols, sq_tile, cb=0):
        inv = 1.0 / ncols
        mu_t, m2_t, var_t, sd_t = (tiny[:, cb:cb + 1], tiny[:, cb + 1:cb + 2],
                                   tiny[:, cb + 2:cb + 3], tiny[:, cb + 3:cb + 4])
        rstd_t, nmu_t = tiny[:, cb + 4:cb + 5], tiny[:, cb + 5:cb + 6]
        ph.run("vector", "reduce_sum", mu_t, xtile, AX.X)
        v_sq = ph.run("scalar", "activation", sq_tile, xtile, AF.Square, inc=s_act)
        ph.wait("vector", s_act, v_sq)
        v_ss = ph.run("vector", "reduce_sum", var_t, sq_tile, AX.X, inc=s_dve)
        ph.wait("vector", s_dve, v_ss)
        ph.runs("vector", s_dve, "tensor_scalar", mu_t, mu_t, inv, None, ALU.mult)
        ph.runs("vector", s_dve, "tensor_tensor", m2_t, mu_t, mu_t, ALU.mult)
        ph.runs("vector", s_dve, "scalar_tensor_tensor", var_t, var_t, inv, m2_t,
                ALU.mult, ALU.subtract)
        v_var = ph.runs("vector", s_dve, "tensor_scalar", nmu_t, mu_t, -1.0, None,
                        ALU.mult)
        ph.wait("scalar", s_dve, v_var)
        v_sd = ph.run("scalar", "activation", sd_t, var_t, AF.Sqrt,
                      bias=eps_col[:], inc=s_act)
        ph.wait("vector", s_act, v_sd)
        v_r = ph.run("vector", "reciprocal", rstd_t, sd_t, inc=s_dve)
        ph.wait("vector", s_dve, v_r)
        return nmu_t, rstd_t

    if UNTIL >= 1:
        st = contextlib.ExitStack()
        xt = [sb(st, f"p0_x{i}", [128, D], F32) for i in range(2)]
        sqt = [sb(st, f"p0_sq{i}", [128, D], F32) for i in range(2)]
        lnt = [sb(st, f"p0_ln{i}", [128, D], BF16) for i in range(2)]
        ph = Phase(nc)
        phase_entry(ph)
        for e in ("scalar", "vector", "gpsimd", "tensor", "sync"):
            ph.wait(e, s_pool, init_done)
        ln_done = [0, 0]
        ln_stored = [0, 0]
        for i in range(S // 128):
            j = i % 2
            ph.wait("sync", s_dve, ln_done[j])
            v_x = ph.dma(dx[j], xt[j][:], xin[i * 128:(i + 1) * 128, :])
            ph.wait("vector", dx[j], v_x)
            ph.wait("scalar", dx[j], v_x)
            nmu, rstd = ln_stats(ph, xt[j][:], D, sqt[j][:], cb=8 + 6 * j)
            ph.wait("vector", dst[j], ln_stored[j])
            v_ln = ph.run("vector", "tensor_scalar", lnt[j][:], xt[j][:],
                          nmu, rstd, ALU.add, ALU.mult, inc=s_dve)
            ln_done[j] = v_ln
            ph.wait("sync", s_dve, v_ln)
            dstap = bass.AP(ln1s, i * 128 * 128, [[128, 128], [S * 128, 16], [1, 128]])
            ln_stored[j] = ph.dma(dst[j], dstap, lnt[j][:])
        ph.emit()
        st.close()

    if UNTIL >= 2:
        st_fm = contextlib.ExitStack()
        ln1_fm = [sb(st_fm, f"ln1fm{dc}", [128, S], BF16) for dc in range(16)]
        st = contextlib.ExitStack()
        ph = Phase(nc)
        phase_entry(ph)
        for dc in range(16):
            src = bass.AP(ln1s, dc * S * 128, [[128, S], [1, 128]])
            ph.dma(dfm[0], ln1_fm[dc][:], src, transpose=True)
        v_lnfm = dfm[0].n
        ph.wait("tensor", dfm[0], v_lnfm)
        ph.wait("gpsimd", dfm[0], v_lnfm)

        ps_busy = [0, 0, 0, 0]

        def mm_group(ph, ps_ap, slot, pieces):
            if slot is not None and ps_busy[slot]:
                ph.wait("tensor", s_act, ps_busy[slot])
            n = len(pieces)
            v = None
            for idx, (l, r) in enumerate(pieces):
                if idx == n - 1:
                    v = ph.run("tensor", "matmul", ps_ap, l, r,
                               start=(idx == 0), stop=True, inc=s_pe)
                else:
                    ph.run("tensor", "matmul", ps_ap, l, r,
                           start=(idx == 0), stop=False)
            return v

        wkf = [sb(st, f"p2_wkf{i}", [128, 2048], F32) for i in range(2)]
        wkb = [sb(st, f"p2_wkb{i}", [128, 2048], BF16) for i in range(2)]
        kst = [sb(st, f"p2_kst{i}", [128, S], BF16) for i in range(2)]
        wk_cast = [0, 0]
        wk_used = [0, 0]
        kst_stored = [0, 0]

        for idx in range(32):
            kf = idx % 16
            is_k = idx < 16
            colbase = (D + kf * 128) if is_k else (kf * 128)
            ntok = S if is_k else T
            scratch = Ks if is_k else Qs
            j = idx % 2
            ph.wait("sync", s_pool, wk_cast[j])
            src = bass.AP(wqkv, colbase, [[3 * D, 128], [128 * 3 * D, 16], [1, 128]])
            v_w = ph.dma(dw[j], wkf[j][:], src)
            ph.wait("gpsimd", dw[j], v_w)
            ph.wait("gpsimd", s_pe, wk_used[j])
            v_cast = ph.run("gpsimd", "tensor_copy", wkb[j][:], wkf[j][:],
                            inc=s_pool)
            wk_cast[j] = v_cast
            ph.wait("tensor", s_pool, v_cast)
            ph.wait("scalar", dkst[j], kst_stored[j])
            for t in range(ntok // 512):
                slot = t % 4
                pieces = [(wkb[j][:, dc * 128:(dc + 1) * 128],
                           ln1_fm[dc][:, t * 512:(t + 1) * 512]) for dc in range(16)]
                v_mm = mm_group(ph, ps_rot[slot], slot, pieces)
                ph.wait("scalar", s_pe, v_mm)
                v_ev = ph.run("scalar", "copy", kst[j][:, t * 512:(t + 1) * 512],
                              ps_rot[slot], inc=s_act)
                ps_busy[slot] = v_ev
            wk_used[j] = v_mm
            ph.wait("sync", s_act, v_ev)
            kst_stored[j] = ph.dma(dkst[j], scratch[kf * 128:(kf + 1) * 128, :ntok],
                                   kst[j][:, :ntok])

        ph.emit()
        st.close()
        st = contextlib.ExitStack()
        ph = Phase(nc)
        phase_entry(ph)
        wvf = sb(st, "p2_wvf", [128, 8192], F32)
        wvb = sb(st, "p2_wvb", [128, 8192], BF16)
        vst = [sb(st, f"p2_vst{i}", [128, 512], BF16) for i in range(4)]
        vst_stored = [0, 0, 0, 0]
        wv_cast = 0
        wv_used = 0
        for vt in range(4):
            ph.wait("sync", s_pool, wv_cast)
            src = bass.AP(wqkv, 2 * D + vt * 512,
                          [[3 * D, 128], [128 * 3 * D, 16], [1, 512]])
            v_w = ph.dma(dvw, wvf[:], src)
            ph.wait("gpsimd", dvw, v_w)
            ph.wait("gpsimd", s_pe, wv_used)
            wv_cast = ph.run("gpsimd", "tensor_copy", wvb[:], wvf[:], inc=s_pool)
            ph.wait("tensor", s_pool, wv_cast)
            for tokc in range(32):
                slot = tokc % 4
                pieces = [(ln1_fm[dc][:, tokc * 128:(tokc + 1) * 128],
                           wvb[:, dc * 512:(dc + 1) * 512]) for dc in range(16)]
                v_mm = mm_group(ph, ps_rot[slot], slot, pieces)
                sj = tokc % 4
                ph.wait("scalar", dvst[sj], vst_stored[sj])
                ph.wait("scalar", s_pe, v_mm)
                v_ev = ph.run("scalar", "copy", vst[sj][:], ps_rot[slot], inc=s_act)
                ps_busy[slot] = v_ev
                ph.wait("sync", s_act, v_ev)
                vst_stored[sj] = ph.dma(
                    dvst[sj], Vs[tokc * 128:(tokc + 1) * 128, vt * 512:(vt + 1) * 512],
                    vst[sj][:])
            wv_used = v_mm
        ph.emit()
        st.close()
        st_fm.close()

    comb_st = contextlib.ExitStack()
    if UNTIL >= 3:
        combined = [sb(comb_st, f"comb{hc}", [128, T], BF16) for hc in range(16)]
        st = contextlib.ExitStack()
        kh = [sb(st, f"p3_kh{i}", [128, S], BF16) for i in range(2)]
        vh = [sb(st, f"p3_vh{i}", [128, S], BF16) for i in range(2)]
        qh = [sb(st, f"p3_qh{i}", [128, T], BF16) for i in range(2)]
        pbuf = sb(st, "p3_p", [128, 32 * 512], BF16)
        sqq = sb(st, "p3_sqq", [128, S], BF16)
        lacc = sb(st, "p3_lacc", [128, 512], F32)
        lacc2 = sb(st, "p3_lacc2", [128, 512], F32)
        rb_sb = [sb(st, f"p3_rb{i}", [128, 512], F32) for i in range(2)]
        negc = sb(st, "p3_negc", [128, 1], F32)
        rinv = sb(st, "p3_rinv", [1, 512], F32)
        w1m = sb(st, "p3_w1m", [128, 2048], F32)
        w1ab = sb(st, "p3_w1ab", [128, 2048], BF16)
        ph = Phase(nc)
        phase_entry(ph)

        w1m_used = 0

        def w1_mean_piece(i):
            nonlocal w1m_used
            rc, cp = i // 4, i % 4
            ph.wait("sync", s_act, w1m_used)
            v_w = ph.dma(dwm, w1m[:],
                         w1t[rc * 128:(rc + 1) * 128, cp * 2048:(cp + 1) * 2048])
            ph.wait("scalar", dwm, v_w)
            ph.wait("scalar", s_act, w1m_used)
            ph.wait("scalar", s_dve, s_dve.n)
            va = ph.run("scalar", "activation", w1ab[:], w1m[:], AF.Abs,
                        accum_out=tiny[:, 6:7], inc=s_act)
            w1m_used = va
            ph.wait("vector", s_act, va)
            ph.runs("vector", s_dve, "tensor_tensor", w1acc[:], w1acc[:],
                    tiny[:, 6:7], ALU.add)

        def head_loads(h):
            hp = h % 2
            ph.dma(dkh, kh[hp][:], Ks[h * 128:(h + 1) * 128, :])
            ph.dma(dkh, vh[hp][:],
                   bass.AP(Vs, h * 128, [[D, 128], [D * 128, 32], [1, 128]]))
            return ph.dma(dkh, qh[hp][:], Qs[h * 128:(h + 1) * 128, :])

        v_kvq_prev = head_loads(0)
        for h in range(NH):
            hp = h % 2
            kht, vht, qht = kh[hp], vh[hp], qh[hp]
            v_kvq = v_kvq_prev
            ph.wait("tensor", s_dve, s_dve.n)
            ph.wait("tensor", s_act, s_act.n)
            ph.wait("scalar", dkh, v_kvq)
            va = ph.run("scalar", "activation", sqq[:, :T], qht[:], AF.Square,
                        inc=s_act)
            ph.wait("tensor", s_act, va)
            vqn = None
            for t in range(4):
                vqn = ph.run("tensor", "matmul", ps_a[0:1, t * 512:(t + 1) * 512],
                             ones_col_bf[:], sqq[:, t * 512:(t + 1) * 512],
                             start=True, stop=True,
                             inc=s_pe if t == 3 else None) or vqn
            ph.wait("vector", s_pe, s_pe.n)
            v1 = ph.run("vector", "tensor_reduce", tiny[:1, 0:1], ps_a[0:1, 0:T],
                        AX.X, ALU.max, inc=s_dve)
            ph.wait("scalar", s_pe, vqn)
            va = ph.run("scalar", "activation", sqq[:], kht[:], AF.Square,
                        inc=s_act)
            ph.wait("tensor", s_act, va)
            ph.wait("tensor", s_dve, v1)
            kmx = tiny[:1, 1:2]
            for rnd in range(2):
                for t in range(4):
                    tt = rnd * 4 + t
                    ph.run("tensor", "matmul", ps_a[0:1, t * 512:(t + 1) * 512],
                           ones_col_bf[:], sqq[:, tt * 512:(tt + 1) * 512],
                           start=True, stop=True,
                           inc=s_pe if t == 3 else None)
                ph.wait("vector", s_pe, s_pe.n)
                ph.wait("vector", s_dve, s_dve.n)
                vr = ph.run("vector", "tensor_reduce",
                            kmx if rnd == 0 else tiny[:1, 2:3],
                            ps_a[0:1, 0:T], AX.X, ALU.max, inc=s_dve)
                if rnd == 1:
                    ph.wait("vector", s_dve, vr)
                    ph.runs("vector", s_dve, "tensor_tensor", kmx, kmx,
                            tiny[:1, 2:3], ALU.max)
                else:
                    ph.wait("tensor", s_dve, vr)
            ph.runs("vector", s_dve, "tensor_tensor", tiny[:1, 2:3], tiny[:1, 0:1],
                    kmx, ALU.mult)
            v3 = ph.runs("vector", s_dve, "tensor_scalar", tiny[:1, 2:3],
                         tiny[:1, 2:3], 1.0 / 128.0, None, ALU.mult)
            ph.wait("scalar", s_dve, v3)
            v4 = ph.run("scalar", "activation", tiny[:1, 3:4], tiny[:1, 2:3],
                        AF.Sqrt, inc=s_act)
            ph.wait("vector", s_act, v4)
            v5 = ph.run("vector", "tensor_scalar", tiny[:1, 4:5], tiny[:1, 3:4],
                        -1.0, None, ALU.mult, inc=s_dve)
            ph.wait("tensor", s_dve, v5)
            v6 = ph.run("tensor", "matmul", ps_a[:, 0:1], ones_row_f[:],
                        tiny[:1, 4:5], start=True, stop=True, inc=s_pe)
            ph.wait("scalar", s_pe, v6)
            vc = ph.run("scalar", "copy", negc[:], ps_a[:, 0:1], inc=s_act)
            if h + 1 < NH:
                ph.wait("sync", s_pe, s_pe.n)
                ph.wait("sync", s_dve, s_dve.n)
                ph.wait("sync", s_act, s_act.n)
                v_kvq_prev = head_loads(h + 1)

            ph.wait("tensor", dkh, v_kvq)
            prev_tail = [0, 0]
            pend = None
            evac_pend = {}
            prev_rb = 0
            lacc_done = [0] * 16
            lacc2_done = [0] * 16
            for g in range(4):
                exp_vals = []
                for pr in range(16):
                    plt = ps_lt[pr % 2]
                    if pr >= 2:
                        ph.wait("tensor", s_act, exp_vals[pr - 2])
                    else:
                        ph.wait("tensor", s_act, prev_tail[pr])
                    for sub in range(2):
                        kc = pr * 2 + sub
                        v_mm = ph.run("tensor", "matmul",
                                      plt[:, sub * 512:(sub + 1) * 512],
                                      kht[:, kc * 128:(kc + 1) * 128],
                                      qht[:, g * 512:(g + 1) * 512],
                                      start=True, stop=True,
                                      inc=s_pe if sub == 1 else None)
                    ph.wait("scalar", s_pe, v_mm)
                    ph.wait("scalar", s_act, vc)
                    ph.wait("scalar", s_dve, lacc_done[pr])
                    ph.wait("scalar", s_pool, lacc2_done[pr])
                    ev = ph.run("scalar", "activation",
                                pbuf[:, pr * 1024:(pr + 1) * 1024], plt[:],
                                AF.Exp, bias=negc[:], scale=INV_SQRT_HD,
                                inc=s_act)
                    exp_vals.append(ev)
                prev_tail = [exp_vals[14], exp_vals[15]]
                if pend is not None:
                    gg, v_lp = pend
                    ph.wait("tensor", s_dve, v_lp)
                    v_lm = ph.run("tensor", "matmul", ps_a[0:1, 512:1024],
                                  ones_col_f[:], lacc[:], start=True,
                                  stop=True, inc=s_pe)
                    ph.wait("vector", s_pe, v_lm)
                    v_r = ph.run("vector", "reciprocal", rinv[:],
                                 ps_a[0:1, 512:1024], inc=s_dve)
                    ph.wait("vector", s_dve, v_r)
                    ph.wait("tensor", s_dve, v_r)
                    ph.wait("tensor", s_act, prev_rb)
                    v_bm = ph.run("tensor", "matmul", ps_a[:, 1536:2048],
                                  ones_row_f[:], rinv[:], start=True,
                                  stop=True, inc=s_pe)
                    ph.wait("scalar", s_pe, v_bm)
                    v_rb = ph.run("scalar", "copy", rb_sb[gg % 2][:],
                                  ps_a[:, 1536:2048], inc=s_act)
                    prev_rb = v_rb
                    ph.wait("vector", s_act, v_rb)
                    evac_pend[gg] = ph.run(
                        "vector", "tensor_tensor",
                        combined[h][:, gg * 512:(gg + 1) * 512],
                        ps_a[:, 512 + (gg % 2) * 512:1024 + (gg % 2) * 512]
                        if False else
                        (ps_a[:, 1024:1536] if gg % 2 == 0 else ps_a[:, 0:512]),
                        rb_sb[gg % 2][:], ALU.mult, inc=s_dve)
                ph.wait("vector", s_act, exp_vals[0])
                ph.runs("vector", s_dve, "tensor_copy", lacc[:], pbuf[:, 0:512])
                lacc_done[0] = ph.runs("vector", s_dve, "tensor_tensor",
                                       lacc[:], lacc[:], pbuf[:, 512:1024],
                                       ALU.add)
                ph.wait("gpsimd", s_act, exp_vals[11])
                if pend is not None:
                    ph.wait("gpsimd", s_dve, pend[1])
                ph.runs("gpsimd", s_pool, "tensor_copy", lacc2[:],
                        pbuf[:, 11 * 1024:11 * 1024 + 512])
                for pr in range(1, 11):
                    ph.wait("vector", s_act, exp_vals[pr])
                    for sub in range(2):
                        off = pr * 1024 + sub * 512
                        lacc_done[pr] = ph.runs("vector", s_dve,
                                                "tensor_tensor", lacc[:],
                                                lacc[:], pbuf[:, off:off + 512],
                                                ALU.add)
                first = True
                for pr in range(11, 16):
                    ph.wait("gpsimd", s_act, exp_vals[pr])
                    for sub in range(2):
                        if first:
                            first = False
                            continue
                        off = pr * 1024 + sub * 512
                        lacc2_done[pr] = ph.runs("gpsimd", s_pool,
                                                 "tensor_tensor", lacc2[:],
                                                 lacc2[:],
                                                 pbuf[:, off:off + 512],
                                                 ALU.add)
                lacc2_done[11] = lacc2_done[12]
                ph.wait("vector", s_pool, s_pool.n)
                v_l = ph.run("vector", "tensor_tensor", lacc[:], lacc[:],
                             lacc2[:], ALU.add, inc=s_dve)
                for pr in range(11, 16):
                    lacc_done[pr] = 0
                pso = ps_a[:, 1024:1536] if g % 2 == 0 else ps_a[:, 0:512]
                ph.wait("tensor", s_act, exp_vals[15])
                if evac_pend.get(g - 2) is not None:
                    ph.wait("tensor", s_dve, evac_pend[g - 2])
                if g == 0:
                    ph.wait("tensor", s_act, vc)
                for kc in range(32):
                    v_pv = ph.run("tensor", "matmul", pso,
                                  vht[:, kc * 128:(kc + 1) * 128],
                                  pbuf[:, kc * 512:(kc + 1) * 512],
                                  start=(kc == 0), stop=(kc == 31),
                                  inc=s_pe if kc == 31 else None)
                pend = (g, v_l)
                ph.wait("vector", s_pe, v_pv)
            gg, v_lp = pend
            ph.wait("tensor", s_dve, v_lp)
            v_lm = ph.run("tensor", "matmul", ps_a[0:1, 512:1024],
                          ones_col_f[:], lacc[:], start=True, stop=True,
                          inc=s_pe)
            ph.wait("vector", s_pe, v_lm)
            v_r = ph.run("vector", "reciprocal", rinv[:], ps_a[0:1, 512:1024],
                         inc=s_dve)
            ph.wait("vector", s_dve, v_r)
            ph.wait("tensor", s_dve, v_r)
            ph.wait("tensor", s_act, prev_rb)
            v_bm = ph.run("tensor", "matmul", ps_a[:, 1536:2048],
                          ones_row_f[:], rinv[:], start=True, stop=True,
                          inc=s_pe)
            ph.wait("scalar", s_pe, v_bm)
            v_rb = ph.run("scalar", "copy", rb_sb[gg % 2][:],
                          ps_a[:, 1536:2048], inc=s_act)
            ph.wait("vector", s_act, v_rb)
            ph.run("vector", "tensor_tensor",
                   combined[h][:, gg * 512:(gg + 1) * 512],
                   ps_a[:, 1024:1536] if gg % 2 == 0 else ps_a[:, 0:512],
                   rb_sb[gg % 2][:], ALU.mult, inc=s_dve)
            w1_mean_piece(h * 4)
            for pp in range(1, 4):
                w1_mean_piece(h * 4 + pp)

        ph.wait("tensor", s_dve, s_dve.n)
        v = ph.run("tensor", "matmul", ps_a[0:1, 0:1], ones_col_f[:], w1acc[:],
                   start=True, stop=True, inc=s_pe)
        ph.wait("vector", s_pe, v)
        v = ph.runs("vector", s_dve, "tensor_scalar", tiny[:1, 7:8],
                    ps_a[0:1, 0:1], 1.0 / 16777216.0, EPS, ALU.mult, ALU.max)
        ph.wait("tensor", s_dve, v)
        v = ph.run("tensor", "matmul", ps_a[:, 0:1], ones_row_f[:],
                   tiny[:1, 7:8], start=True, stop=True, inc=s_pe)
        ph.wait("scalar", s_pe, v)
        v = ph.run("scalar", "copy", mw1_col[:], ps_a[:, 0:1], inc=s_act)
        ph.wait("vector", s_act, v)
        v = ph.run("vector", "reciprocal", sw1_col[:], mw1_col[:], inc=s_dve)
        ph.wait("vector", s_dve, v)
        ph.emit()
        st.close()

    if UNTIL >= 4:
        st = contextlib.ExitStack()
        wof = sb(st, "p4_wof", [128, 4096], F32)
        wob = [sb(st, f"p4_wob{hc}", [128, 2048], BF16) for hc in range(16)]
        xtt = [sb(st, f"p4_x{i}", [128, D], F32) for i in range(2)]
        x2t = [sb(st, f"p4_x2{i}", [128, D], F32) for i in range(2)]
        ln2t = sb(st, "p4_ln2", [128, D], F32)
        sq2t = sb(st, "p4_sq2", [128, D], F32)
        z1t = sb(st, "p4_z1", [128, D], BF16)
        ph = Phase(nc)
        phase_entry(ph)
        wof_used = 0
        for jp in range(8):
            ph.wait("sync", s_pool, wof_used)
            src = bass.AP(wout, jp * 256, [[D, 128], [128 * D, 16], [1, 256]])
            v_w = ph.dma(dvw, wof[:], src)
            ph.wait("gpsimd", dvw, v_w)
            for hc in range(16):
                inc = s_pool if hc == 15 else None
                v_c = ph.run("gpsimd", "tensor_copy",
                             wob[hc][:, jp * 256:(jp + 1) * 256],
                             wof[:, hc * 256:(hc + 1) * 256], inc=inc)
            wof_used = v_c
        ph.wait("tensor", s_pool, s_pool.n)
        x2_stored = [0, 0]
        z1_stored = 0
        for tc in range(16):
            j = tc % 2
            ph.wait("sync", s_dve, s_dve.n)
            v_x = ph.dma(dx[j], xtt[j][:], xin[tc * 128:(tc + 1) * 128, :])
            for of_t in range(4):
                for hc in range(16):
                    inc = s_pe if hc == 15 else None
                    v_mm = ph.run("tensor", "matmul",
                                  ps_a[:, of_t * 512:(of_t + 1) * 512],
                                  combined[hc][:, tc * 128:(tc + 1) * 128],
                                  wob[hc][:, of_t * 512:(of_t + 1) * 512],
                                  start=(hc == 0), stop=(hc == 15), inc=inc)
            ph.wait("vector", dst[j], x2_stored[j])
            ph.wait("vector", s_pe, v_mm)
            ph.wait("vector", dx[j], v_x)
            v_x2 = ph.run("vector", "tensor_tensor", x2t[j][:], ps_a[:],
                          xtt[j][:], ALU.add, inc=s_dve)
            ph.wait("tensor", s_dve, v_x2)
            ph.wait("sync", s_dve, v_x2)
            x2_stored[j] = ph.dma(dst[j], x2s[tc * 128:(tc + 1) * 128, :],
                                  x2t[j][:])
            ph.wait("scalar", s_dve, v_x2)
            nmu, rstd = ln_stats(ph, x2t[j][:], D, sq2t[:], cb=8 + 6 * j)
            ph.runs("vector", s_dve, "tensor_scalar", ln2t[:], x2t[j][:],
                    nmu, rstd, ALU.add, ALU.mult)
            v_a = ph.run("vector", "tensor_reduce", tiny[:, 6:7], ln2t[:],
                         AX.X, ALU.max, True, inc=s_dve)
            ph.wait("vector", s_dve, v_a)
            ph.runs("vector", s_dve, "tensor_scalar", tiny[:, 6:7], tiny[:, 6:7],
                    EPS, None, ALU.max)
            v_r = ph.run("vector", "reciprocal", tiny[:, 7:8], tiny[:, 6:7],
                         inc=s_dve)
            ph.wait("vector", s_dve, v_r)
            ph.runs("vector", s_dve, "tensor_scalar", s1_cols[:, tc:tc + 1],
                    tiny[:, 7:8], 127.0, None, ALU.mult)
            ph.runs("vector", s_dve, "scalar_tensor_tensor", al1_cols[:, tc:tc + 1],
                    tiny[:, 6:7], 1.0 / 127.0, mw1_col[:], ALU.mult, ALU.mult)
            ph.wait("vector", dz1, z1_stored)
            v_z = ph.custom(ROUND_TS, out=z1t[:], in0=ln2t[:],
                            s0=s1_cols[:, tc:tc + 1], s1=MAGIC, inc=s_dve)
            ph.wait("sync", s_dve, v_z)
            dstap = bass.AP(z1s, tc * 128 * 128, [[128, 128], [T * 128, 16], [1, 128]])
            z1_stored = ph.dma(dz1, dstap, z1t[:])
        ph.emit()
        st.close()
    comb_st.close()

    if UNTIL >= 5:
        st = contextlib.ExitStack()
        z1_fm = [sb(st, f"z1fm{dc}", [128, T], BF16) for dc in range(16)]
        w1f = [sb(st, f"p6_w1f{i}", [128, 4096], F32) for i in range(2)]
        w1q = [sb(st, f"p6_w1q{i}", [128, 8192], BF16) for i in range(2)]
        gst = sb(st, "p6_gst", [128, 8192], F16)
        gtmp = [sb(st, f"p6_g{i}", [128, 512], F32) for i in range(2)]
        w2m = sb(st, "p6_w2m", [128, 2048], F32)
        w2ab = sb(st, "p6_w2ab", [128, 2048], BF16)
        ph = Phase(nc)
        phase_entry(ph)
        for dc in range(16):
            src = bass.AP(z1s, dc * T * 128, [[128, T], [1, 128]])
            ph.dma(dfm[0], z1_fm[dc][:], src, transpose=True)
        ph.wait("tensor", dfm[0], dfm[0].n)

        w2m_used = 0

        def w2_mean_piece(i):
            nonlocal w2m_used
            ph.wait("sync", s_act, w2m_used)
            v_w = ph.dma(dwm, w2m[:], w2t[i * 128:(i + 1) * 128, :])
            ph.wait("scalar", dwm, v_w)
            ph.wait("scalar", s_act, w2m_used)
            ph.wait("scalar", s_dve, s_dve.n)
            va = ph.run("scalar", "activation", w2ab[:], w2m[:], AF.Abs,
                        accum_out=tiny[:, 6:7], inc=s_act)
            w2m_used = va
            ph.wait("vector", s_act, va)
            ph.runs("vector", s_dve, "tensor_tensor", w2acc[:], w2acc[:],
                    tiny[:, 6:7], ALU.add)

        w1q_ready = [0, 0]
        w1q_used = [0, 0]
        gst_stored = 0
        gdone = {}
        gcopy = {}
        gi = 0
        for hs in range(16):
            j = hs % 2
            ph.wait("vector", s_pe, w1q_used[j])
            for half in range(2):
                src = bass.AP(w1t, hs * 512 + half * 256,
                              [[H, 128], [128 * H, 16], [1, 256]])
                ph.wait("sync", s_dve, w1q_ready[j])
                v_w = ph.dma(dw[j], w1f[j][:], src)
                ph.wait("vector", dw[j], v_w)
                v_rnd = ph.custom(ROUND_TS, out=w1f[j][:], in0=w1f[j][:],
                                  s0=sw1_col[:], s1=MAGIC, inc=s_dve)
                ph.wait("vector", s_dve, v_rnd)
                w1q_dst = bass.AP(w1q[j], half * 256,
                                  [[8192, 128], [512, 16], [1, 256]])
                v_q = ph.run("vector", "tensor_scalar", w1q_dst, w1f[j][:],
                             -1.0, 1.0, ALU.max, ALU.min, inc=s_dve)
                w1q_ready[j] = v_q
            ph.wait("tensor", s_dve, w1q_ready[j])
            ph.wait("gpsimd", dgst, gst_stored)
            for tc in range(16):
                slot = gi % 4
                plt = ps_lt[slot // 2]
                ps_ap = plt[:, (slot % 2) * 512:(slot % 2 + 1) * 512]
                if gdone.get(gi - 4):
                    ph.wait("tensor", s_act, gdone[gi - 4])
                for dc in range(16):
                    inc = s_pe if dc == 15 else None
                    v_mm = ph.run("tensor", "matmul", ps_ap,
                                  z1_fm[dc][:, tc * 128:(tc + 1) * 128],
                                  w1q[j][:, dc * 512:(dc + 1) * 512],
                                  start=(dc == 0), stop=(dc == 15), inc=inc)
                gj = tc % 2
                ph.wait("scalar", s_pe, v_mm)
                if gcopy.get(gi - 2):
                    ph.wait("scalar", s_pool, gcopy[gi - 2])
                ph.wait("scalar", s_dve, s_dve.n)
                v_g = ph.run("scalar", "activation", gtmp[gj][:], ps_ap,
                             AF.Gelu_apprx_tanh, scale=al1_cols[:, tc:tc + 1],
                             inc=s_act)
                gdone[gi] = v_g
                ph.wait("vector", s_act, v_g)
                v_red = ph.run("vector", "tensor_reduce", tiny[:, 0:1],
                               gtmp[gj][:], AX.X, ALU.max, True, inc=s_dve)
                ph.wait("vector", s_dve, v_red)
                ph.runs("vector", s_dve, "tensor_tensor", a2_cols[:, tc:tc + 1],
                        a2_cols[:, tc:tc + 1], tiny[:, 0:1], ALU.max)
                ph.wait("gpsimd", s_act, v_g)
                gcopy[gi] = ph.run("gpsimd", "tensor_copy",
                                   gst[:, tc * 512:(tc + 1) * 512], gtmp[gj][:],
                                   inc=s_pool)
                gi += 1
            w1q_used[j] = v_mm
            ph.wait("sync", s_pool, gcopy[gi - 1])
            for hc4 in range(4):
                dstap = bass.AP(gs, (hs * 4 + hc4) * T * 128,
                                [[128, 128], [128 * 128, 16], [1, 128]])
                srcap = bass.AP(gst, hc4 * 128, [[8192, 128], [512, 16], [1, 128]])
                gst_stored = ph.dma(dgst, dstap, srcap)
            w2_mean_piece(hs)

        for i in range(16, 64):
            w2_mean_piece(i)
        ph.wait("tensor", s_dve, s_dve.n)
        ph.wait("tensor", s_act, s_act.n)
        v = ph.run("tensor", "matmul", ps_a[0:1, 0:1], ones_col_f[:], w2acc[:],
                   start=True, stop=True, inc=s_pe)
        ph.wait("vector", s_pe, v)
        v = ph.runs("vector", s_dve, "tensor_scalar", tiny[:1, 7:8],
                    ps_a[0:1, 0:1], 1.0 / 16777216.0, EPS, ALU.mult, ALU.max)
        ph.wait("tensor", s_dve, v)
        v = ph.run("tensor", "matmul", ps_a[:, 0:1], ones_row_f[:],
                   tiny[:1, 7:8], start=True, stop=True, inc=s_pe)
        ph.wait("scalar", s_pe, v)
        v = ph.run("scalar", "copy", mw2_col[:], ps_a[:, 0:1], inc=s_act)
        ph.wait("vector", s_act, v)
        v = ph.run("vector", "reciprocal", sw2_col[:], mw2_col[:], inc=s_dve)
        ph.wait("vector", s_dve, v)
        s2_stored = [0, 0]
        for tc in range(16):
            cb = (tc % 2) * 3
            ph.wait("vector", ds2[tc % 2], s2_stored[tc % 2])
            ph.runs("vector", s_dve, "tensor_scalar", tiny[:, cb:cb + 1],
                    a2_cols[:, tc:tc + 1], EPS, None, ALU.max)
            v_r = ph.run("vector", "reciprocal", tiny[:, cb + 1:cb + 2],
                         tiny[:, cb:cb + 1], inc=s_dve)
            ph.wait("vector", s_dve, v_r)
            ph.runs("vector", s_dve, "tensor_scalar", tiny[:, cb + 2:cb + 3],
                    tiny[:, cb + 1:cb + 2], 127.0, None, ALU.mult)
            vv = ph.run("vector", "scalar_tensor_tensor", al2_cols[:, tc:tc + 1],
                        tiny[:, cb:cb + 1], 1.0 / 127.0, mw2_col[:], ALU.mult,
                        ALU.mult, inc=s_dve)
            ph.wait("sync", s_dve, vv)
            s2_stored[tc % 2] = ph.dma(ds2[tc % 2],
                                       s2s[0:1, tc * 128:(tc + 1) * 128],
                                       tiny[:, cb + 2:cb + 3])
        ph.emit()
        st.close()

    if UNTIL >= 6:
        st = contextlib.ExitStack()
        s2row = sb(st, "p7_s2row", [1, T], F32)
        s2b = sb(st, "p7_s2b", [128, T], F32)
        acc = [sb(st, f"p7_acc{i}", [128, D], F32) for i in range(8)]
        gfm = [sb(st, f"p7_gfm{i}", [128, 1024], F16) for i in range(2)]
        z2fm = [sb(st, f"p7_z2{i}", [128, 1024], BF16) for i in range(16)]
        w2f = [sb(st, f"p7_w2f{i}", [128, 2048], F32) for i in range(2)]
        w2q = [sb(st, f"p7_w2q{i}", [128, 2048], BF16) for i in range(16)]
        x2tt = [sb(st, "p8_x2", [128, D], F32)] * 2
        ot = x2tt
        ph = Phase(nc)
        phase_entry(ph)
        v_s2 = ph.dma(dwm, s2row[:], s2s[:])
        ph.wait("tensor", dwm, v_s2)
        for t in range(4):
            v = ph.run("tensor", "matmul", ps_a[:, t * 512:(t + 1) * 512],
                       ones_row_f[:], s2row[:, t * 512:(t + 1) * 512],
                       start=True, stop=True, inc=s_pe if t == 3 else None)
        ph.wait("scalar", s_pe, v)
        v_sb = ph.run("scalar", "copy", s2b[:], ps_a[:], inc=s_act)
        ph.wait("vector", s_act, v_sb)
        ph.wait("tensor", s_act, v_sb)

        stages = [(hf, hsl) for hf in range(2) for hsl in range(8)]
        set_ready = [0, 0]
        set_mm_done = [0, 0]
        gfm_used = [0, 0]
        w2f_used = [0, 0]
        pacc_lt = [ps_lt0[:, 0:512], ps_lt0[:, 512:1024],
                   ps_lt1[:, 0:512], ps_lt1[:, 512:1024]]

        def quant_chunk(stage_i, k8):
            hf, hsl = stages[stage_i]
            p = stage_i % 2
            hc = hsl * 8 + k8
            j = (stage_i * 8 + k8) % 2
            ph.wait("sync", s_dve, max(gfm_used[j], w2f_used[j]))
            ph.wait("sync", s_pe, set_mm_done[p])
            src = bass.AP(gs, hc * T * 128 + hf * 1024 * 128,
                          [[128, 1024], [1, 128]])
            v_g = ph.dma(dfm[j], gfm[j][:], src, transpose=True)
            v_w = ph.dma(dw[j], w2f[j][:], w2t[hc * 128:(hc + 1) * 128, :])
            ph.wait("vector", dfm[j], v_g)
            ph.wait("vector", s_pe, set_mm_done[p])
            tok = slice(hf * 1024, (hf + 1) * 1024)
            gfm_used[j] = ph.custom(ROUND_TT, out=z2fm[p * 8 + k8][:],
                                    in0=gfm[j][:], in1=s2b[:, tok], s0=MAGIC,
                                    inc=s_dve)
            ph.wait("vector", dw[j], v_w)
            w2f_used[j] = ph.custom(ROUND_CLIP, out=w2q[p * 8 + k8][:],
                                    in0=w2f[j][:], s0=sw2_col[:], s1=MAGIC,
                                    inc=s_dve)
            set_ready[p] = w2f_used[j]

        out_stored = [0, 0]
        x2_used = [0, 0]
        evac_vals = {}
        for k8 in range(8):
            quant_chunk(0, k8)
        for si, (hf, hsl) in enumerate(stages):
            p = si % 2
            ph.wait("tensor", s_dve, set_ready[p])
            for tc8 in range(8):
                par = tc8 % 2
                if evac_vals.get(tc8 - 2) is not None:
                    ph.wait("tensor", s_dve, evac_vals[tc8 - 2])
                for k8 in range(8):
                    for nt in range(4):
                        pac = ps_a[:, nt * 512:(nt + 1) * 512] if par == 0                             else pacc_lt[nt]
                        inc = s_pe if (k8 == 7 and nt == 3) else None
                        v_mm = ph.run(
                            "tensor", "matmul", pac,
                            z2fm[p * 8 + k8][:, tc8 * 128:(tc8 + 1) * 128],
                            w2q[p * 8 + k8][:, nt * 512:(nt + 1) * 512],
                            start=(k8 == 0), stop=(k8 == 7), inc=inc)
                ph.wait("vector", s_pe, v_mm)
                first = (hsl == 0)
                if par == 0:
                    if first:
                        v_e = ph.run("vector", "tensor_copy", acc[tc8][:],
                                     ps_a[:], inc=s_dve)
                    else:
                        v_e = ph.run("vector", "tensor_tensor", acc[tc8][:],
                                     acc[tc8][:], ps_a[:], ALU.add, inc=s_dve)
                else:
                    for hh in range(2):
                        m, a_sl = (ps_lt[hh][:],
                                   acc[tc8][:, hh * 1024:(hh + 1) * 1024])
                        if first:
                            v_e = ph.run("vector", "tensor_copy", a_sl, m,
                                         inc=s_dve)
                        else:
                            v_e = ph.run("vector", "tensor_tensor", a_sl,
                                         a_sl, m, ALU.add, inc=s_dve)
                evac_vals[tc8] = v_e
                if si + 1 < len(stages):
                    quant_chunk(si + 1, tc8)
            set_mm_done[p] = v_mm
            if hsl == 7:
                ph.wait("vector", s_dve, s_dve.n)
                for tc8 in range(8):
                    tc = hf * 8 + tc8
                    ph.wait("sync", s_dve, x2_used[0])
                    ph.wait("sync", dst[0], out_stored[0])
                    v_x2 = ph.dma(dx[0], x2tt[0][:],
                                  x2s[tc * 128:(tc + 1) * 128, :])
                    ph.wait("vector", dx[0], v_x2)
                    v_o = ph.run("vector", "scalar_tensor_tensor", ot[0][:],
                                 acc[tc8][:], al2_cols[:, tc:tc + 1],
                                 x2tt[0][:], ALU.mult, ALU.add, inc=s_dve)
                    x2_used[0] = v_o
                    ph.wait("sync", s_dve, v_o)
                    out_stored[0] = ph.dma(dst[0],
                                           out[tc * 128:(tc + 1) * 128, :],
                                           ot[0][:])
                evac_vals = {}
        for l in store_lanes:
            ph.wait("sync", l, l.n)
        ph.emit()
        st.close()

    mybir.codegen_inst_isa_subclasses(nc)
    octx.close()
    return nc


def kernel(L0_attn_x, L0_attn_w_qkv, L0_attn_w_out, L0_ffn_w1, L0_ffn_w2):
    x = np.ascontiguousarray(np.asarray(L0_attn_x, dtype=np.float32))
    wqkv = np.ascontiguousarray(np.asarray(L0_attn_w_qkv, dtype=np.float32))
    wout = np.ascontiguousarray(np.asarray(L0_attn_w_out, dtype=np.float32))
    w1t = np.ascontiguousarray(np.asarray(L0_ffn_w1, dtype=np.float32).T)
    w2t = np.ascontiguousarray(np.asarray(L0_ffn_w2, dtype=np.float32).T)

    nc = _build()
    in_maps = []
    for c in range(8):
        b, half = c // 2, c % 2
        own = x[b, half * T:(half + 1) * T]
        other = x[b, (1 - half) * T:(2 - half) * T]
        xc = np.ascontiguousarray(np.concatenate([own, other], axis=0))
        in_maps.append({"xin": xc, "wqkv": wqkv, "wout": wout,
                        "w1t": w1t, "w2t": w2t})
    res = run_bass_kernel_spmd(nc, in_maps, core_ids=list(range(8)))
    o = np.empty((4, 4096, 2048), dtype=np.float32)
    for c in range(8):
        b, half = c // 2, c % 2
        o[b, half * T:(half + 1) * T] = res.results[c]["out"]
    return o


# version 06
# speedup vs baseline: 1.1982x; 1.0149x over previous
import contextlib
import numpy as np
import ml_dtypes
import concourse.bass as bass
import concourse.mybir as mybir
from concourse.bass_utils import run_bass_kernel_spmd
from concourse import dve_ops as _dvo
from concourse.dve_op import DveOp
from concourse.dve_spec import Spec, Src0, Src1, C0, C1, lower as _dve_lower

F32 = mybir.dt.float32
BF16 = mybir.dt.bfloat16
F16 = mybir.dt.float16
AF = mybir.ActivationFunctionType
ALU = mybir.AluOpType
AX = mybir.AxisListType

MAGIC = 12582912.0
INV_SQRT_HD = 0.08838834764831845
EPS = 1e-5

S = 4096
T = 2048
D = 2048
H = 8192
NH = 16


def _as_f(x, like):
    dt = np.float64 if np.asarray(like).dtype == np.float64 else np.float32
    return np.asarray(x, dtype=dt)


def _ref_round_ts(in0, in1, s0, s1, imm2):
    return np.round(_as_f(in0, in0) * _as_f(s0, in0))


def _ref_round_tt(in0, in1, s0, s1, imm2):
    return np.round(_as_f(in0, in0) * _as_f(in1, in0))


def _ref_round_clip(in0, in1, s0, s1, imm2):
    return np.minimum(np.maximum(np.round(_as_f(in0, in0) * _as_f(s0, in0)),
                                 -1.0), 1.0)


def _register_round_ops():
    from concourse.dve_spec import Zero, One, maxx, minn
    ops = {}
    defs = (
        ("ROUND_MUL_TS_ANT", Spec(body=((Src0 * C0) + C1) - C1), _ref_round_ts),
        ("ROUND_MUL_TT_ANT", Spec(body=((Src0 * Src1) + C0) - C0), _ref_round_tt),
        ("ROUND_CLIP_TS_ANT",
         Spec(body=minn(maxx(((Src0 * C0) + C1) - C1, Zero - One), One)),
         _ref_round_clip),
    )
    replaceable = ["DVEHASH32L_DRAIN", "DVEHASH32L_BODY", "DVEHASH32L_SEED"]
    for name, spec, ref in defs:
        existing = next((o for o in _dvo.OPS if o.name == name), None)
        if existing is not None:
            ops[name] = existing
            continue
        compiled = _dve_lower(spec, ver="v3", perf_en=False, name=name)
        op = DveOp(name, spec, subdim=False,
                   uops_sha={"v3": compiled.sha("v3")}, reference=ref)
        if len(_dvo.OPS) < 31:
            _dvo.OPS.append(op)
            row = len(_dvo.OPS)
        else:
            victim = next(n for n in replaceable
                          if any(o.name == n for o in _dvo.OPS))
            idx = next(i for i, o in enumerate(_dvo.OPS) if o.name == victim)
            replaceable.remove(victim)
            _dvo.OPS[idx] = op
            row = idx + 1
            del _dvo._SUB_OPCODE_FOR_NAME[victim]
            _dvo.CUSTOM_DVE_SPECS.pop(victim, None)
            _dvo.CUSTOM_DVE_REFS.pop(victim, None)
        _dvo._SUB_OPCODE_FOR_NAME[name] = row
        _dvo.CUSTOM_DVE_SPECS[name] = spec
        _dvo.CUSTOM_DVE_REFS[name] = ref
        ops[name] = op
    return (ops["ROUND_MUL_TS_ANT"], ops["ROUND_MUL_TT_ANT"],
            ops["ROUND_CLIP_TS_ANT"])


ROUND_TS, ROUND_TT, ROUND_CLIP = _register_round_ops()

ENGINES = ("sync", "tensor", "scalar", "vector", "gpsimd")
UNTIL = 99
DEBUG_SCRATCH = False


class Sem:
    def __init__(self, h):
        self.h = h
        self.n = 0


class Phase:
    def __init__(self, nc):
        self.nc = nc
        self.q = {e: [] for e in ENGINES}

    def on(self, eng, fn):
        self.q[eng].append(fn)

    def wait(self, eng, sem, val):
        if val and val > 0:
            self.on(eng, lambda e, _s=sem.h, _v=val: e.wait_ge(_s, _v))

    def run(self, eng, method, *args, inc=None, **kw):
        v = None
        if inc is not None:
            inc.n += 1
            v = inc.n
            def _f(e, _m=method, _a=args, _k=kw, _s=inc.h):
                getattr(e, _m)(*_a, **_k).then_inc(_s, 1)
        else:
            def _f(e, _m=method, _a=args, _k=kw):
                getattr(e, _m)(*_a, **_k)
        self.on(eng, _f)
        return v

    def runs(self, eng, sem, method, *args, **kw):
        v = self.run(eng, method, *args, inc=sem, **kw)
        self.wait(eng, sem, v)
        return v

    def custom(self, op, inc=None, **kw):
        v = None
        if inc is not None:
            inc.n += 1
            v = inc.n
            def _f(e, _o=op, _k=kw, _s=inc.h):
                e._custom_dve(_o, **_k).then_inc(_s, 1)
        else:
            def _f(e, _o=op, _k=kw):
                e._custom_dve(_o, **_k)
        self.on("vector", _f)
        return v

    def dma(self, sem, dst, src, transpose=False):
        sem.n += 16
        v = sem.n
        if transpose:
            self.on("sync", lambda e, _d=dst, _s=src, _h=sem.h:
                    e.dma_start(_d, _s, transpose=True).then_inc(_h, 16))
        else:
            self.on("sync", lambda e, _d=dst, _s=src, _h=sem.h:
                    e.dma_start(_d, _s).then_inc(_h, 16))
        return v

    def emit(self):
        with self.nc.Block() as block:
            @block.sync
            def _(e):
                for fn in self.q["sync"]:
                    fn(e)

            @block.tensor
            def _(e):
                for fn in self.q["tensor"]:
                    fn(e)

            @block.scalar
            def _(e):
                for fn in self.q["scalar"]:
                    fn(e)

            @block.vector
            def _(e):
                for fn in self.q["vector"]:
                    fn(e)

            @block.gpsimd
            def _(e):
                for fn in self.q["gpsimd"]:
                    fn(e)


def _build():
    nc = bass.Bass("TRN2", target_bir_lowering=False)

    xin = nc.dram_tensor("xin", [S, D], F32, kind="ExternalInput")
    wqkv = nc.dram_tensor("wqkv", [D, 3 * D], F32, kind="ExternalInput")
    wout = nc.dram_tensor("wout", [D, D], F32, kind="ExternalInput")
    w1t = nc.dram_tensor("w1t", [D, H], F32, kind="ExternalInput")
    w2t = nc.dram_tensor("w2t", [H, D], F32, kind="ExternalInput")
    out = nc.dram_tensor("out", [T, D], F32, kind="ExternalOutput")

    sk = "ExternalOutput" if DEBUG_SCRATCH else "Internal"
    ln1s = nc.dram_tensor("ln1s", [16, S, 128], BF16, kind=sk)
    Ks = nc.dram_tensor("Ks", [D, S], BF16, kind=sk)
    Vs = nc.dram_tensor("Vs", [S, D], BF16, kind=sk)
    Qs = nc.dram_tensor("Qs", [D, T], BF16, kind=sk)
    x2s = nc.dram_tensor("x2s", [T, D], F32, kind=sk)
    z1s = nc.dram_tensor("z1s", [16, T, 128], BF16, kind=sk)
    gs = nc.dram_tensor("gs", [64, T, 128], F16, kind=sk)
    s2s = nc.dram_tensor("s2s", [1, T], F32, kind=sk)

    octx = contextlib.ExitStack()

    def mksem(name):
        return Sem(octx.enter_context(nc.semaphore(name)))

    dx = [mksem("dx0"), mksem("dx1")]
    dst = [mksem("dst0"), mksem("dst1")]
    dw = [mksem("dw0"), mksem("dw1")]
    dkst = [mksem("dkst0"), mksem("dkst1")]
    dvst = [mksem(f"dvst{i}") for i in range(4)]
    dvw = mksem("dvw")
    dfm = [mksem("dfm0"), mksem("dfm1")]
    dkh = mksem("dkh")
    dwm = mksem("dwm")
    dgst = mksem("dgst")
    dz1 = mksem("dz1")
    ds2 = [mksem("ds2a"), mksem("ds2b")]
    store_lanes = dst + dkst + dvst + [dgst, dz1] + ds2
    all_lanes = dx + dst + dw + dkst + dvst + [dvw] + dfm + [dkh, dwm, dgst, dz1] + ds2
    s_pe = Sem(octx.enter_context(nc.semaphore("s_pe")))
    s_act = Sem(octx.enter_context(nc.semaphore("s_act")))
    s_dve = Sem(octx.enter_context(nc.semaphore("s_dve")))
    s_pool = Sem(octx.enter_context(nc.semaphore("s_pool")))

    def sb(st, name, shape, dtype):
        return st.enter_context(nc.sbuf_tensor(name, shape, dtype))

    def phase_entry(ph):
        for eng in ENGINES:
            for l in all_lanes:
                ph.wait(eng, l, l.n)
            for sem in (s_pe, s_act, s_dve, s_pool):
                ph.wait(eng, sem, sem.n)

    ones_col_bf = sb(octx, "ones_col_bf", [128, 1], BF16)
    ones_col_f = sb(octx, "ones_col_f", [128, 1], F32)
    ones_row_f = sb(octx, "ones_row_f", [1, 128], F32)
    eps_col = sb(octx, "eps_col", [128, 1], F32)
    mw1_col = sb(octx, "mw1_col", [128, 1], F32)
    sw1_col = sb(octx, "sw1_col", [128, 1], F32)
    mw2_col = sb(octx, "mw2_col", [128, 1], F32)
    sw2_col = sb(octx, "sw2_col", [128, 1], F32)
    s1_cols = sb(octx, "s1_cols", [128, 16], F32)
    al1_cols = sb(octx, "al1_cols", [128, 16], F32)
    a2_cols = sb(octx, "a2_cols", [128, 16], F32)
    al2_cols = sb(octx, "al2_cols", [128, 16], F32)
    w1acc = sb(octx, "w1acc", [128, 1], F32)
    w2acc = sb(octx, "w2acc", [128, 1], F32)
    tiny = sb(octx, "tiny", [128, 32], F32)

    ps_a = octx.enter_context(nc.psum_tensor("ps_a", [128, 2048], F32))
    ps_lt0 = octx.enter_context(nc.psum_tensor("ps_lt0", [128, 1024], F32))
    ps_lt1 = octx.enter_context(nc.psum_tensor("ps_lt1", [128, 1024], F32))
    ps_lt = [ps_lt0, ps_lt1]
    ps_rot = [ps_lt0[:, 0:512], ps_lt0[:, 512:1024],
              ps_lt1[:, 0:512], ps_lt1[:, 512:1024]]

    ph = Phase(nc)
    for tile, val in ((ones_col_bf, 1.0), (ones_col_f, 1.0), (ones_row_f, 1.0),
                      (eps_col, EPS), (w1acc, 0.0), (w2acc, 0.0), (a2_cols, 0.0)):
        ph.run("gpsimd", "memset", tile[:], val, inc=s_pool)
    init_done = s_pool.n
    ph.emit()

    def ln_stats(ph, xtile, ncols, sq_tile, cb=0):
        inv = 1.0 / ncols
        mu_t, m2_t, var_t, sd_t = (tiny[:, cb:cb + 1], tiny[:, cb + 1:cb + 2],
                                   tiny[:, cb + 2:cb + 3], tiny[:, cb + 3:cb + 4])
        rstd_t, nmu_t = tiny[:, cb + 4:cb + 5], tiny[:, cb + 5:cb + 6]
        ph.run("vector", "reduce_sum", mu_t, xtile, AX.X)
        v_sq = ph.run("scalar", "activation", sq_tile, xtile, AF.Square, inc=s_act)
        ph.wait("vector", s_act, v_sq)
        v_ss = ph.run("vector", "reduce_sum", var_t, sq_tile, AX.X, inc=s_dve)
        ph.wait("vector", s_dve, v_ss)
        ph.runs("vector", s_dve, "tensor_scalar", mu_t, mu_t, inv, None, ALU.mult)
        ph.runs("vector", s_dve, "tensor_tensor", m2_t, mu_t, mu_t, ALU.mult)
        ph.runs("vector", s_dve, "scalar_tensor_tensor", var_t, var_t, inv, m2_t,
                ALU.mult, ALU.subtract)
        v_var = ph.runs("vector", s_dve, "tensor_scalar", nmu_t, mu_t, -1.0, None,
                        ALU.mult)
        ph.wait("scalar", s_dve, v_var)
        v_sd = ph.run("scalar", "activation", sd_t, var_t, AF.Sqrt,
                      bias=eps_col[:], inc=s_act)
        ph.wait("vector", s_act, v_sd)
        v_r = ph.run("vector", "reciprocal", rstd_t, sd_t, inc=s_dve)
        ph.wait("vector", s_dve, v_r)
        return nmu_t, rstd_t

    if UNTIL >= 1:
        st = contextlib.ExitStack()
        xt = [sb(st, f"p0_x{i}", [128, D], F32) for i in range(2)]
        sqt = [sb(st, f"p0_sq{i}", [128, D], F32) for i in range(2)]
        lnt = [sb(st, f"p0_ln{i}", [128, D], BF16) for i in range(2)]
        ph = Phase(nc)
        phase_entry(ph)
        for e in ("scalar", "vector", "gpsimd", "tensor", "sync"):
            ph.wait(e, s_pool, init_done)
        ln_done = [0, 0]
        ln_stored = [0, 0]
        for i in range(S // 128):
            j = i % 2
            ph.wait("sync", s_dve, ln_done[j])
            v_x = ph.dma(dx[j], xt[j][:], xin[i * 128:(i + 1) * 128, :])
            ph.wait("vector", dx[j], v_x)
            ph.wait("scalar", dx[j], v_x)
            nmu, rstd = ln_stats(ph, xt[j][:], D, sqt[j][:], cb=8 + 6 * j)
            ph.wait("vector", dst[j], ln_stored[j])
            v_ln = ph.run("vector", "tensor_scalar", lnt[j][:], xt[j][:],
                          nmu, rstd, ALU.add, ALU.mult, inc=s_dve)
            ln_done[j] = v_ln
            ph.wait("sync", s_dve, v_ln)
            dstap = bass.AP(ln1s, i * 128 * 128, [[128, 128], [S * 128, 16], [1, 128]])
            ln_stored[j] = ph.dma(dst[j], dstap, lnt[j][:])
        ph.emit()
        st.close()

    if UNTIL >= 2:
        st_fm = contextlib.ExitStack()
        ln1_fm = [sb(st_fm, f"ln1fm{dc}", [128, S], BF16) for dc in range(16)]
        st = contextlib.ExitStack()
        ph = Phase(nc)
        phase_entry(ph)
        for dc in range(16):
            src = bass.AP(ln1s, dc * S * 128, [[128, S], [1, 128]])
            ph.dma(dfm[0], ln1_fm[dc][:], src, transpose=True)
        v_lnfm = dfm[0].n
        ph.wait("tensor", dfm[0], v_lnfm)
        ph.wait("gpsimd", dfm[0], v_lnfm)

        ps_busy = [0, 0, 0, 0]

        def mm_group(ph, ps_ap, slot, pieces):
            if slot is not None and ps_busy[slot]:
                ph.wait("tensor", s_act, ps_busy[slot])
            n = len(pieces)
            v = None
            for idx, (l, r) in enumerate(pieces):
                if idx == n - 1:
                    v = ph.run("tensor", "matmul", ps_ap, l, r,
                               start=(idx == 0), stop=True, inc=s_pe)
                else:
                    ph.run("tensor", "matmul", ps_ap, l, r,
                           start=(idx == 0), stop=False)
            return v

        wkf = [sb(st, f"p2_wkf{i}", [128, 2048], F32) for i in range(2)]
        wkb = [sb(st, f"p2_wkb{i}", [128, 2048], BF16) for i in range(2)]
        kst = [sb(st, f"p2_kst{i}", [128, S], BF16) for i in range(2)]
        wk_cast = [0, 0]
        wk_used = [0, 0]
        kst_stored = [0, 0]

        for idx in range(32):
            kf = idx % 16
            is_k = idx < 16
            colbase = (D + kf * 128) if is_k else (kf * 128)
            ntok = S if is_k else T
            scratch = Ks if is_k else Qs
            j = idx % 2
            ph.wait("sync", s_pool, wk_cast[j])
            src = bass.AP(wqkv, colbase, [[3 * D, 128], [128 * 3 * D, 16], [1, 128]])
            v_w = ph.dma(dw[j], wkf[j][:], src)
            ph.wait("gpsimd", dw[j], v_w)
            ph.wait("gpsimd", s_pe, wk_used[j])
            v_cast = ph.run("gpsimd", "tensor_copy", wkb[j][:], wkf[j][:],
                            inc=s_pool)
            wk_cast[j] = v_cast
            ph.wait("tensor", s_pool, v_cast)
            ph.wait("scalar", dkst[j], kst_stored[j])
            for t in range(ntok // 512):
                slot = t % 4
                pieces = [(wkb[j][:, dc * 128:(dc + 1) * 128],
                           ln1_fm[dc][:, t * 512:(t + 1) * 512]) for dc in range(16)]
                v_mm = mm_group(ph, ps_rot[slot], slot, pieces)
                ph.wait("scalar", s_pe, v_mm)
                v_ev = ph.run("scalar", "copy", kst[j][:, t * 512:(t + 1) * 512],
                              ps_rot[slot], inc=s_act)
                ps_busy[slot] = v_ev
            wk_used[j] = v_mm
            ph.wait("sync", s_act, v_ev)
            kst_stored[j] = ph.dma(dkst[j], scratch[kf * 128:(kf + 1) * 128, :ntok],
                                   kst[j][:, :ntok])

        ph.emit()
        st.close()
        st = contextlib.ExitStack()
        ph = Phase(nc)
        phase_entry(ph)
        wvf = [sb(st, f"p2_wvf{i}", [128, 4096], F32) for i in range(2)]
        wvb = [sb(st, f"p2_wvb{i}", [128, 8192], BF16) for i in range(2)]
        vst = [sb(st, f"p2_vst{i}", [128, 512], BF16) for i in range(4)]
        vst_stored = [0, 0, 0, 0]
        wv_cast = [0, 0]
        wv_used = [0, 0]
        wvf_cast = [0, 0]
        for vt in range(4):
            wp = vt % 2
            for hv in range(2):
                hj = hv % 2
                lane = dvw if hj == 0 else dfm[1]
                ph.wait("sync", s_pool, wvf_cast[hj])
                src = bass.AP(wqkv, 2 * D + vt * 512 + hv * 8 * 3 * D * 128,
                              [[3 * D, 128], [128 * 3 * D, 8], [1, 512]])
                v_w = ph.dma(lane, wvf[hj][:], src)
                ph.wait("gpsimd", lane, v_w)
                if hv == 0:
                    ph.wait("gpsimd", s_pe, wv_used[wp])
                vcst = ph.run("gpsimd", "tensor_copy",
                              wvb[wp][:, hv * 4096:(hv + 1) * 4096],
                              wvf[hj][:], inc=s_pool)
                wvf_cast[hj] = vcst
            wv_cast[wp] = vcst
            ph.wait("tensor", s_pool, wv_cast[wp])
            for tokc in range(32):
                slot = tokc % 4
                pieces = [(ln1_fm[dc][:, tokc * 128:(tokc + 1) * 128],
                           wvb[wp][:, dc * 512:(dc + 1) * 512]) for dc in range(16)]
                v_mm = mm_group(ph, ps_rot[slot], slot, pieces)
                sj = tokc % 4
                ph.wait("scalar", dvst[sj], vst_stored[sj])
                ph.wait("scalar", s_pe, v_mm)
                v_ev = ph.run("scalar", "copy", vst[sj][:], ps_rot[slot], inc=s_act)
                ps_busy[slot] = v_ev
                ph.wait("sync", s_act, v_ev)
                vst_stored[sj] = ph.dma(
                    dvst[sj], Vs[tokc * 128:(tokc + 1) * 128, vt * 512:(vt + 1) * 512],
                    vst[sj][:])
            wv_used[wp] = v_mm
        ph.emit()
        st.close()
        st_fm.close()

    comb_st = contextlib.ExitStack()
    if UNTIL >= 3:
        combined = [sb(comb_st, f"comb{hc}", [128, T], BF16) for hc in range(16)]
        st = contextlib.ExitStack()
        kh = [sb(st, f"p3_kh{i}", [128, S], BF16) for i in range(2)]
        vh = [sb(st, f"p3_vh{i}", [128, S], BF16) for i in range(2)]
        qh = [sb(st, f"p3_qh{i}", [128, T], BF16) for i in range(2)]
        pbuf = sb(st, "p3_p", [128, 32 * 512], BF16)
        sqq = sb(st, "p3_sqq", [128, S], BF16)
        lacc = sb(st, "p3_lacc", [128, 512], F32)
        lacc2 = sb(st, "p3_lacc2", [128, 512], F32)
        rb_sb = [sb(st, f"p3_rb{i}", [128, 512], F32) for i in range(2)]
        negc = sb(st, "p3_negc", [128, 1], F32)
        rinv = sb(st, "p3_rinv", [1, 512], F32)
        w1m = sb(st, "p3_w1m", [128, 2048], F32)
        w1ab = sb(st, "p3_w1ab", [128, 2048], BF16)
        ph = Phase(nc)
        phase_entry(ph)

        w1m_used = 0

        def w1_mean_piece(i):
            nonlocal w1m_used
            rc, cp = i // 4, i % 4
            ph.wait("sync", s_act, w1m_used)
            v_w = ph.dma(dwm, w1m[:],
                         w1t[rc * 128:(rc + 1) * 128, cp * 2048:(cp + 1) * 2048])
            ph.wait("scalar", dwm, v_w)
            ph.wait("scalar", s_act, w1m_used)
            ph.wait("scalar", s_dve, s_dve.n)
            va = ph.run("scalar", "activation", w1ab[:], w1m[:], AF.Abs,
                        accum_out=tiny[:, 6:7], inc=s_act)
            w1m_used = va
            ph.wait("vector", s_act, va)
            ph.runs("vector", s_dve, "tensor_tensor", w1acc[:], w1acc[:],
                    tiny[:, 6:7], ALU.add)

        def head_loads(h):
            hp = h % 2
            ph.dma(dkh, kh[hp][:], Ks[h * 128:(h + 1) * 128, :])
            ph.dma(dkh, vh[hp][:],
                   bass.AP(Vs, h * 128, [[D, 128], [D * 128, 32], [1, 128]]))
            return ph.dma(dkh, qh[hp][:], Qs[h * 128:(h + 1) * 128, :])

        v_kvq_prev = head_loads(0)
        for h in range(NH):
            hp = h % 2
            kht, vht, qht = kh[hp], vh[hp], qh[hp]
            v_kvq = v_kvq_prev
            ph.wait("tensor", s_dve, s_dve.n)
            ph.wait("tensor", s_act, s_act.n)
            ph.wait("scalar", dkh, v_kvq)
            va = ph.run("scalar", "activation", sqq[:, :T], qht[:], AF.Square,
                        inc=s_act)
            ph.wait("tensor", s_act, va)
            vqn = None
            for t in range(4):
                vqn = ph.run("tensor", "matmul", ps_a[0:1, t * 512:(t + 1) * 512],
                             ones_col_bf[:], sqq[:, t * 512:(t + 1) * 512],
                             start=True, stop=True,
                             inc=s_pe if t == 3 else None) or vqn
            ph.wait("vector", s_pe, s_pe.n)
            v1 = ph.run("vector", "tensor_reduce", tiny[:1, 0:1], ps_a[0:1, 0:T],
                        AX.X, ALU.max, inc=s_dve)
            ph.wait("scalar", s_pe, vqn)
            va = ph.run("scalar", "activation", sqq[:], kht[:], AF.Square,
                        inc=s_act)
            ph.wait("tensor", s_act, va)
            ph.wait("tensor", s_dve, v1)
            kmx = tiny[:1, 1:2]
            for rnd in range(2):
                for t in range(4):
                    tt = rnd * 4 + t
                    ph.run("tensor", "matmul", ps_a[0:1, t * 512:(t + 1) * 512],
                           ones_col_bf[:], sqq[:, tt * 512:(tt + 1) * 512],
                           start=True, stop=True,
                           inc=s_pe if t == 3 else None)
                ph.wait("vector", s_pe, s_pe.n)
                ph.wait("vector", s_dve, s_dve.n)
                vr = ph.run("vector", "tensor_reduce",
                            kmx if rnd == 0 else tiny[:1, 2:3],
                            ps_a[0:1, 0:T], AX.X, ALU.max, inc=s_dve)
                if rnd == 1:
                    ph.wait("vector", s_dve, vr)
                    ph.runs("vector", s_dve, "tensor_tensor", kmx, kmx,
                            tiny[:1, 2:3], ALU.max)
                else:
                    ph.wait("tensor", s_dve, vr)
            ph.runs("vector", s_dve, "tensor_tensor", tiny[:1, 2:3], tiny[:1, 0:1],
                    kmx, ALU.mult)
            v3 = ph.runs("vector", s_dve, "tensor_scalar", tiny[:1, 2:3],
                         tiny[:1, 2:3], 1.0 / 128.0, None, ALU.mult)
            ph.wait("scalar", s_dve, v3)
            v4 = ph.run("scalar", "activation", tiny[:1, 3:4], tiny[:1, 2:3],
                        AF.Sqrt, inc=s_act)
            ph.wait("vector", s_act, v4)
            v5 = ph.run("vector", "tensor_scalar", tiny[:1, 4:5], tiny[:1, 3:4],
                        -1.0, None, ALU.mult, inc=s_dve)
            ph.wait("tensor", s_dve, v5)
            v6 = ph.run("tensor", "matmul", ps_a[:, 0:1], ones_row_f[:],
                        tiny[:1, 4:5], start=True, stop=True, inc=s_pe)
            ph.wait("scalar", s_pe, v6)
            vc = ph.run("scalar", "copy", negc[:], ps_a[:, 0:1], inc=s_act)
            if h + 1 < NH:
                ph.wait("sync", s_pe, s_pe.n)
                ph.wait("sync", s_dve, s_dve.n)
                ph.wait("sync", s_act, s_act.n)
                v_kvq_prev = head_loads(h + 1)

            ph.wait("tensor", dkh, v_kvq)
            prev_tail = [0, 0]
            pend = None
            evac_pend = {}
            prev_rb = 0
            prev_recip = 0
            for g in range(4):
                if pend is not None:
                    gg = pend
                    v_r = ph.run("vector", "reciprocal", rinv[:],
                                 ps_a[0:1, 512:1024], inc=s_dve)
                    prev_recip = v_r
                    ph.wait("vector", s_dve, v_r)
                    ph.wait("tensor", s_dve, v_r)
                    ph.wait("tensor", s_act, prev_rb)
                    v_bm = ph.run("tensor", "matmul", ps_a[:, 1536:2048],
                                  ones_row_f[:], rinv[:], start=True,
                                  stop=True, inc=s_pe)
                    ph.wait("scalar", s_pe, v_bm)
                    v_rb = ph.run("scalar", "copy", rb_sb[gg % 2][:],
                                  ps_a[:, 1536:2048], inc=s_act)
                    prev_rb = v_rb
                    ph.wait("vector", s_act, v_rb)
                    evac_pend[gg] = ph.run(
                        "vector", "tensor_tensor",
                        combined[h][:, gg * 512:(gg + 1) * 512],
                        ps_a[:, 1024:1536] if gg % 2 == 0 else ps_a[:, 0:512],
                        rb_sb[gg % 2][:], ALU.mult, inc=s_dve)
                exp_vals = []
                for pr in range(16):
                    plt = ps_lt[pr % 2]
                    if pr >= 2:
                        ph.wait("tensor", s_act, exp_vals[pr - 2])
                    else:
                        ph.wait("tensor", s_act, prev_tail[pr])
                    for sub in range(2):
                        kc = pr * 2 + sub
                        v_mm = ph.run("tensor", "matmul",
                                      plt[:, sub * 512:(sub + 1) * 512],
                                      kht[:, kc * 128:(kc + 1) * 128],
                                      qht[:, g * 512:(g + 1) * 512],
                                      start=True, stop=True,
                                      inc=s_pe if sub == 1 else None)
                    if pr == 2:
                        ph.wait("tensor", s_dve, prev_recip)
                    if pr >= 2:
                        for sub in range(2):
                            kc = (pr - 2) * 2 + sub
                            ph.run("tensor", "matmul", ps_a[0:1, 512:1024],
                                   ones_col_bf[:],
                                   pbuf[:, kc * 512:(kc + 1) * 512],
                                   start=(kc == 0), stop=False)
                    ph.wait("scalar", s_pe, v_mm)
                    ph.wait("scalar", s_act, vc)
                    ev = ph.run("scalar", "activation",
                                pbuf[:, pr * 1024:(pr + 1) * 1024], plt[:],
                                AF.Exp, bias=negc[:], scale=INV_SQRT_HD,
                                inc=s_act)
                    exp_vals.append(ev)
                prev_tail = [exp_vals[14], exp_vals[15]]
                ph.wait("tensor", s_act, exp_vals[15])
                v_l = None
                for kc in range(28, 32):
                    v_l = ph.run("tensor", "matmul", ps_a[0:1, 512:1024],
                                 ones_col_bf[:],
                                 pbuf[:, kc * 512:(kc + 1) * 512],
                                 start=False, stop=(kc == 31),
                                 inc=s_pe if kc == 31 else None)
                pso = ps_a[:, 1024:1536] if g % 2 == 0 else ps_a[:, 0:512]
                if evac_pend.get(g - 2) is not None:
                    ph.wait("tensor", s_dve, evac_pend[g - 2])
                if g == 0:
                    ph.wait("tensor", s_act, vc)
                for kc in range(32):
                    v_pv = ph.run("tensor", "matmul", pso,
                                  vht[:, kc * 128:(kc + 1) * 128],
                                  pbuf[:, kc * 512:(kc + 1) * 512],
                                  start=(kc == 0), stop=(kc == 31),
                                  inc=s_pe if kc == 31 else None)
                pend = g
                ph.wait("vector", s_pe, v_pv)
            gg = pend
            v_r = ph.run("vector", "reciprocal", rinv[:], ps_a[0:1, 512:1024],
                         inc=s_dve)
            ph.wait("vector", s_dve, v_r)
            ph.wait("tensor", s_dve, v_r)
            ph.wait("tensor", s_act, prev_rb)
            v_bm = ph.run("tensor", "matmul", ps_a[:, 1536:2048],
                          ones_row_f[:], rinv[:], start=True, stop=True,
                          inc=s_pe)
            ph.wait("scalar", s_pe, v_bm)
            v_rb = ph.run("scalar", "copy", rb_sb[gg % 2][:],
                          ps_a[:, 1536:2048], inc=s_act)
            ph.wait("vector", s_act, v_rb)
            ph.run("vector", "tensor_tensor",
                   combined[h][:, gg * 512:(gg + 1) * 512],
                   ps_a[:, 1024:1536] if gg % 2 == 0 else ps_a[:, 0:512],
                   rb_sb[gg % 2][:], ALU.mult, inc=s_dve)
            w1_mean_piece(h * 4)
            for pp in range(1, 4):
                w1_mean_piece(h * 4 + pp)

        ph.wait("tensor", s_dve, s_dve.n)
        v = ph.run("tensor", "matmul", ps_a[0:1, 0:1], ones_col_f[:], w1acc[:],
                   start=True, stop=True, inc=s_pe)
        ph.wait("vector", s_pe, v)
        v = ph.runs("vector", s_dve, "tensor_scalar", tiny[:1, 7:8],
                    ps_a[0:1, 0:1], 1.0 / 16777216.0, EPS, ALU.mult, ALU.max)
        ph.wait("tensor", s_dve, v)
        v = ph.run("tensor", "matmul", ps_a[:, 0:1], ones_row_f[:],
                   tiny[:1, 7:8], start=True, stop=True, inc=s_pe)
        ph.wait("scalar", s_pe, v)
        v = ph.run("scalar", "copy", mw1_col[:], ps_a[:, 0:1], inc=s_act)
        ph.wait("vector", s_act, v)
        v = ph.run("vector", "reciprocal", sw1_col[:], mw1_col[:], inc=s_dve)
        ph.wait("vector", s_dve, v)
        ph.emit()
        st.close()

    if UNTIL >= 4:
        st = contextlib.ExitStack()
        wof = [sb(st, f"p4_wof{i}", [128, 4096], F32) for i in range(2)]
        wob = [sb(st, f"p4_wob{hc}", [128, 2048], BF16) for hc in range(16)]
        xtt = [sb(st, f"p4_x{i}", [128, D], F32) for i in range(2)]
        x2t = [sb(st, f"p4_x2{i}", [128, D], F32) for i in range(2)]
        ln2t = sb(st, "p4_ln2", [128, D], F32)
        sq2t = ln2t
        z1t = sb(st, "p4_z1", [128, D], BF16)
        ph = Phase(nc)
        phase_entry(ph)
        wof_used = [0, 0]
        for jp in range(8):
            wj = jp % 2
            ph.wait("sync", s_pool, wof_used[wj])
            src = bass.AP(wout, jp * 256, [[D, 128], [128 * D, 16], [1, 256]])
            v_w = ph.dma(dw[wj], wof[wj][:], src)
            ph.wait("gpsimd", dw[wj], v_w)
            for hc in range(16):
                inc = s_pool if hc == 15 else None
                v_c = ph.run("gpsimd", "tensor_copy",
                             wob[hc][:, jp * 256:(jp + 1) * 256],
                             wof[wj][:, hc * 256:(hc + 1) * 256], inc=inc)
            wof_used[wj] = v_c
        ph.wait("tensor", s_pool, s_pool.n)
        x2_stored = [0, 0]
        z1_stored = 0
        for tc in range(16):
            j = tc % 2
            ph.wait("sync", s_dve, s_dve.n)
            v_x = ph.dma(dx[j], xtt[j][:], xin[tc * 128:(tc + 1) * 128, :])
            for of_t in range(4):
                for hc in range(16):
                    inc = s_pe if hc == 15 else None
                    v_mm = ph.run("tensor", "matmul",
                                  ps_a[:, of_t * 512:(of_t + 1) * 512],
                                  combined[hc][:, tc * 128:(tc + 1) * 128],
                                  wob[hc][:, of_t * 512:(of_t + 1) * 512],
                                  start=(hc == 0), stop=(hc == 15), inc=inc)
            ph.wait("vector", dst[j], x2_stored[j])
            ph.wait("vector", s_pe, v_mm)
            ph.wait("vector", dx[j], v_x)
            v_x2 = ph.run("vector", "tensor_tensor", x2t[j][:], ps_a[:],
                          xtt[j][:], ALU.add, inc=s_dve)
            ph.wait("tensor", s_dve, v_x2)
            ph.wait("sync", s_dve, v_x2)
            x2_stored[j] = ph.dma(dst[j], x2s[tc * 128:(tc + 1) * 128, :],
                                  x2t[j][:])
            ph.wait("scalar", s_dve, s_dve.n)
            nmu, rstd = ln_stats(ph, x2t[j][:], D, sq2t[:], cb=8 + 6 * j)
            ph.runs("vector", s_dve, "tensor_scalar", ln2t[:], x2t[j][:],
                    nmu, rstd, ALU.add, ALU.mult)
            v_a = ph.run("vector", "tensor_reduce", tiny[:, 6:7], ln2t[:],
                         AX.X, ALU.max, True, inc=s_dve)
            ph.wait("vector", s_dve, v_a)
            ph.runs("vector", s_dve, "tensor_scalar", tiny[:, 6:7], tiny[:, 6:7],
                    EPS, None, ALU.max)
            v_r = ph.run("vector", "reciprocal", tiny[:, 7:8], tiny[:, 6:7],
                         inc=s_dve)
            ph.wait("vector", s_dve, v_r)
            ph.runs("vector", s_dve, "tensor_scalar", s1_cols[:, tc:tc + 1],
                    tiny[:, 7:8], 127.0, None, ALU.mult)
            ph.runs("vector", s_dve, "scalar_tensor_tensor", al1_cols[:, tc:tc + 1],
                    tiny[:, 6:7], 1.0 / 127.0, mw1_col[:], ALU.mult, ALU.mult)
            ph.wait("vector", dz1, z1_stored)
            v_z = ph.custom(ROUND_TS, out=z1t[:], in0=ln2t[:],
                            s0=s1_cols[:, tc:tc + 1], s1=MAGIC, inc=s_dve)
            ph.wait("sync", s_dve, v_z)
            dstap = bass.AP(z1s, tc * 128 * 128, [[128, 128], [T * 128, 16], [1, 128]])
            z1_stored = ph.dma(dz1, dstap, z1t[:])
        ph.emit()
        st.close()
    comb_st.close()

    if UNTIL >= 5:
        st = contextlib.ExitStack()
        z1_fm = [sb(st, f"z1fm{dc}", [128, T], BF16) for dc in range(16)]
        w1f = [sb(st, f"p6_w1f{i}", [128, 4096], F32) for i in range(2)]
        w1q = [sb(st, f"p6_w1q{i}", [128, 8192], BF16) for i in range(2)]
        gst = sb(st, "p6_gst", [128, 8192], F16)
        gtmp = [sb(st, f"p6_g{i}", [128, 512], F32) for i in range(2)]
        w2m = sb(st, "p6_w2m", [128, 2048], F32)
        w2ab = sb(st, "p6_w2ab", [128, 2048], BF16)
        ph = Phase(nc)
        phase_entry(ph)
        for dc in range(16):
            src = bass.AP(z1s, dc * T * 128, [[128, T], [1, 128]])
            ph.dma(dfm[0], z1_fm[dc][:], src, transpose=True)
        ph.wait("tensor", dfm[0], dfm[0].n)

        w2m_used = 0

        def w2_mean_piece(i):
            nonlocal w2m_used
            ph.wait("sync", s_act, w2m_used)
            v_w = ph.dma(dwm, w2m[:], w2t[i * 128:(i + 1) * 128, :])
            ph.wait("scalar", dwm, v_w)
            ph.wait("scalar", s_act, w2m_used)
            ph.wait("scalar", s_dve, s_dve.n)
            va = ph.run("scalar", "activation", w2ab[:], w2m[:], AF.Abs,
                        accum_out=tiny[:, 6:7], inc=s_act)
            w2m_used = va
            ph.wait("vector", s_act, va)
            ph.runs("vector", s_dve, "tensor_tensor", w2acc[:], w2acc[:],
                    tiny[:, 6:7], ALU.add)

        w1q_ready = [0, 0]
        w1q_used = [0, 0]
        gst_stored = 0
        gdone = {}
        gcopy = {}
        gi = 0
        for hs in range(16):
            j = hs % 2
            ph.wait("vector", s_pe, w1q_used[j])
            for half in range(2):
                src = bass.AP(w1t, hs * 512 + half * 256,
                              [[H, 128], [128 * H, 16], [1, 256]])
                ph.wait("sync", s_dve, w1q_ready[j])
                v_w = ph.dma(dw[j], w1f[j][:], src)
                ph.wait("vector", dw[j], v_w)
                v_rnd = ph.custom(ROUND_TS, out=w1f[j][:], in0=w1f[j][:],
                                  s0=sw1_col[:], s1=MAGIC, inc=s_dve)
                ph.wait("vector", s_dve, v_rnd)
                w1q_dst = bass.AP(w1q[j], half * 256,
                                  [[8192, 128], [512, 16], [1, 256]])
                v_q = ph.run("vector", "tensor_scalar", w1q_dst, w1f[j][:],
                             -1.0, 1.0, ALU.max, ALU.min, inc=s_dve)
                w1q_ready[j] = v_q
            ph.wait("tensor", s_dve, w1q_ready[j])
            ph.wait("gpsimd", dgst, gst_stored)
            for tc in range(16):
                slot = gi % 4
                plt = ps_lt[slot // 2]
                ps_ap = plt[:, (slot % 2) * 512:(slot % 2 + 1) * 512]
                if gdone.get(gi - 4):
                    ph.wait("tensor", s_act, gdone[gi - 4])
                for dc in range(16):
                    inc = s_pe if dc == 15 else None
                    v_mm = ph.run("tensor", "matmul", ps_ap,
                                  z1_fm[dc][:, tc * 128:(tc + 1) * 128],
                                  w1q[j][:, dc * 512:(dc + 1) * 512],
                                  start=(dc == 0), stop=(dc == 15), inc=inc)
                gj = tc % 2
                ph.wait("scalar", s_pe, v_mm)
                if gcopy.get(gi - 2):
                    ph.wait("scalar", s_pool, gcopy[gi - 2])
                ph.wait("scalar", s_dve, s_dve.n)
                v_g = ph.run("scalar", "activation", gtmp[gj][:], ps_ap,
                             AF.Gelu_apprx_tanh, scale=al1_cols[:, tc:tc + 1],
                             inc=s_act)
                gdone[gi] = v_g
                ph.wait("vector", s_act, v_g)
                v_red = ph.run("vector", "tensor_reduce", tiny[:, 0:1],
                               gtmp[gj][:], AX.X, ALU.max, True, inc=s_dve)
                ph.wait("vector", s_dve, v_red)
                ph.runs("vector", s_dve, "tensor_tensor", a2_cols[:, tc:tc + 1],
                        a2_cols[:, tc:tc + 1], tiny[:, 0:1], ALU.max)
                ph.wait("gpsimd", s_act, v_g)
                gcopy[gi] = ph.run("gpsimd", "tensor_copy",
                                   gst[:, tc * 512:(tc + 1) * 512], gtmp[gj][:],
                                   inc=s_pool)
                gi += 1
            w1q_used[j] = v_mm
            ph.wait("sync", s_pool, gcopy[gi - 1])
            for hc4 in range(4):
                dstap = bass.AP(gs, (hs * 4 + hc4) * T * 128,
                                [[128, 128], [128 * 128, 16], [1, 128]])
                srcap = bass.AP(gst, hc4 * 128, [[8192, 128], [512, 16], [1, 128]])
                gst_stored = ph.dma(dgst, dstap, srcap)
            for pp in range(4):
                w2_mean_piece(hs * 4 + pp)

        ph.wait("tensor", s_dve, s_dve.n)
        ph.wait("tensor", s_act, s_act.n)
        v = ph.run("tensor", "matmul", ps_a[0:1, 0:1], ones_col_f[:], w2acc[:],
                   start=True, stop=True, inc=s_pe)
        ph.wait("vector", s_pe, v)
        v = ph.runs("vector", s_dve, "tensor_scalar", tiny[:1, 7:8],
                    ps_a[0:1, 0:1], 1.0 / 16777216.0, EPS, ALU.mult, ALU.max)
        ph.wait("tensor", s_dve, v)
        v = ph.run("tensor", "matmul", ps_a[:, 0:1], ones_row_f[:],
                   tiny[:1, 7:8], start=True, stop=True, inc=s_pe)
        ph.wait("scalar", s_pe, v)
        v = ph.run("scalar", "copy", mw2_col[:], ps_a[:, 0:1], inc=s_act)
        ph.wait("vector", s_act, v)
        v = ph.run("vector", "reciprocal", sw2_col[:], mw2_col[:], inc=s_dve)
        ph.wait("vector", s_dve, v)
        s2_stored = [0, 0]
        for tc in range(16):
            cb = (tc % 2) * 3
            ph.wait("vector", ds2[tc % 2], s2_stored[tc % 2])
            ph.runs("vector", s_dve, "tensor_scalar", tiny[:, cb:cb + 1],
                    a2_cols[:, tc:tc + 1], EPS, None, ALU.max)
            v_r = ph.run("vector", "reciprocal", tiny[:, cb + 1:cb + 2],
                         tiny[:, cb:cb + 1], inc=s_dve)
            ph.wait("vector", s_dve, v_r)
            ph.runs("vector", s_dve, "tensor_scalar", tiny[:, cb + 2:cb + 3],
                    tiny[:, cb + 1:cb + 2], 127.0, None, ALU.mult)
            vv = ph.run("vector", "scalar_tensor_tensor", al2_cols[:, tc:tc + 1],
                        tiny[:, cb:cb + 1], 1.0 / 127.0, mw2_col[:], ALU.mult,
                        ALU.mult, inc=s_dve)
            ph.wait("sync", s_dve, vv)
            s2_stored[tc % 2] = ph.dma(ds2[tc % 2],
                                       s2s[0:1, tc * 128:(tc + 1) * 128],
                                       tiny[:, cb + 2:cb + 3])
        ph.emit()
        st.close()

    if UNTIL >= 6:
        st = contextlib.ExitStack()
        s2row = sb(st, "p7_s2row", [1, T], F32)
        s2b = sb(st, "p7_s2b", [128, T], F32)
        acc = [sb(st, f"p7_acc{i}", [128, D], F32) for i in range(8)]
        gfm = [sb(st, f"p7_gfm{i}", [128, 1024], F16) for i in range(2)]
        z2fm = [sb(st, f"p7_z2{i}", [128, 1024], BF16) for i in range(16)]
        w2f = [sb(st, f"p7_w2f{i}", [128, 2048], F32) for i in range(2)]
        w2q = [sb(st, f"p7_w2q{i}", [128, 2048], BF16) for i in range(16)]
        x2tt = [sb(st, "p8_x2", [128, D], F32)] * 2
        ot = x2tt
        ph = Phase(nc)
        phase_entry(ph)
        v_s2 = ph.dma(dwm, s2row[:], s2s[:])
        ph.wait("tensor", dwm, v_s2)
        for t in range(4):
            v = ph.run("tensor", "matmul", ps_a[:, t * 512:(t + 1) * 512],
                       ones_row_f[:], s2row[:, t * 512:(t + 1) * 512],
                       start=True, stop=True, inc=s_pe if t == 3 else None)
        ph.wait("scalar", s_pe, v)
        v_sb = ph.run("scalar", "copy", s2b[:], ps_a[:], inc=s_act)
        ph.wait("vector", s_act, v_sb)
        ph.wait("tensor", s_act, v_sb)

        stages = [(hf, hsl) for hf in range(2) for hsl in range(8)]
        set_ready = [0, 0]
        set_mm_done = [0, 0]
        gfm_used = [0, 0]
        w2f_used = [0, 0]
        pacc_lt = [ps_lt0[:, 0:512], ps_lt0[:, 512:1024],
                   ps_lt1[:, 0:512], ps_lt1[:, 512:1024]]

        def quant_chunk(stage_i, k8):
            hf, hsl = stages[stage_i]
            p = stage_i % 2
            hc = hsl * 8 + k8
            j = (stage_i * 8 + k8) % 2
            ph.wait("sync", s_dve, max(gfm_used[j], w2f_used[j]))
            ph.wait("sync", s_pe, set_mm_done[p])
            src = bass.AP(gs, hc * T * 128 + hf * 1024 * 128,
                          [[128, 1024], [1, 128]])
            v_g = ph.dma(dfm[j], gfm[j][:], src, transpose=True)
            v_w = ph.dma(dw[j], w2f[j][:], w2t[hc * 128:(hc + 1) * 128, :])
            ph.wait("vector", dfm[j], v_g)
            ph.wait("vector", s_pe, set_mm_done[p])
            tok = slice(hf * 1024, (hf + 1) * 1024)
            gfm_used[j] = ph.custom(ROUND_TT, out=z2fm[p * 8 + k8][:],
                                    in0=gfm[j][:], in1=s2b[:, tok], s0=MAGIC,
                                    inc=s_dve)
            ph.wait("vector", dw[j], v_w)
            w2f_used[j] = ph.custom(ROUND_CLIP, out=w2q[p * 8 + k8][:],
                                    in0=w2f[j][:], s0=sw2_col[:], s1=MAGIC,
                                    inc=s_dve)
            set_ready[p] = w2f_used[j]

        out_stored = [0, 0]
        x2_used = [0, 0]
        evac_vals = {}
        for k8 in range(8):
            quant_chunk(0, k8)
        for si, (hf, hsl) in enumerate(stages):
            p = si % 2
            ph.wait("tensor", s_dve, set_ready[p])
            for tc8 in range(8):
                par = tc8 % 2
                if evac_vals.get(tc8 - 2) is not None:
                    ph.wait("tensor", s_dve, evac_vals[tc8 - 2])
                for k8 in range(8):
                    for nt in range(4):
                        pac = ps_a[:, nt * 512:(nt + 1) * 512] if par == 0                             else pacc_lt[nt]
                        inc = s_pe if (k8 == 7 and nt == 3) else None
                        v_mm = ph.run(
                            "tensor", "matmul", pac,
                            z2fm[p * 8 + k8][:, tc8 * 128:(tc8 + 1) * 128],
                            w2q[p * 8 + k8][:, nt * 512:(nt + 1) * 512],
                            start=(k8 == 0), stop=(k8 == 7), inc=inc)
                ph.wait("vector", s_pe, v_mm)
                first = (hsl == 0)
                if par == 0:
                    if first:
                        v_e = ph.run("vector", "tensor_copy", acc[tc8][:],
                                     ps_a[:], inc=s_dve)
                    else:
                        v_e = ph.run("vector", "tensor_tensor", acc[tc8][:],
                                     acc[tc8][:], ps_a[:], ALU.add, inc=s_dve)
                else:
                    for hh in range(2):
                        m, a_sl = (ps_lt[hh][:],
                                   acc[tc8][:, hh * 1024:(hh + 1) * 1024])
                        if first:
                            v_e = ph.run("vector", "tensor_copy", a_sl, m,
                                         inc=s_dve)
                        else:
                            v_e = ph.run("vector", "tensor_tensor", a_sl,
                                         a_sl, m, ALU.add, inc=s_dve)
                evac_vals[tc8] = v_e
                if si + 1 < len(stages):
                    quant_chunk(si + 1, tc8)
            set_mm_done[p] = v_mm
            if hsl == 7:
                ph.wait("vector", s_dve, s_dve.n)
                for tc8 in range(8):
                    tc = hf * 8 + tc8
                    ph.wait("sync", s_dve, x2_used[0])
                    ph.wait("sync", dst[0], out_stored[0])
                    v_x2 = ph.dma(dx[0], x2tt[0][:],
                                  x2s[tc * 128:(tc + 1) * 128, :])
                    ph.wait("vector", dx[0], v_x2)
                    v_o = ph.run("vector", "scalar_tensor_tensor", ot[0][:],
                                 acc[tc8][:], al2_cols[:, tc:tc + 1],
                                 x2tt[0][:], ALU.mult, ALU.add, inc=s_dve)
                    x2_used[0] = v_o
                    ph.wait("sync", s_dve, v_o)
                    out_stored[0] = ph.dma(dst[0],
                                           out[tc * 128:(tc + 1) * 128, :],
                                           ot[0][:])
                evac_vals = {}
        for l in store_lanes:
            ph.wait("sync", l, l.n)
        ph.emit()
        st.close()

    mybir.codegen_inst_isa_subclasses(nc)
    octx.close()
    return nc


def kernel(L0_attn_x, L0_attn_w_qkv, L0_attn_w_out, L0_ffn_w1, L0_ffn_w2):
    x = np.ascontiguousarray(np.asarray(L0_attn_x, dtype=np.float32))
    wqkv = np.ascontiguousarray(np.asarray(L0_attn_w_qkv, dtype=np.float32))
    wout = np.ascontiguousarray(np.asarray(L0_attn_w_out, dtype=np.float32))
    w1t = np.ascontiguousarray(np.asarray(L0_ffn_w1, dtype=np.float32).T)
    w2t = np.ascontiguousarray(np.asarray(L0_ffn_w2, dtype=np.float32).T)

    nc = _build()
    in_maps = []
    for c in range(8):
        b, half = c // 2, c % 2
        own = x[b, half * T:(half + 1) * T]
        other = x[b, (1 - half) * T:(2 - half) * T]
        xc = np.ascontiguousarray(np.concatenate([own, other], axis=0))
        in_maps.append({"xin": xc, "wqkv": wqkv, "wout": wout,
                        "w1t": w1t, "w2t": w2t})
    res = run_bass_kernel_spmd(nc, in_maps, core_ids=list(range(8)))
    o = np.empty((4, 4096, 2048), dtype=np.float32)
    for c in range(8):
        b, half = c // 2, c % 2
        o[b, half * T:(half + 1) * T] = res.results[c]["out"]
    return o


# version 07
# speedup vs baseline: 1.2009x; 1.0023x over previous
import contextlib
import numpy as np
import ml_dtypes
import concourse.bass as bass
import concourse.mybir as mybir
from concourse.bass_utils import run_bass_kernel_spmd
from concourse import dve_ops as _dvo
from concourse.dve_op import DveOp
from concourse.dve_spec import Spec, Src0, Src1, C0, C1, C2, lower as _dve_lower

F32 = mybir.dt.float32
BF16 = mybir.dt.bfloat16
F16 = mybir.dt.float16
AF = mybir.ActivationFunctionType
ALU = mybir.AluOpType
AX = mybir.AxisListType

MAGIC = 12582912.0
INV_SQRT_HD = 0.08838834764831845
EPS = 1e-5

S = 4096
T = 2048
D = 2048
H = 8192
NH = 16


def _as_f(x, like):
    dt = np.float64 if np.asarray(like).dtype == np.float64 else np.float32
    return np.asarray(x, dtype=dt)


def _ref_round_ts(in0, in1, s0, s1, imm2):
    return np.round(_as_f(in0, in0) * _as_f(s0, in0))


def _ref_round_tt(in0, in1, s0, s1, imm2):
    return np.round(_as_f(in0, in0) * _as_f(in1, in0))


def _ref_round_clip(in0, in1, s0, s1, imm2):
    return np.minimum(np.maximum(np.round(_as_f(in0, in0) * _as_f(s0, in0)),
                                 -1.0), 1.0)


def _ref_round_affine(in0, in1, s0, s1, imm2):
    return np.round(_as_f(in0, in0) * _as_f(s0, in0) + _as_f(s1, in0))


def _register_round_ops():
    from concourse.dve_spec import Zero, One, maxx, minn
    ops = {}
    defs = (
        ("ROUND_MUL_TS_ANT", Spec(body=((Src0 * C0) + C1) - C1), _ref_round_ts),
        ("ROUND_MUL_TT_ANT", Spec(body=((Src0 * Src1) + C0) - C0), _ref_round_tt),
        ("ROUND_CLIP_TS_ANT",
         Spec(body=minn(maxx(((Src0 * C0) + C1) - C1, Zero - One), One)),
         _ref_round_clip),
        ("ROUND_AFFINE_ANT",
         Spec(body=(((Src0 * C0) + C1) + C2) - C2),
         _ref_round_affine),
    )
    replaceable = ["DVEHASH32L_DRAIN", "DVEHASH32L_BODY", "DVEHASH32L_SEED"]
    for name, spec, ref in defs:
        existing = next((o for o in _dvo.OPS if o.name == name), None)
        if existing is not None:
            ops[name] = existing
            continue
        compiled = _dve_lower(spec, ver="v3", perf_en=False, name=name)
        op = DveOp(name, spec, subdim=False,
                   uops_sha={"v3": compiled.sha("v3")}, reference=ref)
        if len(_dvo.OPS) < 31:
            _dvo.OPS.append(op)
            row = len(_dvo.OPS)
        else:
            victim = next(n for n in replaceable
                          if any(o.name == n for o in _dvo.OPS))
            idx = next(i for i, o in enumerate(_dvo.OPS) if o.name == victim)
            replaceable.remove(victim)
            _dvo.OPS[idx] = op
            row = idx + 1
            del _dvo._SUB_OPCODE_FOR_NAME[victim]
            _dvo.CUSTOM_DVE_SPECS.pop(victim, None)
            _dvo.CUSTOM_DVE_REFS.pop(victim, None)
        _dvo._SUB_OPCODE_FOR_NAME[name] = row
        _dvo.CUSTOM_DVE_SPECS[name] = spec
        _dvo.CUSTOM_DVE_REFS[name] = ref
        ops[name] = op
    return (ops["ROUND_MUL_TS_ANT"], ops["ROUND_MUL_TT_ANT"],
            ops["ROUND_CLIP_TS_ANT"], ops["ROUND_AFFINE_ANT"])


ROUND_TS, ROUND_TT, ROUND_CLIP, ROUND_AFF = _register_round_ops()

ENGINES = ("sync", "tensor", "scalar", "vector", "gpsimd")
UNTIL = 99
DEBUG_SCRATCH = False


class Sem:
    def __init__(self, h):
        self.h = h
        self.n = 0


class Phase:
    def __init__(self, nc):
        self.nc = nc
        self.q = {e: [] for e in ENGINES}

    def on(self, eng, fn):
        self.q[eng].append(fn)

    def wait(self, eng, sem, val):
        if val and val > 0:
            self.on(eng, lambda e, _s=sem.h, _v=val: e.wait_ge(_s, _v))

    def run(self, eng, method, *args, inc=None, **kw):
        v = None
        if inc is not None:
            inc.n += 1
            v = inc.n
            def _f(e, _m=method, _a=args, _k=kw, _s=inc.h):
                getattr(e, _m)(*_a, **_k).then_inc(_s, 1)
        else:
            def _f(e, _m=method, _a=args, _k=kw):
                getattr(e, _m)(*_a, **_k)
        self.on(eng, _f)
        return v

    def runs(self, eng, sem, method, *args, **kw):
        v = self.run(eng, method, *args, inc=sem, **kw)
        self.wait(eng, sem, v)
        return v

    def custom(self, op, inc=None, **kw):
        v = None
        if inc is not None:
            inc.n += 1
            v = inc.n
            def _f(e, _o=op, _k=kw, _s=inc.h):
                e._custom_dve(_o, **_k).then_inc(_s, 1)
        else:
            def _f(e, _o=op, _k=kw):
                e._custom_dve(_o, **_k)
        self.on("vector", _f)
        return v

    def dma(self, sem, dst, src, transpose=False):
        sem.n += 16
        v = sem.n
        if transpose:
            self.on("sync", lambda e, _d=dst, _s=src, _h=sem.h:
                    e.dma_start(_d, _s, transpose=True).then_inc(_h, 16))
        else:
            self.on("sync", lambda e, _d=dst, _s=src, _h=sem.h:
                    e.dma_start(_d, _s).then_inc(_h, 16))
        return v

    def emit(self):
        with self.nc.Block() as block:
            @block.sync
            def _(e):
                for fn in self.q["sync"]:
                    fn(e)

            @block.tensor
            def _(e):
                for fn in self.q["tensor"]:
                    fn(e)

            @block.scalar
            def _(e):
                for fn in self.q["scalar"]:
                    fn(e)

            @block.vector
            def _(e):
                for fn in self.q["vector"]:
                    fn(e)

            @block.gpsimd
            def _(e):
                for fn in self.q["gpsimd"]:
                    fn(e)


def _build():
    nc = bass.Bass("TRN2", target_bir_lowering=False)

    xin = nc.dram_tensor("xin", [S, D], F32, kind="ExternalInput")
    wqkv = nc.dram_tensor("wqkv", [D, 3 * D], F32, kind="ExternalInput")
    wout = nc.dram_tensor("wout", [D, D], F32, kind="ExternalInput")
    w1t = nc.dram_tensor("w1t", [D, H], F32, kind="ExternalInput")
    w2t = nc.dram_tensor("w2t", [H, D], F32, kind="ExternalInput")
    out = nc.dram_tensor("out", [T, D], F32, kind="ExternalOutput")

    sk = "ExternalOutput" if DEBUG_SCRATCH else "Internal"
    ln1s = nc.dram_tensor("ln1s", [16, S, 128], BF16, kind=sk)
    Ks = nc.dram_tensor("Ks", [D, S], BF16, kind=sk)
    Vs = nc.dram_tensor("Vs", [S, D], BF16, kind=sk)
    Qs = nc.dram_tensor("Qs", [D, T], BF16, kind=sk)
    x2s = nc.dram_tensor("x2s", [T, D], F32, kind=sk)
    z1s = nc.dram_tensor("z1s", [16, T, 128], BF16, kind=sk)
    gs = nc.dram_tensor("gs", [64, T, 128], F16, kind=sk)
    s2s = nc.dram_tensor("s2s", [1, T], F32, kind=sk)

    octx = contextlib.ExitStack()

    def mksem(name):
        return Sem(octx.enter_context(nc.semaphore(name)))

    dx = [mksem("dx0"), mksem("dx1")]
    dst = [mksem("dst0"), mksem("dst1")]
    dw = [mksem("dw0"), mksem("dw1")]
    dkst = [mksem("dkst0"), mksem("dkst1")]
    dvst = [mksem(f"dvst{i}") for i in range(4)]
    dvw = mksem("dvw")
    dfm = [mksem("dfm0"), mksem("dfm1")]
    dkh = mksem("dkh")
    dwm = mksem("dwm")
    dgst = mksem("dgst")
    dz1 = mksem("dz1")
    ds2 = [mksem("ds2a"), mksem("ds2b")]
    store_lanes = dst + dkst + dvst + [dgst, dz1] + ds2
    all_lanes = dx + dst + dw + dkst + dvst + [dvw] + dfm + [dkh, dwm, dgst, dz1] + ds2
    s_pe = Sem(octx.enter_context(nc.semaphore("s_pe")))
    s_act = Sem(octx.enter_context(nc.semaphore("s_act")))
    s_dve = Sem(octx.enter_context(nc.semaphore("s_dve")))
    s_pool = Sem(octx.enter_context(nc.semaphore("s_pool")))

    def sb(st, name, shape, dtype):
        return st.enter_context(nc.sbuf_tensor(name, shape, dtype))

    def phase_entry(ph):
        for eng in ENGINES:
            for l in all_lanes:
                ph.wait(eng, l, l.n)
            for sem in (s_pe, s_act, s_dve, s_pool):
                ph.wait(eng, sem, sem.n)

    ones_col_bf = sb(octx, "ones_col_bf", [128, 1], BF16)
    ones_col_f = sb(octx, "ones_col_f", [128, 1], F32)
    ones_row_f = sb(octx, "ones_row_f", [1, 128], F32)
    eps_col = sb(octx, "eps_col", [128, 1], F32)
    mw1_col = sb(octx, "mw1_col", [128, 1], F32)
    sw1_col = sb(octx, "sw1_col", [128, 1], F32)
    mw2_col = sb(octx, "mw2_col", [128, 1], F32)
    sw2_col = sb(octx, "sw2_col", [128, 1], F32)
    s1_cols = sb(octx, "s1_cols", [128, 16], F32)
    al1_cols = sb(octx, "al1_cols", [128, 16], F32)
    a2_cols = sb(octx, "a2_cols", [128, 16], F32)
    al2_cols = sb(octx, "al2_cols", [128, 16], F32)
    w1acc = sb(octx, "w1acc", [128, 1], F32)
    w2acc = sb(octx, "w2acc", [128, 1], F32)
    tiny = sb(octx, "tiny", [128, 32], F32)

    ps_a = octx.enter_context(nc.psum_tensor("ps_a", [128, 2048], F32))
    ps_lt0 = octx.enter_context(nc.psum_tensor("ps_lt0", [128, 1024], F32))
    ps_lt1 = octx.enter_context(nc.psum_tensor("ps_lt1", [128, 1024], F32))
    ps_lt = [ps_lt0, ps_lt1]
    ps_rot = [ps_lt0[:, 0:512], ps_lt0[:, 512:1024],
              ps_lt1[:, 0:512], ps_lt1[:, 512:1024]]

    ph = Phase(nc)
    for tile, val in ((ones_col_bf, 1.0), (ones_col_f, 1.0), (ones_row_f, 1.0),
                      (eps_col, EPS), (w1acc, 0.0), (w2acc, 0.0), (a2_cols, 0.0)):
        ph.run("gpsimd", "memset", tile[:], val, inc=s_pool)
    init_done = s_pool.n
    ph.emit()

    def ln_stats(ph, xtile, ncols, sq_tile, cb=0):
        inv = 1.0 / ncols
        mu_t, m2_t, var_t, sd_t = (tiny[:, cb:cb + 1], tiny[:, cb + 1:cb + 2],
                                   tiny[:, cb + 2:cb + 3], tiny[:, cb + 3:cb + 4])
        rstd_t, nmu_t = tiny[:, cb + 4:cb + 5], tiny[:, cb + 5:cb + 6]
        ph.run("vector", "reduce_sum", mu_t, xtile, AX.X)
        v_sq = ph.run("scalar", "activation", sq_tile, xtile, AF.Square, inc=s_act)
        ph.wait("vector", s_act, v_sq)
        v_ss = ph.run("vector", "reduce_sum", var_t, sq_tile, AX.X, inc=s_dve)
        ph.wait("vector", s_dve, v_ss)
        ph.runs("vector", s_dve, "tensor_scalar", mu_t, mu_t, inv, None, ALU.mult)
        ph.runs("vector", s_dve, "tensor_tensor", m2_t, mu_t, mu_t, ALU.mult)
        ph.runs("vector", s_dve, "scalar_tensor_tensor", var_t, var_t, inv, m2_t,
                ALU.mult, ALU.subtract)
        v_var = ph.runs("vector", s_dve, "tensor_scalar", nmu_t, mu_t, -1.0, None,
                        ALU.mult)
        ph.wait("scalar", s_dve, v_var)
        v_sd = ph.run("scalar", "activation", sd_t, var_t, AF.Sqrt,
                      bias=eps_col[:], inc=s_act)
        ph.wait("vector", s_act, v_sd)
        v_r = ph.run("vector", "reciprocal", rstd_t, sd_t, inc=s_dve)
        ph.wait("vector", s_dve, v_r)
        return nmu_t, rstd_t

    if UNTIL >= 1:
        st = contextlib.ExitStack()
        xt = [sb(st, f"p0_x{i}", [128, D], F32) for i in range(2)]
        sqt = [sb(st, f"p0_sq{i}", [128, D], F32) for i in range(2)]
        lnt = [sb(st, f"p0_ln{i}", [128, D], BF16) for i in range(2)]
        ph = Phase(nc)
        phase_entry(ph)
        for e in ("scalar", "vector", "gpsimd", "tensor", "sync"):
            ph.wait(e, s_pool, init_done)
        ln_done = [0, 0]
        ln_stored = [0, 0]
        for i in range(S // 128):
            j = i % 2
            ph.wait("sync", s_dve, ln_done[j])
            v_x = ph.dma(dx[j], xt[j][:], xin[i * 128:(i + 1) * 128, :])
            ph.wait("vector", dx[j], v_x)
            ph.wait("scalar", dx[j], v_x)
            nmu, rstd = ln_stats(ph, xt[j][:], D, sqt[j][:], cb=8 + 6 * j)
            ph.wait("vector", dst[j], ln_stored[j])
            v_ln = ph.run("vector", "tensor_scalar", lnt[j][:], xt[j][:],
                          nmu, rstd, ALU.add, ALU.mult, inc=s_dve)
            ln_done[j] = v_ln
            ph.wait("sync", s_dve, v_ln)
            dstap = bass.AP(ln1s, i * 128 * 128, [[128, 128], [S * 128, 16], [1, 128]])
            ln_stored[j] = ph.dma(dst[j], dstap, lnt[j][:])
        ph.emit()
        st.close()

    if UNTIL >= 2:
        st_fm = contextlib.ExitStack()
        ln1_fm = [sb(st_fm, f"ln1fm{dc}", [128, S], BF16) for dc in range(16)]
        st = contextlib.ExitStack()
        ph = Phase(nc)
        phase_entry(ph)
        for dc in range(16):
            src = bass.AP(ln1s, dc * S * 128, [[128, S], [1, 128]])
            ph.dma(dfm[0], ln1_fm[dc][:], src, transpose=True)
        v_lnfm = dfm[0].n
        ph.wait("tensor", dfm[0], v_lnfm)
        ph.wait("gpsimd", dfm[0], v_lnfm)

        ps_busy = [0, 0, 0, 0]

        def mm_group(ph, ps_ap, slot, pieces):
            if slot is not None and ps_busy[slot]:
                ph.wait("tensor", s_act, ps_busy[slot])
            n = len(pieces)
            v = None
            for idx, (l, r) in enumerate(pieces):
                if idx == n - 1:
                    v = ph.run("tensor", "matmul", ps_ap, l, r,
                               start=(idx == 0), stop=True, inc=s_pe)
                else:
                    ph.run("tensor", "matmul", ps_ap, l, r,
                           start=(idx == 0), stop=False)
            return v

        wkf = [sb(st, f"p2_wkf{i}", [128, 2048], F32) for i in range(2)]
        wkb = [sb(st, f"p2_wkb{i}", [128, 2048], BF16) for i in range(2)]
        kst = [sb(st, f"p2_kst{i}", [128, S], BF16) for i in range(2)]
        wk_cast = [0, 0]
        wk_used = [0, 0]
        kst_stored = [0, 0]

        for idx in range(32):
            kf = idx % 16
            is_k = idx < 16
            colbase = (D + kf * 128) if is_k else (kf * 128)
            ntok = S if is_k else T
            scratch = Ks if is_k else Qs
            j = idx % 2
            ph.wait("sync", s_pool, wk_cast[j])
            src = bass.AP(wqkv, colbase, [[3 * D, 128], [128 * 3 * D, 16], [1, 128]])
            v_w = ph.dma(dw[j], wkf[j][:], src)
            ph.wait("gpsimd", dw[j], v_w)
            ph.wait("gpsimd", s_pe, wk_used[j])
            v_cast = ph.run("gpsimd", "tensor_copy", wkb[j][:], wkf[j][:],
                            inc=s_pool)
            wk_cast[j] = v_cast
            ph.wait("tensor", s_pool, v_cast)
            ph.wait("scalar", dkst[j], kst_stored[j])
            for t in range(ntok // 512):
                slot = t % 4
                pieces = [(wkb[j][:, dc * 128:(dc + 1) * 128],
                           ln1_fm[dc][:, t * 512:(t + 1) * 512]) for dc in range(16)]
                v_mm = mm_group(ph, ps_rot[slot], slot, pieces)
                ph.wait("scalar", s_pe, v_mm)
                v_ev = ph.run("scalar", "copy", kst[j][:, t * 512:(t + 1) * 512],
                              ps_rot[slot], inc=s_act)
                ps_busy[slot] = v_ev
            wk_used[j] = v_mm
            ph.wait("sync", s_act, v_ev)
            kst_stored[j] = ph.dma(dkst[j], scratch[kf * 128:(kf + 1) * 128, :ntok],
                                   kst[j][:, :ntok])

        ph.emit()
        st.close()
        st = contextlib.ExitStack()
        ph = Phase(nc)
        phase_entry(ph)
        wvf = [sb(st, f"p2_wvf{i}", [128, 4096], F32) for i in range(2)]
        wvb = [sb(st, f"p2_wvb{i}", [128, 8192], BF16) for i in range(2)]
        vst = [sb(st, f"p2_vst{i}", [128, 512], BF16) for i in range(4)]
        vst_stored = [0, 0, 0, 0]
        wv_cast = [0, 0]
        wv_used = [0, 0]
        wvf_cast = [0, 0]
        for vt in range(4):
            wp = vt % 2
            for hv in range(2):
                hj = hv % 2
                lane = dvw if hj == 0 else dfm[1]
                ph.wait("sync", s_pool, wvf_cast[hj])
                src = bass.AP(wqkv, 2 * D + vt * 512 + hv * 8 * 3 * D * 128,
                              [[3 * D, 128], [128 * 3 * D, 8], [1, 512]])
                v_w = ph.dma(lane, wvf[hj][:], src)
                ph.wait("gpsimd", lane, v_w)
                if hv == 0:
                    ph.wait("gpsimd", s_pe, wv_used[wp])
                vcst = ph.run("gpsimd", "tensor_copy",
                              wvb[wp][:, hv * 4096:(hv + 1) * 4096],
                              wvf[hj][:], inc=s_pool)
                wvf_cast[hj] = vcst
            wv_cast[wp] = vcst
            ph.wait("tensor", s_pool, wv_cast[wp])
            for tokc in range(32):
                slot = tokc % 4
                pieces = [(ln1_fm[dc][:, tokc * 128:(tokc + 1) * 128],
                           wvb[wp][:, dc * 512:(dc + 1) * 512]) for dc in range(16)]
                v_mm = mm_group(ph, ps_rot[slot], slot, pieces)
                sj = tokc % 4
                ph.wait("scalar", dvst[sj], vst_stored[sj])
                ph.wait("scalar", s_pe, v_mm)
                v_ev = ph.run("scalar", "copy", vst[sj][:], ps_rot[slot], inc=s_act)
                ps_busy[slot] = v_ev
                ph.wait("sync", s_act, v_ev)
                vst_stored[sj] = ph.dma(
                    dvst[sj], Vs[tokc * 128:(tokc + 1) * 128, vt * 512:(vt + 1) * 512],
                    vst[sj][:])
            wv_used[wp] = v_mm
        ph.emit()
        st.close()
        st_fm.close()

    comb_st = contextlib.ExitStack()
    if UNTIL >= 3:
        combined = [sb(comb_st, f"comb{hc}", [128, T], BF16) for hc in range(16)]
        st = contextlib.ExitStack()
        kh = [sb(st, f"p3_kh{i}", [128, S], BF16) for i in range(2)]
        vh = [sb(st, f"p3_vh{i}", [128, S], BF16) for i in range(2)]
        qh = [sb(st, f"p3_qh{i}", [128, T], BF16) for i in range(2)]
        pbuf = sb(st, "p3_p", [128, 32 * 512], BF16)
        sqq = sb(st, "p3_sqq", [128, S], BF16)
        lacc = sb(st, "p3_lacc", [128, 512], F32)
        lacc2 = sb(st, "p3_lacc2", [128, 512], F32)
        rb_sb = [sb(st, f"p3_rb{i}", [128, 512], F32) for i in range(2)]
        negc = sb(st, "p3_negc", [128, 1], F32)
        rinv = sb(st, "p3_rinv", [1, 512], F32)
        w1m = sb(st, "p3_w1m", [128, 2048], F32)
        w1ab = sb(st, "p3_w1ab", [128, 2048], BF16)
        ph = Phase(nc)
        phase_entry(ph)

        w1m_used = 0

        def w1_mean_piece(i):
            nonlocal w1m_used
            rc, cp = i // 4, i % 4
            ph.wait("sync", s_act, w1m_used)
            v_w = ph.dma(dwm, w1m[:],
                         w1t[rc * 128:(rc + 1) * 128, cp * 2048:(cp + 1) * 2048])
            ph.wait("scalar", dwm, v_w)
            ph.wait("scalar", s_act, w1m_used)
            ph.wait("scalar", s_dve, s_dve.n)
            va = ph.run("scalar", "activation", w1ab[:], w1m[:], AF.Abs,
                        accum_out=tiny[:, 6:7], inc=s_act)
            w1m_used = va
            ph.wait("vector", s_act, va)
            ph.runs("vector", s_dve, "tensor_tensor", w1acc[:], w1acc[:],
                    tiny[:, 6:7], ALU.add)

        def head_loads(h):
            hp = h % 2
            ph.dma(dkh, kh[hp][:], Ks[h * 128:(h + 1) * 128, :])
            ph.dma(dkh, vh[hp][:],
                   bass.AP(Vs, h * 128, [[D, 128], [D * 128, 32], [1, 128]]))
            return ph.dma(dkh, qh[hp][:], Qs[h * 128:(h + 1) * 128, :])

        v_kvq_prev = head_loads(0)
        for h in range(NH):
            hp = h % 2
            kht, vht, qht = kh[hp], vh[hp], qh[hp]
            v_kvq = v_kvq_prev
            ph.wait("tensor", s_dve, s_dve.n)
            ph.wait("tensor", s_act, s_act.n)
            ph.wait("scalar", dkh, v_kvq)
            va = ph.run("scalar", "activation", sqq[:, :T], qht[:], AF.Square,
                        inc=s_act)
            ph.wait("tensor", s_act, va)
            vqn = None
            for t in range(4):
                vqn = ph.run("tensor", "matmul", ps_a[0:1, t * 512:(t + 1) * 512],
                             ones_col_bf[:], sqq[:, t * 512:(t + 1) * 512],
                             start=True, stop=True,
                             inc=s_pe if t == 3 else None) or vqn
            ph.wait("vector", s_pe, s_pe.n)
            v1 = ph.run("vector", "tensor_reduce", tiny[:1, 0:1], ps_a[0:1, 0:T],
                        AX.X, ALU.max, inc=s_dve)
            ph.wait("scalar", s_pe, vqn)
            va = ph.run("scalar", "activation", sqq[:], kht[:], AF.Square,
                        inc=s_act)
            ph.wait("tensor", s_act, va)
            ph.wait("tensor", s_dve, v1)
            kmx = tiny[:1, 1:2]
            for rnd in range(2):
                for t in range(4):
                    tt = rnd * 4 + t
                    ph.run("tensor", "matmul", ps_a[0:1, t * 512:(t + 1) * 512],
                           ones_col_bf[:], sqq[:, tt * 512:(tt + 1) * 512],
                           start=True, stop=True,
                           inc=s_pe if t == 3 else None)
                ph.wait("vector", s_pe, s_pe.n)
                ph.wait("vector", s_dve, s_dve.n)
                vr = ph.run("vector", "tensor_reduce",
                            kmx if rnd == 0 else tiny[:1, 2:3],
                            ps_a[0:1, 0:T], AX.X, ALU.max, inc=s_dve)
                if rnd == 1:
                    ph.wait("vector", s_dve, vr)
                    ph.runs("vector", s_dve, "tensor_tensor", kmx, kmx,
                            tiny[:1, 2:3], ALU.max)
                else:
                    ph.wait("tensor", s_dve, vr)
            ph.runs("vector", s_dve, "tensor_tensor", tiny[:1, 2:3], tiny[:1, 0:1],
                    kmx, ALU.mult)
            v3 = ph.runs("vector", s_dve, "tensor_scalar", tiny[:1, 2:3],
                         tiny[:1, 2:3], 1.0 / 128.0, None, ALU.mult)
            ph.wait("scalar", s_dve, v3)
            v4 = ph.run("scalar", "activation", tiny[:1, 3:4], tiny[:1, 2:3],
                        AF.Sqrt, inc=s_act)
            ph.wait("vector", s_act, v4)
            v5 = ph.run("vector", "tensor_scalar", tiny[:1, 4:5], tiny[:1, 3:4],
                        -1.0, None, ALU.mult, inc=s_dve)
            ph.wait("tensor", s_dve, v5)
            v6 = ph.run("tensor", "matmul", ps_a[:, 0:1], ones_row_f[:],
                        tiny[:1, 4:5], start=True, stop=True, inc=s_pe)
            ph.wait("scalar", s_pe, v6)
            vc = ph.run("scalar", "copy", negc[:], ps_a[:, 0:1], inc=s_act)
            if h + 1 < NH:
                ph.wait("sync", s_pe, s_pe.n)
                ph.wait("sync", s_dve, s_dve.n)
                ph.wait("sync", s_act, s_act.n)
                v_kvq_prev = head_loads(h + 1)

            ph.wait("tensor", dkh, v_kvq)
            prev_tail = [0, 0]
            pend = None
            evac_pend = {}
            prev_rb = 0
            prev_recip = 0
            for g in range(4):
                if pend is not None:
                    gg = pend
                    v_r = ph.run("vector", "reciprocal", rinv[:],
                                 ps_a[0:1, 512:1024], inc=s_dve)
                    prev_recip = v_r
                    ph.wait("vector", s_dve, v_r)
                    ph.wait("tensor", s_dve, v_r)
                    ph.wait("tensor", s_act, prev_rb)
                    v_bm = ph.run("tensor", "matmul", ps_a[:, 1536:2048],
                                  ones_row_f[:], rinv[:], start=True,
                                  stop=True, inc=s_pe)
                    ph.wait("scalar", s_pe, v_bm)
                    v_rb = ph.run("scalar", "copy", rb_sb[gg % 2][:],
                                  ps_a[:, 1536:2048], inc=s_act)
                    prev_rb = v_rb
                    ph.wait("vector", s_act, v_rb)
                    evac_pend[gg] = ph.run(
                        "vector", "tensor_tensor",
                        combined[h][:, gg * 512:(gg + 1) * 512],
                        ps_a[:, 1024:1536] if gg % 2 == 0 else ps_a[:, 0:512],
                        rb_sb[gg % 2][:], ALU.mult, inc=s_dve)
                exp_vals = []
                for pr in range(16):
                    plt = ps_lt[pr % 2]
                    if pr >= 2:
                        ph.wait("tensor", s_act, exp_vals[pr - 2])
                    else:
                        ph.wait("tensor", s_act, prev_tail[pr])
                    for sub in range(2):
                        kc = pr * 2 + sub
                        v_mm = ph.run("tensor", "matmul",
                                      plt[:, sub * 512:(sub + 1) * 512],
                                      kht[:, kc * 128:(kc + 1) * 128],
                                      qht[:, g * 512:(g + 1) * 512],
                                      start=True, stop=True,
                                      inc=s_pe if sub == 1 else None)
                    if pr == 2:
                        ph.wait("tensor", s_dve, prev_recip)
                    if pr >= 2:
                        for sub in range(2):
                            kc = (pr - 2) * 2 + sub
                            ph.run("tensor", "matmul", ps_a[0:1, 512:1024],
                                   ones_col_bf[:],
                                   pbuf[:, kc * 512:(kc + 1) * 512],
                                   start=(kc == 0), stop=False)
                    ph.wait("scalar", s_pe, v_mm)
                    ph.wait("scalar", s_act, vc)
                    ev = ph.run("scalar", "activation",
                                pbuf[:, pr * 1024:(pr + 1) * 1024], plt[:],
                                AF.Exp, bias=negc[:], scale=INV_SQRT_HD,
                                inc=s_act)
                    exp_vals.append(ev)
                prev_tail = [exp_vals[14], exp_vals[15]]
                ph.wait("tensor", s_act, exp_vals[15])
                v_l = None
                for kc in range(28, 32):
                    v_l = ph.run("tensor", "matmul", ps_a[0:1, 512:1024],
                                 ones_col_bf[:],
                                 pbuf[:, kc * 512:(kc + 1) * 512],
                                 start=False, stop=(kc == 31),
                                 inc=s_pe if kc == 31 else None)
                pso = ps_a[:, 1024:1536] if g % 2 == 0 else ps_a[:, 0:512]
                if evac_pend.get(g - 2) is not None:
                    ph.wait("tensor", s_dve, evac_pend[g - 2])
                if g == 0:
                    ph.wait("tensor", s_act, vc)
                for kc in range(32):
                    v_pv = ph.run("tensor", "matmul", pso,
                                  vht[:, kc * 128:(kc + 1) * 128],
                                  pbuf[:, kc * 512:(kc + 1) * 512],
                                  start=(kc == 0), stop=(kc == 31),
                                  inc=s_pe if kc == 31 else None)
                pend = g
                ph.wait("vector", s_pe, v_pv)
            gg = pend
            v_r = ph.run("vector", "reciprocal", rinv[:], ps_a[0:1, 512:1024],
                         inc=s_dve)
            ph.wait("vector", s_dve, v_r)
            ph.wait("tensor", s_dve, v_r)
            ph.wait("tensor", s_act, prev_rb)
            v_bm = ph.run("tensor", "matmul", ps_a[:, 1536:2048],
                          ones_row_f[:], rinv[:], start=True, stop=True,
                          inc=s_pe)
            ph.wait("scalar", s_pe, v_bm)
            v_rb = ph.run("scalar", "copy", rb_sb[gg % 2][:],
                          ps_a[:, 1536:2048], inc=s_act)
            ph.wait("vector", s_act, v_rb)
            ph.run("vector", "tensor_tensor",
                   combined[h][:, gg * 512:(gg + 1) * 512],
                   ps_a[:, 1024:1536] if gg % 2 == 0 else ps_a[:, 0:512],
                   rb_sb[gg % 2][:], ALU.mult, inc=s_dve)
            w1_mean_piece(h * 4)
            for pp in range(1, 4):
                w1_mean_piece(h * 4 + pp)

        ph.wait("tensor", s_dve, s_dve.n)
        v = ph.run("tensor", "matmul", ps_a[0:1, 0:1], ones_col_f[:], w1acc[:],
                   start=True, stop=True, inc=s_pe)
        ph.wait("vector", s_pe, v)
        v = ph.runs("vector", s_dve, "tensor_scalar", tiny[:1, 7:8],
                    ps_a[0:1, 0:1], 1.0 / 16777216.0, EPS, ALU.mult, ALU.max)
        ph.wait("tensor", s_dve, v)
        v = ph.run("tensor", "matmul", ps_a[:, 0:1], ones_row_f[:],
                   tiny[:1, 7:8], start=True, stop=True, inc=s_pe)
        ph.wait("scalar", s_pe, v)
        v = ph.run("scalar", "copy", mw1_col[:], ps_a[:, 0:1], inc=s_act)
        ph.wait("vector", s_act, v)
        v = ph.run("vector", "reciprocal", sw1_col[:], mw1_col[:], inc=s_dve)
        ph.wait("vector", s_dve, v)
        ph.emit()
        st.close()

    if UNTIL >= 4:
        st = contextlib.ExitStack()
        wof = [sb(st, f"p4_wof{i}", [128, 4096], F32) for i in range(2)]
        wob = [sb(st, f"p4_wob{hc}", [128, 2048], BF16) for hc in range(16)]
        xtt = [sb(st, f"p4_x{i}", [128, D], F32) for i in range(2)]
        x2t = [sb(st, f"p4_x2{i}", [128, D], F32) for i in range(2)]
        ln2t = sb(st, "p4_ln2", [128, D], F32)
        sq2t = ln2t
        z1t = sb(st, "p4_z1", [128, D], BF16)
        ph = Phase(nc)
        phase_entry(ph)
        wof_used = [0, 0]
        for jp in range(8):
            wj = jp % 2
            ph.wait("sync", s_pool, wof_used[wj])
            src = bass.AP(wout, jp * 256, [[D, 128], [128 * D, 16], [1, 256]])
            v_w = ph.dma(dw[wj], wof[wj][:], src)
            ph.wait("gpsimd", dw[wj], v_w)
            for hc in range(16):
                inc = s_pool if hc == 15 else None
                v_c = ph.run("gpsimd", "tensor_copy",
                             wob[hc][:, jp * 256:(jp + 1) * 256],
                             wof[wj][:, hc * 256:(hc + 1) * 256], inc=inc)
            wof_used[wj] = v_c
        ph.wait("tensor", s_pool, s_pool.n)
        x2_stored = [0, 0]
        z1_stored = 0
        srow_all = tiny[:, 8:24]
        ssq_all = s1_cols
        mx_all = a2cols_tmp = None
        mx_all = sb(st, "p4_mx", [128, 16], F32)
        mn_all = sb(st, "p4_mn", [128, 16], F32)
        nmu_all = sb(st, "p4_nmu", [128, 16], F32)
        var_all = sb(st, "p4_var", [128, 16], F32)
        rstd_all = sb(st, "p4_rstd", [128, 16], F32)
        rs1_all = sb(st, "p4_rs1", [128, 16], F32)
        bias_all = sb(st, "p4_bias", [128, 16], F32)
        for tc in range(16):
            j = tc % 2
            ph.wait("sync", s_dve, s_dve.n)
            v_x = ph.dma(dx[j], xtt[j][:], xin[tc * 128:(tc + 1) * 128, :])
            for of_t in range(4):
                for hc in range(16):
                    inc = s_pe if hc == 15 else None
                    v_mm = ph.run("tensor", "matmul",
                                  ps_a[:, of_t * 512:(of_t + 1) * 512],
                                  combined[hc][:, tc * 128:(tc + 1) * 128],
                                  wob[hc][:, of_t * 512:(of_t + 1) * 512],
                                  start=(hc == 0), stop=(hc == 15), inc=inc)
            ph.wait("vector", dst[j], x2_stored[j])
            ph.wait("vector", s_pe, v_mm)
            ph.wait("vector", dx[j], v_x)
            v_x2 = ph.run("vector", "tensor_tensor", x2t[j][:], ps_a[:],
                          xtt[j][:], ALU.add, inc=s_dve)
            ph.wait("tensor", s_dve, v_x2)
            ph.wait("sync", s_dve, v_x2)
            x2_stored[j] = ph.dma(dst[j], x2s[tc * 128:(tc + 1) * 128, :],
                                  x2t[j][:])
            ph.run("vector", "reduce_sum", srow_all[:, tc:tc + 1], x2t[j][:],
                   AX.X)
            ph.wait("scalar", s_dve, s_dve.n)
            v_sq = ph.run("scalar", "activation", sq2t[:], x2t[j][:],
                          AF.Square, inc=s_act)
            ph.wait("vector", s_act, v_sq)
            ph.run("vector", "reduce_sum", ssq_all[:, tc:tc + 1], sq2t[:], AX.X)
            ph.run("vector", "tensor_reduce", mx_all[:, tc:tc + 1], x2t[j][:],
                   AX.X, ALU.max)
            v_red = ph.run("vector", "tensor_reduce", mn_all[:, tc:tc + 1],
                           x2t[j][:], AX.X, ALU.min, inc=s_dve)
        ph.wait("vector", s_dve, s_dve.n)
        inv = 1.0 / D
        ph.runs("vector", s_dve, "tensor_scalar", nmu_all[:], srow_all[:],
                -inv, None, ALU.mult)
        ph.runs("vector", s_dve, "tensor_tensor", var_all[:], nmu_all[:],
                nmu_all[:], ALU.mult)
        v = ph.runs("vector", s_dve, "scalar_tensor_tensor", var_all[:],
                    ssq_all[:], inv, var_all[:], ALU.mult, ALU.subtract)
        ph.wait("scalar", s_dve, v)
        v_sd = ph.run("scalar", "activation", var_all[:], var_all[:], AF.Sqrt,
                      bias=eps_col[:], inc=s_act)
        ph.wait("vector", s_act, v_sd)
        v_r = ph.run("vector", "reciprocal", rstd_all[:], var_all[:], inc=s_dve)
        ph.wait("vector", s_dve, v_r)
        ph.runs("vector", s_dve, "tensor_tensor", mx_all[:], mx_all[:],
                nmu_all[:], ALU.add)
        ph.runs("vector", s_dve, "scalar_tensor_tensor", mn_all[:], mn_all[:],
                -1.0, nmu_all[:], ALU.mult, ALU.subtract)
        ph.runs("vector", s_dve, "tensor_tensor", mx_all[:], mx_all[:],
                mn_all[:], ALU.max)
        ph.runs("vector", s_dve, "tensor_tensor", mx_all[:], mx_all[:],
                rstd_all[:], ALU.mult)
        ph.runs("vector", s_dve, "tensor_scalar", mx_all[:], mx_all[:],
                EPS, None, ALU.max)
        v_r = ph.run("vector", "reciprocal", rs1_all[:], mx_all[:], inc=s_dve)
        ph.wait("vector", s_dve, v_r)
        ph.runs("vector", s_dve, "tensor_scalar", s1_cols[:], rs1_all[:],
                127.0, None, ALU.mult)
        ph.runs("vector", s_dve, "tensor_scalar", al1_cols[:],
                mx_all[:], mw1_col[:], 1.0 / 127.0, ALU.mult, ALU.mult)
        ph.runs("vector", s_dve, "tensor_tensor", rs1_all[:], rstd_all[:],
                s1_cols[:], ALU.mult)
        v_b = ph.runs("vector", s_dve, "tensor_tensor", bias_all[:],
                      nmu_all[:], rs1_all[:], ALU.mult)
        ph.wait("sync", s_dve, v_b)
        ph.wait("sync", dst[0], dst[0].n)
        ph.wait("sync", dst[1], dst[1].n)
        for tc in range(16):
            j = tc % 2
            ph.wait("sync", s_dve, s_dve.n)
            v_x = ph.dma(dx[j], x2t[j][:], x2s[tc * 128:(tc + 1) * 128, :])
            ph.wait("vector", dx[j], v_x)
            ph.wait("vector", dz1, z1_stored)
            v_z = ph.custom(ROUND_AFF, out=z1t[:], in0=x2t[j][:],
                            s0=rs1_all[:, tc:tc + 1], s1=bias_all[:, tc:tc + 1],
                            imm2=MAGIC, inc=s_dve)
            ph.wait("sync", s_dve, v_z)
            dstap = bass.AP(z1s, tc * 128 * 128, [[128, 128], [T * 128, 16], [1, 128]])
            z1_stored = ph.dma(dz1, dstap, z1t[:])
        ph.emit()
        st.close()
    comb_st.close()

    if UNTIL >= 5:
        st = contextlib.ExitStack()
        z1_fm = [sb(st, f"z1fm{dc}", [128, T], BF16) for dc in range(16)]
        w1f = [sb(st, f"p6_w1f{i}", [128, 4096], F32) for i in range(2)]
        w1q = [sb(st, f"p6_w1q{i}", [128, 8192], BF16) for i in range(2)]
        gst = sb(st, "p6_gst", [128, 8192], F16)
        gtmp = [sb(st, f"p6_g{i}", [128, 512], F32) for i in range(2)]
        w2m = sb(st, "p6_w2m", [128, 2048], F32)
        w2ab = sb(st, "p6_w2ab", [128, 2048], BF16)
        ph = Phase(nc)
        phase_entry(ph)
        for dc in range(16):
            src = bass.AP(z1s, dc * T * 128, [[128, T], [1, 128]])
            ph.dma(dfm[0], z1_fm[dc][:], src, transpose=True)
        ph.wait("tensor", dfm[0], dfm[0].n)

        w2m_used = 0

        def w2_mean_piece(i):
            nonlocal w2m_used
            ph.wait("sync", s_act, w2m_used)
            v_w = ph.dma(dwm, w2m[:], w2t[i * 128:(i + 1) * 128, :])
            ph.wait("scalar", dwm, v_w)
            ph.wait("scalar", s_act, w2m_used)
            ph.wait("scalar", s_dve, s_dve.n)
            va = ph.run("scalar", "activation", w2ab[:], w2m[:], AF.Abs,
                        accum_out=tiny[:, 6:7], inc=s_act)
            w2m_used = va
            ph.wait("vector", s_act, va)
            ph.runs("vector", s_dve, "tensor_tensor", w2acc[:], w2acc[:],
                    tiny[:, 6:7], ALU.add)

        w1q_ready = [0, 0]
        w1q_used = [0, 0]
        gst_stored = 0
        gdone = {}
        gcopy = {}
        gi = 0
        for hs in range(16):
            j = hs % 2
            ph.wait("vector", s_pe, w1q_used[j])
            for half in range(2):
                src = bass.AP(w1t, hs * 512 + half * 256,
                              [[H, 128], [128 * H, 16], [1, 256]])
                ph.wait("sync", s_dve, w1q_ready[j])
                v_w = ph.dma(dw[j], w1f[j][:], src)
                ph.wait("vector", dw[j], v_w)
                v_rnd = ph.custom(ROUND_TS, out=w1f[j][:], in0=w1f[j][:],
                                  s0=sw1_col[:], s1=MAGIC, inc=s_dve)
                ph.wait("vector", s_dve, v_rnd)
                w1q_dst = bass.AP(w1q[j], half * 256,
                                  [[8192, 128], [512, 16], [1, 256]])
                v_q = ph.run("vector", "tensor_scalar", w1q_dst, w1f[j][:],
                             -1.0, 1.0, ALU.max, ALU.min, inc=s_dve)
                w1q_ready[j] = v_q
            ph.wait("tensor", s_dve, w1q_ready[j])
            ph.wait("gpsimd", dgst, gst_stored)
            for tc in range(16):
                slot = gi % 4
                plt = ps_lt[slot // 2]
                ps_ap = plt[:, (slot % 2) * 512:(slot % 2 + 1) * 512]
                if gdone.get(gi - 4):
                    ph.wait("tensor", s_act, gdone[gi - 4])
                for dc in range(16):
                    inc = s_pe if dc == 15 else None
                    v_mm = ph.run("tensor", "matmul", ps_ap,
                                  z1_fm[dc][:, tc * 128:(tc + 1) * 128],
                                  w1q[j][:, dc * 512:(dc + 1) * 512],
                                  start=(dc == 0), stop=(dc == 15), inc=inc)
                gj = tc % 2
                ph.wait("scalar", s_pe, v_mm)
                if gcopy.get(gi - 2):
                    ph.wait("scalar", s_pool, gcopy[gi - 2])
                ph.wait("scalar", s_dve, s_dve.n)
                v_g = ph.run("scalar", "activation", gtmp[gj][:], ps_ap,
                             AF.Gelu_apprx_tanh, scale=al1_cols[:, tc:tc + 1],
                             inc=s_act)
                gdone[gi] = v_g
                ph.wait("vector", s_act, v_g)
                v_red = ph.run("vector", "tensor_reduce", tiny[:, 0:1],
                               gtmp[gj][:], AX.X, ALU.max, True, inc=s_dve)
                ph.wait("vector", s_dve, v_red)
                ph.runs("vector", s_dve, "tensor_tensor", a2_cols[:, tc:tc + 1],
                        a2_cols[:, tc:tc + 1], tiny[:, 0:1], ALU.max)
                ph.wait("gpsimd", s_act, v_g)
                gcopy[gi] = ph.run("gpsimd", "tensor_copy",
                                   gst[:, tc * 512:(tc + 1) * 512], gtmp[gj][:],
                                   inc=s_pool)
                gi += 1
            w1q_used[j] = v_mm
            ph.wait("sync", s_pool, gcopy[gi - 1])
            for hc4 in range(4):
                dstap = bass.AP(gs, (hs * 4 + hc4) * T * 128,
                                [[128, 128], [128 * 128, 16], [1, 128]])
                srcap = bass.AP(gst, hc4 * 128, [[8192, 128], [512, 16], [1, 128]])
                gst_stored = ph.dma(dgst, dstap, srcap)
            for pp in range(4):
                w2_mean_piece(hs * 4 + pp)

        ph.wait("tensor", s_dve, s_dve.n)
        ph.wait("tensor", s_act, s_act.n)
        v = ph.run("tensor", "matmul", ps_a[0:1, 0:1], ones_col_f[:], w2acc[:],
                   start=True, stop=True, inc=s_pe)
        ph.wait("vector", s_pe, v)
        v = ph.runs("vector", s_dve, "tensor_scalar", tiny[:1, 7:8],
                    ps_a[0:1, 0:1], 1.0 / 16777216.0, EPS, ALU.mult, ALU.max)
        ph.wait("tensor", s_dve, v)
        v = ph.run("tensor", "matmul", ps_a[:, 0:1], ones_row_f[:],
                   tiny[:1, 7:8], start=True, stop=True, inc=s_pe)
        ph.wait("scalar", s_pe, v)
        v = ph.run("scalar", "copy", mw2_col[:], ps_a[:, 0:1], inc=s_act)
        ph.wait("vector", s_act, v)
        v = ph.run("vector", "reciprocal", sw2_col[:], mw2_col[:], inc=s_dve)
        ph.wait("vector", s_dve, v)
        s2_stored = [0, 0]
        for tc in range(16):
            cb = (tc % 2) * 3
            ph.wait("vector", ds2[tc % 2], s2_stored[tc % 2])
            ph.runs("vector", s_dve, "tensor_scalar", tiny[:, cb:cb + 1],
                    a2_cols[:, tc:tc + 1], EPS, None, ALU.max)
            v_r = ph.run("vector", "reciprocal", tiny[:, cb + 1:cb + 2],
                         tiny[:, cb:cb + 1], inc=s_dve)
            ph.wait("vector", s_dve, v_r)
            ph.runs("vector", s_dve, "tensor_scalar", tiny[:, cb + 2:cb + 3],
                    tiny[:, cb + 1:cb + 2], 127.0, None, ALU.mult)
            vv = ph.run("vector", "scalar_tensor_tensor", al2_cols[:, tc:tc + 1],
                        tiny[:, cb:cb + 1], 1.0 / 127.0, mw2_col[:], ALU.mult,
                        ALU.mult, inc=s_dve)
            ph.wait("sync", s_dve, vv)
            s2_stored[tc % 2] = ph.dma(ds2[tc % 2],
                                       s2s[0:1, tc * 128:(tc + 1) * 128],
                                       tiny[:, cb + 2:cb + 3])
        ph.emit()
        st.close()

    if UNTIL >= 6:
        st = contextlib.ExitStack()
        s2row = sb(st, "p7_s2row", [1, T], F32)
        s2b = sb(st, "p7_s2b", [128, T], F32)
        acc = [sb(st, f"p7_acc{i}", [128, D], F32) for i in range(8)]
        gfm = [sb(st, f"p7_gfm{i}", [128, 1024], F16) for i in range(2)]
        z2fm = [sb(st, f"p7_z2{i}", [128, 1024], BF16) for i in range(16)]
        w2f = [sb(st, f"p7_w2f{i}", [128, 2048], F32) for i in range(2)]
        w2q = [sb(st, f"p7_w2q{i}", [128, 2048], BF16) for i in range(16)]
        x2tt = [sb(st, "p8_x2", [128, D], F32)] * 2
        ot = x2tt
        ph = Phase(nc)
        phase_entry(ph)
        v_s2 = ph.dma(dwm, s2row[:], s2s[:])
        ph.wait("tensor", dwm, v_s2)
        for t in range(4):
            v = ph.run("tensor", "matmul", ps_a[:, t * 512:(t + 1) * 512],
                       ones_row_f[:], s2row[:, t * 512:(t + 1) * 512],
                       start=True, stop=True, inc=s_pe if t == 3 else None)
        ph.wait("scalar", s_pe, v)
        v_sb = ph.run("scalar", "copy", s2b[:], ps_a[:], inc=s_act)
        ph.wait("vector", s_act, v_sb)
        ph.wait("tensor", s_act, v_sb)

        stages = [(hf, hsl) for hf in range(2) for hsl in range(8)]
        set_ready = [0, 0]
        set_mm_done = [0, 0]
        gfm_used = [0, 0]
        w2f_used = [0, 0]
        pacc_lt = [ps_lt0[:, 0:512], ps_lt0[:, 512:1024],
                   ps_lt1[:, 0:512], ps_lt1[:, 512:1024]]

        def quant_chunk(stage_i, k8):
            hf, hsl = stages[stage_i]
            p = stage_i % 2
            hc = hsl * 8 + k8
            j = (stage_i * 8 + k8) % 2
            ph.wait("sync", s_dve, max(gfm_used[j], w2f_used[j]))
            ph.wait("sync", s_pe, set_mm_done[p])
            src = bass.AP(gs, hc * T * 128 + hf * 1024 * 128,
                          [[128, 1024], [1, 128]])
            v_g = ph.dma(dfm[j], gfm[j][:], src, transpose=True)
            v_w = ph.dma(dw[j], w2f[j][:], w2t[hc * 128:(hc + 1) * 128, :])
            ph.wait("vector", dfm[j], v_g)
            ph.wait("vector", s_pe, set_mm_done[p])
            tok = slice(hf * 1024, (hf + 1) * 1024)
            gfm_used[j] = ph.custom(ROUND_TT, out=z2fm[p * 8 + k8][:],
                                    in0=gfm[j][:], in1=s2b[:, tok], s0=MAGIC,
                                    inc=s_dve)
            ph.wait("vector", dw[j], v_w)
            w2f_used[j] = ph.custom(ROUND_CLIP, out=w2q[p * 8 + k8][:],
                                    in0=w2f[j][:], s0=sw2_col[:], s1=MAGIC,
                                    inc=s_dve)
            set_ready[p] = w2f_used[j]

        out_stored = [0, 0]
        x2_used = [0, 0]
        evac_vals = {}
        for k8 in range(8):
            quant_chunk(0, k8)
        for si, (hf, hsl) in enumerate(stages):
            p = si % 2
            ph.wait("tensor", s_dve, set_ready[p])
            for tc8 in range(8):
                par = tc8 % 2
                if evac_vals.get(tc8 - 2) is not None:
                    ph.wait("tensor", s_dve, evac_vals[tc8 - 2])
                for k8 in range(8):
                    for nt in range(4):
                        pac = ps_a[:, nt * 512:(nt + 1) * 512] if par == 0                             else pacc_lt[nt]
                        inc = s_pe if (k8 == 7 and nt == 3) else None
                        v_mm = ph.run(
                            "tensor", "matmul", pac,
                            z2fm[p * 8 + k8][:, tc8 * 128:(tc8 + 1) * 128],
                            w2q[p * 8 + k8][:, nt * 512:(nt + 1) * 512],
                            start=(k8 == 0), stop=(k8 == 7), inc=inc)
                ph.wait("vector", s_pe, v_mm)
                first = (hsl == 0)
                if par == 0:
                    if first:
                        v_e = ph.run("vector", "tensor_copy", acc[tc8][:],
                                     ps_a[:], inc=s_dve)
                    else:
                        v_e = ph.run("vector", "tensor_tensor", acc[tc8][:],
                                     acc[tc8][:], ps_a[:], ALU.add, inc=s_dve)
                else:
                    for hh in range(2):
                        m, a_sl = (ps_lt[hh][:],
                                   acc[tc8][:, hh * 1024:(hh + 1) * 1024])
                        if first:
                            v_e = ph.run("vector", "tensor_copy", a_sl, m,
                                         inc=s_dve)
                        else:
                            v_e = ph.run("vector", "tensor_tensor", a_sl,
                                         a_sl, m, ALU.add, inc=s_dve)
                evac_vals[tc8] = v_e
                if si + 1 < len(stages):
                    quant_chunk(si + 1, tc8)
            set_mm_done[p] = v_mm
            if hsl == 7:
                ph.wait("vector", s_dve, s_dve.n)
                for tc8 in range(8):
                    tc = hf * 8 + tc8
                    ph.wait("sync", s_dve, x2_used[0])
                    ph.wait("sync", dst[0], out_stored[0])
                    v_x2 = ph.dma(dx[0], x2tt[0][:],
                                  x2s[tc * 128:(tc + 1) * 128, :])
                    ph.wait("vector", dx[0], v_x2)
                    v_o = ph.run("vector", "scalar_tensor_tensor", ot[0][:],
                                 acc[tc8][:], al2_cols[:, tc:tc + 1],
                                 x2tt[0][:], ALU.mult, ALU.add, inc=s_dve)
                    x2_used[0] = v_o
                    ph.wait("sync", s_dve, v_o)
                    out_stored[0] = ph.dma(dst[0],
                                           out[tc * 128:(tc + 1) * 128, :],
                                           ot[0][:])
                evac_vals = {}
        for l in store_lanes:
            ph.wait("sync", l, l.n)
        ph.emit()
        st.close()

    mybir.codegen_inst_isa_subclasses(nc)
    octx.close()
    return nc


def kernel(L0_attn_x, L0_attn_w_qkv, L0_attn_w_out, L0_ffn_w1, L0_ffn_w2):
    x = np.ascontiguousarray(np.asarray(L0_attn_x, dtype=np.float32))
    wqkv = np.ascontiguousarray(np.asarray(L0_attn_w_qkv, dtype=np.float32))
    wout = np.ascontiguousarray(np.asarray(L0_attn_w_out, dtype=np.float32))
    w1t = np.ascontiguousarray(np.asarray(L0_ffn_w1, dtype=np.float32).T)
    w2t = np.ascontiguousarray(np.asarray(L0_ffn_w2, dtype=np.float32).T)

    nc = _build()
    in_maps = []
    for c in range(8):
        b, half = c // 2, c % 2
        own = x[b, half * T:(half + 1) * T]
        other = x[b, (1 - half) * T:(2 - half) * T]
        xc = np.ascontiguousarray(np.concatenate([own, other], axis=0))
        in_maps.append({"xin": xc, "wqkv": wqkv, "wout": wout,
                        "w1t": w1t, "w2t": w2t})
    res = run_bass_kernel_spmd(nc, in_maps, core_ids=list(range(8)))
    o = np.empty((4, 4096, 2048), dtype=np.float32)
    for c in range(8):
        b, half = c // 2, c % 2
        o[b, half * T:(half + 1) * T] = res.results[c]["out"]
    return o


# version 09
# speedup vs baseline: 1.2241x; 1.0193x over previous
import contextlib
import numpy as np
import ml_dtypes
import concourse.bass as bass
import concourse.mybir as mybir
from concourse.bass_utils import run_bass_kernel_spmd
from concourse import dve_ops as _dvo
from concourse.dve_op import DveOp
from concourse.dve_spec import Spec, Src0, Src1, C0, C1, C2, lower as _dve_lower

F32 = mybir.dt.float32
BF16 = mybir.dt.bfloat16
F16 = mybir.dt.float16
AF = mybir.ActivationFunctionType
ALU = mybir.AluOpType
AX = mybir.AxisListType

MAGIC = 12582912.0
INV_SQRT_HD = 0.08838834764831845
EPS = 1e-5

S = 4096
T = 2048
D = 2048
H = 8192
NH = 16


def _as_f(x, like):
    dt = np.float64 if np.asarray(like).dtype == np.float64 else np.float32
    return np.asarray(x, dtype=dt)


def _ref_round_ts(in0, in1, s0, s1, imm2):
    return np.round(_as_f(in0, in0) * _as_f(s0, in0))


def _ref_round_tt(in0, in1, s0, s1, imm2):
    return np.round(_as_f(in0, in0) * _as_f(in1, in0))


def _ref_round_clip(in0, in1, s0, s1, imm2):
    return np.minimum(np.maximum(np.round(_as_f(in0, in0) * _as_f(s0, in0)),
                                 -1.0), 1.0)


def _ref_round_affine(in0, in1, s0, s1, imm2):
    return np.round(_as_f(in0, in0) * _as_f(s0, in0) + _as_f(s1, in0))


def _register_round_ops():
    from concourse.dve_spec import Zero, One, maxx, minn
    ops = {}
    defs = (
        ("ROUND_MUL_TS_ANT", Spec(body=((Src0 * C0) + C1) - C1), _ref_round_ts),
        ("ROUND_MUL_TT_ANT", Spec(body=((Src0 * Src1) + C0) - C0), _ref_round_tt),
        ("ROUND_CLIP_TS_ANT",
         Spec(body=minn(maxx(((Src0 * C0) + C1) - C1, Zero - One), One)),
         _ref_round_clip),
        ("ROUND_AFFINE_ANT",
         Spec(body=(((Src0 * C0) + C1) + C2) - C2),
         _ref_round_affine),
    )
    replaceable = ["DVEHASH32L_DRAIN", "DVEHASH32L_BODY", "DVEHASH32L_SEED"]
    for name, spec, ref in defs:
        existing = next((o for o in _dvo.OPS if o.name == name), None)
        if existing is not None:
            ops[name] = existing
            continue
        compiled = _dve_lower(spec, ver="v3", perf_en=False, name=name)
        op = DveOp(name, spec, subdim=False,
                   uops_sha={"v3": compiled.sha("v3")}, reference=ref)
        if len(_dvo.OPS) < 31:
            _dvo.OPS.append(op)
            row = len(_dvo.OPS)
        else:
            victim = next(n for n in replaceable
                          if any(o.name == n for o in _dvo.OPS))
            idx = next(i for i, o in enumerate(_dvo.OPS) if o.name == victim)
            replaceable.remove(victim)
            _dvo.OPS[idx] = op
            row = idx + 1
            del _dvo._SUB_OPCODE_FOR_NAME[victim]
            _dvo.CUSTOM_DVE_SPECS.pop(victim, None)
            _dvo.CUSTOM_DVE_REFS.pop(victim, None)
        _dvo._SUB_OPCODE_FOR_NAME[name] = row
        _dvo.CUSTOM_DVE_SPECS[name] = spec
        _dvo.CUSTOM_DVE_REFS[name] = ref
        ops[name] = op
    return (ops["ROUND_MUL_TS_ANT"], ops["ROUND_MUL_TT_ANT"],
            ops["ROUND_CLIP_TS_ANT"], ops["ROUND_AFFINE_ANT"])


ROUND_TS, ROUND_TT, ROUND_CLIP, ROUND_AFF = _register_round_ops()

ENGINES = ("sync", "tensor", "scalar", "vector", "gpsimd")
UNTIL = 99
DEBUG_SCRATCH = False


class Sem:
    def __init__(self, h):
        self.h = h
        self.n = 0


class Phase:
    def __init__(self, nc):
        self.nc = nc
        self.q = {e: [] for e in ENGINES}

    def on(self, eng, fn):
        self.q[eng].append(fn)

    def wait(self, eng, sem, val):
        if val and val > 0:
            self.on(eng, lambda e, _s=sem.h, _v=val: e.wait_ge(_s, _v))

    def run(self, eng, method, *args, inc=None, **kw):
        v = None
        if inc is not None:
            inc.n += 1
            v = inc.n
            def _f(e, _m=method, _a=args, _k=kw, _s=inc.h):
                getattr(e, _m)(*_a, **_k).then_inc(_s, 1)
        else:
            def _f(e, _m=method, _a=args, _k=kw):
                getattr(e, _m)(*_a, **_k)
        self.on(eng, _f)
        return v

    def runs(self, eng, sem, method, *args, **kw):
        v = self.run(eng, method, *args, inc=sem, **kw)
        self.wait(eng, sem, v)
        return v

    def custom(self, op, inc=None, **kw):
        v = None
        if inc is not None:
            inc.n += 1
            v = inc.n
            def _f(e, _o=op, _k=kw, _s=inc.h):
                e._custom_dve(_o, **_k).then_inc(_s, 1)
        else:
            def _f(e, _o=op, _k=kw):
                e._custom_dve(_o, **_k)
        self.on("vector", _f)
        return v

    def dma(self, sem, dst, src, transpose=False):
        sem.n += 16
        v = sem.n
        if transpose:
            self.on("sync", lambda e, _d=dst, _s=src, _h=sem.h:
                    e.dma_start(_d, _s, transpose=True).then_inc(_h, 16))
        else:
            self.on("sync", lambda e, _d=dst, _s=src, _h=sem.h:
                    e.dma_start(_d, _s).then_inc(_h, 16))
        return v

    def emit(self):
        with self.nc.Block() as block:
            @block.sync
            def _(e):
                for fn in self.q["sync"]:
                    fn(e)

            @block.tensor
            def _(e):
                for fn in self.q["tensor"]:
                    fn(e)

            @block.scalar
            def _(e):
                for fn in self.q["scalar"]:
                    fn(e)

            @block.vector
            def _(e):
                for fn in self.q["vector"]:
                    fn(e)

            @block.gpsimd
            def _(e):
                for fn in self.q["gpsimd"]:
                    fn(e)


def _build():
    nc = bass.Bass("TRN2", target_bir_lowering=False)

    xin = nc.dram_tensor("xin", [S, D], F32, kind="ExternalInput")
    wqkv = nc.dram_tensor("wqkv", [D, 3 * D], F32, kind="ExternalInput")
    wout = nc.dram_tensor("wout", [D, D], F32, kind="ExternalInput")
    w1t = nc.dram_tensor("w1t", [D, H], F32, kind="ExternalInput")
    w2t = nc.dram_tensor("w2t", [H, D], F32, kind="ExternalInput")
    out = nc.dram_tensor("out", [T, D], F32, kind="ExternalOutput")

    sk = "ExternalOutput" if DEBUG_SCRATCH else "Internal"
    ln1s = nc.dram_tensor("ln1s", [16, S, 128], BF16, kind=sk)
    Ks = nc.dram_tensor("Ks", [D, S], BF16, kind=sk)
    Vs = nc.dram_tensor("Vs", [S, D], BF16, kind=sk)
    Qs = nc.dram_tensor("Qs", [D, T], BF16, kind=sk)
    x2s = nc.dram_tensor("x2s", [T, D], F32, kind=sk)
    z1s = nc.dram_tensor("z1s", [16, T, 128], BF16, kind=sk)
    gs = nc.dram_tensor("gs", [64, T, 128], F16, kind=sk)
    s2s = nc.dram_tensor("s2s", [1, T], F32, kind=sk)

    octx = contextlib.ExitStack()

    def mksem(name):
        return Sem(octx.enter_context(nc.semaphore(name)))

    dx = [mksem("dx0"), mksem("dx1")]
    dst = [mksem("dst0"), mksem("dst1")]
    dw = [mksem("dw0"), mksem("dw1")]
    dkst = [mksem("dkst0"), mksem("dkst1")]
    dvst = [mksem(f"dvst{i}") for i in range(4)]
    dvw = mksem("dvw")
    dfm = [mksem("dfm0"), mksem("dfm1")]
    dkh = [mksem("dkh0"), mksem("dkh1")]
    dwm = mksem("dwm")
    dgst = mksem("dgst")
    dz1 = mksem("dz1")
    ds2 = [mksem("ds2a"), mksem("ds2b")]
    store_lanes = dst + dkst + dvst + [dgst, dz1] + ds2
    all_lanes = dx + dst + dw + dkst + dvst + [dvw] + dfm + dkh + [dwm, dgst, dz1] + ds2
    s_pe = Sem(octx.enter_context(nc.semaphore("s_pe")))
    s_act = Sem(octx.enter_context(nc.semaphore("s_act")))
    s_dve = Sem(octx.enter_context(nc.semaphore("s_dve")))
    s_pool = Sem(octx.enter_context(nc.semaphore("s_pool")))

    def sb(st, name, shape, dtype):
        return st.enter_context(nc.sbuf_tensor(name, shape, dtype))

    def phase_entry(ph):
        for eng in ENGINES:
            for l in all_lanes:
                ph.wait(eng, l, l.n)
            for sem in (s_pe, s_act, s_dve, s_pool):
                ph.wait(eng, sem, sem.n)

    ones_col_bf = sb(octx, "ones_col_bf", [128, 1], BF16)
    ones_col_f = sb(octx, "ones_col_f", [128, 1], F32)
    ones_row_f = sb(octx, "ones_row_f", [1, 128], F32)
    eps_col = sb(octx, "eps_col", [128, 1], F32)
    mw1_col = sb(octx, "mw1_col", [128, 1], F32)
    sw1_col = sb(octx, "sw1_col", [128, 1], F32)
    mw2_col = sb(octx, "mw2_col", [128, 1], F32)
    sw2_col = sb(octx, "sw2_col", [128, 1], F32)
    s1_cols = sb(octx, "s1_cols", [128, 16], F32)
    al1_cols = sb(octx, "al1_cols", [128, 16], F32)
    a2_cols = sb(octx, "a2_cols", [128, 16], F32)
    al2_cols = sb(octx, "al2_cols", [128, 16], F32)
    qn_row = sb(octx, "qn_row", [1, 16], F32)
    kn_row = sb(octx, "kn_row", [1, 16], F32)
    negc_cols = sb(octx, "negc_cols", [128, 16], F32)
    w1acc = sb(octx, "w1acc", [128, 1], F32)
    w2acc = sb(octx, "w2acc", [128, 1], F32)
    tiny = sb(octx, "tiny", [128, 32], F32)

    ps_a = octx.enter_context(nc.psum_tensor("ps_a", [128, 2048], F32))
    ps_lt0 = octx.enter_context(nc.psum_tensor("ps_lt0", [128, 1024], F32))
    ps_lt1 = octx.enter_context(nc.psum_tensor("ps_lt1", [128, 1024], F32))
    ps_lt = [ps_lt0, ps_lt1]
    ps_rot = [ps_lt0[:, 0:512], ps_lt0[:, 512:1024],
              ps_lt1[:, 0:512], ps_lt1[:, 512:1024]]

    ph = Phase(nc)
    for tile, val in ((ones_col_bf, 1.0), (ones_col_f, 1.0), (ones_row_f, 1.0),
                      (eps_col, EPS), (w1acc, 0.0), (w2acc, 0.0), (a2_cols, 0.0)):
        ph.run("gpsimd", "memset", tile[:], val, inc=s_pool)
    init_done = s_pool.n
    ph.emit()

    def ln_stats(ph, xtile, ncols, sq_tile, cb=0):
        inv = 1.0 / ncols
        mu_t, m2_t, var_t, sd_t = (tiny[:, cb:cb + 1], tiny[:, cb + 1:cb + 2],
                                   tiny[:, cb + 2:cb + 3], tiny[:, cb + 3:cb + 4])
        rstd_t, nmu_t = tiny[:, cb + 4:cb + 5], tiny[:, cb + 5:cb + 6]
        ph.run("vector", "reduce_sum", mu_t, xtile, AX.X)
        v_sq = ph.run("scalar", "activation", sq_tile, xtile, AF.Square, inc=s_act)
        ph.wait("vector", s_act, v_sq)
        v_ss = ph.run("vector", "reduce_sum", var_t, sq_tile, AX.X, inc=s_dve)
        ph.wait("vector", s_dve, v_ss)
        ph.runs("vector", s_dve, "tensor_scalar", mu_t, mu_t, inv, None, ALU.mult)
        ph.runs("vector", s_dve, "tensor_tensor", m2_t, mu_t, mu_t, ALU.mult)
        ph.runs("vector", s_dve, "scalar_tensor_tensor", var_t, var_t, inv, m2_t,
                ALU.mult, ALU.subtract)
        v_var = ph.runs("vector", s_dve, "tensor_scalar", nmu_t, mu_t, -1.0, None,
                        ALU.mult)
        ph.wait("scalar", s_dve, v_var)
        v_sd = ph.run("scalar", "activation", sd_t, var_t, AF.Sqrt,
                      bias=eps_col[:], inc=s_act)
        ph.wait("vector", s_act, v_sd)
        v_r = ph.run("vector", "reciprocal", rstd_t, sd_t, inc=s_dve)
        ph.wait("vector", s_dve, v_r)
        return nmu_t, rstd_t

    if UNTIL >= 1:
        st = contextlib.ExitStack()
        xt = [sb(st, f"p0_x{i}", [128, D], F32) for i in range(2)]
        sqt = [sb(st, f"p0_sq{i}", [128, D], F32) for i in range(2)]
        lnt = [sb(st, f"p0_ln{i}", [128, D], BF16) for i in range(2)]
        ph = Phase(nc)
        phase_entry(ph)
        for e in ("scalar", "vector", "gpsimd", "tensor", "sync"):
            ph.wait(e, s_pool, init_done)
        ln_done = [0, 0]
        ln_stored = [0, 0]
        for i in range(S // 128):
            j = i % 2
            ph.wait("sync", s_dve, ln_done[j])
            v_x = ph.dma(dx[j], xt[j][:], xin[i * 128:(i + 1) * 128, :])
            ph.wait("vector", dx[j], v_x)
            ph.wait("scalar", dx[j], v_x)
            nmu, rstd = ln_stats(ph, xt[j][:], D, sqt[j][:], cb=8 + 6 * j)
            ph.wait("vector", dst[j], ln_stored[j])
            v_ln = ph.run("vector", "tensor_scalar", lnt[j][:], xt[j][:],
                          nmu, rstd, ALU.add, ALU.mult, inc=s_dve)
            ln_done[j] = v_ln
            ph.wait("sync", s_dve, v_ln)
            dstap = bass.AP(ln1s, i * 128 * 128, [[128, 128], [S * 128, 16], [1, 128]])
            ln_stored[j] = ph.dma(dst[j], dstap, lnt[j][:])
        ph.emit()
        st.close()

    if UNTIL >= 2:
        st_fm = contextlib.ExitStack()
        ln1_fm = [sb(st_fm, f"ln1fm{dc}", [128, S], BF16) for dc in range(16)]
        st = contextlib.ExitStack()
        ph = Phase(nc)
        phase_entry(ph)
        for dc in range(16):
            src = bass.AP(ln1s, dc * S * 128, [[128, S], [1, 128]])
            ph.dma(dfm[0], ln1_fm[dc][:], src, transpose=True)
        v_lnfm = dfm[0].n
        ph.wait("tensor", dfm[0], v_lnfm)
        ph.wait("gpsimd", dfm[0], v_lnfm)

        ps_busy = [0, 0, 0, 0]

        def mm_group(ph, ps_ap, slot, pieces):
            if slot is not None and ps_busy[slot]:
                ph.wait("tensor", s_act, ps_busy[slot])
            n = len(pieces)
            v = None
            for idx, (l, r) in enumerate(pieces):
                if idx == n - 1:
                    v = ph.run("tensor", "matmul", ps_ap, l, r,
                               start=(idx == 0), stop=True, inc=s_pe)
                else:
                    ph.run("tensor", "matmul", ps_ap, l, r,
                           start=(idx == 0), stop=False)
            return v

        wkf = [sb(st, f"p2_wkf{i}", [128, 2048], F32) for i in range(2)]
        wkb = [sb(st, f"p2_wkb{i}", [128, 2048], BF16) for i in range(2)]
        kst = [sb(st, f"p2_kst{i}", [128, S], BF16) for i in range(2)]
        p2sq = sb(st, "p2_sq", [128, S], BF16)
        wk_cast = [0, 0]
        wk_used = [0, 0]
        kst_stored = [0, 0]
        norm_red = 0
        sq_used = 0

        for idx in range(32):
            kf = idx % 16
            is_k = idx < 16
            colbase = (D + kf * 128) if is_k else (kf * 128)
            ntok = S if is_k else T
            scratch = Ks if is_k else Qs
            j = idx % 2
            ph.wait("sync", s_pool, wk_cast[j])
            src = bass.AP(wqkv, colbase, [[3 * D, 128], [128 * 3 * D, 16], [1, 128]])
            v_w = ph.dma(dw[j], wkf[j][:], src)
            ph.wait("gpsimd", dw[j], v_w)
            ph.wait("gpsimd", s_pe, wk_used[j])
            v_cast = ph.run("gpsimd", "tensor_copy", wkb[j][:], wkf[j][:],
                            inc=s_pool)
            wk_cast[j] = v_cast
            ph.wait("tensor", s_pool, v_cast)
            ph.wait("scalar", dkst[j], kst_stored[j])
            for t in range(ntok // 512):
                slot = t % 4
                pieces = [(wkb[j][:, dc * 128:(dc + 1) * 128],
                           ln1_fm[dc][:, t * 512:(t + 1) * 512]) for dc in range(16)]
                v_mm = mm_group(ph, ps_rot[slot], slot, pieces)
                ph.wait("scalar", s_pe, v_mm)
                v_ev = ph.run("scalar", "copy", kst[j][:, t * 512:(t + 1) * 512],
                              ps_rot[slot], inc=s_act)
                ps_busy[slot] = v_ev
            wk_used[j] = v_mm
            ph.wait("sync", s_act, v_ev)
            kst_stored[j] = ph.dma(dkst[j], scratch[kf * 128:(kf + 1) * 128, :ntok],
                                   kst[j][:, :ntok])
            ph.wait("scalar", s_act, v_ev)
            ph.wait("scalar", s_pe, sq_used)
            v_sq = ph.run("scalar", "activation", p2sq[:, :ntok],
                          kst[j][:, :ntok], AF.Square, inc=s_act)
            ph.wait("tensor", s_act, v_sq)
            row = kn_row if is_k else qn_row
            for r in range(ntok // 2048):
                ph.wait("tensor", s_dve, norm_red)
                for t in range(4):
                    sq_used = ph.run("tensor", "matmul",
                                     ps_a[0:1, t * 512:(t + 1) * 512],
                                     ones_col_bf[:],
                                     p2sq[:, r * 2048 + t * 512:
                                          r * 2048 + (t + 1) * 512],
                                     start=True, stop=True,
                                     inc=s_pe if t == 3 else None) or sq_used
                ph.wait("vector", s_pe, sq_used)
                ph.wait("vector", s_dve, s_dve.n)
                norm_red = ph.run("vector", "tensor_reduce", tiny[:1, 0:1],
                                  ps_a[0:1, 0:2048], AX.X, ALU.max, inc=s_dve)
                ph.wait("vector", s_dve, norm_red)
                if r == 0:
                    ph.runs("vector", s_dve, "tensor_copy",
                            row[0:1, kf:kf + 1], tiny[:1, 0:1])
                else:
                    ph.runs("vector", s_dve, "tensor_tensor",
                            row[0:1, kf:kf + 1], row[0:1, kf:kf + 1],
                            tiny[:1, 0:1], ALU.max)

        ph.emit()
        st.close()
        st = contextlib.ExitStack()
        ph = Phase(nc)
        phase_entry(ph)
        wvf = [sb(st, f"p2_wvf{i}", [128, 4096], F32) for i in range(2)]
        wvb = [sb(st, f"p2_wvb{i}", [128, 8192], BF16) for i in range(2)]
        vst = [sb(st, f"p2_vst{i}", [128, 512], BF16) for i in range(4)]
        vst_stored = [0, 0, 0, 0]
        wv_cast = [0, 0]
        wv_used = [0, 0]
        wvf_cast = [0, 0]
        for vt in range(4):
            wp = vt % 2
            for hv in range(2):
                hj = hv % 2
                lane = dvw if hj == 0 else dfm[1]
                ph.wait("sync", s_pool, wvf_cast[hj])
                src = bass.AP(wqkv, 2 * D + vt * 512 + hv * 8 * 3 * D * 128,
                              [[3 * D, 128], [128 * 3 * D, 8], [1, 512]])
                v_w = ph.dma(lane, wvf[hj][:], src)
                ph.wait("gpsimd", lane, v_w)
                if hv == 0:
                    ph.wait("gpsimd", s_pe, wv_used[wp])
                vcst = ph.run("gpsimd", "tensor_copy",
                              wvb[wp][:, hv * 4096:(hv + 1) * 4096],
                              wvf[hj][:], inc=s_pool)
                wvf_cast[hj] = vcst
            wv_cast[wp] = vcst
            ph.wait("tensor", s_pool, wv_cast[wp])
            for tokc in range(32):
                slot = tokc % 4
                pieces = [(ln1_fm[dc][:, tokc * 128:(tokc + 1) * 128],
                           wvb[wp][:, dc * 512:(dc + 1) * 512]) for dc in range(16)]
                v_mm = mm_group(ph, ps_rot[slot], slot, pieces)
                sj = tokc % 4
                ph.wait("scalar", dvst[sj], vst_stored[sj])
                ph.wait("scalar", s_pe, v_mm)
                v_ev = ph.run("scalar", "copy", vst[sj][:], ps_rot[slot], inc=s_act)
                ps_busy[slot] = v_ev
                ph.wait("sync", s_act, v_ev)
                vst_stored[sj] = ph.dma(
                    dvst[sj], Vs[tokc * 128:(tokc + 1) * 128, vt * 512:(vt + 1) * 512],
                    vst[sj][:])
            wv_used[wp] = v_mm
        ph.emit()
        st.close()
        st_fm.close()

    comb_st = contextlib.ExitStack()
    if UNTIL >= 3:
        combined = [sb(comb_st, f"comb{hc}", [128, T], BF16) for hc in range(16)]
        st = contextlib.ExitStack()
        kh = [sb(st, f"p3_kh{i}", [128, S], BF16) for i in range(2)]
        vh = [sb(st, f"p3_vh{i}", [128, S], BF16) for i in range(2)]
        qh = [sb(st, f"p3_qh{i}", [128, T], BF16) for i in range(2)]
        pbuf = sb(st, "p3_p", [128, 32 * 512], BF16)
        ltree = sb(st, "p3_ltree", [128, 24 * 512], F32)
        rb_sb = [sb(st, f"p3_rb{i}", [128, 512], F32) for i in range(2)]
        rinv = sb(st, "p3_rinv", [1, 512], F32)
        w1m = sb(st, "p3_w1m", [128, 2048], F32)
        w1ab = sb(st, "p3_w1ab", [128, 2048], BF16)
        ph = Phase(nc)
        phase_entry(ph)

        w1m_used = 0

        def w1_mean_piece(i):
            nonlocal w1m_used
            rc, cp = i // 4, i % 4
            ph.wait("sync", s_act, w1m_used)
            v_w = ph.dma(dwm, w1m[:],
                         w1t[rc * 128:(rc + 1) * 128, cp * 2048:(cp + 1) * 2048])
            ph.wait("scalar", dwm, v_w)
            ph.wait("scalar", s_act, w1m_used)
            ph.wait("scalar", s_dve, s_dve.n)
            va = ph.run("scalar", "activation", w1ab[:], w1m[:], AF.Abs,
                        accum_out=tiny[:, 6:7], inc=s_act)
            w1m_used = va
            ph.wait("vector", s_act, va)
            ph.runs("vector", s_dve, "tensor_tensor", w1acc[:], w1acc[:],
                    tiny[:, 6:7], ALU.add)

        def head_loads(h):
            hp = h % 2
            ph.dma(dkh[hp], kh[hp][:], Ks[h * 128:(h + 1) * 128, :])
            ph.dma(dkh[hp], vh[hp][:],
                   bass.AP(Vs, h * 128, [[D, 128], [D * 128, 32], [1, 128]]))
            return ph.dma(dkh[hp], qh[hp][:], Qs[h * 128:(h + 1) * 128, :])

        ph.runs("vector", s_dve, "tensor_tensor", qn_row[:], qn_row[:],
                kn_row[:], ALU.mult)
        v3 = ph.runs("vector", s_dve, "tensor_scalar", qn_row[:], qn_row[:],
                     1.0 / 128.0, None, ALU.mult)
        ph.wait("scalar", s_dve, v3)
        v4 = ph.run("scalar", "activation", qn_row[:], qn_row[:], AF.Sqrt,
                    inc=s_act)
        ph.wait("vector", s_act, v4)
        v5 = ph.run("vector", "tensor_scalar", qn_row[:], qn_row[:],
                    -1.0, None, ALU.mult, inc=s_dve)
        ph.wait("tensor", s_dve, v5)
        v6 = ph.run("tensor", "matmul", ps_a[:, 0:16], ones_row_f[:],
                    qn_row[:], start=True, stop=True, inc=s_pe)
        ph.wait("scalar", s_pe, v6)
        v_negc = ph.run("scalar", "copy", negc_cols[:], ps_a[:, 0:16],
                        inc=s_act)
        v_kvq_prev = head_loads(0)
        for h in range(NH):
            hp = h % 2
            kht, vht, qht = kh[hp], vh[hp], qh[hp]
            v_kvq = v_kvq_prev
            ph.wait("tensor", s_dve, s_dve.n)
            ph.wait("tensor", s_act, s_act.n)
            ph.wait("scalar", dkh[hp], v_kvq)
            if h + 1 < NH:
                ph.wait("sync", s_pe, s_pe.n)
                ph.wait("sync", s_dve, s_dve.n)
                ph.wait("sync", s_act, s_act.n)
                v_kvq_prev = head_loads(h + 1)

            ph.wait("tensor", dkh[hp], v_kvq)
            prev_tail = [0, 0]
            tree_l1 = [0] * 16
            l1_guard = 0
            prev_recip = 0
            pend_lmm = 0
            pend = None
            evac_pend = {}
            prev_rb = 0
            prev_recip = 0
            for g in range(4):
                if pend is not None:
                    gg = pend
                    ph.wait("vector", s_pe, pend_lmm)
                    v_r = ph.run("vector", "reciprocal", rinv[:],
                                 ps_a[0:1, 512:1024], inc=s_dve)
                    prev_recip = v_r
                    ph.wait("vector", s_dve, v_r)
                    ph.wait("tensor", s_dve, v_r)
                    ph.wait("tensor", s_act, prev_rb)
                    v_bm = ph.run("tensor", "matmul", ps_a[:, 1536:2048],
                                  ones_row_f[:], rinv[:], start=True,
                                  stop=True, inc=s_pe)
                    ph.wait("scalar", s_pe, v_bm)
                    v_rb = ph.run("scalar", "copy", rb_sb[gg % 2][:],
                                  ps_a[:, 1536:2048], inc=s_act)
                    prev_rb = v_rb
                    ph.wait("vector", s_act, v_rb)
                    evac_pend[gg] = ph.run(
                        "vector", "tensor_tensor",
                        combined[h][:, gg * 512:(gg + 1) * 512],
                        ps_a[:, 1024:1536] if gg % 2 == 0 else ps_a[:, 0:512],
                        rb_sb[gg % 2][:], ALU.mult, inc=s_dve)
                exp_vals = []
                for pr in range(16):
                    plt = ps_lt[pr % 2]
                    if pr >= 2:
                        ph.wait("tensor", s_act, exp_vals[pr - 2])
                    else:
                        ph.wait("tensor", s_act, prev_tail[pr])
                    for sub in range(2):
                        kc = pr * 2 + sub
                        v_mm = ph.run("tensor", "matmul",
                                      plt[:, sub * 512:(sub + 1) * 512],
                                      kht[:, kc * 128:(kc + 1) * 128],
                                      qht[:, g * 512:(g + 1) * 512],
                                      start=True, stop=True,
                                      inc=s_pe if sub == 1 else None)
                    ph.wait("scalar", s_pe, v_mm)
                    ph.wait("scalar", s_act, v_negc)
                    if pr < 10:
                        ph.wait("scalar", s_dve, tree_l1[pr])
                    else:
                        ph.wait("scalar", s_pool, tree_l1[pr])
                    ev = ph.run("scalar", "activation",
                                pbuf[:, pr * 1024:(pr + 1) * 1024], plt[:],
                                AF.Exp, bias=negc_cols[:, h:h + 1],
                                scale=INV_SQRT_HD, inc=s_act)
                    exp_vals.append(ev)
                prev_tail = [exp_vals[14], exp_vals[15]]
                for pr in range(16):
                    eng, sem = ("vector", s_dve) if pr < 10 else ("gpsimd", s_pool)
                    ph.wait(eng, s_act, exp_vals[pr])
                    if pr < 10:
                        ph.wait(eng, s_pe, l1_guard)
                    tree_l1[pr] = ph.run(
                        eng, "tensor_tensor", ltree[:, pr * 512:(pr + 1) * 512],
                        pbuf[:, 2 * pr * 512:(2 * pr + 1) * 512],
                        pbuf[:, (2 * pr + 1) * 512:(2 * pr + 2) * 512],
                        ALU.add, inc=sem)
                ph.wait("vector", s_dve, tree_l1[9])
                ph.wait("vector", s_pool, tree_l1[15])
                for ib, ob, width in ((0, 16, 8), (16, 0, 4), (0, 16, 2),
                                      (16, 0, 1)):
                    for i in range(width):
                        a = ltree[:, (ib + 2 * i) * 512:(ib + 2 * i + 1) * 512]
                        b = ltree[:, (ib + 2 * i + 1) * 512:(ib + 2 * i + 2) * 512]
                        o = ltree[:, (ob + i) * 512:(ob + i + 1) * 512]
                        inc = s_dve if i == width - 1 else None
                        v_tl = ph.run("vector", "tensor_tensor", o, a, b,
                                      ALU.add, inc=inc)
                    ph.wait("vector", s_dve, v_tl)
                v_l = v_tl
                ph.wait("tensor", s_dve, v_l)
                if prev_recip:
                    ph.wait("tensor", s_dve, prev_recip)
                v_lmm = ph.run("tensor", "matmul", ps_a[0:1, 512:1024],
                               ones_col_f[:], ltree[:, 0:512],
                               start=True, stop=True, inc=s_pe)
                l1_guard = v_lmm
                pso = ps_a[:, 1024:1536] if g % 2 == 0 else ps_a[:, 0:512]
                if evac_pend.get(g - 2) is not None:
                    ph.wait("tensor", s_dve, evac_pend[g - 2])
                if g == 0 and h == 0:
                    ph.wait("tensor", s_act, v_negc)
                for kc in range(32):
                    v_pv = ph.run("tensor", "matmul", pso,
                                  vht[:, kc * 128:(kc + 1) * 128],
                                  pbuf[:, kc * 512:(kc + 1) * 512],
                                  start=(kc == 0), stop=(kc == 31),
                                  inc=s_pe if kc == 31 else None)
                pend = g
                pend_lmm = v_lmm
                ph.wait("vector", s_pe, v_pv)
            gg = pend
            ph.wait("vector", s_pe, pend_lmm)
            v_r = ph.run("vector", "reciprocal", rinv[:], ps_a[0:1, 512:1024],
                         inc=s_dve)
            ph.wait("vector", s_dve, v_r)
            ph.wait("tensor", s_dve, v_r)
            ph.wait("tensor", s_act, prev_rb)
            v_bm = ph.run("tensor", "matmul", ps_a[:, 1536:2048],
                          ones_row_f[:], rinv[:], start=True, stop=True,
                          inc=s_pe)
            ph.wait("scalar", s_pe, v_bm)
            v_rb = ph.run("scalar", "copy", rb_sb[gg % 2][:],
                          ps_a[:, 1536:2048], inc=s_act)
            ph.wait("vector", s_act, v_rb)
            ph.run("vector", "tensor_tensor",
                   combined[h][:, gg * 512:(gg + 1) * 512],
                   ps_a[:, 1024:1536] if gg % 2 == 0 else ps_a[:, 0:512],
                   rb_sb[gg % 2][:], ALU.mult, inc=s_dve)
            w1_mean_piece(h * 4)
            for pp in range(1, 4):
                w1_mean_piece(h * 4 + pp)

        ph.wait("tensor", s_dve, s_dve.n)
        v = ph.run("tensor", "matmul", ps_a[0:1, 0:1], ones_col_f[:], w1acc[:],
                   start=True, stop=True, inc=s_pe)
        ph.wait("vector", s_pe, v)
        v = ph.runs("vector", s_dve, "tensor_scalar", tiny[:1, 7:8],
                    ps_a[0:1, 0:1], 1.0 / 16777216.0, EPS, ALU.mult, ALU.max)
        ph.wait("tensor", s_dve, v)
        v = ph.run("tensor", "matmul", ps_a[:, 0:1], ones_row_f[:],
                   tiny[:1, 7:8], start=True, stop=True, inc=s_pe)
        ph.wait("scalar", s_pe, v)
        v = ph.run("scalar", "copy", mw1_col[:], ps_a[:, 0:1], inc=s_act)
        ph.wait("vector", s_act, v)
        v = ph.run("vector", "reciprocal", sw1_col[:], mw1_col[:], inc=s_dve)
        ph.wait("vector", s_dve, v)
        ph.emit()
        st.close()

    if UNTIL >= 4:
        st = contextlib.ExitStack()
        wof = [sb(st, f"p4_wof{i}", [128, 4096], F32) for i in range(2)]
        wob = [sb(st, f"p4_wob{hc}", [128, 2048], BF16) for hc in range(16)]
        xtt = [sb(st, f"p4_x{i}", [128, D], F32) for i in range(2)]
        x2t = [sb(st, f"p4_x2{i}", [128, D], F32) for i in range(2)]
        ln2t = sb(st, "p4_ln2", [128, D], F32)
        sq2t = ln2t
        z1t = sb(st, "p4_z1", [128, D], BF16)
        ph = Phase(nc)
        phase_entry(ph)
        wof_used = [0, 0]
        for jp in range(8):
            wj = jp % 2
            ph.wait("sync", s_pool, wof_used[wj])
            src = bass.AP(wout, jp * 256, [[D, 128], [128 * D, 16], [1, 256]])
            v_w = ph.dma(dw[wj], wof[wj][:], src)
            ph.wait("gpsimd", dw[wj], v_w)
            for hc in range(16):
                inc = s_pool if hc == 15 else None
                v_c = ph.run("gpsimd", "tensor_copy",
                             wob[hc][:, jp * 256:(jp + 1) * 256],
                             wof[wj][:, hc * 256:(hc + 1) * 256], inc=inc)
            wof_used[wj] = v_c
        ph.wait("tensor", s_pool, s_pool.n)
        x2_stored = [0, 0]
        z1_stored = 0
        srow_all = tiny[:, 8:24]
        ssq_all = s1_cols
        mx_all = a2cols_tmp = None
        mx_all = sb(st, "p4_mx", [128, 16], F32)
        mn_all = sb(st, "p4_mn", [128, 16], F32)
        nmu_all = sb(st, "p4_nmu", [128, 16], F32)
        var_all = sb(st, "p4_var", [128, 16], F32)
        rstd_all = sb(st, "p4_rstd", [128, 16], F32)
        rs1_all = sb(st, "p4_rs1", [128, 16], F32)
        bias_all = sb(st, "p4_bias", [128, 16], F32)
        for tc in range(16):
            j = tc % 2
            ph.wait("sync", s_dve, s_dve.n)
            v_x = ph.dma(dx[j], xtt[j][:], xin[tc * 128:(tc + 1) * 128, :])
            for of_t in range(4):
                for hc in range(16):
                    inc = s_pe if hc == 15 else None
                    v_mm = ph.run("tensor", "matmul",
                                  ps_a[:, of_t * 512:(of_t + 1) * 512],
                                  combined[hc][:, tc * 128:(tc + 1) * 128],
                                  wob[hc][:, of_t * 512:(of_t + 1) * 512],
                                  start=(hc == 0), stop=(hc == 15), inc=inc)
            ph.wait("vector", dst[j], x2_stored[j])
            ph.wait("vector", s_pe, v_mm)
            ph.wait("vector", dx[j], v_x)
            v_x2 = ph.run("vector", "tensor_tensor", x2t[j][:], ps_a[:],
                          xtt[j][:], ALU.add, inc=s_dve)
            ph.wait("tensor", s_dve, v_x2)
            ph.wait("sync", s_dve, v_x2)
            x2_stored[j] = ph.dma(dst[j], x2s[tc * 128:(tc + 1) * 128, :],
                                  x2t[j][:])
            ph.run("vector", "reduce_sum", srow_all[:, tc:tc + 1], x2t[j][:],
                   AX.X)
            ph.wait("scalar", s_dve, s_dve.n)
            v_sq = ph.run("scalar", "activation", sq2t[:], x2t[j][:],
                          AF.Square, inc=s_act)
            ph.wait("vector", s_act, v_sq)
            ph.run("vector", "reduce_sum", ssq_all[:, tc:tc + 1], sq2t[:], AX.X)
            ph.run("vector", "tensor_reduce", mx_all[:, tc:tc + 1], x2t[j][:],
                   AX.X, ALU.max)
            v_red = ph.run("vector", "tensor_reduce", mn_all[:, tc:tc + 1],
                           x2t[j][:], AX.X, ALU.min, inc=s_dve)
        ph.wait("vector", s_dve, s_dve.n)
        inv = 1.0 / D
        ph.runs("vector", s_dve, "tensor_scalar", nmu_all[:], srow_all[:],
                -inv, None, ALU.mult)
        ph.runs("vector", s_dve, "tensor_tensor", var_all[:], nmu_all[:],
                nmu_all[:], ALU.mult)
        v = ph.runs("vector", s_dve, "scalar_tensor_tensor", var_all[:],
                    ssq_all[:], inv, var_all[:], ALU.mult, ALU.subtract)
        ph.wait("scalar", s_dve, v)
        v_sd = ph.run("scalar", "activation", var_all[:], var_all[:], AF.Sqrt,
                      bias=eps_col[:], inc=s_act)
        ph.wait("vector", s_act, v_sd)
        v_r = ph.run("vector", "reciprocal", rstd_all[:], var_all[:], inc=s_dve)
        ph.wait("vector", s_dve, v_r)
        ph.runs("vector", s_dve, "tensor_tensor", mx_all[:], mx_all[:],
                nmu_all[:], ALU.add)
        ph.runs("vector", s_dve, "scalar_tensor_tensor", mn_all[:], mn_all[:],
                -1.0, nmu_all[:], ALU.mult, ALU.subtract)
        ph.runs("vector", s_dve, "tensor_tensor", mx_all[:], mx_all[:],
                mn_all[:], ALU.max)
        ph.runs("vector", s_dve, "tensor_tensor", mx_all[:], mx_all[:],
                rstd_all[:], ALU.mult)
        ph.runs("vector", s_dve, "tensor_scalar", mx_all[:], mx_all[:],
                EPS, None, ALU.max)
        v_r = ph.run("vector", "reciprocal", rs1_all[:], mx_all[:], inc=s_dve)
        ph.wait("vector", s_dve, v_r)
        ph.runs("vector", s_dve, "tensor_scalar", s1_cols[:], rs1_all[:],
                127.0, None, ALU.mult)
        ph.runs("vector", s_dve, "tensor_scalar", al1_cols[:],
                mx_all[:], mw1_col[:], 1.0 / 127.0, ALU.mult, ALU.mult)
        ph.runs("vector", s_dve, "tensor_tensor", rs1_all[:], rstd_all[:],
                s1_cols[:], ALU.mult)
        v_b = ph.runs("vector", s_dve, "tensor_tensor", bias_all[:],
                      nmu_all[:], rs1_all[:], ALU.mult)
        ph.wait("sync", s_dve, v_b)
        ph.wait("sync", dst[0], dst[0].n)
        ph.wait("sync", dst[1], dst[1].n)
        for tc in range(16):
            j = tc % 2
            ph.wait("sync", s_dve, s_dve.n)
            v_x = ph.dma(dx[j], x2t[j][:], x2s[tc * 128:(tc + 1) * 128, :])
            ph.wait("vector", dx[j], v_x)
            ph.wait("vector", dz1, z1_stored)
            v_z = ph.custom(ROUND_AFF, out=z1t[:], in0=x2t[j][:],
                            s0=rs1_all[:, tc:tc + 1], s1=bias_all[:, tc:tc + 1],
                            imm2=MAGIC, inc=s_dve)
            ph.wait("sync", s_dve, v_z)
            dstap = bass.AP(z1s, tc * 128 * 128, [[128, 128], [T * 128, 16], [1, 128]])
            z1_stored = ph.dma(dz1, dstap, z1t[:])
        ph.emit()
        st.close()
    comb_st.close()

    if UNTIL >= 5:
        st = contextlib.ExitStack()
        z1_fm = [sb(st, f"z1fm{dc}", [128, T], BF16) for dc in range(16)]
        w1f = [sb(st, f"p6_w1f{i}", [128, 4096], F32) for i in range(2)]
        w1q = [sb(st, f"p6_w1q{i}", [128, 8192], BF16) for i in range(2)]
        gst = sb(st, "p6_gst", [128, 8192], F16)
        gred = sb(st, "p6_gred", [128, 16], F32)
        gtmp = [sb(st, f"p6_g{i}", [128, 512], F32) for i in range(2)]
        w2m = sb(st, "p6_w2m", [128, 2048], F32)
        w2ab = sb(st, "p6_w2ab", [128, 2048], BF16)
        ph = Phase(nc)
        phase_entry(ph)
        for dc in range(16):
            src = bass.AP(z1s, dc * T * 128, [[128, T], [1, 128]])
            ph.dma(dfm[0], z1_fm[dc][:], src, transpose=True)
        ph.wait("tensor", dfm[0], dfm[0].n)

        w2m_used = 0

        def w2_mean_piece(i):
            nonlocal w2m_used
            ph.wait("sync", s_act, w2m_used)
            v_w = ph.dma(dwm, w2m[:], w2t[i * 128:(i + 1) * 128, :])
            ph.wait("scalar", dwm, v_w)
            ph.wait("scalar", s_act, w2m_used)
            ph.wait("scalar", s_dve, s_dve.n)
            va = ph.run("scalar", "activation", w2ab[:], w2m[:], AF.Abs,
                        accum_out=tiny[:, 6:7], inc=s_act)
            w2m_used = va
            ph.wait("vector", s_act, va)
            ph.runs("vector", s_dve, "tensor_tensor", w2acc[:], w2acc[:],
                    tiny[:, 6:7], ALU.add)

        w1q_ready = [0, 0]
        w1q_used = [0, 0]
        gst_stored = 0
        gdone = {}
        gcopy = {}
        gi = 0

        def w1_quant_half(hs2, half):
            j2 = hs2 % 2
            src = bass.AP(w1t, hs2 * 512 + half * 256,
                          [[H, 128], [128 * H, 16], [1, 256]])
            ph.wait("sync", s_dve, w1q_ready[j2])
            ph.wait("sync", s_pe, w1q_used[j2])
            v_w = ph.dma(dw[j2], w1f[j2][:], src)
            ph.wait("vector", dw[j2], v_w)
            ph.wait("vector", s_pe, w1q_used[j2])
            v_rnd = ph.custom(ROUND_TS, out=w1f[j2][:], in0=w1f[j2][:],
                              s0=sw1_col[:], s1=MAGIC, inc=s_dve)
            ph.wait("vector", s_dve, v_rnd)
            w1q_dst = bass.AP(w1q[j2], half * 256,
                              [[8192, 128], [512, 16], [1, 256]])
            v_q = ph.run("vector", "tensor_scalar", w1q_dst, w1f[j2][:],
                         -1.0, 1.0, ALU.max, ALU.min, inc=s_dve)
            w1q_ready[j2] = v_q

        w1_quant_half(0, 0)
        w1_quant_half(0, 1)
        for hs in range(16):
            j = hs % 2
            ph.wait("tensor", s_dve, w1q_ready[j])
            ph.wait("gpsimd", dgst, gst_stored)
            redv = {}
            for tc in range(16):
                slot = gi % 4
                plt = ps_lt[slot // 2]
                ps_ap = plt[:, (slot % 2) * 512:(slot % 2 + 1) * 512]
                if gdone.get(gi - 4):
                    ph.wait("tensor", s_act, gdone[gi - 4])
                for dc in range(16):
                    inc = s_pe if dc == 15 else None
                    v_mm = ph.run("tensor", "matmul", ps_ap,
                                  z1_fm[dc][:, tc * 128:(tc + 1) * 128],
                                  w1q[j][:, dc * 512:(dc + 1) * 512],
                                  start=(dc == 0), stop=(dc == 15), inc=inc)
                gj = tc % 2
                ph.wait("scalar", s_pe, v_mm)
                if gcopy.get(gi - 2):
                    ph.wait("scalar", s_pool, gcopy[gi - 2])
                if redv.get(tc - 2):
                    ph.wait("scalar", s_dve, redv[tc - 2])
                v_g = ph.run("scalar", "activation", gtmp[gj][:], ps_ap,
                             AF.Gelu_apprx_tanh, scale=al1_cols[:, tc:tc + 1],
                             inc=s_act)
                gdone[gi] = v_g
                ph.wait("vector", s_act, v_g)
                v_red = ph.run("vector", "tensor_reduce", gred[:, tc:tc + 1],
                               gtmp[gj][:], AX.X, ALU.max, True, inc=s_dve)
                redv[tc] = v_red
                ph.wait("gpsimd", s_act, v_g)
                gcopy[gi] = ph.run("gpsimd", "tensor_copy",
                                   gst[:, tc * 512:(tc + 1) * 512], gtmp[gj][:],
                                   inc=s_pool)
                gi += 1
                if hs + 1 < 16 and tc in (4, 12):
                    w1_quant_half(hs + 1, 0 if tc == 4 else 1)
            w1q_used[j] = v_mm
            ph.wait("vector", s_dve, v_red)
            ph.runs("vector", s_dve, "tensor_tensor", a2_cols[:], a2_cols[:],
                    gred[:], ALU.max)
            ph.wait("sync", s_pool, gcopy[gi - 1])
            for hc4 in range(4):
                dstap = bass.AP(gs, (hs * 4 + hc4) * T * 128,
                                [[128, 128], [128 * 128, 16], [1, 128]])
                srcap = bass.AP(gst, hc4 * 128, [[8192, 128], [512, 16], [1, 128]])
                gst_stored = ph.dma(dgst, dstap, srcap)
            for pp in range(4):
                w2_mean_piece(hs * 4 + pp)

        ph.wait("tensor", s_dve, s_dve.n)
        ph.wait("tensor", s_act, s_act.n)
        v = ph.run("tensor", "matmul", ps_a[0:1, 0:1], ones_col_f[:], w2acc[:],
                   start=True, stop=True, inc=s_pe)
        ph.wait("vector", s_pe, v)
        v = ph.runs("vector", s_dve, "tensor_scalar", tiny[:1, 7:8],
                    ps_a[0:1, 0:1], 1.0 / 16777216.0, EPS, ALU.mult, ALU.max)
        ph.wait("tensor", s_dve, v)
        v = ph.run("tensor", "matmul", ps_a[:, 0:1], ones_row_f[:],
                   tiny[:1, 7:8], start=True, stop=True, inc=s_pe)
        ph.wait("scalar", s_pe, v)
        v = ph.run("scalar", "copy", mw2_col[:], ps_a[:, 0:1], inc=s_act)
        ph.wait("vector", s_act, v)
        v = ph.run("vector", "reciprocal", sw2_col[:], mw2_col[:], inc=s_dve)
        ph.wait("vector", s_dve, v)
        a2c = sb(st, "p6_a2c", [128, 16], F32)
        s2c = sb(st, "p6_s2c", [128, 16], F32)
        ph.runs("vector", s_dve, "tensor_scalar", a2c[:], a2_cols[:],
                EPS, None, ALU.max)
        v_r = ph.run("vector", "reciprocal", s2c[:], a2c[:], inc=s_dve)
        ph.wait("vector", s_dve, v_r)
        ph.runs("vector", s_dve, "tensor_scalar", s2c[:], s2c[:],
                127.0, None, ALU.mult)
        vv = ph.runs("vector", s_dve, "tensor_scalar", al2_cols[:], a2c[:],
                     mw2_col[:], 1.0 / 127.0, ALU.mult, ALU.mult)
        ph.wait("sync", s_dve, vv)
        for tc in range(16):
            ph.dma(ds2[tc % 2], s2s[0:1, tc * 128:(tc + 1) * 128],
                   s2c[:, tc:tc + 1])
        ph.emit()
        st.close()

    if UNTIL >= 6:
        st = contextlib.ExitStack()
        s2row = sb(st, "p7_s2row", [1, T], F32)
        s2b = sb(st, "p7_s2b", [128, T], F32)
        acc = [sb(st, f"p7_acc{i}", [128, D], F32) for i in range(8)]
        gfm = [sb(st, f"p7_gfm{i}", [128, 1024], F16) for i in range(2)]
        z2fm = [sb(st, f"p7_z2{i}", [128, 1024], BF16) for i in range(16)]
        w2f = [sb(st, f"p7_w2f{i}", [128, 2048], F32) for i in range(2)]
        w2q = [sb(st, f"p7_w2q{i}", [128, 2048], BF16) for i in range(16)]
        x2tt = [sb(st, "p8_x2", [128, D], F32)] * 2
        ot = x2tt
        ph = Phase(nc)
        phase_entry(ph)
        v_s2 = ph.dma(dwm, s2row[:], s2s[:])
        ph.wait("tensor", dwm, v_s2)
        for t in range(4):
            v = ph.run("tensor", "matmul", ps_a[:, t * 512:(t + 1) * 512],
                       ones_row_f[:], s2row[:, t * 512:(t + 1) * 512],
                       start=True, stop=True, inc=s_pe if t == 3 else None)
        ph.wait("scalar", s_pe, v)
        v_sb = ph.run("scalar", "copy", s2b[:], ps_a[:], inc=s_act)
        ph.wait("vector", s_act, v_sb)
        ph.wait("tensor", s_act, v_sb)

        stages = [(hf, hsl) for hf in range(2) for hsl in range(8)]
        set_ready = [0, 0]
        set_mm_done = [0, 0]
        gfm_used = [0, 0]
        w2f_used = [0, 0]
        pacc_lt = [ps_lt0[:, 0:512], ps_lt0[:, 512:1024],
                   ps_lt1[:, 0:512], ps_lt1[:, 512:1024]]

        def quant_chunk(stage_i, k8):
            hf, hsl = stages[stage_i]
            p = stage_i % 2
            hc = hsl * 8 + k8
            j = (stage_i * 8 + k8) % 2
            ph.wait("sync", s_dve, max(gfm_used[j], w2f_used[j]))
            ph.wait("sync", s_pe, set_mm_done[p])
            src = bass.AP(gs, hc * T * 128 + hf * 1024 * 128,
                          [[128, 1024], [1, 128]])
            v_g = ph.dma(dfm[j], gfm[j][:], src, transpose=True)
            v_w = ph.dma(dw[j], w2f[j][:], w2t[hc * 128:(hc + 1) * 128, :])
            ph.wait("vector", dfm[j], v_g)
            ph.wait("vector", s_pe, set_mm_done[p])
            tok = slice(hf * 1024, (hf + 1) * 1024)
            gfm_used[j] = ph.custom(ROUND_TT, out=z2fm[p * 8 + k8][:],
                                    in0=gfm[j][:], in1=s2b[:, tok], s0=MAGIC,
                                    inc=s_dve)
            ph.wait("vector", dw[j], v_w)
            w2f_used[j] = ph.custom(ROUND_CLIP, out=w2q[p * 8 + k8][:],
                                    in0=w2f[j][:], s0=sw2_col[:], s1=MAGIC,
                                    inc=s_dve)
            set_ready[p] = w2f_used[j]

        out_stored = [0, 0]
        x2_used = [0, 0]
        evac_vals = {}
        for k8 in range(8):
            quant_chunk(0, k8)
        for si, (hf, hsl) in enumerate(stages):
            p = si % 2
            ph.wait("tensor", s_dve, set_ready[p])
            for tc8 in range(8):
                par = tc8 % 2
                if evac_vals.get(tc8 - 2) is not None:
                    ph.wait("tensor", s_dve, evac_vals[tc8 - 2])
                for k8 in range(8):
                    for nt in range(4):
                        pac = ps_a[:, nt * 512:(nt + 1) * 512] if par == 0                             else pacc_lt[nt]
                        inc = s_pe if (k8 == 7 and nt == 3) else None
                        v_mm = ph.run(
                            "tensor", "matmul", pac,
                            z2fm[p * 8 + k8][:, tc8 * 128:(tc8 + 1) * 128],
                            w2q[p * 8 + k8][:, nt * 512:(nt + 1) * 512],
                            start=(k8 == 0), stop=(k8 == 7), inc=inc)
                ph.wait("vector", s_pe, v_mm)
                first = (hsl == 0)
                if par == 0:
                    if first:
                        v_e = ph.run("vector", "tensor_copy", acc[tc8][:],
                                     ps_a[:], inc=s_dve)
                    else:
                        v_e = ph.run("vector", "tensor_tensor", acc[tc8][:],
                                     acc[tc8][:], ps_a[:], ALU.add, inc=s_dve)
                else:
                    for hh in range(2):
                        m, a_sl = (ps_lt[hh][:],
                                   acc[tc8][:, hh * 1024:(hh + 1) * 1024])
                        if first:
                            v_e = ph.run("vector", "tensor_copy", a_sl, m,
                                         inc=s_dve)
                        else:
                            v_e = ph.run("vector", "tensor_tensor", a_sl,
                                         a_sl, m, ALU.add, inc=s_dve)
                evac_vals[tc8] = v_e
                if si + 1 < len(stages):
                    quant_chunk(si + 1, tc8)
            set_mm_done[p] = v_mm
            if hsl == 7:
                ph.wait("vector", s_dve, s_dve.n)
                for tc8 in range(8):
                    tc = hf * 8 + tc8
                    ph.wait("sync", s_dve, x2_used[0])
                    ph.wait("sync", dst[0], out_stored[0])
                    v_x2 = ph.dma(dx[0], x2tt[0][:],
                                  x2s[tc * 128:(tc + 1) * 128, :])
                    ph.wait("vector", dx[0], v_x2)
                    v_o = ph.run("vector", "scalar_tensor_tensor", ot[0][:],
                                 acc[tc8][:], al2_cols[:, tc:tc + 1],
                                 x2tt[0][:], ALU.mult, ALU.add, inc=s_dve)
                    x2_used[0] = v_o
                    ph.wait("sync", s_dve, v_o)
                    out_stored[0] = ph.dma(dst[0],
                                           out[tc * 128:(tc + 1) * 128, :],
                                           ot[0][:])
                evac_vals = {}
        for l in store_lanes:
            ph.wait("sync", l, l.n)
        ph.emit()
        st.close()

    mybir.codegen_inst_isa_subclasses(nc)
    octx.close()
    return nc


def kernel(L0_attn_x, L0_attn_w_qkv, L0_attn_w_out, L0_ffn_w1, L0_ffn_w2):
    x = np.ascontiguousarray(np.asarray(L0_attn_x, dtype=np.float32))
    wqkv = np.ascontiguousarray(np.asarray(L0_attn_w_qkv, dtype=np.float32))
    wout = np.ascontiguousarray(np.asarray(L0_attn_w_out, dtype=np.float32))
    w1t = np.ascontiguousarray(np.asarray(L0_ffn_w1, dtype=np.float32).T)
    w2t = np.ascontiguousarray(np.asarray(L0_ffn_w2, dtype=np.float32).T)

    nc = _build()
    in_maps = []
    for c in range(8):
        b, half = c // 2, c % 2
        own = x[b, half * T:(half + 1) * T]
        other = x[b, (1 - half) * T:(2 - half) * T]
        xc = np.ascontiguousarray(np.concatenate([own, other], axis=0))
        in_maps.append({"xin": xc, "wqkv": wqkv, "wout": wout,
                        "w1t": w1t, "w2t": w2t})
    res = run_bass_kernel_spmd(nc, in_maps, core_ids=list(range(8)))
    o = np.empty((4, 4096, 2048), dtype=np.float32)
    for c in range(8):
        b, half = c // 2, c % 2
        o[b, half * T:(half + 1) * T] = res.results[c]["out"]
    return o


# version 11
# speedup vs baseline: 1.4737x; 1.2039x over previous
import contextlib
import numpy as np
import ml_dtypes
import concourse.bass as bass
import concourse.mybir as mybir
from concourse.bass_utils import run_bass_kernel_spmd
from concourse import dve_ops as _dvo
from concourse.dve_op import DveOp
from concourse.dve_spec import Spec, Src0, Src1, C0, C1, C2, lower as _dve_lower

F32 = mybir.dt.float32
BF16 = mybir.dt.bfloat16
F16 = mybir.dt.float16
AF = mybir.ActivationFunctionType
ALU = mybir.AluOpType
AX = mybir.AxisListType

MAGIC = 12582912.0
INV_SQRT_HD = 0.08838834764831845
EPS = 1e-5

S = 4096
T = 2048
D = 2048
H = 8192
NH = 16


def _as_f(x, like):
    dt = np.float64 if np.asarray(like).dtype == np.float64 else np.float32
    return np.asarray(x, dtype=dt)


def _ref_round_ts(in0, in1, s0, s1, imm2):
    return np.round(_as_f(in0, in0) * _as_f(s0, in0))


def _ref_round_tt(in0, in1, s0, s1, imm2):
    return np.round(_as_f(in0, in0) * _as_f(in1, in0))


def _ref_round_clip(in0, in1, s0, s1, imm2):
    return np.minimum(np.maximum(np.round(_as_f(in0, in0) * _as_f(s0, in0)),
                                 -1.0), 1.0)


def _ref_round_affine(in0, in1, s0, s1, imm2):
    return np.round(_as_f(in0, in0) * _as_f(s0, in0) + _as_f(s1, in0))


def _register_round_ops():
    from concourse.dve_spec import Zero, One, maxx, minn
    ops = {}
    defs = (
        ("ROUND_MUL_TS_ANT", Spec(body=((Src0 * C0) + C1) - C1), _ref_round_ts),
        ("ROUND_MUL_TT_ANT", Spec(body=((Src0 * Src1) + C0) - C0), _ref_round_tt),
        ("ROUND_CLIP_TS_ANT",
         Spec(body=minn(maxx(((Src0 * C0) + C1) - C1, Zero - One), One)),
         _ref_round_clip),
        ("ROUND_AFFINE_ANT",
         Spec(body=(((Src0 * C0) + C1) + C2) - C2),
         _ref_round_affine),
    )
    replaceable = ["DVEHASH32L_DRAIN", "DVEHASH32L_BODY", "DVEHASH32L_SEED"]
    for name, spec, ref in defs:
        existing = next((o for o in _dvo.OPS if o.name == name), None)
        if existing is not None:
            ops[name] = existing
            continue
        compiled = _dve_lower(spec, ver="v3", perf_en=False, name=name)
        op = DveOp(name, spec, subdim=False,
                   uops_sha={"v3": compiled.sha("v3")}, reference=ref)
        if len(_dvo.OPS) < 31:
            _dvo.OPS.append(op)
            row = len(_dvo.OPS)
        else:
            victim = next(n for n in replaceable
                          if any(o.name == n for o in _dvo.OPS))
            idx = next(i for i, o in enumerate(_dvo.OPS) if o.name == victim)
            replaceable.remove(victim)
            _dvo.OPS[idx] = op
            row = idx + 1
            del _dvo._SUB_OPCODE_FOR_NAME[victim]
            _dvo.CUSTOM_DVE_SPECS.pop(victim, None)
            _dvo.CUSTOM_DVE_REFS.pop(victim, None)
        _dvo._SUB_OPCODE_FOR_NAME[name] = row
        _dvo.CUSTOM_DVE_SPECS[name] = spec
        _dvo.CUSTOM_DVE_REFS[name] = ref
        ops[name] = op
    return (ops["ROUND_MUL_TS_ANT"], ops["ROUND_MUL_TT_ANT"],
            ops["ROUND_CLIP_TS_ANT"], ops["ROUND_AFFINE_ANT"])


ROUND_TS, ROUND_TT, ROUND_CLIP, ROUND_AFF = _register_round_ops()

ENGINES = ("sync", "tensor", "scalar", "vector", "gpsimd")
UNTIL = 99
DEBUG_SCRATCH = False


class Sem:
    def __init__(self, h):
        self.h = h
        self.n = 0


class Phase:
    def __init__(self, nc):
        self.nc = nc
        self.q = {e: [] for e in ENGINES}

    def on(self, eng, fn):
        self.q[eng].append(fn)

    def wait(self, eng, sem, val):
        if val and val > 0:
            self.on(eng, lambda e, _s=sem.h, _v=val: e.wait_ge(_s, _v))

    def run(self, eng, method, *args, inc=None, **kw):
        v = None
        if inc is not None:
            inc.n += 1
            v = inc.n
            def _f(e, _m=method, _a=args, _k=kw, _s=inc.h):
                getattr(e, _m)(*_a, **_k).then_inc(_s, 1)
        else:
            def _f(e, _m=method, _a=args, _k=kw):
                getattr(e, _m)(*_a, **_k)
        self.on(eng, _f)
        return v

    def runs(self, eng, sem, method, *args, **kw):
        v = self.run(eng, method, *args, inc=sem, **kw)
        self.wait(eng, sem, v)
        return v

    def custom(self, op, inc=None, **kw):
        v = None
        if inc is not None:
            inc.n += 1
            v = inc.n
            def _f(e, _o=op, _k=kw, _s=inc.h):
                e._custom_dve(_o, **_k).then_inc(_s, 1)
        else:
            def _f(e, _o=op, _k=kw):
                e._custom_dve(_o, **_k)
        self.on("vector", _f)
        return v

    def dma(self, sem, dst, src, transpose=False):
        sem.n += 16
        v = sem.n
        if transpose:
            self.on("sync", lambda e, _d=dst, _s=src, _h=sem.h:
                    e.dma_start(_d, _s, transpose=True).then_inc(_h, 16))
        else:
            self.on("sync", lambda e, _d=dst, _s=src, _h=sem.h:
                    e.dma_start(_d, _s).then_inc(_h, 16))
        return v

    def emit(self):
        with self.nc.Block() as block:
            @block.sync
            def _(e):
                for fn in self.q["sync"]:
                    fn(e)

            @block.tensor
            def _(e):
                for fn in self.q["tensor"]:
                    fn(e)

            @block.scalar
            def _(e):
                for fn in self.q["scalar"]:
                    fn(e)

            @block.vector
            def _(e):
                for fn in self.q["vector"]:
                    fn(e)

            @block.gpsimd
            def _(e):
                for fn in self.q["gpsimd"]:
                    fn(e)


def _build():
    nc = bass.Bass("TRN2", target_bir_lowering=False)

    xin = nc.dram_tensor("xin", [S, D], F32, kind="ExternalInput")
    wqkv = nc.dram_tensor("wqkv", [D, 3 * D], F32, kind="ExternalInput")
    wout = nc.dram_tensor("wout", [D, D], F32, kind="ExternalInput")
    w1t = nc.dram_tensor("w1t", [D, H], F32, kind="ExternalInput")
    w2t = nc.dram_tensor("w2t", [H, D], F32, kind="ExternalInput")
    out = nc.dram_tensor("out", [T, D], F32, kind="ExternalOutput")

    sk = "ExternalOutput" if DEBUG_SCRATCH else "Internal"
    ln1s = nc.dram_tensor("ln1s", [16, S, 128], BF16, kind=sk)
    Ks = nc.dram_tensor("Ks", [D, S], BF16, kind=sk)
    Vs = nc.dram_tensor("Vs", [S, D], BF16, kind=sk)
    Qs = nc.dram_tensor("Qs", [D, T], BF16, kind=sk)
    x2s = nc.dram_tensor("x2s", [T, D], F32, kind=sk)
    z1s = nc.dram_tensor("z1s", [16, T, 128], BF16, kind=sk)
    gs = nc.dram_tensor("gs", [64, T, 128], F16, kind=sk)
    s2s = nc.dram_tensor("s2s", [1, T], F32, kind=sk)

    octx = contextlib.ExitStack()

    def mksem(name):
        return Sem(octx.enter_context(nc.semaphore(name)))

    dx = [mksem(f"dx{i}") for i in range(8)]
    dst = [mksem("dst0"), mksem("dst1")]
    dw = [mksem("dw0"), mksem("dw1")]
    dkst = [mksem("dkst0"), mksem("dkst1")]
    dvst = [mksem(f"dvst{i}") for i in range(4)]
    dvw = mksem("dvw")
    dfm = [mksem("dfm0"), mksem("dfm1")]
    dkh = [mksem("dkh0"), mksem("dkh1")]
    dwm = mksem("dwm")
    dgst = mksem("dgst")
    dz1 = mksem("dz1")
    ds2 = [mksem("ds2a"), mksem("ds2b")]
    store_lanes = dst + dkst + dvst + [dgst, dz1] + ds2
    all_lanes = dx + dst + dw + dkst + dvst + [dvw] + dfm + dkh + [dwm, dgst, dz1] + ds2
    s_pe = Sem(octx.enter_context(nc.semaphore("s_pe")))
    s_act = Sem(octx.enter_context(nc.semaphore("s_act")))
    s_dve = Sem(octx.enter_context(nc.semaphore("s_dve")))
    s_pool = Sem(octx.enter_context(nc.semaphore("s_pool")))

    def sb(st, name, shape, dtype):
        return st.enter_context(nc.sbuf_tensor(name, shape, dtype))

    def phase_entry(ph):
        for eng in ENGINES:
            for l in all_lanes:
                ph.wait(eng, l, l.n)
            for sem in (s_pe, s_act, s_dve, s_pool):
                ph.wait(eng, sem, sem.n)

    ones_col_bf = sb(octx, "ones_col_bf", [128, 1], BF16)
    ones_col_f = sb(octx, "ones_col_f", [128, 1], F32)
    ones_row_f = sb(octx, "ones_row_f", [1, 128], F32)
    eps_col = sb(octx, "eps_col", [128, 1], F32)
    mw1_col = sb(octx, "mw1_col", [128, 1], F32)
    sw1_col = sb(octx, "sw1_col", [128, 1], F32)
    mw2_col = sb(octx, "mw2_col", [128, 1], F32)
    sw2_col = sb(octx, "sw2_col", [128, 1], F32)
    s1_cols = sb(octx, "s1_cols", [128, 16], F32)
    al1_cols = sb(octx, "al1_cols", [128, 16], F32)
    a2_cols = sb(octx, "a2_cols", [128, 16], F32)
    al2_cols = sb(octx, "al2_cols", [128, 16], F32)
    qn_row = sb(octx, "qn_row", [1, 16], F32)
    kn_row = sb(octx, "kn_row", [1, 16], F32)
    negc_cols = sb(octx, "negc_cols", [128, 16], F32)
    w1acc = sb(octx, "w1acc", [128, 1], F32)
    w2acc = sb(octx, "w2acc", [128, 1], F32)
    tiny = sb(octx, "tiny", [128, 32], F32)

    ps_a = octx.enter_context(nc.psum_tensor("ps_a", [128, 2048], F32))
    ps_lt0 = octx.enter_context(nc.psum_tensor("ps_lt0", [128, 1024], F32))
    ps_lt1 = octx.enter_context(nc.psum_tensor("ps_lt1", [128, 1024], F32))
    ps_lt = [ps_lt0, ps_lt1]
    ps_rot = [ps_lt0[:, 0:512], ps_lt0[:, 512:1024],
              ps_lt1[:, 0:512], ps_lt1[:, 512:1024]]

    ph = Phase(nc)
    for tile, val in ((ones_col_bf, 1.0), (ones_col_f, 1.0), (ones_row_f, 1.0),
                      (eps_col, EPS), (w1acc, 0.0), (w2acc, 0.0), (a2_cols, 0.0)):
        ph.run("gpsimd", "memset", tile[:], val, inc=s_pool)
    init_done = s_pool.n
    ph.emit()

    def ln_stats(ph, xtile, ncols, sq_tile, cb=0):
        inv = 1.0 / ncols
        mu_t, m2_t, var_t, sd_t = (tiny[:, cb:cb + 1], tiny[:, cb + 1:cb + 2],
                                   tiny[:, cb + 2:cb + 3], tiny[:, cb + 3:cb + 4])
        rstd_t, nmu_t = tiny[:, cb + 4:cb + 5], tiny[:, cb + 5:cb + 6]
        ph.run("vector", "reduce_sum", mu_t, xtile, AX.X)
        v_sq = ph.run("scalar", "activation", sq_tile, xtile, AF.Square, inc=s_act)
        ph.wait("vector", s_act, v_sq)
        v_ss = ph.run("vector", "reduce_sum", var_t, sq_tile, AX.X, inc=s_dve)
        ph.wait("vector", s_dve, v_ss)
        ph.runs("vector", s_dve, "tensor_scalar", mu_t, mu_t, inv, None, ALU.mult)
        ph.runs("vector", s_dve, "tensor_tensor", m2_t, mu_t, mu_t, ALU.mult)
        ph.runs("vector", s_dve, "scalar_tensor_tensor", var_t, var_t, inv, m2_t,
                ALU.mult, ALU.subtract)
        v_var = ph.runs("vector", s_dve, "tensor_scalar", nmu_t, mu_t, -1.0, None,
                        ALU.mult)
        ph.wait("scalar", s_dve, v_var)
        v_sd = ph.run("scalar", "activation", sd_t, var_t, AF.Sqrt,
                      bias=eps_col[:], inc=s_act)
        ph.wait("vector", s_act, v_sd)
        v_r = ph.run("vector", "reciprocal", rstd_t, sd_t, inc=s_dve)
        ph.wait("vector", s_dve, v_r)
        return nmu_t, rstd_t

    if UNTIL >= 1:
        st = contextlib.ExitStack()
        xt = [sb(st, f"p0_x{i}", [128, D], F32) for i in range(8)]
        sqt = [sb(st, f"p0_sq{i}", [128, D], F32) for i in range(2)]
        lnt = [sb(st, f"p0_ln{i}", [128, D], BF16) for i in range(2)]
        mus = sb(st, "p0_mus", [128, 32], F32)
        ssqs = sb(st, "p0_ssqs", [128, 32], F32)
        muc = sb(st, "p0_muc", [128, 32], F32)
        varc = sb(st, "p0_varc", [128, 32], F32)
        rstdc = sb(st, "p0_rstdc", [128, 32], F32)
        nmbc = sb(st, "p0_nmbc", [128, 32], F32)
        ph = Phase(nc)
        phase_entry(ph)
        for e in ("scalar", "vector", "gpsimd", "tensor", "sync"):
            ph.wait(e, s_pool, init_done)
        norm_done = [0] * 8
        ssq_red = {}
        inv = 1.0 / D
        for grp in range(4):
            for k in range(8):
                i = grp * 8 + k
                ph.wait("sync", s_act, norm_done[k])
                v_x = ph.dma(dx[k], xt[k][:], xin[i * 128:(i + 1) * 128, :])
                ph.wait("vector", dx[k], v_x)
                ph.run("vector", "reduce_sum", mus[:, i:i + 1], xt[k][:], AX.X)
                ph.wait("scalar", dx[k], v_x)
                if ssq_red.get(i - 2) is not None:
                    ph.wait("scalar", s_dve, ssq_red[i - 2])
                v_sq = ph.run("scalar", "activation", sqt[k % 2][:], xt[k][:],
                              AF.Square, inc=s_act)
                ph.wait("vector", s_act, v_sq)
                ssq_red[i] = ph.run("vector", "reduce_sum",
                                    ssqs[:, i:i + 1], sqt[k % 2][:], AX.X,
                                    inc=s_dve)
            c0, c1 = grp * 8, grp * 8 + 8
            ph.wait("vector", s_dve, ssq_red[c1 - 1])
            ph.runs("vector", s_dve, "tensor_scalar", muc[:, c0:c1],
                    mus[:, c0:c1], inv, None, ALU.mult)
            ph.runs("vector", s_dve, "tensor_tensor", nmbc[:, c0:c1],
                    muc[:, c0:c1], muc[:, c0:c1], ALU.mult)
            v_var = ph.runs("vector", s_dve, "scalar_tensor_tensor",
                            varc[:, c0:c1], ssqs[:, c0:c1], inv,
                            nmbc[:, c0:c1], ALU.mult, ALU.subtract)
            ph.wait("scalar", s_dve, v_var)
            v_sd = ph.run("scalar", "activation", rstdc[:, c0:c1],
                          varc[:, c0:c1], AF.Sqrt, bias=eps_col[:], inc=s_act)
            ph.wait("vector", s_act, v_sd)
            ph.runs("vector", s_dve, "reciprocal", rstdc[:, c0:c1],
                    rstdc[:, c0:c1])
            v_nmb = ph.runs("vector", s_dve, "scalar_tensor_tensor",
                            nmbc[:, c0:c1], muc[:, c0:c1], -1.0,
                            rstdc[:, c0:c1], ALU.mult, ALU.mult)
            ph.wait("scalar", s_dve, v_nmb)
            for k in range(8):
                i = grp * 8 + k
                ph.wait("scalar", dx[k], dx[k].n)
                ph.wait("scalar", dst[k % 2], dst[k % 2].n)
                v_ln = ph.run("scalar", "activation", lnt[k % 2][:], xt[k][:],
                              AF.Identity, scale=rstdc[:, i:i + 1],
                              bias=nmbc[:, i:i + 1], inc=s_act)
                norm_done[k] = v_ln
                ph.wait("sync", s_act, v_ln)
                dstap = bass.AP(ln1s, i * 128 * 128,
                                [[128, 128], [S * 128, 16], [1, 128]])
                ph.dma(dst[k % 2], dstap, lnt[k % 2][:])
        ph.emit()
        st.close()

    if UNTIL >= 2:
        st_fm = contextlib.ExitStack()
        ln1_fm = [sb(st_fm, f"ln1fm{dc}", [128, S], BF16) for dc in range(16)]
        st = contextlib.ExitStack()
        ph = Phase(nc)
        phase_entry(ph)
        for dc in range(16):
            src = bass.AP(ln1s, dc * S * 128, [[128, S], [1, 128]])
            ph.dma(dfm[0], ln1_fm[dc][:], src, transpose=True)
        v_lnfm = dfm[0].n
        ph.wait("tensor", dfm[0], v_lnfm)
        ph.wait("gpsimd", dfm[0], v_lnfm)

        ps_busy = [0, 0, 0, 0]

        def mm_group(ph, ps_ap, slot, pieces):
            if slot is not None and ps_busy[slot]:
                ph.wait("tensor", s_act, ps_busy[slot])
            n = len(pieces)
            v = None
            for idx, (l, r) in enumerate(pieces):
                if idx == n - 1:
                    v = ph.run("tensor", "matmul", ps_ap, l, r,
                               start=(idx == 0), stop=True, inc=s_pe)
                else:
                    ph.run("tensor", "matmul", ps_ap, l, r,
                           start=(idx == 0), stop=False)
            return v

        wkf = [sb(st, f"p2_wkf{i}", [128, 2048], F32) for i in range(2)]
        wkb = [sb(st, f"p2_wkb{i}", [128, 2048], BF16) for i in range(2)]
        kst = [sb(st, f"p2_kst{i}", [128, S], BF16) for i in range(2)]
        p2sq = sb(st, "p2_sq", [128, S], BF16)
        wk_cast = [0, 0]
        wk_used = [0, 0]
        kst_stored = [0, 0]
        norm_red = 0
        sq_used = 0

        for idx in range(32):
            kf = idx % 16
            is_k = idx < 16
            colbase = (D + kf * 128) if is_k else (kf * 128)
            ntok = S if is_k else T
            scratch = Ks if is_k else Qs
            j = idx % 2
            ph.wait("sync", s_pool, wk_cast[j])
            src = bass.AP(wqkv, colbase, [[3 * D, 128], [128 * 3 * D, 16], [1, 128]])
            v_w = ph.dma(dw[j], wkf[j][:], src)
            ph.wait("gpsimd", dw[j], v_w)
            ph.wait("gpsimd", s_pe, wk_used[j])
            v_cast = ph.run("gpsimd", "tensor_copy", wkb[j][:], wkf[j][:],
                            inc=s_pool)
            wk_cast[j] = v_cast
            ph.wait("tensor", s_pool, v_cast)
            ph.wait("scalar", dkst[j], kst_stored[j])
            for t in range(ntok // 512):
                slot = t % 4
                pieces = [(wkb[j][:, dc * 128:(dc + 1) * 128],
                           ln1_fm[dc][:, t * 512:(t + 1) * 512]) for dc in range(16)]
                v_mm = mm_group(ph, ps_rot[slot], slot, pieces)
                ph.wait("scalar", s_pe, v_mm)
                v_ev = ph.run("scalar", "copy", kst[j][:, t * 512:(t + 1) * 512],
                              ps_rot[slot], inc=s_act)
                ps_busy[slot] = v_ev
            wk_used[j] = v_mm
            ph.wait("sync", s_act, v_ev)
            kst_stored[j] = ph.dma(dkst[j], scratch[kf * 128:(kf + 1) * 128, :ntok],
                                   kst[j][:, :ntok])
            ph.wait("scalar", s_act, v_ev)
            ph.wait("scalar", s_pe, sq_used)
            v_sq = ph.run("scalar", "activation", p2sq[:, :ntok],
                          kst[j][:, :ntok], AF.Square, inc=s_act)
            ph.wait("tensor", s_act, v_sq)
            row = kn_row if is_k else qn_row
            for r in range(ntok // 2048):
                ph.wait("tensor", s_dve, norm_red)
                for t in range(4):
                    sq_used = ph.run("tensor", "matmul",
                                     ps_a[0:1, t * 512:(t + 1) * 512],
                                     ones_col_bf[:],
                                     p2sq[:, r * 2048 + t * 512:
                                          r * 2048 + (t + 1) * 512],
                                     start=True, stop=True,
                                     inc=s_pe if t == 3 else None) or sq_used
                ph.wait("vector", s_pe, sq_used)
                ph.wait("vector", s_dve, s_dve.n)
                norm_red = ph.run("vector", "tensor_reduce", tiny[:1, 0:1],
                                  ps_a[0:1, 0:2048], AX.X, ALU.max, inc=s_dve)
                ph.wait("vector", s_dve, norm_red)
                if r == 0:
                    ph.runs("vector", s_dve, "tensor_copy",
                            row[0:1, kf:kf + 1], tiny[:1, 0:1])
                else:
                    ph.runs("vector", s_dve, "tensor_tensor",
                            row[0:1, kf:kf + 1], row[0:1, kf:kf + 1],
                            tiny[:1, 0:1], ALU.max)

        ph.emit()
        st.close()
        st = contextlib.ExitStack()
        ph = Phase(nc)
        phase_entry(ph)
        wvf = [sb(st, f"p2_wvf{i}", [128, 4096], F32) for i in range(2)]
        wvb = [sb(st, f"p2_wvb{i}", [128, 8192], BF16) for i in range(2)]
        vst = [sb(st, f"p2_vst{i}", [128, 512], BF16) for i in range(4)]
        vst_stored = [0, 0, 0, 0]
        wv_cast = [0, 0]
        wv_used = [0, 0]
        wvf_cast = [0, 0]
        for vt in range(4):
            wp = vt % 2
            for hv in range(2):
                hj = hv % 2
                lane = dvw if hj == 0 else dfm[1]
                ph.wait("sync", s_pool, wvf_cast[hj])
                src = bass.AP(wqkv, 2 * D + vt * 512 + hv * 8 * 3 * D * 128,
                              [[3 * D, 128], [128 * 3 * D, 8], [1, 512]])
                v_w = ph.dma(lane, wvf[hj][:], src)
                ph.wait("gpsimd", lane, v_w)
                if hv == 0:
                    ph.wait("gpsimd", s_pe, wv_used[wp])
                vcst = ph.run("gpsimd", "tensor_copy",
                              wvb[wp][:, hv * 4096:(hv + 1) * 4096],
                              wvf[hj][:], inc=s_pool)
                wvf_cast[hj] = vcst
            wv_cast[wp] = vcst
            ph.wait("tensor", s_pool, wv_cast[wp])
            for tokc in range(32):
                slot = tokc % 4
                pieces = [(ln1_fm[dc][:, tokc * 128:(tokc + 1) * 128],
                           wvb[wp][:, dc * 512:(dc + 1) * 512]) for dc in range(16)]
                v_mm = mm_group(ph, ps_rot[slot], slot, pieces)
                sj = tokc % 4
                ph.wait("scalar", dvst[sj], vst_stored[sj])
                ph.wait("scalar", s_pe, v_mm)
                v_ev = ph.run("scalar", "copy", vst[sj][:], ps_rot[slot], inc=s_act)
                ps_busy[slot] = v_ev
                ph.wait("sync", s_act, v_ev)
                vst_stored[sj] = ph.dma(
                    dvst[sj], Vs[tokc * 128:(tokc + 1) * 128, vt * 512:(vt + 1) * 512],
                    vst[sj][:])
            wv_used[wp] = v_mm
        ph.emit()
        st.close()
        st_fm.close()

    comb_st = contextlib.ExitStack()
    if UNTIL >= 3:
        combined = [sb(comb_st, f"comb{hc}", [128, T], BF16) for hc in range(16)]
        st = contextlib.ExitStack()
        kh = [sb(st, f"p3_kh{i}", [128, S], BF16) for i in range(2)]
        vh = [sb(st, f"p3_vh{i}", [128, S], BF16) for i in range(2)]
        qh = [sb(st, f"p3_qh{i}", [128, T], BF16) for i in range(2)]
        pbuf = sb(st, "p3_p", [128, 32 * 512], BF16)
        rb_sb = [sb(st, f"p3_rb{i}", [128, 512], F32) for i in range(2)]
        rinv = sb(st, "p3_rinv", [1, 512], F32)
        w1m = sb(st, "p3_w1m", [128, 2048], F32)
        w1ab = sb(st, "p3_w1ab", [128, 2048], BF16)
        ph = Phase(nc)
        phase_entry(ph)

        w1m_used = 0

        def w1_mean_piece(i):
            nonlocal w1m_used
            rc, cp = i // 4, i % 4
            ph.wait("sync", s_act, w1m_used)
            v_w = ph.dma(dwm, w1m[:],
                         w1t[rc * 128:(rc + 1) * 128, cp * 2048:(cp + 1) * 2048])
            ph.wait("scalar", dwm, v_w)
            ph.wait("scalar", s_act, w1m_used)
            ph.wait("scalar", s_dve, s_dve.n)
            va = ph.run("scalar", "activation", w1ab[:], w1m[:], AF.Abs,
                        accum_out=tiny[:, 6:7], inc=s_act)
            w1m_used = va
            ph.wait("vector", s_act, va)
            ph.runs("vector", s_dve, "tensor_tensor", w1acc[:], w1acc[:],
                    tiny[:, 6:7], ALU.add)

        def head_loads(h):
            hp = h % 2
            ph.dma(dkh[hp], kh[hp][:], Ks[h * 128:(h + 1) * 128, :])
            ph.dma(dkh[hp], vh[hp][:],
                   bass.AP(Vs, h * 128, [[D, 128], [D * 128, 32], [1, 128]]))
            return ph.dma(dkh[hp], qh[hp][:], Qs[h * 128:(h + 1) * 128, :])

        ph.runs("vector", s_dve, "tensor_tensor", qn_row[:], qn_row[:],
                kn_row[:], ALU.mult)
        v3 = ph.runs("vector", s_dve, "tensor_scalar", qn_row[:], qn_row[:],
                     1.0 / 128.0, None, ALU.mult)
        ph.wait("scalar", s_dve, v3)
        v4 = ph.run("scalar", "activation", qn_row[:], qn_row[:], AF.Sqrt,
                    inc=s_act)
        ph.wait("vector", s_act, v4)
        v5 = ph.run("vector", "tensor_scalar", qn_row[:], qn_row[:],
                    -1.0, None, ALU.mult, inc=s_dve)
        ph.wait("tensor", s_dve, v5)
        v6 = ph.run("tensor", "matmul", ps_a[:, 0:16], ones_row_f[:],
                    qn_row[:], start=True, stop=True, inc=s_pe)
        ph.wait("scalar", s_pe, v6)
        v_negc = ph.run("scalar", "copy", negc_cols[:], ps_a[:, 0:16],
                        inc=s_act)
        v_kvq_prev = head_loads(0)
        for h in range(NH):
            hp = h % 2
            kht, vht, qht = kh[hp], vh[hp], qh[hp]
            v_kvq = v_kvq_prev
            ph.wait("tensor", s_dve, s_dve.n)
            ph.wait("tensor", s_act, s_act.n)
            ph.wait("scalar", dkh[hp], v_kvq)
            if h + 1 < NH:
                ph.wait("sync", s_pe, s_pe.n)
                ph.wait("sync", s_dve, s_dve.n)
                ph.wait("sync", s_act, s_act.n)
                v_kvq_prev = head_loads(h + 1)

            ph.wait("tensor", dkh[hp], v_kvq)
            prev_tail = [0, 0]
            pend = None
            evac_pend = {}
            prev_recip = 0
            prev_lcopy = 0
            pend_lmm = 0
            for g in range(4):
                if pend is not None:
                    gg = pend
                    ph.wait("scalar", s_pe, pend_lmm)
                    v_lc = ph.run("scalar", "copy", rinv[:],
                                  ps_a[0:1, 512:1024], inc=s_act)
                    prev_lcopy = v_lc
                    ph.wait("tensor", s_act, v_lc)
                    ph.wait("tensor", s_dve, prev_recip)
                    v_bm = ph.run("tensor", "matmul", ps_a[:, 1536:2048],
                                  ones_row_f[:], rinv[:], start=True,
                                  stop=True, inc=s_pe)
                    ph.wait("vector", s_pe, v_bm)
                    if evac_pend.get(gg - 2) is not None:
                        ph.wait("vector", s_dve, evac_pend[gg - 2])
                    v_rc = ph.run("vector", "reciprocal", rb_sb[gg % 2][:],
                                  ps_a[:, 1536:2048], inc=s_dve)
                    prev_recip = v_rc
                    ph.wait("vector", s_dve, v_rc)
                    evac_pend[gg] = ph.run(
                        "vector", "tensor_tensor",
                        combined[h][:, gg * 512:(gg + 1) * 512],
                        ps_a[:, 1024:1536] if gg % 2 == 0 else ps_a[:, 0:512],
                        rb_sb[gg % 2][:], ALU.mult, inc=s_dve)
                exp_vals = []
                for pr in range(16):
                    plt = ps_lt[pr % 2]
                    if pr >= 2:
                        ph.wait("tensor", s_act, exp_vals[pr - 2])
                    else:
                        ph.wait("tensor", s_act, prev_tail[pr])
                    for sub in range(2):
                        kc = pr * 2 + sub
                        v_mm = ph.run("tensor", "matmul",
                                      plt[:, sub * 512:(sub + 1) * 512],
                                      kht[:, kc * 128:(kc + 1) * 128],
                                      qht[:, g * 512:(g + 1) * 512],
                                      start=True, stop=True,
                                      inc=s_pe if sub == 1 else None)
                    if pr == 2:
                        ph.wait("tensor", s_act, prev_lcopy)
                    if pr >= 2:
                        for sub in range(2):
                            kc = (pr - 2) * 2 + sub
                            ph.run("tensor", "matmul", ps_a[0:1, 512:1024],
                                   ones_col_bf[:],
                                   pbuf[:, kc * 512:(kc + 1) * 512],
                                   start=(kc == 0), stop=False)
                    ph.wait("scalar", s_pe, v_mm)
                    ph.wait("scalar", s_act, v_negc)
                    ev = ph.run("scalar", "activation",
                                pbuf[:, pr * 1024:(pr + 1) * 1024], plt[:],
                                AF.Exp, bias=negc_cols[:, h:h + 1],
                                scale=INV_SQRT_HD, inc=s_act)
                    exp_vals.append(ev)
                prev_tail = [exp_vals[14], exp_vals[15]]
                ph.wait("tensor", s_act, exp_vals[15])
                v_l = None
                for kc in range(28, 32):
                    v_l = ph.run("tensor", "matmul", ps_a[0:1, 512:1024],
                                 ones_col_bf[:],
                                 pbuf[:, kc * 512:(kc + 1) * 512],
                                 start=False, stop=(kc == 31),
                                 inc=s_pe if kc == 31 else None)
                pso = ps_a[:, 1024:1536] if g % 2 == 0 else ps_a[:, 0:512]
                if evac_pend.get(g - 2) is not None:
                    ph.wait("tensor", s_dve, evac_pend[g - 2])
                if g == 0 and h == 0:
                    ph.wait("tensor", s_act, v_negc)
                for kc in range(32):
                    v_pv = ph.run("tensor", "matmul", pso,
                                  vht[:, kc * 128:(kc + 1) * 128],
                                  pbuf[:, kc * 512:(kc + 1) * 512],
                                  start=(kc == 0), stop=(kc == 31),
                                  inc=s_pe if kc == 31 else None)
                pend = g
                pend_lmm = v_l
            gg = pend
            ph.wait("scalar", s_pe, pend_lmm)
            v_lc = ph.run("scalar", "copy", rinv[:], ps_a[0:1, 512:1024],
                          inc=s_act)
            ph.wait("tensor", s_act, v_lc)
            ph.wait("tensor", s_dve, prev_recip)
            v_bm = ph.run("tensor", "matmul", ps_a[:, 1536:2048],
                          ones_row_f[:], rinv[:], start=True, stop=True,
                          inc=s_pe)
            ph.wait("vector", s_pe, v_bm)
            if evac_pend.get(gg - 2) is not None:
                ph.wait("vector", s_dve, evac_pend[gg - 2])
            v_rc = ph.run("vector", "reciprocal", rb_sb[gg % 2][:],
                          ps_a[:, 1536:2048], inc=s_dve)
            ph.wait("vector", s_dve, v_rc)
            ph.run("vector", "tensor_tensor",
                   combined[h][:, gg * 512:(gg + 1) * 512],
                   ps_a[:, 1024:1536] if gg % 2 == 0 else ps_a[:, 0:512],
                   rb_sb[gg % 2][:], ALU.mult, inc=s_dve)
            w1_mean_piece(h * 4)
            for pp in range(1, 4):
                w1_mean_piece(h * 4 + pp)

        ph.wait("tensor", s_dve, s_dve.n)
        v = ph.run("tensor", "matmul", ps_a[0:1, 0:1], ones_col_f[:], w1acc[:],
                   start=True, stop=True, inc=s_pe)
        ph.wait("vector", s_pe, v)
        v = ph.runs("vector", s_dve, "tensor_scalar", tiny[:1, 7:8],
                    ps_a[0:1, 0:1], 1.0 / 16777216.0, EPS, ALU.mult, ALU.max)
        ph.wait("tensor", s_dve, v)
        v = ph.run("tensor", "matmul", ps_a[:, 0:1], ones_row_f[:],
                   tiny[:1, 7:8], start=True, stop=True, inc=s_pe)
        ph.wait("scalar", s_pe, v)
        v = ph.run("scalar", "copy", mw1_col[:], ps_a[:, 0:1], inc=s_act)
        ph.wait("vector", s_act, v)
        v = ph.run("vector", "reciprocal", sw1_col[:], mw1_col[:], inc=s_dve)
        ph.wait("vector", s_dve, v)
        ph.emit()
        st.close()

    if UNTIL >= 4:
        st = contextlib.ExitStack()
        wof = [sb(st, f"p4_wof{i}", [128, 4096], F32) for i in range(2)]
        wob = [sb(st, f"p4_wob{hc}", [128, 2048], BF16) for hc in range(16)]
        xtt = [sb(st, f"p4_x{i}", [128, D], F32) for i in range(2)]
        x2t = [sb(st, f"p4_x2{i}", [128, D], F32) for i in range(2)]
        ln2t = sb(st, "p4_ln2", [128, D], F32)
        sq2t = ln2t
        z1t = sb(st, "p4_z1", [128, D], BF16)
        ph = Phase(nc)
        phase_entry(ph)
        wof_used = [0, 0]
        for jp in range(8):
            wj = jp % 2
            ph.wait("sync", s_pool, wof_used[wj])
            src = bass.AP(wout, jp * 256, [[D, 128], [128 * D, 16], [1, 256]])
            v_w = ph.dma(dw[wj], wof[wj][:], src)
            ph.wait("gpsimd", dw[wj], v_w)
            for hc in range(16):
                inc = s_pool if hc == 15 else None
                v_c = ph.run("gpsimd", "tensor_copy",
                             wob[hc][:, jp * 256:(jp + 1) * 256],
                             wof[wj][:, hc * 256:(hc + 1) * 256], inc=inc)
            wof_used[wj] = v_c
        ph.wait("tensor", s_pool, s_pool.n)
        x2_stored = [0, 0]
        z1_stored = 0
        srow_all = tiny[:, 8:24]
        ssq_all = s1_cols
        mx_all = a2cols_tmp = None
        mx_all = sb(st, "p4_mx", [128, 16], F32)
        mn_all = sb(st, "p4_mn", [128, 16], F32)
        nmu_all = sb(st, "p4_nmu", [128, 16], F32)
        var_all = sb(st, "p4_var", [128, 16], F32)
        rstd_all = sb(st, "p4_rstd", [128, 16], F32)
        rs1_all = sb(st, "p4_rs1", [128, 16], F32)
        bias_all = sb(st, "p4_bias", [128, 16], F32)
        for tc in range(16):
            j = tc % 2
            ph.wait("sync", s_dve, s_dve.n)
            v_x = ph.dma(dx[j], xtt[j][:], xin[tc * 128:(tc + 1) * 128, :])
            for of_t in range(4):
                for hc in range(16):
                    inc = s_pe if hc == 15 else None
                    v_mm = ph.run("tensor", "matmul",
                                  ps_a[:, of_t * 512:(of_t + 1) * 512],
                                  combined[hc][:, tc * 128:(tc + 1) * 128],
                                  wob[hc][:, of_t * 512:(of_t + 1) * 512],
                                  start=(hc == 0), stop=(hc == 15), inc=inc)
            ph.wait("vector", dst[j], x2_stored[j])
            ph.wait("vector", s_pe, v_mm)
            ph.wait("vector", dx[j], v_x)
            v_x2 = ph.run("vector", "tensor_tensor", x2t[j][:], ps_a[:],
                          xtt[j][:], ALU.add, inc=s_dve)
            ph.wait("tensor", s_dve, v_x2)
            ph.wait("sync", s_dve, v_x2)
            x2_stored[j] = ph.dma(dst[j], x2s[tc * 128:(tc + 1) * 128, :],
                                  x2t[j][:])
            ph.run("vector", "reduce_sum", srow_all[:, tc:tc + 1], x2t[j][:],
                   AX.X)
            ph.wait("scalar", s_dve, s_dve.n)
            v_sq = ph.run("scalar", "activation", sq2t[:], x2t[j][:],
                          AF.Square, inc=s_act)
            ph.wait("vector", s_act, v_sq)
            ph.run("vector", "reduce_sum", ssq_all[:, tc:tc + 1], sq2t[:], AX.X)
            ph.run("vector", "tensor_reduce", mx_all[:, tc:tc + 1], x2t[j][:],
                   AX.X, ALU.max)
            v_red = ph.run("vector", "tensor_reduce", mn_all[:, tc:tc + 1],
                           x2t[j][:], AX.X, ALU.min, inc=s_dve)
        ph.wait("vector", s_dve, s_dve.n)
        inv = 1.0 / D
        ph.runs("vector", s_dve, "tensor_scalar", nmu_all[:], srow_all[:],
                -inv, None, ALU.mult)
        ph.runs("vector", s_dve, "tensor_tensor", var_all[:], nmu_all[:],
                nmu_all[:], ALU.mult)
        v = ph.runs("vector", s_dve, "scalar_tensor_tensor", var_all[:],
                    ssq_all[:], inv, var_all[:], ALU.mult, ALU.subtract)
        ph.wait("scalar", s_dve, v)
        v_sd = ph.run("scalar", "activation", var_all[:], var_all[:], AF.Sqrt,
                      bias=eps_col[:], inc=s_act)
        ph.wait("vector", s_act, v_sd)
        v_r = ph.run("vector", "reciprocal", rstd_all[:], var_all[:], inc=s_dve)
        ph.wait("vector", s_dve, v_r)
        ph.runs("vector", s_dve, "tensor_tensor", mx_all[:], mx_all[:],
                nmu_all[:], ALU.add)
        ph.runs("vector", s_dve, "scalar_tensor_tensor", mn_all[:], mn_all[:],
                -1.0, nmu_all[:], ALU.mult, ALU.subtract)
        ph.runs("vector", s_dve, "tensor_tensor", mx_all[:], mx_all[:],
                mn_all[:], ALU.max)
        ph.runs("vector", s_dve, "tensor_tensor", mx_all[:], mx_all[:],
                rstd_all[:], ALU.mult)
        ph.runs("vector", s_dve, "tensor_scalar", mx_all[:], mx_all[:],
                EPS, None, ALU.max)
        v_r = ph.run("vector", "reciprocal", rs1_all[:], mx_all[:], inc=s_dve)
        ph.wait("vector", s_dve, v_r)
        ph.runs("vector", s_dve, "tensor_scalar", s1_cols[:], rs1_all[:],
                127.0, None, ALU.mult)
        ph.runs("vector", s_dve, "tensor_scalar", al1_cols[:],
                mx_all[:], mw1_col[:], 1.0 / 127.0, ALU.mult, ALU.mult)
        ph.runs("vector", s_dve, "tensor_tensor", rs1_all[:], rstd_all[:],
                s1_cols[:], ALU.mult)
        v_b = ph.runs("vector", s_dve, "tensor_tensor", bias_all[:],
                      nmu_all[:], rs1_all[:], ALU.mult)
        ph.wait("sync", s_dve, v_b)
        ph.wait("sync", dst[0], dst[0].n)
        ph.wait("sync", dst[1], dst[1].n)
        for tc in range(16):
            j = tc % 2
            ph.wait("sync", s_dve, s_dve.n)
            v_x = ph.dma(dx[j], x2t[j][:], x2s[tc * 128:(tc + 1) * 128, :])
            ph.wait("vector", dx[j], v_x)
            ph.wait("vector", dz1, z1_stored)
            v_z = ph.custom(ROUND_AFF, out=z1t[:], in0=x2t[j][:],
                            s0=rs1_all[:, tc:tc + 1], s1=bias_all[:, tc:tc + 1],
                            imm2=MAGIC, inc=s_dve)
            ph.wait("sync", s_dve, v_z)
            dstap = bass.AP(z1s, tc * 128 * 128, [[128, 128], [T * 128, 16], [1, 128]])
            z1_stored = ph.dma(dz1, dstap, z1t[:])
        ph.emit()
        st.close()
    comb_st.close()

    if UNTIL >= 5:
        st = contextlib.ExitStack()
        z1_fm = [sb(st, f"z1fm{dc}", [128, T], BF16) for dc in range(16)]
        w1f = [sb(st, f"p6_w1f{i}", [128, 4096], F32) for i in range(2)]
        w1q = [sb(st, f"p6_w1q{i}", [128, 8192], BF16) for i in range(2)]
        gst = sb(st, "p6_gst", [128, 8192], F16)
        gred = sb(st, "p6_gred", [128, 16], F32)
        gtmp = [sb(st, f"p6_g{i}", [128, 512], F32) for i in range(2)]
        w2m = sb(st, "p6_w2m", [128, 2048], F32)
        w2ab = sb(st, "p6_w2ab", [128, 2048], BF16)
        ph = Phase(nc)
        phase_entry(ph)
        for dc in range(16):
            src = bass.AP(z1s, dc * T * 128, [[128, T], [1, 128]])
            ph.dma(dfm[0], z1_fm[dc][:], src, transpose=True)
        ph.wait("tensor", dfm[0], dfm[0].n)

        w2m_used = 0

        def w2_mean_piece(i):
            nonlocal w2m_used
            ph.wait("sync", s_act, w2m_used)
            v_w = ph.dma(dwm, w2m[:], w2t[i * 128:(i + 1) * 128, :])
            ph.wait("scalar", dwm, v_w)
            ph.wait("scalar", s_act, w2m_used)
            ph.wait("scalar", s_dve, s_dve.n)
            va = ph.run("scalar", "activation", w2ab[:], w2m[:], AF.Abs,
                        accum_out=tiny[:, 6:7], inc=s_act)
            w2m_used = va
            ph.wait("vector", s_act, va)
            ph.runs("vector", s_dve, "tensor_tensor", w2acc[:], w2acc[:],
                    tiny[:, 6:7], ALU.add)

        w1q_ready = [0, 0]
        w1q_used = [0, 0]
        gst_stored = 0
        gdone = {}
        gcopy = {}
        gi = 0

        w1f_free = [0, 0]

        def w1_load_half(hs2, half):
            src = bass.AP(w1t, hs2 * 512 + half * 256,
                          [[H, 128], [128 * H, 16], [1, 256]])
            ph.wait("sync", s_dve, w1f_free[half])
            return ph.dma(dw[half], w1f[half][:], src)

        def w1_quant_q(hs2, half, q, v_w):
            j2 = hs2 % 2
            ph.wait("vector", dw[half], v_w)
            ph.wait("vector", s_pe, w1q_used[j2])
            fsl = w1f[half][:, q * 2048:(q + 1) * 2048]
            v_rnd = ph.custom(ROUND_TS, out=fsl, in0=fsl,
                              s0=sw1_col[:], s1=MAGIC, inc=s_dve)
            ph.wait("vector", s_dve, v_rnd)
            w1q_dst = bass.AP(w1q[j2], half * 256 + q * 8 * 512,
                              [[8192, 128], [512, 8], [1, 256]])
            v_q = ph.run("vector", "tensor_scalar", w1q_dst, fsl,
                         -1.0, 1.0, ALU.max, ALU.min, inc=s_dve)
            if q == 1:
                w1f_free[half] = v_q
            w1q_ready[j2] = v_q
            return v_q

        vw_next = [w1_load_half(0, 0), w1_load_half(0, 1)]
        for _half in range(2):
            for _q in range(2):
                w1_quant_q(0, _half, _q, vw_next[_half])
        for hs in range(16):
            j = hs % 2
            ph.wait("tensor", s_dve, w1q_ready[j])
            ph.wait("gpsimd", dgst, gst_stored)
            redv = {}
            for tc in range(16):
                slot = gi % 4
                plt = ps_lt[slot // 2]
                ps_ap = plt[:, (slot % 2) * 512:(slot % 2 + 1) * 512]
                if gdone.get(gi - 4):
                    ph.wait("tensor", s_act, gdone[gi - 4])
                for dc in range(16):
                    inc = s_pe if dc == 15 else None
                    v_mm = ph.run("tensor", "matmul", ps_ap,
                                  z1_fm[dc][:, tc * 128:(tc + 1) * 128],
                                  w1q[j][:, dc * 512:(dc + 1) * 512],
                                  start=(dc == 0), stop=(dc == 15), inc=inc)
                gj = tc % 2
                ph.wait("scalar", s_pe, v_mm)
                if gcopy.get(gi - 2):
                    ph.wait("scalar", s_pool, gcopy[gi - 2])
                if redv.get(tc - 2):
                    ph.wait("scalar", s_dve, redv[tc - 2])
                v_g = ph.run("scalar", "activation", gtmp[gj][:], ps_ap,
                             AF.Gelu_apprx_tanh, scale=al1_cols[:, tc:tc + 1],
                             inc=s_act)
                gdone[gi] = v_g
                ph.wait("vector", s_act, v_g)
                v_red = ph.run("vector", "tensor_reduce", gred[:, tc:tc + 1],
                               gtmp[gj][:], AX.X, ALU.max, True, inc=s_dve)
                redv[tc] = v_red
                ph.wait("gpsimd", s_act, v_g)
                gcopy[gi] = ph.run("gpsimd", "tensor_copy",
                                   gst[:, tc * 512:(tc + 1) * 512], gtmp[gj][:],
                                   inc=s_pool)
                gi += 1
                if hs + 1 < 16:
                    if tc == 1:
                        vw_next[0] = w1_load_half(hs + 1, 0)
                    elif tc == 2:
                        vw_next[1] = w1_load_half(hs + 1, 1)
                    elif tc in (5, 7):
                        w1_quant_q(hs + 1, 0, 0 if tc == 5 else 1, vw_next[0])
                    elif tc in (9, 11):
                        w1_quant_q(hs + 1, 1, 0 if tc == 9 else 1, vw_next[1])
            w1q_used[j] = v_mm
            ph.wait("vector", s_dve, v_red)
            ph.runs("vector", s_dve, "tensor_tensor", a2_cols[:], a2_cols[:],
                    gred[:], ALU.max)
            ph.wait("sync", s_pool, gcopy[gi - 1])
            for hc4 in range(4):
                dstap = bass.AP(gs, (hs * 4 + hc4) * T * 128,
                                [[128, 128], [128 * 128, 16], [1, 128]])
                srcap = bass.AP(gst, hc4 * 128, [[8192, 128], [512, 16], [1, 128]])
                gst_stored = ph.dma(dgst, dstap, srcap)
            for pp in range(4):
                w2_mean_piece(hs * 4 + pp)

        ph.wait("tensor", s_dve, s_dve.n)
        ph.wait("tensor", s_act, s_act.n)
        v = ph.run("tensor", "matmul", ps_a[0:1, 0:1], ones_col_f[:], w2acc[:],
                   start=True, stop=True, inc=s_pe)
        ph.wait("vector", s_pe, v)
        v = ph.runs("vector", s_dve, "tensor_scalar", tiny[:1, 7:8],
                    ps_a[0:1, 0:1], 1.0 / 16777216.0, EPS, ALU.mult, ALU.max)
        ph.wait("tensor", s_dve, v)
        v = ph.run("tensor", "matmul", ps_a[:, 0:1], ones_row_f[:],
                   tiny[:1, 7:8], start=True, stop=True, inc=s_pe)
        ph.wait("scalar", s_pe, v)
        v = ph.run("scalar", "copy", mw2_col[:], ps_a[:, 0:1], inc=s_act)
        ph.wait("vector", s_act, v)
        v = ph.run("vector", "reciprocal", sw2_col[:], mw2_col[:], inc=s_dve)
        ph.wait("vector", s_dve, v)
        a2c = sb(st, "p6_a2c", [128, 16], F32)
        s2c = sb(st, "p6_s2c", [128, 16], F32)
        ph.runs("vector", s_dve, "tensor_scalar", a2c[:], a2_cols[:],
                EPS, None, ALU.max)
        v_r = ph.run("vector", "reciprocal", s2c[:], a2c[:], inc=s_dve)
        ph.wait("vector", s_dve, v_r)
        ph.runs("vector", s_dve, "tensor_scalar", s2c[:], s2c[:],
                127.0, None, ALU.mult)
        vv = ph.runs("vector", s_dve, "tensor_scalar", al2_cols[:], a2c[:],
                     mw2_col[:], 1.0 / 127.0, ALU.mult, ALU.mult)
        ph.wait("sync", s_dve, vv)
        for tc in range(16):
            ph.dma(ds2[tc % 2], s2s[0:1, tc * 128:(tc + 1) * 128],
                   s2c[:, tc:tc + 1])
        ph.emit()
        st.close()

    if UNTIL >= 6:
        st = contextlib.ExitStack()
        s2row = sb(st, "p7_s2row", [1, T], F32)
        s2b = sb(st, "p7_s2b", [128, T], F32)
        acc = [sb(st, f"p7_acc{i}", [128, D], F32) for i in range(8)]
        gfm = [sb(st, f"p7_gfm{i}", [128, 1024], F16) for i in range(2)]
        z2fm = [sb(st, f"p7_z2{i}", [128, 1024], BF16) for i in range(16)]
        w2f = [sb(st, f"p7_w2f{i}", [128, 2048], F32) for i in range(2)]
        w2q = [sb(st, f"p7_w2q{i}", [128, 2048], BF16) for i in range(16)]
        x2tt = [sb(st, "p8_x2", [128, D], F32)] * 2
        ot = x2tt
        ph = Phase(nc)
        phase_entry(ph)
        v_s2 = ph.dma(dwm, s2row[:], s2s[:])
        ph.wait("tensor", dwm, v_s2)
        for t in range(4):
            v = ph.run("tensor", "matmul", ps_a[:, t * 512:(t + 1) * 512],
                       ones_row_f[:], s2row[:, t * 512:(t + 1) * 512],
                       start=True, stop=True, inc=s_pe if t == 3 else None)
        ph.wait("scalar", s_pe, v)
        v_sb = ph.run("scalar", "copy", s2b[:], ps_a[:], inc=s_act)
        ph.wait("vector", s_act, v_sb)
        ph.wait("tensor", s_act, v_sb)

        stages = [(hf, hsl) for hf in range(2) for hsl in range(8)]
        set_ready = [0, 0]
        set_mm_done = [0, 0]
        gfm_used = [0, 0]
        w2f_used = [0, 0]
        pacc_lt = [ps_lt0[:, 0:512], ps_lt0[:, 512:1024],
                   ps_lt1[:, 0:512], ps_lt1[:, 512:1024]]

        def quant_chunk(stage_i, k8):
            hf, hsl = stages[stage_i]
            p = stage_i % 2
            hc = hsl * 8 + k8
            j = (stage_i * 8 + k8) % 2
            ph.wait("sync", s_dve, max(gfm_used[j], w2f_used[j]))
            ph.wait("sync", s_pe, set_mm_done[p])
            src = bass.AP(gs, hc * T * 128 + hf * 1024 * 128,
                          [[128, 1024], [1, 128]])
            v_g = ph.dma(dfm[j], gfm[j][:], src, transpose=True)
            v_w = ph.dma(dw[j], w2f[j][:], w2t[hc * 128:(hc + 1) * 128, :])
            ph.wait("vector", dfm[j], v_g)
            ph.wait("vector", s_pe, set_mm_done[p])
            tok = slice(hf * 1024, (hf + 1) * 1024)
            gfm_used[j] = ph.custom(ROUND_TT, out=z2fm[p * 8 + k8][:],
                                    in0=gfm[j][:], in1=s2b[:, tok], s0=MAGIC,
                                    inc=s_dve)
            ph.wait("vector", dw[j], v_w)
            w2f_used[j] = ph.custom(ROUND_CLIP, out=w2q[p * 8 + k8][:],
                                    in0=w2f[j][:], s0=sw2_col[:], s1=MAGIC,
                                    inc=s_dve)
            set_ready[p] = w2f_used[j]

        out_stored = [0, 0]
        x2_used = [0, 0]
        evac_vals = {}
        for k8 in range(8):
            quant_chunk(0, k8)
        for si, (hf, hsl) in enumerate(stages):
            p = si % 2
            ph.wait("tensor", s_dve, set_ready[p])
            for tc8 in range(8):
                par = tc8 % 2
                if evac_vals.get(tc8 - 2) is not None:
                    ph.wait("tensor", s_dve, evac_vals[tc8 - 2])
                for k8 in range(8):
                    for nt in range(4):
                        pac = ps_a[:, nt * 512:(nt + 1) * 512] if par == 0                             else pacc_lt[nt]
                        inc = s_pe if (k8 == 7 and nt == 3) else None
                        v_mm = ph.run(
                            "tensor", "matmul", pac,
                            z2fm[p * 8 + k8][:, tc8 * 128:(tc8 + 1) * 128],
                            w2q[p * 8 + k8][:, nt * 512:(nt + 1) * 512],
                            start=(k8 == 0), stop=(k8 == 7), inc=inc)
                ph.wait("vector", s_pe, v_mm)
                first = (hsl == 0)
                if par == 0:
                    if first:
                        v_e = ph.run("vector", "tensor_copy", acc[tc8][:],
                                     ps_a[:], inc=s_dve)
                    else:
                        v_e = ph.run("vector", "tensor_tensor", acc[tc8][:],
                                     acc[tc8][:], ps_a[:], ALU.add, inc=s_dve)
                else:
                    for hh in range(2):
                        m, a_sl = (ps_lt[hh][:],
                                   acc[tc8][:, hh * 1024:(hh + 1) * 1024])
                        if first:
                            v_e = ph.run("vector", "tensor_copy", a_sl, m,
                                         inc=s_dve)
                        else:
                            v_e = ph.run("vector", "tensor_tensor", a_sl,
                                         a_sl, m, ALU.add, inc=s_dve)
                evac_vals[tc8] = v_e
                if si + 1 < len(stages):
                    quant_chunk(si + 1, tc8)
            set_mm_done[p] = v_mm
            if hsl == 7:
                ph.wait("vector", s_dve, s_dve.n)
                for tc8 in range(8):
                    tc = hf * 8 + tc8
                    ph.wait("sync", s_dve, x2_used[0])
                    ph.wait("sync", dst[0], out_stored[0])
                    v_x2 = ph.dma(dx[0], x2tt[0][:],
                                  x2s[tc * 128:(tc + 1) * 128, :])
                    ph.wait("vector", dx[0], v_x2)
                    v_o = ph.run("vector", "scalar_tensor_tensor", ot[0][:],
                                 acc[tc8][:], al2_cols[:, tc:tc + 1],
                                 x2tt[0][:], ALU.mult, ALU.add, inc=s_dve)
                    x2_used[0] = v_o
                    ph.wait("sync", s_dve, v_o)
                    out_stored[0] = ph.dma(dst[0],
                                           out[tc * 128:(tc + 1) * 128, :],
                                           ot[0][:])
                evac_vals = {}
        for l in store_lanes:
            ph.wait("sync", l, l.n)
        ph.emit()
        st.close()

    mybir.codegen_inst_isa_subclasses(nc)
    octx.close()
    return nc


def kernel(L0_attn_x, L0_attn_w_qkv, L0_attn_w_out, L0_ffn_w1, L0_ffn_w2):
    x = np.ascontiguousarray(np.asarray(L0_attn_x, dtype=np.float32))
    wqkv = np.ascontiguousarray(np.asarray(L0_attn_w_qkv, dtype=np.float32))
    wout = np.ascontiguousarray(np.asarray(L0_attn_w_out, dtype=np.float32))
    w1t = np.ascontiguousarray(np.asarray(L0_ffn_w1, dtype=np.float32).T)
    w2t = np.ascontiguousarray(np.asarray(L0_ffn_w2, dtype=np.float32).T)

    nc = _build()
    in_maps = []
    for c in range(8):
        b, half = c // 2, c % 2
        own = x[b, half * T:(half + 1) * T]
        other = x[b, (1 - half) * T:(2 - half) * T]
        xc = np.ascontiguousarray(np.concatenate([own, other], axis=0))
        in_maps.append({"xin": xc, "wqkv": wqkv, "wout": wout,
                        "w1t": w1t, "w2t": w2t})
    res = run_bass_kernel_spmd(nc, in_maps, core_ids=list(range(8)))
    o = np.empty((4, 4096, 2048), dtype=np.float32)
    for c in range(8):
        b, half = c // 2, c % 2
        o[b, half * T:(half + 1) * T] = res.results[c]["out"]
    return o


# version 12
# speedup vs baseline: 1.5702x; 1.0655x over previous
import contextlib
import numpy as np
import ml_dtypes
import concourse.bass as bass
import concourse.mybir as mybir
from concourse.bass_utils import run_bass_kernel_spmd
from concourse import dve_ops as _dvo
from concourse.dve_op import DveOp
from concourse.dve_spec import Spec, Src0, Src1, C0, C1, C2, lower as _dve_lower

F32 = mybir.dt.float32
BF16 = mybir.dt.bfloat16
F16 = mybir.dt.float16
AF = mybir.ActivationFunctionType
ALU = mybir.AluOpType
AX = mybir.AxisListType

MAGIC = 12582912.0
INV_SQRT_HD = 0.08838834764831845
EPS = 1e-5

S = 4096
T = 2048
D = 2048
H = 8192
NH = 16


def _as_f(x, like):
    dt = np.float64 if np.asarray(like).dtype == np.float64 else np.float32
    return np.asarray(x, dtype=dt)


def _ref_round_ts(in0, in1, s0, s1, imm2):
    return np.round(_as_f(in0, in0) * _as_f(s0, in0))


def _ref_round_tt(in0, in1, s0, s1, imm2):
    return np.round(_as_f(in0, in0) * _as_f(in1, in0))


def _ref_round_clip(in0, in1, s0, s1, imm2):
    return np.minimum(np.maximum(np.round(_as_f(in0, in0) * _as_f(s0, in0)),
                                 -1.0), 1.0)


def _ref_round_affine(in0, in1, s0, s1, imm2):
    return np.round(_as_f(in0, in0) * _as_f(s0, in0) + _as_f(s1, in0))


def _register_round_ops():
    from concourse.dve_spec import Zero, One, maxx, minn
    ops = {}
    defs = (
        ("ROUND_MUL_TS_ANT", Spec(body=((Src0 * C0) + C1) - C1), _ref_round_ts),
        ("ROUND_MUL_TT_ANT", Spec(body=((Src0 * Src1) + C0) - C0), _ref_round_tt),
        ("ROUND_CLIP_TS_ANT",
         Spec(body=minn(maxx(((Src0 * C0) + C1) - C1, Zero - One), One)),
         _ref_round_clip),
        ("ROUND_AFFINE_ANT",
         Spec(body=(((Src0 * C0) + C1) + C2) - C2),
         _ref_round_affine),
    )
    replaceable = ["DVEHASH32L_DRAIN", "DVEHASH32L_BODY", "DVEHASH32L_SEED"]
    for name, spec, ref in defs:
        existing = next((o for o in _dvo.OPS if o.name == name), None)
        if existing is not None:
            ops[name] = existing
            continue
        compiled = _dve_lower(spec, ver="v3", perf_en=False, name=name)
        op = DveOp(name, spec, subdim=False,
                   uops_sha={"v3": compiled.sha("v3")}, reference=ref)
        if len(_dvo.OPS) < 31:
            _dvo.OPS.append(op)
            row = len(_dvo.OPS)
        else:
            victim = next(n for n in replaceable
                          if any(o.name == n for o in _dvo.OPS))
            idx = next(i for i, o in enumerate(_dvo.OPS) if o.name == victim)
            replaceable.remove(victim)
            _dvo.OPS[idx] = op
            row = idx + 1
            del _dvo._SUB_OPCODE_FOR_NAME[victim]
            _dvo.CUSTOM_DVE_SPECS.pop(victim, None)
            _dvo.CUSTOM_DVE_REFS.pop(victim, None)
        _dvo._SUB_OPCODE_FOR_NAME[name] = row
        _dvo.CUSTOM_DVE_SPECS[name] = spec
        _dvo.CUSTOM_DVE_REFS[name] = ref
        ops[name] = op
    return (ops["ROUND_MUL_TS_ANT"], ops["ROUND_MUL_TT_ANT"],
            ops["ROUND_CLIP_TS_ANT"], ops["ROUND_AFFINE_ANT"])


ROUND_TS, ROUND_TT, ROUND_CLIP, ROUND_AFF = _register_round_ops()

ENGINES = ("sync", "tensor", "scalar", "vector", "gpsimd")
UNTIL = 99
DEBUG_SCRATCH = False


class Sem:
    def __init__(self, h):
        self.h = h
        self.n = 0


class Phase:
    def __init__(self, nc):
        self.nc = nc
        self.q = {e: [] for e in ENGINES}

    def on(self, eng, fn):
        self.q[eng].append(fn)

    def wait(self, eng, sem, val):
        if val and val > 0:
            self.on(eng, lambda e, _s=sem.h, _v=val: e.wait_ge(_s, _v))

    def run(self, eng, method, *args, inc=None, **kw):
        v = None
        if inc is not None:
            inc.n += 1
            v = inc.n
            def _f(e, _m=method, _a=args, _k=kw, _s=inc.h):
                getattr(e, _m)(*_a, **_k).then_inc(_s, 1)
        else:
            def _f(e, _m=method, _a=args, _k=kw):
                getattr(e, _m)(*_a, **_k)
        self.on(eng, _f)
        return v

    def runs(self, eng, sem, method, *args, **kw):
        v = self.run(eng, method, *args, inc=sem, **kw)
        self.wait(eng, sem, v)
        return v

    def custom(self, op, inc=None, **kw):
        v = None
        if inc is not None:
            inc.n += 1
            v = inc.n
            def _f(e, _o=op, _k=kw, _s=inc.h):
                e._custom_dve(_o, **_k).then_inc(_s, 1)
        else:
            def _f(e, _o=op, _k=kw):
                e._custom_dve(_o, **_k)
        self.on("vector", _f)
        return v

    def dma(self, sem, dst, src, transpose=False, eng="sync"):
        sem.n += 16
        v = sem.n
        if transpose:
            self.on(eng, lambda e, _d=dst, _s=src, _h=sem.h:
                    e.dma_start(_d, _s, transpose=True).then_inc(_h, 16))
        else:
            self.on(eng, lambda e, _d=dst, _s=src, _h=sem.h:
                    e.dma_start(_d, _s).then_inc(_h, 16))
        return v

    def emit(self):
        with self.nc.Block() as block:
            @block.sync
            def _(e):
                for fn in self.q["sync"]:
                    fn(e)

            @block.tensor
            def _(e):
                for fn in self.q["tensor"]:
                    fn(e)

            @block.scalar
            def _(e):
                for fn in self.q["scalar"]:
                    fn(e)

            @block.vector
            def _(e):
                for fn in self.q["vector"]:
                    fn(e)

            @block.gpsimd
            def _(e):
                for fn in self.q["gpsimd"]:
                    fn(e)


def _build():
    nc = bass.Bass("TRN2", target_bir_lowering=False)

    xin = nc.dram_tensor("xin", [S, D], F32, kind="ExternalInput")
    wqkv = nc.dram_tensor("wqkv", [D, 3 * D], F32, kind="ExternalInput")
    wout = nc.dram_tensor("wout", [D, D], F32, kind="ExternalInput")
    w1t = nc.dram_tensor("w1t", [D, H], F32, kind="ExternalInput")
    w2t = nc.dram_tensor("w2t", [H, D], F32, kind="ExternalInput")
    out = nc.dram_tensor("out", [T, D], F32, kind="ExternalOutput")

    sk = "ExternalOutput" if DEBUG_SCRATCH else "Internal"
    ln1s = nc.dram_tensor("ln1s", [16, S, 128], BF16, kind=sk)
    Ks = nc.dram_tensor("Ks", [D, S], BF16, kind=sk)
    Vs = nc.dram_tensor("Vs", [S, D], BF16, kind=sk)
    Qs = nc.dram_tensor("Qs", [D, T], BF16, kind=sk)
    x2s = nc.dram_tensor("x2s", [T, D], F32, kind=sk)
    z1s = nc.dram_tensor("z1s", [16, T, 128], BF16, kind=sk)
    gs = nc.dram_tensor("gs", [64, T, 128], F16, kind=sk)
    s2s = nc.dram_tensor("s2s", [1, T], F32, kind=sk)

    octx = contextlib.ExitStack()

    def mksem(name):
        return Sem(octx.enter_context(nc.semaphore(name)))

    dx = [mksem(f"dx{i}") for i in range(8)]
    dst = [mksem("dst0"), mksem("dst1")]
    dw = [mksem("dw0"), mksem("dw1")]
    dkst = [mksem("dkst0"), mksem("dkst1")]
    dvst = [mksem(f"dvst{i}") for i in range(4)]
    dvw = mksem("dvw")
    dfm = [mksem("dfm0"), mksem("dfm1")]
    dkh = [mksem("dkh0"), mksem("dkh1")]
    dwm = [mksem("dwm0"), mksem("dwm1")]
    dgst = [mksem("dgst0"), mksem("dgst1")]
    dz1 = mksem("dz1")
    ds2 = [mksem("ds2a"), mksem("ds2b")]
    store_lanes = dst + dkst + dvst + dgst + [dz1] + ds2
    all_lanes = dx + dst + dw + dkst + dvst + [dvw] + dfm + dkh + dwm + dgst + [dz1] + ds2
    s_pe = Sem(octx.enter_context(nc.semaphore("s_pe")))
    s_act = Sem(octx.enter_context(nc.semaphore("s_act")))
    s_dve = Sem(octx.enter_context(nc.semaphore("s_dve")))
    s_pool = Sem(octx.enter_context(nc.semaphore("s_pool")))

    def sb(st, name, shape, dtype):
        return st.enter_context(nc.sbuf_tensor(name, shape, dtype))

    def phase_entry(ph):
        for eng in ENGINES:
            for l in all_lanes:
                ph.wait(eng, l, l.n)
            for sem in (s_pe, s_act, s_dve, s_pool):
                ph.wait(eng, sem, sem.n)

    ones_col_bf = sb(octx, "ones_col_bf", [128, 1], BF16)
    ones_col_f = sb(octx, "ones_col_f", [128, 1], F32)
    ones_row_f = sb(octx, "ones_row_f", [1, 128], F32)
    eps_col = sb(octx, "eps_col", [128, 1], F32)
    mw1_col = sb(octx, "mw1_col", [128, 1], F32)
    sw1_col = sb(octx, "sw1_col", [128, 1], F32)
    mw2_col = sb(octx, "mw2_col", [128, 1], F32)
    sw2_col = sb(octx, "sw2_col", [128, 1], F32)
    s1_cols = sb(octx, "s1_cols", [128, 16], F32)
    al1_cols = sb(octx, "al1_cols", [128, 16], F32)
    a2_cols = sb(octx, "a2_cols", [128, 16], F32)
    al2_cols = sb(octx, "al2_cols", [128, 16], F32)
    qn_row = sb(octx, "qn_row", [1, 16], F32)
    kn_row = sb(octx, "kn_row", [1, 16], F32)
    negc_cols = sb(octx, "negc_cols", [128, 16], F32)
    w1acc = sb(octx, "w1acc", [128, 1], F32)
    w2acc = sb(octx, "w2acc", [128, 1], F32)
    tiny = sb(octx, "tiny", [128, 32], F32)

    ps_a = octx.enter_context(nc.psum_tensor("ps_a", [128, 2048], F32))
    ps_lt0 = octx.enter_context(nc.psum_tensor("ps_lt0", [128, 1024], F32))
    ps_lt1 = octx.enter_context(nc.psum_tensor("ps_lt1", [128, 1024], F32))
    ps_lt = [ps_lt0, ps_lt1]
    ps_rot = [ps_lt0[:, 0:512], ps_lt0[:, 512:1024],
              ps_lt1[:, 0:512], ps_lt1[:, 512:1024]]

    ph = Phase(nc)
    for tile, val in ((ones_col_bf, 1.0), (ones_col_f, 1.0), (ones_row_f, 1.0),
                      (eps_col, EPS), (w1acc, 0.0), (w2acc, 0.0), (a2_cols, 0.0)):
        ph.run("gpsimd", "memset", tile[:], val, inc=s_pool)
    init_done = s_pool.n
    ph.emit()

    def ln_stats(ph, xtile, ncols, sq_tile, cb=0):
        inv = 1.0 / ncols
        mu_t, m2_t, var_t, sd_t = (tiny[:, cb:cb + 1], tiny[:, cb + 1:cb + 2],
                                   tiny[:, cb + 2:cb + 3], tiny[:, cb + 3:cb + 4])
        rstd_t, nmu_t = tiny[:, cb + 4:cb + 5], tiny[:, cb + 5:cb + 6]
        ph.run("vector", "reduce_sum", mu_t, xtile, AX.X)
        v_sq = ph.run("scalar", "activation", sq_tile, xtile, AF.Square, inc=s_act)
        ph.wait("vector", s_act, v_sq)
        v_ss = ph.run("vector", "reduce_sum", var_t, sq_tile, AX.X, inc=s_dve)
        ph.wait("vector", s_dve, v_ss)
        ph.runs("vector", s_dve, "tensor_scalar", mu_t, mu_t, inv, None, ALU.mult)
        ph.runs("vector", s_dve, "tensor_tensor", m2_t, mu_t, mu_t, ALU.mult)
        ph.runs("vector", s_dve, "scalar_tensor_tensor", var_t, var_t, inv, m2_t,
                ALU.mult, ALU.subtract)
        v_var = ph.runs("vector", s_dve, "tensor_scalar", nmu_t, mu_t, -1.0, None,
                        ALU.mult)
        ph.wait("scalar", s_dve, v_var)
        v_sd = ph.run("scalar", "activation", sd_t, var_t, AF.Sqrt,
                      bias=eps_col[:], inc=s_act)
        ph.wait("vector", s_act, v_sd)
        v_r = ph.run("vector", "reciprocal", rstd_t, sd_t, inc=s_dve)
        ph.wait("vector", s_dve, v_r)
        return nmu_t, rstd_t

    if UNTIL >= 1:
        st = contextlib.ExitStack()
        xt = [sb(st, f"p0_x{i}", [128, D], F32) for i in range(8)]
        sqt = [sb(st, f"p0_sq{i}", [128, D], F32) for i in range(2)]
        lnt = [sb(st, f"p0_ln{i}", [128, D], BF16) for i in range(2)]
        mus = sb(st, "p0_mus", [128, 32], F32)
        ssqs = sb(st, "p0_ssqs", [128, 32], F32)
        muc = sb(st, "p0_muc", [128, 32], F32)
        varc = sb(st, "p0_varc", [128, 32], F32)
        rstdc = sb(st, "p0_rstdc", [128, 32], F32)
        nmbc = sb(st, "p0_nmbc", [128, 32], F32)
        ph = Phase(nc)
        phase_entry(ph)
        for e in ("scalar", "vector", "gpsimd", "tensor", "sync"):
            ph.wait(e, s_pool, init_done)
        norm_done = [0] * 8
        ssq_red = {}
        inv = 1.0 / D
        for grp in range(4):
            for k in range(8):
                i = grp * 8 + k
                ph.wait("sync", s_act, norm_done[k])
                v_x = ph.dma(dx[k], xt[k][:], xin[i * 128:(i + 1) * 128, :])
                ph.wait("vector", dx[k], v_x)
                ph.run("vector", "reduce_sum", mus[:, i:i + 1], xt[k][:], AX.X)
                ph.wait("scalar", dx[k], v_x)
                if ssq_red.get(i - 2) is not None:
                    ph.wait("scalar", s_dve, ssq_red[i - 2])
                v_sq = ph.run("scalar", "activation", sqt[k % 2][:], xt[k][:],
                              AF.Square, inc=s_act)
                ph.wait("vector", s_act, v_sq)
                ssq_red[i] = ph.run("vector", "reduce_sum",
                                    ssqs[:, i:i + 1], sqt[k % 2][:], AX.X,
                                    inc=s_dve)
            c0, c1 = grp * 8, grp * 8 + 8
            ph.wait("vector", s_dve, ssq_red[c1 - 1])
            ph.runs("vector", s_dve, "tensor_scalar", muc[:, c0:c1],
                    mus[:, c0:c1], inv, None, ALU.mult)
            ph.runs("vector", s_dve, "tensor_tensor", nmbc[:, c0:c1],
                    muc[:, c0:c1], muc[:, c0:c1], ALU.mult)
            v_var = ph.runs("vector", s_dve, "scalar_tensor_tensor",
                            varc[:, c0:c1], ssqs[:, c0:c1], inv,
                            nmbc[:, c0:c1], ALU.mult, ALU.subtract)
            ph.wait("scalar", s_dve, v_var)
            v_sd = ph.run("scalar", "activation", rstdc[:, c0:c1],
                          varc[:, c0:c1], AF.Sqrt, bias=eps_col[:], inc=s_act)
            ph.wait("vector", s_act, v_sd)
            ph.runs("vector", s_dve, "reciprocal", rstdc[:, c0:c1],
                    rstdc[:, c0:c1])
            v_nmb = ph.runs("vector", s_dve, "scalar_tensor_tensor",
                            nmbc[:, c0:c1], muc[:, c0:c1], -1.0,
                            rstdc[:, c0:c1], ALU.mult, ALU.mult)
            ph.wait("scalar", s_dve, v_nmb)
            for k in range(8):
                i = grp * 8 + k
                ph.wait("scalar", dx[k], dx[k].n)
                ph.wait("scalar", dst[k % 2], dst[k % 2].n)
                v_ln = ph.run("scalar", "activation", lnt[k % 2][:], xt[k][:],
                              AF.Identity, scale=rstdc[:, i:i + 1],
                              bias=nmbc[:, i:i + 1], inc=s_act)
                norm_done[k] = v_ln
                ph.wait("sync", s_act, v_ln)
                dstap = bass.AP(ln1s, i * 128 * 128,
                                [[128, 128], [S * 128, 16], [1, 128]])
                ph.dma(dst[k % 2], dstap, lnt[k % 2][:])
        ph.emit()
        st.close()

    if UNTIL >= 2:
        st_fm = contextlib.ExitStack()
        ln1_fm = [sb(st_fm, f"ln1fm{dc}", [128, S], BF16) for dc in range(16)]
        st = contextlib.ExitStack()
        ph = Phase(nc)
        phase_entry(ph)
        for dc in range(16):
            src = bass.AP(ln1s, dc * S * 128, [[128, S], [1, 128]])
            ph.dma(dfm[0], ln1_fm[dc][:], src, transpose=True)
        v_lnfm = dfm[0].n
        ph.wait("tensor", dfm[0], v_lnfm)
        ph.wait("gpsimd", dfm[0], v_lnfm)

        ps_busy = [0, 0, 0, 0]

        def mm_group(ph, ps_ap, slot, pieces):
            if slot is not None and ps_busy[slot]:
                ph.wait("tensor", s_act, ps_busy[slot])
            n = len(pieces)
            v = None
            for idx, (l, r) in enumerate(pieces):
                if idx == n - 1:
                    v = ph.run("tensor", "matmul", ps_ap, l, r,
                               start=(idx == 0), stop=True, inc=s_pe)
                else:
                    ph.run("tensor", "matmul", ps_ap, l, r,
                           start=(idx == 0), stop=False)
            return v

        wkf = [sb(st, f"p2_wkf{i}", [128, 2048], F32) for i in range(2)]
        wkb = [sb(st, f"p2_wkb{i}", [128, 2048], BF16) for i in range(2)]
        kst = [sb(st, f"p2_kst{i}", [128, S], BF16) for i in range(2)]
        p2sq = sb(st, "p2_sq", [128, S], BF16)
        wk_cast = [0, 0]
        wk_used = [0, 0]
        kst_stored = [0, 0]
        norm_red = 0
        sq_used = 0

        def wk_load(idx2):
            kf2 = idx2 % 16
            colbase2 = (D + kf2 * 128) if idx2 < 16 else (kf2 * 128)
            j2 = idx2 % 2
            ph.wait("sync", s_pool, wk_cast[j2])
            src = bass.AP(wqkv, colbase2,
                          [[3 * D, 128], [128 * 3 * D, 16], [1, 128]])
            v_w = ph.dma(dw[j2], wkf[j2][:], src)
            ph.wait("gpsimd", dw[j2], v_w)
            ph.wait("gpsimd", s_pe, wk_used[j2])
            wk_cast[j2] = ph.run("gpsimd", "tensor_copy", wkb[j2][:],
                                 wkf[j2][:], inc=s_pool)

        wk_load(0)
        for idx in range(32):
            kf = idx % 16
            is_k = idx < 16
            ntok = S if is_k else T
            scratch = Ks if is_k else Qs
            j = idx % 2
            if idx + 1 < 32:
                wk_load(idx + 1)
            ph.wait("tensor", s_pool, wk_cast[j])
            ph.wait("scalar", dkst[j], kst_stored[j])
            for t in range(ntok // 512):
                slot = t % 4
                pieces = [(wkb[j][:, dc * 128:(dc + 1) * 128],
                           ln1_fm[dc][:, t * 512:(t + 1) * 512]) for dc in range(16)]
                v_mm = mm_group(ph, ps_rot[slot], slot, pieces)
                ph.wait("scalar", s_pe, v_mm)
                v_ev = ph.run("scalar", "copy", kst[j][:, t * 512:(t + 1) * 512],
                              ps_rot[slot], inc=s_act)
                ps_busy[slot] = v_ev
            wk_used[j] = v_mm
            ph.wait("sync", s_act, v_ev)
            kst_stored[j] = ph.dma(dkst[j], scratch[kf * 128:(kf + 1) * 128, :ntok],
                                   kst[j][:, :ntok])
            ph.wait("scalar", s_act, v_ev)
            ph.wait("scalar", s_pe, sq_used)
            v_sq = ph.run("scalar", "activation", p2sq[:, :ntok],
                          kst[j][:, :ntok], AF.Square, inc=s_act)
            ph.wait("tensor", s_act, v_sq)
            row = kn_row if is_k else qn_row
            for r in range(ntok // 2048):
                ph.wait("tensor", s_dve, norm_red)
                for t in range(4):
                    sq_used = ph.run("tensor", "matmul",
                                     ps_a[0:1, t * 512:(t + 1) * 512],
                                     ones_col_bf[:],
                                     p2sq[:, r * 2048 + t * 512:
                                          r * 2048 + (t + 1) * 512],
                                     start=True, stop=True,
                                     inc=s_pe if t == 3 else None) or sq_used
                ph.wait("vector", s_pe, sq_used)
                ph.wait("vector", s_dve, s_dve.n)
                norm_red = ph.run("vector", "tensor_reduce", tiny[:1, 0:1],
                                  ps_a[0:1, 0:2048], AX.X, ALU.max, inc=s_dve)
                ph.wait("vector", s_dve, norm_red)
                if r == 0:
                    ph.runs("vector", s_dve, "tensor_copy",
                            row[0:1, kf:kf + 1], tiny[:1, 0:1])
                else:
                    ph.runs("vector", s_dve, "tensor_tensor",
                            row[0:1, kf:kf + 1], row[0:1, kf:kf + 1],
                            tiny[:1, 0:1], ALU.max)

        ph.emit()
        st.close()
        st = contextlib.ExitStack()
        ph = Phase(nc)
        phase_entry(ph)
        wvf = [sb(st, f"p2_wvf{i}", [128, 4096], F32) for i in range(2)]
        wvb = [sb(st, f"p2_wvb{i}", [128, 8192], BF16) for i in range(2)]
        vst = [sb(st, f"p2_vst{i}", [128, 512], BF16) for i in range(4)]
        vst_stored = [0, 0, 0, 0]
        wv_cast = [0, 0]
        wv_used = [0, 0]
        wvf_cast = [0, 0]
        for vt in range(4):
            wp = vt % 2
            for hv in range(2):
                hj = hv % 2
                lane = dvw if hj == 0 else dfm[1]
                ph.wait("sync", s_pool, wvf_cast[hj])
                src = bass.AP(wqkv, 2 * D + vt * 512 + hv * 8 * 3 * D * 128,
                              [[3 * D, 128], [128 * 3 * D, 8], [1, 512]])
                v_w = ph.dma(lane, wvf[hj][:], src)
                ph.wait("gpsimd", lane, v_w)
                if hv == 0:
                    ph.wait("gpsimd", s_pe, wv_used[wp])
                vcst = ph.run("gpsimd", "tensor_copy",
                              wvb[wp][:, hv * 4096:(hv + 1) * 4096],
                              wvf[hj][:], inc=s_pool)
                wvf_cast[hj] = vcst
            wv_cast[wp] = vcst
            ph.wait("tensor", s_pool, wv_cast[wp])
            for tokc in range(32):
                slot = tokc % 4
                pieces = [(ln1_fm[dc][:, tokc * 128:(tokc + 1) * 128],
                           wvb[wp][:, dc * 512:(dc + 1) * 512]) for dc in range(16)]
                v_mm = mm_group(ph, ps_rot[slot], slot, pieces)
                sj = tokc % 4
                ph.wait("scalar", dvst[sj], vst_stored[sj])
                ph.wait("scalar", s_pe, v_mm)
                v_ev = ph.run("scalar", "copy", vst[sj][:], ps_rot[slot], inc=s_act)
                ps_busy[slot] = v_ev
                ph.wait("sync", s_act, v_ev)
                vst_stored[sj] = ph.dma(
                    dvst[sj], Vs[tokc * 128:(tokc + 1) * 128, vt * 512:(vt + 1) * 512],
                    vst[sj][:])
            wv_used[wp] = v_mm
        ph.emit()
        st.close()
        st_fm.close()

    comb_st = contextlib.ExitStack()
    if UNTIL >= 3:
        combined = [sb(comb_st, f"comb{hc}", [128, T], BF16) for hc in range(16)]
        st = contextlib.ExitStack()
        kh = [sb(st, f"p3_kh{i}", [128, S], BF16) for i in range(2)]
        vh = [sb(st, f"p3_vh{i}", [128, S], BF16) for i in range(2)]
        qh = [sb(st, f"p3_qh{i}", [128, T], BF16) for i in range(2)]
        pbuf = sb(st, "p3_p", [128, 32 * 512], BF16)
        rb_sb = [sb(st, f"p3_rb{i}", [128, 512], F32) for i in range(2)]
        rinv = sb(st, "p3_rinv", [1, 512], F32)
        w1m = [sb(st, f"p3_w1m{i}", [128, 2048], F32) for i in range(2)]
        ph = Phase(nc)
        phase_entry(ph)

        w1m_red = [0, 0]
        w1m_loads = {}
        w1acc_add = {}

        def w1_mean_load(i):
            if i >= 64:
                return
            rc, cp = i // 4, i % 4
            jm = i % 2
            ph.wait("sync", s_dve, w1m_red[jm])
            w1m_loads[i] = ph.dma(
                dwm[jm], w1m[jm][:],
                w1t[rc * 128:(rc + 1) * 128, cp * 2048:(cp + 1) * 2048])

        def w1_mean_piece(i):
            jm = i % 2
            col = tiny[:, 16 + (i % 8):17 + (i % 8)]
            ph.wait("vector", dwm[jm], w1m_loads[i])
            if w1acc_add.get(i - 8):
                ph.wait("vector", s_pool, w1acc_add[i - 8])
            v_red = ph.run("vector", "tensor_reduce", col, w1m[jm][:],
                           AX.X, ALU.add, True, inc=s_dve)
            w1m_red[jm] = v_red
            w1_mean_load(i + 2)
            ph.wait("gpsimd", s_dve, v_red)
            w1acc_add[i] = ph.runs("gpsimd", s_pool, "tensor_tensor",
                                   w1acc[:], w1acc[:], col, ALU.add)

        def head_loads(h):
            hp = h % 2
            ph.dma(dkh[hp], kh[hp][:], Ks[h * 128:(h + 1) * 128, :])
            ph.dma(dkh[hp], vh[hp][:],
                   bass.AP(Vs, h * 128, [[D, 128], [D * 128, 32], [1, 128]]))
            return ph.dma(dkh[hp], qh[hp][:], Qs[h * 128:(h + 1) * 128, :])

        ph.runs("vector", s_dve, "tensor_tensor", qn_row[:], qn_row[:],
                kn_row[:], ALU.mult)
        v3 = ph.runs("vector", s_dve, "tensor_scalar", qn_row[:], qn_row[:],
                     1.0 / 128.0, None, ALU.mult)
        ph.wait("scalar", s_dve, v3)
        v4 = ph.run("scalar", "activation", qn_row[:], qn_row[:], AF.Sqrt,
                    inc=s_act)
        ph.wait("vector", s_act, v4)
        v5 = ph.run("vector", "tensor_scalar", qn_row[:], qn_row[:],
                    -1.0, None, ALU.mult, inc=s_dve)
        ph.wait("tensor", s_dve, v5)
        v6 = ph.run("tensor", "matmul", ps_a[:, 0:16], ones_row_f[:],
                    qn_row[:], start=True, stop=True, inc=s_pe)
        ph.wait("scalar", s_pe, v6)
        v_negc = ph.run("scalar", "copy", negc_cols[:], ps_a[:, 0:16],
                        inc=s_act)
        v_kvq_prev = head_loads(0)
        w1_mean_load(0)
        w1_mean_load(1)
        for h in range(NH):
            hp = h % 2
            kht, vht, qht = kh[hp], vh[hp], qh[hp]
            v_kvq = v_kvq_prev
            ph.wait("tensor", s_dve, s_dve.n)
            ph.wait("tensor", s_act, s_act.n)
            ph.wait("scalar", dkh[hp], v_kvq)
            if h + 1 < NH:
                ph.wait("sync", s_pe, s_pe.n)
                ph.wait("sync", s_dve, s_dve.n)
                ph.wait("sync", s_act, s_act.n)
                v_kvq_prev = head_loads(h + 1)

            ph.wait("tensor", dkh[hp], v_kvq)
            prev_tail = [0, 0]
            pend = None
            evac_pend = {}
            prev_recip = 0
            prev_lcopy = 0
            pend_lmm = 0
            for g in range(4):
                if pend is not None:
                    gg = pend
                    ph.wait("scalar", s_pe, pend_lmm)
                    v_lc = ph.run("scalar", "copy", rinv[:],
                                  ps_a[0:1, 512:1024], inc=s_act)
                    prev_lcopy = v_lc
                    ph.wait("tensor", s_act, v_lc)
                    ph.wait("tensor", s_dve, prev_recip)
                    v_bm = ph.run("tensor", "matmul", ps_a[:, 1536:2048],
                                  ones_row_f[:], rinv[:], start=True,
                                  stop=True, inc=s_pe)
                    ph.wait("vector", s_pe, v_bm)
                    if evac_pend.get(gg - 2) is not None:
                        ph.wait("vector", s_dve, evac_pend[gg - 2])
                    v_rc = ph.run("vector", "reciprocal", rb_sb[gg % 2][:],
                                  ps_a[:, 1536:2048], inc=s_dve)
                    prev_recip = v_rc
                    ph.wait("vector", s_dve, v_rc)
                    evac_pend[gg] = ph.run(
                        "vector", "tensor_tensor",
                        combined[h][:, gg * 512:(gg + 1) * 512],
                        ps_a[:, 1024:1536] if gg % 2 == 0 else ps_a[:, 0:512],
                        rb_sb[gg % 2][:], ALU.mult, inc=s_dve)
                exp_vals = []
                for pr in range(16):
                    plt = ps_lt[pr % 2]
                    if pr >= 2:
                        ph.wait("tensor", s_act, exp_vals[pr - 2])
                    else:
                        ph.wait("tensor", s_act, prev_tail[pr])
                    for sub in range(2):
                        kc = pr * 2 + sub
                        v_mm = ph.run("tensor", "matmul",
                                      plt[:, sub * 512:(sub + 1) * 512],
                                      kht[:, kc * 128:(kc + 1) * 128],
                                      qht[:, g * 512:(g + 1) * 512],
                                      start=True, stop=True,
                                      inc=s_pe if sub == 1 else None)
                    if pr == 2:
                        ph.wait("tensor", s_act, prev_lcopy)
                    if pr >= 2:
                        for sub in range(2):
                            kc = (pr - 2) * 2 + sub
                            ph.run("tensor", "matmul", ps_a[0:1, 512:1024],
                                   ones_col_bf[:],
                                   pbuf[:, kc * 512:(kc + 1) * 512],
                                   start=(kc == 0), stop=False)
                    ph.wait("scalar", s_pe, v_mm)
                    ph.wait("scalar", s_act, v_negc)
                    ev = ph.run("scalar", "activation",
                                pbuf[:, pr * 1024:(pr + 1) * 1024], plt[:],
                                AF.Exp, bias=negc_cols[:, h:h + 1],
                                scale=INV_SQRT_HD, inc=s_act)
                    exp_vals.append(ev)
                prev_tail = [exp_vals[14], exp_vals[15]]
                ph.wait("tensor", s_act, exp_vals[15])
                v_l = None
                for kc in range(28, 32):
                    v_l = ph.run("tensor", "matmul", ps_a[0:1, 512:1024],
                                 ones_col_bf[:],
                                 pbuf[:, kc * 512:(kc + 1) * 512],
                                 start=False, stop=(kc == 31),
                                 inc=s_pe if kc == 31 else None)
                pso = ps_a[:, 1024:1536] if g % 2 == 0 else ps_a[:, 0:512]
                if evac_pend.get(g - 2) is not None:
                    ph.wait("tensor", s_dve, evac_pend[g - 2])
                if g == 0 and h == 0:
                    ph.wait("tensor", s_act, v_negc)
                for kc in range(32):
                    v_pv = ph.run("tensor", "matmul", pso,
                                  vht[:, kc * 128:(kc + 1) * 128],
                                  pbuf[:, kc * 512:(kc + 1) * 512],
                                  start=(kc == 0), stop=(kc == 31),
                                  inc=s_pe if kc == 31 else None)
                pend = g
                pend_lmm = v_l
            gg = pend
            ph.wait("scalar", s_pe, pend_lmm)
            v_lc = ph.run("scalar", "copy", rinv[:], ps_a[0:1, 512:1024],
                          inc=s_act)
            ph.wait("tensor", s_act, v_lc)
            ph.wait("tensor", s_dve, prev_recip)
            v_bm = ph.run("tensor", "matmul", ps_a[:, 1536:2048],
                          ones_row_f[:], rinv[:], start=True, stop=True,
                          inc=s_pe)
            ph.wait("vector", s_pe, v_bm)
            if evac_pend.get(gg - 2) is not None:
                ph.wait("vector", s_dve, evac_pend[gg - 2])
            v_rc = ph.run("vector", "reciprocal", rb_sb[gg % 2][:],
                          ps_a[:, 1536:2048], inc=s_dve)
            ph.wait("vector", s_dve, v_rc)
            ph.run("vector", "tensor_tensor",
                   combined[h][:, gg * 512:(gg + 1) * 512],
                   ps_a[:, 1024:1536] if gg % 2 == 0 else ps_a[:, 0:512],
                   rb_sb[gg % 2][:], ALU.mult, inc=s_dve)
            w1_mean_piece(h * 4)
            for pp in range(1, 4):
                w1_mean_piece(h * 4 + pp)

        ph.wait("tensor", s_dve, s_dve.n)
        ph.wait("tensor", s_pool, s_pool.n)
        v = ph.run("tensor", "matmul", ps_a[0:1, 0:1], ones_col_f[:], w1acc[:],
                   start=True, stop=True, inc=s_pe)
        ph.wait("vector", s_pe, v)
        v = ph.runs("vector", s_dve, "tensor_scalar", tiny[:1, 7:8],
                    ps_a[0:1, 0:1], 1.0 / 16777216.0, EPS, ALU.mult, ALU.max)
        ph.wait("tensor", s_dve, v)
        v = ph.run("tensor", "matmul", ps_a[:, 0:1], ones_row_f[:],
                   tiny[:1, 7:8], start=True, stop=True, inc=s_pe)
        ph.wait("scalar", s_pe, v)
        v = ph.run("scalar", "copy", mw1_col[:], ps_a[:, 0:1], inc=s_act)
        ph.wait("vector", s_act, v)
        v = ph.run("vector", "reciprocal", sw1_col[:], mw1_col[:], inc=s_dve)
        ph.wait("vector", s_dve, v)
        ph.emit()
        st.close()

    if UNTIL >= 4:
        st = contextlib.ExitStack()
        wof = [sb(st, f"p4_wof{i}", [128, 4096], F32) for i in range(2)]
        wob = [sb(st, f"p4_wob{hc}", [128, 2048], BF16) for hc in range(16)]
        xtt = [sb(st, f"p4_x{i}", [128, D], F32) for i in range(2)]
        x2t = [sb(st, f"p4_x2{i}", [128, D], F32) for i in range(2)]
        z1t = sb(st, "p4_z1", [128, D], BF16)
        x2z = sb(st, "p4_x2z", [128, D], F32)
        ph = Phase(nc)
        phase_entry(ph)
        wof_used = [0, 0]
        wob_ready = []
        for jp in range(8):
            wj = jp % 2
            ph.wait("sync", s_pool, wof_used[wj])
            src = bass.AP(wout, jp * 256, [[D, 128], [128 * D, 16], [1, 256]])
            v_w = ph.dma(dw[wj], wof[wj][:], src)
            ph.wait("gpsimd", dw[wj], v_w)
            for hc in range(16):
                inc = s_pool if hc == 15 else None
                v_c = ph.run("gpsimd", "tensor_copy",
                             wob[hc][:, jp * 256:(jp + 1) * 256],
                             wof[wj][:, hc * 256:(hc + 1) * 256], inc=inc)
            wof_used[wj] = v_c
            wob_ready.append(v_c)
        x2_stored = [0, 0]
        z1_stored = 0
        srow_all = tiny[:, 8:24]
        ssq_all = s1_cols
        mx_all = a2cols_tmp = None
        mx_all = sb(st, "p4_mx", [128, 16], F32)
        mn_all = sb(st, "p4_mn", [128, 16], F32)
        nmu_all = sb(st, "p4_nmu", [128, 16], F32)
        var_all = sb(st, "p4_var", [128, 16], F32)
        rstd_all = sb(st, "p4_rstd", [128, 16], F32)
        rs1_all = sb(st, "p4_rs1", [128, 16], F32)
        bias_all = sb(st, "p4_bias", [128, 16], F32)
        inv = 1.0 / D

        def p4_finalize(c0, c1):
            ph.wait("vector", s_dve, s_dve.n)
            ph.wait("vector", s_act, s_act.n)
            ph.runs("vector", s_dve, "tensor_scalar", nmu_all[:, c0:c1],
                    srow_all[:, c0:c1], -inv, None, ALU.mult)
            ph.runs("vector", s_dve, "tensor_tensor", var_all[:, c0:c1],
                    nmu_all[:, c0:c1], nmu_all[:, c0:c1], ALU.mult)
            v = ph.runs("vector", s_dve, "scalar_tensor_tensor",
                        var_all[:, c0:c1], ssq_all[:, c0:c1], inv,
                        var_all[:, c0:c1], ALU.mult, ALU.subtract)
            ph.wait("scalar", s_dve, v)
            v_sd = ph.run("scalar", "activation", var_all[:, c0:c1],
                          var_all[:, c0:c1], AF.Sqrt, bias=eps_col[:],
                          inc=s_act)
            ph.wait("vector", s_act, v_sd)
            v_r = ph.run("vector", "reciprocal", rstd_all[:, c0:c1],
                         var_all[:, c0:c1], inc=s_dve)
            ph.wait("vector", s_dve, v_r)
            ph.runs("vector", s_dve, "tensor_tensor", mx_all[:, c0:c1],
                    mx_all[:, c0:c1], nmu_all[:, c0:c1], ALU.add)
            ph.runs("vector", s_dve, "scalar_tensor_tensor", mn_all[:, c0:c1],
                    mn_all[:, c0:c1], -1.0, nmu_all[:, c0:c1],
                    ALU.mult, ALU.subtract)
            ph.runs("vector", s_dve, "tensor_tensor", mx_all[:, c0:c1],
                    mx_all[:, c0:c1], mn_all[:, c0:c1], ALU.max)
            ph.runs("vector", s_dve, "tensor_tensor", mx_all[:, c0:c1],
                    mx_all[:, c0:c1], rstd_all[:, c0:c1], ALU.mult)
            ph.runs("vector", s_dve, "tensor_scalar", mx_all[:, c0:c1],
                    mx_all[:, c0:c1], EPS, None, ALU.max)
            v_r = ph.run("vector", "reciprocal", rs1_all[:, c0:c1],
                         mx_all[:, c0:c1], inc=s_dve)
            ph.wait("vector", s_dve, v_r)
            ph.runs("vector", s_dve, "tensor_scalar", s1_cols[:, c0:c1],
                    rs1_all[:, c0:c1], 127.0, None, ALU.mult)
            ph.runs("vector", s_dve, "tensor_scalar", al1_cols[:, c0:c1],
                    mx_all[:, c0:c1], mw1_col[:], 1.0 / 127.0,
                    ALU.mult, ALU.mult)
            ph.runs("vector", s_dve, "tensor_tensor", rs1_all[:, c0:c1],
                    rstd_all[:, c0:c1], s1_cols[:, c0:c1], ALU.mult)
            ph.runs("vector", s_dve, "tensor_tensor", bias_all[:, c0:c1],
                    nmu_all[:, c0:c1], rs1_all[:, c0:c1], ALU.mult)

        x2z_used = 0

        def p4_z1_one(tc2):
            nonlocal x2z_used, z1_stored
            ph.wait("sync", dst[tc2 % 2], dst[tc2 % 2].n)
            ph.wait("sync", s_dve, x2z_used)
            v_xz = ph.dma(dx[4], x2z[:], x2s[tc2 * 128:(tc2 + 1) * 128, :])
            ph.wait("vector", dx[4], v_xz)
            ph.wait("vector", dz1, z1_stored)
            v_z = ph.custom(ROUND_AFF, out=z1t[:], in0=x2z[:],
                            s0=rs1_all[:, tc2:tc2 + 1],
                            s1=bias_all[:, tc2:tc2 + 1],
                            imm2=MAGIC, inc=s_dve)
            x2z_used = v_z
            ph.wait("sync", s_dve, v_z)
            dstap = bass.AP(z1s, tc2 * 128 * 128,
                            [[128, 128], [T * 128, 16], [1, 128]])
            z1_stored = ph.dma(dz1, dstap, z1t[:])

        x2add = {}
        sq_done = [0, 0]
        for tc in range(16):
            j = tc % 2
            ph.wait("sync", s_dve, x2add.get(tc - 2, 0))
            v_x = ph.dma(dx[j], xtt[j][:], xin[tc * 128:(tc + 1) * 128, :])
            for of_t in range(4):
                if tc == 0:
                    ph.wait("tensor", s_pool, wob_ready[2 * of_t + 1])
                for hc in range(16):
                    inc = s_pe if hc == 15 else None
                    v_mm = ph.run("tensor", "matmul",
                                  ps_a[:, of_t * 512:(of_t + 1) * 512],
                                  combined[hc][:, tc * 128:(tc + 1) * 128],
                                  wob[hc][:, of_t * 512:(of_t + 1) * 512],
                                  start=(hc == 0), stop=(hc == 15), inc=inc)
            ph.wait("vector", dst[j], x2_stored[j])
            ph.wait("vector", s_pe, v_mm)
            ph.wait("vector", dx[j], v_x)
            ph.wait("vector", s_act, sq_done[j])
            v_x2 = ph.run("vector", "tensor_tensor", x2t[j][:], ps_a[:],
                          xtt[j][:], ALU.add, inc=s_dve)
            x2add[tc] = v_x2
            ph.wait("tensor", s_dve, v_x2)
            ph.wait("sync", s_dve, v_x2)
            x2_stored[j] = ph.dma(dst[j], x2s[tc * 128:(tc + 1) * 128, :],
                                  x2t[j][:])
            ph.wait("vector", s_dve, v_x2)
            ph.run("vector", "reduce_sum", srow_all[:, tc:tc + 1], x2t[j][:],
                   AX.X)
            ph.run("vector", "tensor_reduce", mx_all[:, tc:tc + 1], x2t[j][:],
                   AX.X, ALU.max)
            v_red = ph.run("vector", "tensor_reduce", mn_all[:, tc:tc + 1],
                           x2t[j][:], AX.X, ALU.min, inc=s_dve)
            ph.wait("scalar", s_dve, v_red)
            ph.wait("scalar", dst[j], dst[j].n)
            v_sq = ph.run("scalar", "activation", x2t[j][:], x2t[j][:],
                          AF.Square, accum_out=ssq_all[:, tc:tc + 1],
                          inc=s_act)
            sq_done[j] = v_sq
            if tc % 4 == 3:
                p4_finalize(tc - 3, tc + 1)
            if tc >= 4:
                p4_z1_one(tc - 4)
        for tc2 in range(12, 16):
            p4_z1_one(tc2)
        ph.emit()
        st.close()
    comb_st.close()

    if UNTIL >= 5:
        st = contextlib.ExitStack()
        z1_fm = [sb(st, f"z1fm{dc}", [128, T], BF16) for dc in range(16)]
        w1f = [sb(st, f"p6_w1f{i}", [128, 4096], F32) for i in range(2)]
        w1q = [sb(st, f"p6_w1q{i}", [128, 8192], BF16) for i in range(2)]
        gst = [sb(st, f"p6_gst{i}", [128, 8192], F16) for i in range(2)]
        gred = sb(st, "p6_gred", [128, 16], F32)
        gtmp = [sb(st, f"p6_g{i}", [128, 512], F32) for i in range(2)]
        w2m = sb(st, "p6_w2m", [128, 2048], F32)
        w2ab = sb(st, "p6_w2ab", [128, 2048], BF16)
        ph = Phase(nc)
        phase_entry(ph)
        for dc in range(16):
            src = bass.AP(z1s, dc * T * 128, [[128, T], [1, 128]])
            ph.dma(dfm[0], z1_fm[dc][:], src, transpose=True)
        ph.wait("tensor", dfm[0], dfm[0].n)

        w2m_used = 0

        def w2_mean_piece(i):
            nonlocal w2m_used
            ph.wait("sync", s_act, w2m_used)
            v_w = ph.dma(dwm[0], w2m[:], w2t[i * 128:(i + 1) * 128, :])
            ph.wait("scalar", dwm[0], v_w)
            ph.wait("scalar", s_act, w2m_used)
            ph.wait("scalar", s_dve, s_dve.n)
            va = ph.run("scalar", "activation", w2ab[:], w2m[:], AF.Abs,
                        accum_out=tiny[:, 6:7], inc=s_act)
            w2m_used = va
            ph.wait("vector", s_act, va)
            ph.runs("vector", s_dve, "tensor_tensor", w2acc[:], w2acc[:],
                    tiny[:, 6:7], ALU.add)

        w1q_ready = [0, 0]
        w1q_used = [0, 0]
        gst_stored = [0, 0]
        gdone = {}
        gcopy = {}
        gi = 0

        w1f_free = [0, 0]

        def w1_load_half(hs2, half):
            src = bass.AP(w1t, hs2 * 512 + half * 256,
                          [[H, 128], [128 * H, 16], [1, 256]])
            ph.wait("sync", s_dve, w1f_free[half])
            return ph.dma(dw[half], w1f[half][:], src)

        def w1_quant_q(hs2, half, q, v_w):
            j2 = hs2 % 2
            ph.wait("vector", dw[half], v_w)
            ph.wait("vector", s_pe, w1q_used[j2])
            fsl = w1f[half][:, q * 2048:(q + 1) * 2048]
            v_rnd = ph.custom(ROUND_TS, out=fsl, in0=fsl,
                              s0=sw1_col[:], s1=MAGIC, inc=s_dve)
            ph.wait("vector", s_dve, v_rnd)
            w1q_dst = bass.AP(w1q[j2], half * 256 + q * 8 * 512,
                              [[8192, 128], [512, 8], [1, 256]])
            v_q = ph.run("vector", "tensor_scalar", w1q_dst, fsl,
                         -1.0, 1.0, ALU.max, ALU.min, inc=s_dve)
            if q == 1:
                w1f_free[half] = v_q
            w1q_ready[j2] = v_q
            return v_q

        vw_next = [w1_load_half(0, 0), w1_load_half(0, 1)]
        for _half in range(2):
            for _q in range(2):
                w1_quant_q(0, _half, _q, vw_next[_half])
        for hs in range(16):
            j = hs % 2
            ph.wait("tensor", s_dve, w1q_ready[j])
            ph.wait("gpsimd", dgst[hs % 2], gst_stored[hs % 2])
            redv = {}
            for tc in range(16):
                slot = gi % 4
                plt = ps_lt[slot // 2]
                ps_ap = plt[:, (slot % 2) * 512:(slot % 2 + 1) * 512]
                if gdone.get(gi - 4):
                    ph.wait("tensor", s_act, gdone[gi - 4])
                for dc in range(16):
                    inc = s_pe if dc == 15 else None
                    v_mm = ph.run("tensor", "matmul", ps_ap,
                                  z1_fm[dc][:, tc * 128:(tc + 1) * 128],
                                  w1q[j][:, dc * 512:(dc + 1) * 512],
                                  start=(dc == 0), stop=(dc == 15), inc=inc)
                gj = tc % 2
                ph.wait("scalar", s_pe, v_mm)
                if gcopy.get(gi - 2):
                    ph.wait("scalar", s_pool, gcopy[gi - 2])
                if redv.get(tc - 2):
                    ph.wait("scalar", s_dve, redv[tc - 2])
                v_g = ph.run("scalar", "activation", gtmp[gj][:], ps_ap,
                             AF.Gelu_apprx_tanh, scale=al1_cols[:, tc:tc + 1],
                             inc=s_act)
                gdone[gi] = v_g
                ph.wait("vector", s_act, v_g)
                v_red = ph.run("vector", "tensor_reduce", gred[:, tc:tc + 1],
                               gtmp[gj][:], AX.X, ALU.max, True, inc=s_dve)
                redv[tc] = v_red
                ph.wait("gpsimd", s_act, v_g)
                gcopy[gi] = ph.run("gpsimd", "tensor_copy",
                                   gst[hs % 2][:, tc * 512:(tc + 1) * 512],
                                   gtmp[gj][:], inc=s_pool)
                gi += 1
                if hs + 1 < 16:
                    if tc == 1:
                        vw_next[0] = w1_load_half(hs + 1, 0)
                    elif tc == 2:
                        vw_next[1] = w1_load_half(hs + 1, 1)
                    elif tc in (5, 7):
                        w1_quant_q(hs + 1, 0, 0 if tc == 5 else 1, vw_next[0])
                    elif tc in (9, 11):
                        w1_quant_q(hs + 1, 1, 0 if tc == 9 else 1, vw_next[1])
            w1q_used[j] = v_mm
            ph.wait("vector", s_dve, v_red)
            ph.runs("vector", s_dve, "tensor_tensor", a2_cols[:], a2_cols[:],
                    gred[:], ALU.max)
            ph.wait("sync", s_pool, gcopy[gi - 1])
            for hc4 in range(4):
                dstap = bass.AP(gs, (hs * 4 + hc4) * T * 128,
                                [[128, 128], [128 * 128, 16], [1, 128]])
                srcap = bass.AP(gst[hs % 2], hc4 * 128,
                                [[8192, 128], [512, 16], [1, 128]])
                gst_stored[hs % 2] = ph.dma(dgst[hs % 2], dstap, srcap)
            for pp in range(4):
                w2_mean_piece(hs * 4 + pp)

        ph.wait("tensor", s_dve, s_dve.n)
        ph.wait("tensor", s_act, s_act.n)
        v = ph.run("tensor", "matmul", ps_a[0:1, 0:1], ones_col_f[:], w2acc[:],
                   start=True, stop=True, inc=s_pe)
        ph.wait("vector", s_pe, v)
        v = ph.runs("vector", s_dve, "tensor_scalar", tiny[:1, 7:8],
                    ps_a[0:1, 0:1], 1.0 / 16777216.0, EPS, ALU.mult, ALU.max)
        ph.wait("tensor", s_dve, v)
        v = ph.run("tensor", "matmul", ps_a[:, 0:1], ones_row_f[:],
                   tiny[:1, 7:8], start=True, stop=True, inc=s_pe)
        ph.wait("scalar", s_pe, v)
        v = ph.run("scalar", "copy", mw2_col[:], ps_a[:, 0:1], inc=s_act)
        ph.wait("vector", s_act, v)
        v = ph.run("vector", "reciprocal", sw2_col[:], mw2_col[:], inc=s_dve)
        ph.wait("vector", s_dve, v)
        a2c = sb(st, "p6_a2c", [128, 16], F32)
        s2c = sb(st, "p6_s2c", [128, 16], F32)
        ph.runs("vector", s_dve, "tensor_scalar", a2c[:], a2_cols[:],
                EPS, None, ALU.max)
        v_r = ph.run("vector", "reciprocal", s2c[:], a2c[:], inc=s_dve)
        ph.wait("vector", s_dve, v_r)
        ph.runs("vector", s_dve, "tensor_scalar", s2c[:], s2c[:],
                127.0, None, ALU.mult)
        vv = ph.runs("vector", s_dve, "tensor_scalar", al2_cols[:], a2c[:],
                     mw2_col[:], 1.0 / 127.0, ALU.mult, ALU.mult)
        ph.wait("sync", s_dve, vv)
        for tc in range(16):
            ph.dma(ds2[tc % 2], s2s[0:1, tc * 128:(tc + 1) * 128],
                   s2c[:, tc:tc + 1])
        ph.emit()
        st.close()

    if UNTIL >= 6:
        st = contextlib.ExitStack()
        s2row = sb(st, "p7_s2row", [1, T], F32)
        s2b = sb(st, "p7_s2b", [128, T], F32)
        acc = [sb(st, f"p7_acc{i}", [128, D], F32) for i in range(8)]
        gfm = [sb(st, f"p7_gfm{i}", [128, 1024], F16) for i in range(2)]
        z2fm = [sb(st, f"p7_z2{i}", [128, 1024], BF16) for i in range(16)]
        w2f = [sb(st, f"p7_w2f{i}", [128, 2048], F32) for i in range(2)]
        w2q = [sb(st, f"p7_w2q{i}", [128, 2048], BF16) for i in range(16)]
        x2tt = [sb(st, "p8_x2", [128, D], F32)] * 2
        ot = x2tt
        ph = Phase(nc)
        phase_entry(ph)
        v_s2 = ph.dma(dwm[0], s2row[:], s2s[:])
        ph.wait("tensor", dwm[0], v_s2)
        for t in range(4):
            v = ph.run("tensor", "matmul", ps_a[:, t * 512:(t + 1) * 512],
                       ones_row_f[:], s2row[:, t * 512:(t + 1) * 512],
                       start=True, stop=True, inc=s_pe if t == 3 else None)
        ph.wait("scalar", s_pe, v)
        v_sb = ph.run("scalar", "copy", s2b[:], ps_a[:], inc=s_act)
        ph.wait("vector", s_act, v_sb)
        ph.wait("tensor", s_act, v_sb)

        stages = [(hf, hsl) for hf in range(2) for hsl in range(8)]
        set_ready = [0, 0]
        set_mm_done = [0, 0]
        gfm_used = [0, 0]
        w2f_used = [0, 0]
        pacc_lt = [ps_lt0[:, 0:512], ps_lt0[:, 512:1024],
                   ps_lt1[:, 0:512], ps_lt1[:, 512:1024]]

        def quant_chunk(stage_i, k8):
            hf, hsl = stages[stage_i]
            p = stage_i % 2
            hc = hsl * 8 + k8
            j = (stage_i * 8 + k8) % 2
            ph.wait("sync", s_dve, max(gfm_used[j], w2f_used[j]))
            ph.wait("sync", s_pe, set_mm_done[p])
            src = bass.AP(gs, hc * T * 128 + hf * 1024 * 128,
                          [[128, 1024], [1, 128]])
            v_g = ph.dma(dfm[j], gfm[j][:], src, transpose=True)
            v_w = ph.dma(dw[j], w2f[j][:], w2t[hc * 128:(hc + 1) * 128, :])
            ph.wait("vector", dfm[j], v_g)
            ph.wait("vector", s_pe, set_mm_done[p])
            tok = slice(hf * 1024, (hf + 1) * 1024)
            gfm_used[j] = ph.custom(ROUND_TT, out=z2fm[p * 8 + k8][:],
                                    in0=gfm[j][:], in1=s2b[:, tok], s0=MAGIC,
                                    inc=s_dve)
            ph.wait("vector", dw[j], v_w)
            w2f_used[j] = ph.custom(ROUND_CLIP, out=w2q[p * 8 + k8][:],
                                    in0=w2f[j][:], s0=sw2_col[:], s1=MAGIC,
                                    inc=s_dve)
            set_ready[p] = w2f_used[j]

        out_stored = [0, 0]
        x2_used = [0, 0]
        evac_vals = {}
        for k8 in range(8):
            quant_chunk(0, k8)
        for si, (hf, hsl) in enumerate(stages):
            p = si % 2
            ph.wait("tensor", s_dve, set_ready[p])
            for tc8 in range(8):
                par = tc8 % 2
                if evac_vals.get(tc8 - 2) is not None:
                    ph.wait("tensor", s_dve, evac_vals[tc8 - 2])
                for k8 in range(8):
                    for nt in range(4):
                        pac = ps_a[:, nt * 512:(nt + 1) * 512] if par == 0                             else pacc_lt[nt]
                        inc = s_pe if (k8 == 7 and nt == 3) else None
                        v_mm = ph.run(
                            "tensor", "matmul", pac,
                            z2fm[p * 8 + k8][:, tc8 * 128:(tc8 + 1) * 128],
                            w2q[p * 8 + k8][:, nt * 512:(nt + 1) * 512],
                            start=(k8 == 0), stop=(k8 == 7), inc=inc)
                ph.wait("vector", s_pe, v_mm)
                first = (hsl == 0)
                if par == 0:
                    if first:
                        v_e = ph.run("vector", "tensor_copy", acc[tc8][:],
                                     ps_a[:], inc=s_dve)
                    else:
                        v_e = ph.run("vector", "tensor_tensor", acc[tc8][:],
                                     acc[tc8][:], ps_a[:], ALU.add, inc=s_dve)
                else:
                    for hh in range(2):
                        m, a_sl = (ps_lt[hh][:],
                                   acc[tc8][:, hh * 1024:(hh + 1) * 1024])
                        if first:
                            v_e = ph.run("vector", "tensor_copy", a_sl, m,
                                         inc=s_dve)
                        else:
                            v_e = ph.run("vector", "tensor_tensor", a_sl,
                                         a_sl, m, ALU.add, inc=s_dve)
                evac_vals[tc8] = v_e
                if si + 1 < len(stages):
                    quant_chunk(si + 1, tc8)
            set_mm_done[p] = v_mm
            if hsl == 7:
                ph.wait("vector", s_dve, s_dve.n)
                for tc8 in range(8):
                    tc = hf * 8 + tc8
                    ph.wait("sync", s_dve, x2_used[0])
                    ph.wait("sync", dst[0], out_stored[0])
                    v_x2 = ph.dma(dx[0], x2tt[0][:],
                                  x2s[tc * 128:(tc + 1) * 128, :])
                    ph.wait("vector", dx[0], v_x2)
                    v_o = ph.run("vector", "scalar_tensor_tensor", ot[0][:],
                                 acc[tc8][:], al2_cols[:, tc:tc + 1],
                                 x2tt[0][:], ALU.mult, ALU.add, inc=s_dve)
                    x2_used[0] = v_o
                    ph.wait("sync", s_dve, v_o)
                    out_stored[0] = ph.dma(dst[0],
                                           out[tc * 128:(tc + 1) * 128, :],
                                           ot[0][:])
                evac_vals = {}
        for l in store_lanes:
            ph.wait("sync", l, l.n)
        ph.emit()
        st.close()

    mybir.codegen_inst_isa_subclasses(nc)
    octx.close()
    return nc


def kernel(L0_attn_x, L0_attn_w_qkv, L0_attn_w_out, L0_ffn_w1, L0_ffn_w2):
    x = np.ascontiguousarray(np.asarray(L0_attn_x, dtype=np.float32))
    wqkv = np.ascontiguousarray(np.asarray(L0_attn_w_qkv, dtype=np.float32))
    wout = np.ascontiguousarray(np.asarray(L0_attn_w_out, dtype=np.float32))
    w1t = np.ascontiguousarray(np.asarray(L0_ffn_w1, dtype=np.float32).T)
    w2t = np.ascontiguousarray(np.asarray(L0_ffn_w2, dtype=np.float32).T)

    nc = _build()
    in_maps = []
    for c in range(8):
        b, half = c // 2, c % 2
        own = x[b, half * T:(half + 1) * T]
        other = x[b, (1 - half) * T:(2 - half) * T]
        xc = np.ascontiguousarray(np.concatenate([own, other], axis=0))
        in_maps.append({"xin": xc, "wqkv": wqkv, "wout": wout,
                        "w1t": w1t, "w2t": w2t})
    res = run_bass_kernel_spmd(nc, in_maps, core_ids=list(range(8)))
    o = np.empty((4, 4096, 2048), dtype=np.float32)
    for c in range(8):
        b, half = c // 2, c % 2
        o[b, half * T:(half + 1) * T] = res.results[c]["out"]
    return o
